# version 10
# baseline (speedup 1.0000x reference)
import os
import numpy as np
import ml_dtypes
import concourse.bass as bass
import concourse.mybir as mybir
from concourse.bass_utils import run_bass_kernel_spmd
from contextlib import ExitStack

F32 = mybir.dt.float32
BF16 = mybir.dt.bfloat16
I32 = mybir.dt.int32
AF = mybir.ActivationFunctionType
ALU = mybir.AluOpType
AX = mybir.AxisListType
BF = ml_dtypes.bfloat16

B, T, D = 8, 4096, 1024
NT = T // 128
NDC = D // 128
NEG = -1.0e30
NCORES = 8



class Ctr:
    _uid = 0
    created = []

    def __init__(self, nc, name=None):
        Ctr._uid += 1
        self._cm = nc.semaphore(f"{name or 'c'}_{Ctr._uid}")
        self.sem = self._cm.__enter__()
        self.n = 0
        Ctr.created.append(self)


def recycle_sems(nc, mark):
    ctrs = Ctr.created[mark:]
    if not ctrs:
        return
    nc.clear_and_free_semaphores([c.sem for c in ctrs])
    for c in ctrs:
        c._cm = None
    del Ctr.created[mark:]


ENGS = ("tensor", "vector", "scalar", "gpsimd", "sync")


class Sched:
    _breg_uid = 0
    """Collects per-engine thunks with schedule-time semaphore values; flush()
    emits them inside one nc.Block() (all-engine barrier at entry and exit).

    vector / scalar / gpsimd-compute ops are auto-serialized with a per-engine
    semaphore chain (wait for the previous op's WRITEBACK before issuing): the
    device does not order same-engine RAW/WAW through the queue alone.
    Completion value of the last op on engine e: self.v/.a/.g Ctr .n."""

    PARANOID = bool(os.environ.get("SCHED_PARANOID"))

    def __init__(self, nc):
        self.nc = nc
        self.ops = {e: [] for e in ENGS}
        self.lastw = {e: {} for e in ENGS}
        self.v = Ctr(nc, "sv")
        self.a = Ctr(nc, "sa")
        self.g = Ctr(nc, "sg")
        self._auto = {"vector": self.v, "scalar": self.a, "gpsimd": self.g}
        self._breg_vals = []
        self._breg = {}
        self._allctrs = [self.v, self.a, self.g]

    def bnd(self, v):
        v = int(v)
        if v not in self._breg_vals:
            self._breg_vals.append(v)
        return v

    def op(self, eng, emit, wait=(), inc=None, inck=1, dma=False):
        wl = []
        auto = self._auto.get(eng)
        if inc is not None and inc not in self._allctrs:
            self._allctrs.append(inc)
        if self.PARANOID:
            wait = list(wait) + [(c, c.n) for c in self._allctrs]
        elif auto is not None and not dma:
            wait = list(wait) + [(auto, auto.n)]
        for (c, v) in wait:
            if v > 0 and v > self.lastw[eng].get(id(c), -1):
                wl.append((c.sem, v))
                self.lastw[eng][id(c)] = v
        incs = []
        if inc is not None:
            inc.n += inck
            incs.append((inc.sem, inck))
        if auto is not None and not (dma and inc is not None):
            k = 16 if dma else 1
            auto.n += k
            incs.append((auto.sem, k))

        def f(e, emit=emit, wl=tuple(wl), incs=tuple(incs)):
            for (s, v) in wl:
                e.wait_ge(s, v)
            inst = emit(e)
            for (s, k) in incs:
                r = inst.then_inc(s, k)
                inst = r if r is not None else inst

        self.ops[eng].append(f)

    def flush(self):
        nc = self.nc
        if self._breg_vals:
            vals = list(self._breg_vals)
            cms = []

            def setup(e, vals=vals, cms=cms):
                for v in vals:
                    Sched._breg_uid += 1
                    cm = e.register(f"bnd{v}_{Sched._breg_uid}")
                    rb = cm.__enter__()
                    cms.append(cm)
                    e.reg_mov(rb, v)
                    self._breg[v] = rb

            def teardown(e, cms=cms):
                for cm in cms:
                    cm.__exit__(None, None, None)

            self.ops["gpsimd"] = ([lambda e: setup(e)] + self.ops["gpsimd"]
                                  + [lambda e: teardown(e)])
        with nc.Block() as blk:
            for eng in ENGS:
                fns = self.ops[eng]
                if not fns:
                    continue

                def runner(e, fns=tuple(fns)):
                    for f in fns:
                        f(e)

                getattr(blk, eng)(runner)
        self.ops = {e: [] for e in ENGS}
        self.lastw = {e: {} for e in ENGS}
        self._breg_vals = []
        self._breg = {}




def host_consts():
    i = np.arange(128)
    cmaskT = np.where(i[:, None] > i[None, :], NEG, 0.0).astype(BF)
    negfull = np.full((128, 128), NEG, dtype=BF)
    ident = np.eye(128, dtype=BF)
    identf = np.eye(128, dtype=np.float32)
    uincl = np.where(i[:, None] <= i[None, :], 1.0, 0.0).astype(np.float32)
    iota16 = np.broadcast_to(np.arange(16, dtype=np.float32), (128, 16)).copy()
    iota16b = iota16 + 1e6
    iotacol = np.arange(128, dtype=np.float32).reshape(128, 1)
    return {"c_ident_bf": ident, "c_cmaskT_bf": cmaskT, "c_negfull_bf": negfull,
            "c_ident_f": identf, "c_uincl_f": uincl, "c_iota16": iota16,
            "c_iota16b": iota16b, "c_iotacol": iotacol}


CONST_SHAPES = {"c_ident_bf": ([128, 128], "bf"), "c_cmaskT_bf": ([128, 128], "bf"),
                "c_negfull_bf": ([128, 128], "bf"), "c_ident_f": ([128, 128], "f"),
                "c_uincl_f": ([128, 128], "f"), "c_iota16": ([128, 16], "f"),
                "c_iota16b": ([128, 16], "f"), "c_iotacol": ([128, 1], "f")}


class Common:

    def __init__(self, nc, ctx):
        E = ctx.enter_context
        self.dr = {k: nc.dram_tensor(k, v[0], BF16 if v[1] == "bf" else F32,
                                     kind="ExternalInput")
                   for k, v in CONST_SHAPES.items()}
        self.idb = E(nc.sbuf_tensor("g_idb", [128, 128], BF16))
        self.cmb = E(nc.sbuf_tensor("g_cmb", [128, 128], BF16))
        self.nfb = E(nc.sbuf_tensor("g_nfb", [128, 128], BF16))
        self.idf = E(nc.sbuf_tensor("g_idf", [128, 128], F32))
        self.uincl = E(nc.sbuf_tensor("g_uincl", [128, 128], F32))
        self.iota16 = E(nc.sbuf_tensor("g_iota16", [128, 16], F32))
        self.iota16b = E(nc.sbuf_tensor("g_iota16b", [128, 16], F32))
        self.iotacol = E(nc.sbuf_tensor("g_iotacol", [128, 1], F32))
        self.ones_col_bf = E(nc.sbuf_tensor("g_ones_bf", [128, 1], BF16))
        self.ones_col_f = E(nc.sbuf_tensor("g_ones_f", [128, 1], F32))
        self.ones_row_f = E(nc.sbuf_tensor("g_onesrow_f", [1, 128], F32))
        self.dr_base = nc.dram_tensor("c_base", [1, 1], F32, kind="ExternalInput")

    def load(self, nc):
        s = Sched(nc)
        ld = Ctr(nc, "gld")
        for name, tile in [("c_ident_bf", self.idb), ("c_cmaskT_bf", self.cmb),
                           ("c_negfull_bf", self.nfb), ("c_ident_f", self.idf),
                           ("c_uincl_f", self.uincl), ("c_iota16", self.iota16),
                           ("c_iota16b", self.iota16b), ("c_iotacol", self.iotacol)]:
            s.op("sync", lambda e, n=name, t=tile: e.dma_start(t[:], self.dr[n].ap()),
                 inc=ld, inck=16, dma=True)
        s.op("vector", lambda e: e.memset(self.ones_col_bf[:], 1.0))
        s.op("vector", lambda e: e.memset(self.ones_col_f[:], 1.0))
        s.op("vector", lambda e: e.memset(self.ones_row_f[:], 1.0))
        s.op("sync", lambda e: e.wait_ge(ld.sem, ld.n))
        s.flush()


def emit_ln_tile(s, tmp, hb, ob, eps_tile, h_ready_v):
    dv, ac = s.v, s.a
    s.op("vector", lambda e: e.reduce_sum(tmp["s1"][:], hb[:], axis=AX.X),
         wait=[(dv, h_ready_v)])
    s1_done = dv.n
    s.op("scalar", lambda e: e.activation(tmp["sqs"][:], hb[:], AF.Square),
         wait=[(dv, h_ready_v)])
    sq_done = ac.n
    s.op("vector", lambda e: e.reduce_sum(tmp["s2"][:], tmp["sqs"][:], axis=AX.X),
         wait=[(ac, sq_done)])
    s2_done = dv.n
    s.op("vector", lambda e: e.tensor_scalar(tmp["mmean"][:], tmp["s1"][:], 1.0 / D, None,
                                             op0=ALU.mult))
    s.op("vector", lambda e: e.tensor_scalar(tmp["vvar"][:], tmp["s2"][:], 1.0 / D, None,
                                             op0=ALU.mult))
    s.op("vector", lambda e: e.tensor_tensor(tmp["s1"][:], tmp["mmean"][:], tmp["mmean"][:],
                                             ALU.mult))
    s.op("vector", lambda e: e.tensor_tensor(tmp["vvar"][:], tmp["vvar"][:], tmp["s1"][:],
                                             ALU.subtract))
    var_done = dv.n
    s.op("scalar", lambda e: e.activation(tmp["vvar"][:], tmp["vvar"][:], AF.Sqrt,
                                          bias=eps_tile[:, 0:1]),
         wait=[(dv, var_done)])
    sd_done = ac.n
    s.op("vector", lambda e: e.reciprocal(tmp["rstd"][:], tmp["vvar"][:]),
         wait=[(ac, sd_done)])
    s.op("vector", lambda e: e.tensor_scalar(ob[:], hb[:], tmp["mmean"][:, 0:1],
                                             tmp["rstd"][:, 0:1],
                                             op0=ALU.subtract, op1=ALU.mult))
    return dv.n




def emit_phase_a(nc, ctx, g, x0, wqT, wkT, wvT, woT, x1, tag="a", dbg=None):
    E = ctx.enter_context
    xbf = nc.dram_tensor(f"{tag}_xbf", [T, D], BF16, kind="Internal")
    qpark = nc.dram_tensor(f"{tag}_qpark", [D, T], BF16, kind="Internal")
    ctxT = nc.dram_tensor(f"{tag}_ctxT", [D, T], BF16, kind="Internal")

    K_sb = E(nc.sbuf_tensor(f"{tag}_K", [128, NDC * T], BF16))
    V_sb = E(nc.sbuf_tensor(f"{tag}_V", [128, NT * D], BF16))
    mq2 = E(nc.sbuf_tensor(f"{tag}_mq2", [1, 1], F32))
    mk2 = E(nc.sbuf_tensor(f"{tag}_mk2", [1, 1], F32))
    mqt = E(nc.sbuf_tensor(f"{tag}_mqt", [1, 1], F32))
    mkt = E(nc.sbuf_tensor(f"{tag}_mkt", [1, 1], F32))

    s = Sched(nc)
    ld = Ctr(nc, "ld")
    s.op("gpsimd", lambda e: e.dma_start(xbf.ap(), x0.ap()), inc=ld, inck=16, dma=True)
    s.op("gpsimd", lambda e: e.wait_ge(ld.sem, ld.n), dma=True, inc=ld, inck=0) if False else None
    s.op("sync", lambda e: e.wait_ge(ld.sem, ld.n))
    s.flush()

    with ExitStack() as c1:
        E1 = c1.enter_context
        wq_sb = E1(nc.sbuf_tensor(f"{tag}_wq", [128, NDC * D], BF16))
        wk_sb = E1(nc.sbuf_tensor(f"{tag}_wk", [128, NDC * D], BF16))
        xTc = [E1(nc.sbuf_tensor(f"{tag}_xTc{i}", [128, NDC * 512], BF16)) for i in range(2)]
        qstage = [E1(nc.sbuf_tensor(f"{tag}_qs{i}", [128, 512], BF16)) for i in range(2)]
        sqq = E1(nc.sbuf_tensor(f"{tag}_sqq", [128, 512], F32))
        sqk = E1(nc.sbuf_tensor(f"{tag}_sqk", [128, 512], F32))
        ps_k = E1(nc.psum_tensor(f"{tag}_pk", [128, 512], F32))
        ps_q = E1(nc.psum_tensor(f"{tag}_pq", [128, 512], F32))
        ps_nq = E1(nc.psum_tensor(f"{tag}_pnq", [1, 512], F32))
        ps_nk = E1(nc.psum_tensor(f"{tag}_pnk", [1, 512], F32))

        s = Sched(nc)
        dv, ac = s.v, s.a
        wld = Ctr(nc, "wld")
        xldp = [Ctr(nc, "xld0"), Ctr(nc, "xld1")]
        mm = Ctr(nc, "mm")
        nmm = Ctr(nc, "nmm")
        stq = [Ctr(nc, "stq0"), Ctr(nc, "stq1")]

        s.op("vector", lambda e: e.memset(mq2[:], 0.0))
        s.op("vector", lambda e: e.memset(mk2[:], 0.0))
        for i in range(NDC):
            s.op("gpsimd", lambda e, i=i: e.dma_start(
                wq_sb[:, i * D:(i + 1) * D], wqT.ap()[i * 128:(i + 1) * 128, :]),
                inc=wld, inck=16, dma=True)
            s.op("gpsimd", lambda e, i=i: e.dma_start(
                wk_sb[:, i * D:(i + 1) * D], wkT.ap()[i * 128:(i + 1) * 128, :]),
                inc=wld, inck=16, dma=True)
        wready = wld.n

        mm_after_tc = {}
        kevac = {}
        qevac = {}
        sqk_user = {}
        sqq_user = {}
        norm_free = 0
        for tc in range(T // 512):
            xb = xTc[tc % 2]
            xld = xldp[tc % 2]
            w = [(mm, mm_after_tc.get(tc - 2, 0))]
            for i in range(NDC):
                s.op("sync", lambda e, i=i, xb=xb, tc=tc: e.dma_start(
                    xb[:, i * 512:(i + 1) * 512],
                    xbf.ap()[tc * 512:(tc + 1) * 512, i * 128:(i + 1) * 128],
                    transpose=True), wait=w, inc=xld, inck=16, dma=True)
            xready = xld.n
            for ot in range(NDC):
                def kgrp(e, ot=ot, xb=xb):
                    for i in range(NDC):
                        inst = e.matmul(ps_k[:],
                                        wk_sb[:, i * D + ot * 128: i * D + (ot + 1) * 128],
                                        xb[:, i * 512:(i + 1) * 512],
                                        start=(i == 0), stop=(i == NDC - 1))
                    return inst
                s.op("tensor", kgrp,
                     wait=[(wld, wready), (xld, xready),
                           (ac, kevac.get((tc, ot - 1), kevac.get((tc - 1, NDC - 1), 0)))],
                     inc=mm)
                kd = mm.n
                s.op("scalar", lambda e, tc=tc, ot=ot: e.copy(
                    K_sb[:, ot * T + tc * 512: ot * T + (tc + 1) * 512], ps_k[:]),
                    wait=[(mm, kd)])
                kevac[(tc, ot)] = ac.n
                s.op("scalar", lambda e, tc=tc, ot=ot: e.activation(
                    sqk[:], K_sb[:, ot * T + tc * 512: ot * T + (tc + 1) * 512], AF.Square),
                    wait=[(nmm, sqk_user.get(ot - 1, 0))])
                sqkd = ac.n
                s.op("tensor", lambda e, ot=ot: e.matmul(
                    ps_nk[:], g.ones_col_f[:], sqk[:], start=(ot == 0), stop=(ot == NDC - 1)),
                    wait=[(ac, sqkd), (dv, norm_free)], inc=nmm)
                sqk_user[ot] = nmm.n

                def qgrp(e, ot=ot, xb=xb):
                    for i in range(NDC):
                        inst = e.matmul(ps_q[:],
                                        wq_sb[:, i * D + ot * 128: i * D + (ot + 1) * 128],
                                        xb[:, i * 512:(i + 1) * 512],
                                        start=(i == 0), stop=(i == NDC - 1))
                    return inst
                s.op("tensor", qgrp,
                     wait=[(ac, qevac.get((tc, ot - 1), qevac.get((tc - 1, NDC - 1), 0)))],
                     inc=mm)
                qd = mm.n
                qs = qstage[ot % 2]
                s.op("scalar", lambda e, qs=qs: e.copy(qs[:], ps_q[:]),
                     wait=[(mm, qd), (stq[ot % 2], stq[ot % 2].n)])
                qevac[(tc, ot)] = ac.n
                s.op("scalar", lambda e, qs=qs: e.activation(sqq[:], qs[:], AF.Square),
                     wait=[(nmm, sqq_user.get(ot - 1, 0))])
                sqqd = ac.n
                s.op("sync", lambda e, tc=tc, ot=ot, qs=qs: e.dma_start(
                    qpark.ap()[ot * 128:(ot + 1) * 128, tc * 512:(tc + 1) * 512], qs[:]),
                    wait=[(ac, sqqd)], inc=stq[ot % 2], inck=16, dma=True)
                s.op("tensor", lambda e, ot=ot: e.matmul(
                    ps_nq[:], g.ones_col_f[:], sqq[:], start=(ot == 0), stop=(ot == NDC - 1)),
                    wait=[(ac, sqqd), (dv, norm_free)], inc=nmm)
                sqq_user[ot] = nmm.n
            mm_after_tc[tc] = mm.n
            nval = nmm.n
            s.op("vector", lambda e: e.reduce_max(mkt[:], ps_nk[0:1, :], axis=AX.X),
                 wait=[(nmm, nval)])
            s.op("vector", lambda e: e.tensor_tensor(mk2[:], mk2[:], mkt[:], ALU.max))
            s.op("vector", lambda e: e.reduce_max(mqt[:], ps_nq[0:1, :], axis=AX.X))
            s.op("vector", lambda e: e.tensor_tensor(mq2[:], mq2[:], mqt[:], ALU.max))
            norm_free = dv.n
        s.op("sync", lambda e, a=stq[0].n: e.wait_ge(stq[0].sem, a))
        s.op("sync", lambda e, a=stq[1].n: e.wait_ge(stq[1].sem, a))
        s.flush()

    with ExitStack() as c1:
        E1 = c1.enter_context
        wv_sb = E1(nc.sbuf_tensor(f"{tag}_wv", [128, NDC * D], BF16))
        xTc = [E1(nc.sbuf_tensor(f"{tag}_bxTc{i}", [128, NDC * 512], BF16)) for i in range(2)]
        ps_v = [E1(nc.psum_tensor(f"{tag}_pv{i}", [128, 512], F32)) for i in range(2)]

        s = Sched(nc)
        ac = s.a
        wld = Ctr(nc, "wld")
        xldp = [Ctr(nc, "xld0"), Ctr(nc, "xld1")]
        mm = Ctr(nc, "mm")
        for i in range(NDC):
            s.op("gpsimd", lambda e, i=i: e.dma_start(
                wv_sb[:, i * D:(i + 1) * D], wvT.ap()[i * 128:(i + 1) * 128, :]),
                inc=wld, inck=16, dma=True)
        wready = wld.n
        mm_after_tc = {}
        evac = {}
        for tc in range(T // 512):
            xb = xTc[tc % 2]
            xld = xldp[tc % 2]
            w = [(mm, mm_after_tc.get(tc - 2, 0))]
            for i in range(NDC):
                s.op("sync", lambda e, i=i, xb=xb, tc=tc: e.dma_start(
                    xb[:, i * 512:(i + 1) * 512],
                    xbf.ap()[tc * 512:(tc + 1) * 512, i * 128:(i + 1) * 128],
                    transpose=True), wait=w, inc=xld, inck=16, dma=True)
            xready = xld.n
            for tj in range(4):
                tt = tc * 4 + tj
                for vc in range(2):
                    k = (tj * 2 + vc) % 2
                    pv = ps_v[k]

                    def vgrp(e, tj=tj, vc=vc, xb=xb, pv=pv):
                        for i in range(NDC):
                            inst = e.matmul(pv[:],
                                            xb[:, i * 512 + tj * 128: i * 512 + (tj + 1) * 128],
                                            wv_sb[:, i * D + vc * 512: i * D + (vc + 1) * 512],
                                            start=(i == 0), stop=(i == NDC - 1))
                        return inst
                    s.op("tensor", vgrp,
                         wait=[(wld, wready), (xld, xready), (ac, evac.get(k, 0))], inc=mm)
                    mval = mm.n
                    s.op("scalar", lambda e, tt=tt, vc=vc, pv=pv: e.copy(
                        V_sb[:, tt * D + vc * 512: tt * D + (vc + 1) * 512], pv[:]),
                        wait=[(mm, mval)])
                    evac[k] = ac.n
            mm_after_tc[tc] = mm.n
        s.flush()

    if dbg is not None:
        s = Sched(nc)
        st = Ctr(nc, "dbst")
        if "K" in dbg:
            s.op("sync", lambda e: e.dma_start(dbg["K"].ap(), K_sb[:]), inc=st, inck=16, dma=True)
        if "V" in dbg:
            s.op("sync", lambda e: e.dma_start(dbg["V"].ap(), V_sb[:]), inc=st, inck=16, dma=True)
        if "mq" in dbg:
            s.op("sync", lambda e: e.dma_start(dbg["mq"].ap()[0:1, 0:1], mq2[:]), inc=st, inck=16, dma=True)
            s.op("sync", lambda e: e.dma_start(dbg["mq"].ap()[0:1, 1:2], mk2[:]), inc=st, inck=16, dma=True)
        if "q" in dbg:
            s.op("gpsimd", lambda e: e.dma_start(dbg["q"].ap(), qpark.ap()), inc=st, inck=16, dma=True)
        s.op("sync", lambda e: e.wait_ge(st.sem, st.n))
        s.op("gpsimd", lambda e: e.wait_ge(st.sem, st.n), dma=True, inc=None)
        s.flush()

    with ExitStack() as c2:
        E2 = c2.enter_context
        PT = E2(nc.sbuf_tensor(f"{tag}_PT", [128, NT * 512], BF16))
        Qc = [E2(nc.sbuf_tensor(f"{tag}_Qc{i}", [128, NDC * 512], BF16)) for i in range(2)]
        zr = E2(nc.sbuf_tensor(f"{tag}_zr", [1, 512], F32))
        zb = E2(nc.sbuf_tensor(f"{tag}_zb", [128, 512], F32))
        negB = E2(nc.sbuf_tensor(f"{tag}_negB", [1, 1], F32))
        nbc = E2(nc.sbuf_tensor(f"{tag}_negBc", [128, 1], F32))
        sc1 = E2(nc.sbuf_tensor(f"{tag}_sc1", [1, 1], F32))
        sc2 = E2(nc.sbuf_tensor(f"{tag}_sc2", [1, 1], F32))
        ctstage = [E2(nc.sbuf_tensor(f"{tag}_cts{i}", [128, NDC * 512], BF16)) for i in range(2)]
        ps_s = [E2(nc.psum_tensor(f"{tag}_ps{i}", [128, 512], F32)) for i in range(2)]
        ps_z = E2(nc.psum_tensor(f"{tag}_pz", [1, 512], F32))
        ps_bc = E2(nc.psum_tensor(f"{tag}_pbc", [128, 512], F32))
        ps_ot = [E2(nc.psum_tensor(f"{tag}_pot{i}", [128, 512], F32)) for i in range(4)]

        s = Sched(nc)
        dv, ac = s.v, s.a
        qldp = [Ctr(nc, "qld0"), Ctr(nc, "qld1")]
        mm = Ctr(nc, "mm")
        zmm = Ctr(nc, "zmm")
        bmm = Ctr(nc, "bmm")
        pvm = Ctr(nc, "pvm")
        stc = [Ctr(nc, "stc0"), Ctr(nc, "stc1")]

        s.op("scalar", lambda e: e.activation(sc1[:], mq2[:], AF.Sqrt))
        s.op("scalar", lambda e: e.activation(sc2[:], mk2[:], AF.Sqrt))
        sq_done = ac.n
        s.op("vector", lambda e: e.tensor_tensor(negB[:], sc1[:], sc2[:], ALU.mult),
             wait=[(ac, sq_done)])
        s.op("vector", lambda e: e.tensor_scalar(negB[:], negB[:], -1.0 / 32.0, None,
                                                 op0=ALU.mult))
        nb_done = dv.n
        s.op("tensor", lambda e: e.matmul(ps_bc[0:128, 0:1], g.ones_row_f[:], negB[:],
                                          start=True, stop=True), wait=[(dv, nb_done)], inc=bmm)
        s.op("vector", lambda e: e.tensor_copy(nbc[:], ps_bc[0:128, 0:1]),
             wait=[(bmm, bmm.n)])
        nbc_done = dv.n

        qload_done = {}
        for i in range(NDC):
            s.op("sync", lambda e, i=i: e.dma_start(
                Qc[0][:, i * 512:(i + 1) * 512], qpark.ap()[i * 128:(i + 1) * 128, 0:512]),
                inc=qldp[0], inck=16, dma=True)
        qload_done[0] = qldp[0].n

        s_after_qc = {}
        pv_done_prev = 0
        ot_free = 0
        zrow_free = 0
        bc_free = nbc_done

        for qc in range(T // 512):
            nk_t = 4 * qc + 4
            qb = Qc[qc % 2]
            exp_after = {}
            s_after = {}

            def emit_s(kt, qc=qc, qb=qb):
                psl = ps_s[kt % 2]

                def sgrp(e, kt=kt, qc=qc, qb=qb, psl=psl):
                    masks = []
                    for qs in range(4):
                        qt = qc * 4 + qs
                        if qt == kt:
                            masks.append((qs, g.cmb))
                        elif qt < kt:
                            masks.append((qs, g.nfb))
                    for i in range(NDC):
                        inst = e.matmul(psl[:],
                                        K_sb[:, i * T + kt * 128: i * T + (kt + 1) * 128],
                                        qb[:, i * 512:(i + 1) * 512],
                                        start=(i == 0), stop=(i == NDC - 1 and not masks))
                    for j, (qs, mt) in enumerate(masks):
                        inst = e.matmul(psl[:, qs * 128:(qs + 1) * 128], g.idb[:], mt[:],
                                        start=False, stop=(j == len(masks) - 1))
                    return inst
                s.op("tensor", sgrp,
                     wait=[(qldp[qc % 2], qload_done[qc]), (ac, exp_after.get(kt - 2, 0)),
                           (dv, nbc_done)], inc=mm)
                s_after[kt] = mm.n

            def emit_exp(kt, qc=qc):
                psl = ps_s[kt % 2]
                s.op("scalar", lambda e, kt=kt, psl=psl: e.activation(
                    PT[:, kt * 512:(kt + 1) * 512], psl[:], AF.Exp,
                    bias=nbc[:, 0:1], scale=1.0 / 32.0),
                    wait=[(mm, s_after[kt]), (pvm, pv_done_prev)])
                exp_after[kt] = ac.n

            def emit_z(kt, nk_t=nk_t):
                s.op("tensor", lambda e, kt=kt: e.matmul(
                    ps_z[:], g.ones_col_bf[:], PT[:, kt * 512:(kt + 1) * 512],
                    start=(kt == 0), stop=(kt == nk_t - 1)),
                    wait=[(ac, exp_after[kt]), (dv, zrow_free)], inc=zmm)

            for kt in range(nk_t):
                emit_s(kt)
                emit_exp(kt)
                if kt >= 1:
                    emit_z(kt - 1)
            emit_z(nk_t - 1)
            s_after_qc[qc] = mm.n
            zdone = zmm.n
            if qc + 1 < T // 512:
                for i in range(NDC):
                    s.op("sync", lambda e, i=i, q=qc + 1: e.dma_start(
                        Qc[q % 2][:, i * 512:(i + 1) * 512],
                        qpark.ap()[i * 128:(i + 1) * 128, q * 512:(q + 1) * 512]),
                        wait=[(mm, s_after_qc[qc])], inc=qldp[(qc + 1) % 2], inck=16, dma=True)
                qload_done[qc + 1] = qldp[(qc + 1) % 2].n

            s.op("vector", lambda e: e.reciprocal(zr[:], ps_z[0:1, :]), wait=[(zmm, zdone)])
            zrow_free = dv.n
            s.op("tensor", lambda e: e.matmul(ps_bc[:], g.ones_row_f[:], zr[:],
                                              start=True, stop=True),
                 wait=[(dv, max(zrow_free, bc_free))], inc=bmm)
            s.op("vector", lambda e: e.tensor_copy(zb[:], ps_bc[:]), wait=[(bmm, bmm.n)])
            zb_ready = dv.n
            bc_free = dv.n

            stg = ctstage[qc % 2]
            for half in range(2):
                for vs in range(4):
                    vv = half * 4 + vs

                    def pvgrp(e, vv=vv, nk_t=nk_t, vs=vs):
                        for kt in range(nk_t):
                            inst = e.matmul(ps_ot[vs][:],
                                            V_sb[:, kt * D + vv * 128: kt * D + (vv + 1) * 128],
                                            PT[:, kt * 512:(kt + 1) * 512],
                                            start=(kt == 0), stop=(kt == nk_t - 1))
                        return inst
                    s.op("tensor", pvgrp,
                         wait=[(ac, exp_after[nk_t - 1]), (dv, ot_free)], inc=pvm)
                    pvv = pvm.n
                    s.op("vector", lambda e, vv=vv, vs=vs, stg=stg: e.tensor_tensor(
                        stg[:, vv * 512:(vv + 1) * 512], ps_ot[vs][:], zb[:], ALU.mult),
                        wait=[(pvm, pvv), (dv, zb_ready),
                              (stc[qc % 2], stc[qc % 2].n)])
                ot_free = dv.n
            pv_done_prev = pvm.n
            evac_all = dv.n
            for i in range(NDC):
                s.op("sync", lambda e, i=i, qc=qc, stg=stg: e.dma_start(
                    ctxT.ap()[i * 128:(i + 1) * 128, qc * 512:(qc + 1) * 512],
                    stg[:, i * 512:(i + 1) * 512]), wait=[(dv, evac_all)],
                    inc=stc[qc % 2], inck=16, dma=True)
        s.op("sync", lambda e, a=stc[0].n: e.wait_ge(stc[0].sem, a))
        s.op("sync", lambda e, a=stc[1].n: e.wait_ge(stc[1].sem, a))
        if dbg is not None and "ctx" in dbg:
            dbc = Ctr(nc, "dbc")
            s.op("gpsimd", lambda e: e.dma_start(dbg["ctx"].ap(), ctxT.ap()),
                 inc=dbc, inck=16, dma=True)
            s.op("gpsimd", lambda e, a=dbc: e.wait_ge(a.sem, a.n), dma=True)
        if dbg is not None and "PT" in dbg:
            dbp = Ctr(nc, "dbp")
            s.op("sync", lambda e: e.dma_start(dbg["PT"].ap(), PT[:]), inc=dbp, inck=16, dma=True)
            s.op("sync", lambda e, a=dbp: e.wait_ge(a.sem, a.n))
        s.flush()

    with ExitStack() as c3:
        E3 = c3.enter_context
        wo_sb = E3(nc.sbuf_tensor(f"{tag}_wo", [128, NDC * D], BF16))
        ctc = [E3(nc.sbuf_tensor(f"{tag}_ctc{i}", [128, NDC * 512], BF16)) for i in range(2)]
        x0t = [E3(nc.sbuf_tensor(f"{tag}_x0t{i}", [128, D], F32)) for i in range(2)]
        h_sb = [E3(nc.sbuf_tensor(f"{tag}_h{i}", [128, D], F32)) for i in range(2)]
        x1t = [E3(nc.sbuf_tensor(f"{tag}_x1t{i}", [128, D], F32)) for i in range(2)]
        tmp = {"sqs": E3(nc.sbuf_tensor(f"{tag}_sqs", [128, D], F32)),
               "s1": E3(nc.sbuf_tensor(f"{tag}_s1", [128, 1], F32)),
               "s2": E3(nc.sbuf_tensor(f"{tag}_s2", [128, 1], F32)),
               "mmean": E3(nc.sbuf_tensor(f"{tag}_mmean", [128, 1], F32)),
               "vvar": E3(nc.sbuf_tensor(f"{tag}_vvar", [128, 1], F32)),
               "rstd": E3(nc.sbuf_tensor(f"{tag}_rstd", [128, 1], F32))}
        epsb = E3(nc.sbuf_tensor(f"{tag}_epsb", [128, 1], F32))
        ps_o = [E3(nc.psum_tensor(f"{tag}_po{i}", [128, 512], F32)) for i in range(4)]

        s = Sched(nc)
        dv, ac = s.v, s.a
        wld = Ctr(nc, "wld")
        cldp = [Ctr(nc, "cld0"), Ctr(nc, "cld1")]
        xldr = Ctr(nc, "xldr")
        mm = Ctr(nc, "mm")
        stp = [Ctr(nc, "stp0"), Ctr(nc, "stp1")]

        s.op("vector", lambda e: e.memset(epsb[:], 1e-5))
        for i in range(NDC):
            s.op("gpsimd", lambda e, i=i: e.dma_start(
                wo_sb[:, i * D:(i + 1) * D], woT.ap()[i * 128:(i + 1) * 128, :]),
                inc=wld, inck=16, dma=True)
        wready = wld.n

        ct_done = {}
        for i in range(NDC):
            s.op("sync", lambda e, i=i: e.dma_start(
                ctc[0][:, i * 512:(i + 1) * 512], ctxT.ap()[i * 128:(i + 1) * 128, 0:512]),
                inc=cldp[0], inck=16, dma=True)
        ct_done[0] = cldp[0].n

        mm_after_qc = {}
        x0_user = {}
        ps_user = {}
        for qc in range(T // 512):
            cb = ctc[qc % 2]
            for tj in range(4):
                tt = qc * 4 + tj
                xb = x0t[tt % 2]
                hb = h_sb[tt % 2]
                ob = x1t[tt % 2]
                s.op("sync", lambda e, tt=tt, xb=xb: e.dma_start(
                    xb[:], x0.ap()[tt * 128:(tt + 1) * 128, :]),
                    wait=[(dv, x0_user.get(tt - 2, 0))], inc=xldr, inck=16, dma=True)
                xrd = xldr.n
                for dc in range(2):
                    pslab = (tj * 2 + dc) % 4
                    pp = ps_o[pslab]

                    def ogrp(e, tj=tj, dc=dc, cb=cb, pp=pp):
                        for i in range(NDC):
                            inst = e.matmul(pp[:],
                                            cb[:, i * 512 + tj * 128: i * 512 + (tj + 1) * 128],
                                            wo_sb[:, i * D + dc * 512: i * D + (dc + 1) * 512],
                                            start=(i == 0), stop=(i == NDC - 1))
                        return inst
                    s.op("tensor", ogrp, wait=[(wld, wready), (cldp[qc % 2], ct_done[qc]),
                                               (dv, ps_user.get(pslab, 0))], inc=mm)
                    mval = mm.n
                    s.op("vector", lambda e, dc=dc, xb=xb, hb=hb, pp=pp: e.tensor_tensor(
                        hb[:, dc * 512:(dc + 1) * 512], xb[:, dc * 512:(dc + 1) * 512],
                        pp[:], ALU.add),
                        wait=[(mm, mval), (xldr, xrd), (stp[tt % 2], stp[tt % 2].n)])
                    ps_user[pslab] = dv.n
                x0_user[tt] = dv.n
                h_ready = dv.n
                out_done = emit_ln_tile(s, tmp, hb, ob, epsb, h_ready)
                s.op("sync", lambda e, tt=tt, ob=ob: e.dma_start(
                    x1.ap()[tt * 128:(tt + 1) * 128, :], ob[:]),
                    wait=[(dv, out_done)], inc=stp[tt % 2], inck=16, dma=True)
            mm_after_qc[qc] = mm.n
            if qc + 1 < T // 512:
                for i in range(NDC):
                    s.op("sync", lambda e, i=i, q=qc + 1: e.dma_start(
                        ctc[q % 2][:, i * 512:(i + 1) * 512],
                        ctxT.ap()[i * 128:(i + 1) * 128, q * 512:(q + 1) * 512]),
                        wait=[(mm, mm_after_qc[qc])], inc=cldp[(qc + 1) % 2], inck=16, dma=True)
                ct_done[qc + 1] = cldp[(qc + 1) % 2].n
        s.op("sync", lambda e, a=stp[0].n: e.wait_ge(stp[0].sem, a))
        s.op("sync", lambda e, a=stp[1].n: e.wait_ge(stp[1].sem, a))
        s.flush()



CAP0 = 2560
NE0 = 16
EPC0 = 2
DFF0 = 4096


def emit_phase_b(nc, ctx, g, x1, gwT, w1, w2, coremask, x2, tag="b"):
    E = ctx.enter_context
    NTT = T // 128
    NCH = T // 512
    x1bf_in = nc.dram_tensor(f"{tag}_x1bf_in", [T, D], BF16, kind="Internal")
    x1bf_all = nc.dram_tensor(f"{tag}_x1bf_all", [NCORES * T, D], BF16, kind="Internal")
    hist_in = nc.dram_tensor(f"{tag}_hist_in", [1, 16], F32, kind="Internal")
    hist_all = nc.dram_tensor(f"{tag}_hist_all", [NCORES, 16], F32, kind="Internal")
    idt_loc = nc.dram_tensor(f"{tag}_idt_loc", [NE0 * CAP0, 1], F32, kind="Internal")
    idt_all = nc.dram_tensor(f"{tag}_idt_all", [NE0 * CAP0, 1], F32, kind="Internal")
    yl = nc.dram_tensor(f"{tag}_yl", [EPC0 * CAP0, D], BF16, kind="Internal")
    yall = nc.dram_tensor(f"{tag}_yall", [NE0 * CAP0, D], BF16, kind="Internal")

    sp_sb = E(nc.sbuf_tensor(f"{tag}_sp", [128, NTT], F32))
    w_sb = E(nc.sbuf_tensor(f"{tag}_w", [128, NTT], F32))
    cm_sb = E(nc.sbuf_tensor(f"{tag}_cm", [8, 16], F32))
    base_sb = E(nc.sbuf_tensor(f"{tag}_base", [1, 1], F32))

    rg = [list(range(NCORES))]

    with ExitStack() as c1:
        E1 = c1.enter_context
        gw_sb = E1(nc.sbuf_tensor(f"{tag}_gw", [128, NDC * 16], F32))
        xt = [E1(nc.sbuf_tensor(f"{tag}_xt{i}", [128, D], F32)) for i in range(2)]
        xT = E1(nc.sbuf_tensor(f"{tag}_xT", [128, NDC * 128], F32))
        lg = E1(nc.sbuf_tensor(f"{tag}_lg", [128, 16], F32))
        exl = E1(nc.sbuf_tensor(f"{tag}_exl", [128, 16], F32))
        oh = E1(nc.sbuf_tensor(f"{tag}_oh", [128, 16], F32))
        eq = E1(nc.sbuf_tensor(f"{tag}_eq", [128, 16], F32))
        cnd = E1(nc.sbuf_tensor(f"{tag}_cnd", [128, 16], F32))
        mx = E1(nc.sbuf_tensor(f"{tag}_mx", [128, 1], F32))
        nmx = E1(nc.sbuf_tensor(f"{tag}_nmx", [128, 1], F32))
        zz = E1(nc.sbuf_tensor(f"{tag}_zz", [128, 1], F32))
        rz = E1(nc.sbuf_tensor(f"{tag}_rz", [128, 1], F32))
        am = E1(nc.sbuf_tensor(f"{tag}_am", [128, 1], F32))
        rk = E1(nc.sbuf_tensor(f"{tag}_rk", [128, 1], F32))
        run_row = E1(nc.sbuf_tensor(f"{tag}_run", [1, 16], F32))
        ps_t = [E1(nc.psum_tensor(f"{tag}_pt{i}", [128, 512], F32)) for i in range(2)]
        ps_l = E1(nc.psum_tensor(f"{tag}_pl", [128, 512], F32))
        ps_r = E1(nc.psum_tensor(f"{tag}_pr", [128, 512], F32))
        ps_h = E1(nc.psum_tensor(f"{tag}_ph", [1, 512], F32))

        s = Sched(nc)
        dv, ac = s.v, s.a
        ld = Ctr(nc, "ld")
        xldp = [Ctr(nc, "xld0"), Ctr(nc, "xld1")]
        mm = Ctr(nc, "mm")
        rmm = Ctr(nc, "rmm")
        hmm = Ctr(nc, "hmm")
        cc = Ctr(nc, "cc")

        castc = Ctr(nc, "castc")
        s.op("gpsimd", lambda e: e.dma_start(x1bf_in.ap(), x1.ap()), inc=castc, inck=16, dma=True)
        s.op("gpsimd", lambda e: e.wait_ge(castc.sem, 16), dma=True)
        s.op("gpsimd", lambda e: e.collective_compute(
            "AllGather", ALU.bypass, replica_groups=rg,
            ins=[x1bf_in.ap()], outs=[x1bf_all.ap()]), inc=cc, dma=True)

        for i in range(NDC):
            s.op("sync", lambda e, i=i: e.dma_start(
                gw_sb[:, i * 16:(i + 1) * 16], gwT.ap()[i * 128:(i + 1) * 128, :]),
                inc=ld, inck=16, dma=True)
        gw_ready = ld.n
        s.op("vector", lambda e: e.memset(run_row[:], 0.0))

        tr_evac = 0
        lg_user = 0
        run_user = 0
        for tt in range(NTT):
            xb = xt[tt % 2]
            xld = xldp[tt % 2]
            s.op("sync", lambda e, tt=tt, xb=xb: e.dma_start(
                xb[:], x1.ap()[tt * 128:(tt + 1) * 128, :]),
                wait=[(mm, max(0, mm.n - 9))], inc=xld, inck=16, dma=True)
            xrd = xld.n
            def trg(e, xb=xb):
                for i in range(NDC):
                    pslab = ps_t[i % 2]
                    inst = e.transpose(pslab[:, 0:128], xb[:, i * 128:(i + 1) * 128], g.idf[:])
                return inst
            for i in range(NDC):
                pslab = ps_t[i % 2]
                s.op("tensor", lambda e, i=i, xb=xb, pslab=pslab: e.transpose(
                    pslab[:, 0:128], xb[:, i * 128:(i + 1) * 128], g.idf[:]),
                    wait=[(xld, xrd), (dv, max(0, tr_evac - 1))], inc=mm)
                mval = mm.n
                s.op("vector", lambda e, i=i, pslab=pslab: e.tensor_copy(
                    xT[:, i * 128:(i + 1) * 128], pslab[:, 0:128]),
                    wait=[(mm, mval)])
                tr_evac = dv.n
            def lgrp(e):
                for i in range(NDC):
                    inst = e.matmul(ps_l[:, 0:16], xT[:, i * 128:(i + 1) * 128],
                                    gw_sb[:, i * 16:(i + 1) * 16],
                                    start=(i == 0), stop=(i == NDC - 1))
                return inst
            s.op("tensor", lgrp, wait=[(ld, gw_ready), (dv, tr_evac), (dv, lg_user)], inc=mm)
            lg_mm = mm.n
            s.op("vector", lambda e: e.tensor_copy(lg[:], ps_l[:, 0:16]), wait=[(mm, lg_mm)])
            s.op("vector", lambda e: e.reduce_max(mx[:], lg[:], axis=AX.X))
            s.op("vector", lambda e: e.reduce_max(nmx[:], lg[:], axis=AX.X, negate=True))
            mx_done = dv.n
            s.op("scalar", lambda e: e.activation(exl[:], lg[:], AF.Exp, bias=nmx[:, 0:1]),
                 wait=[(dv, mx_done)])
            ex_done = ac.n
            s.op("vector", lambda e: e.reduce_sum(zz[:], exl[:], axis=AX.X),
                 wait=[(ac, ex_done)])
            s.op("vector", lambda e: e.reciprocal(rz[:], zz[:]))
            s.op("vector", lambda e: e.tensor_scalar(eq[:], lg[:], mx[:, 0:1], None,
                                                     op0=ALU.is_ge))
            s.op("vector", lambda e: e.scalar_tensor_tensor(
                cnd[:], eq[:], -1e6, g.iota16b[:], op0=ALU.mult, op1=ALU.add))
            s.op("vector", lambda e: e.tensor_reduce(am[:], cnd[:], axis=AX.X, op=ALU.min))
            s.op("vector", lambda e: e.tensor_scalar(oh[:], g.iota16[:], am[:, 0:1], None,
                                                     op0=ALU.is_equal))
            oh_done = dv.n
            lg_user = dv.n
            def rgrp(e):
                e.matmul(ps_r[:, 0:16], g.uincl[:], oh[:], start=True, stop=False)
                return e.matmul(ps_r[:, 0:16], g.ones_row_f[:], run_row[:],
                                start=False, stop=True)
            s.op("tensor", rgrp, wait=[(dv, oh_done), (dv, run_user)], inc=rmm)
            r_done = rmm.n
            s.op("tensor", lambda e, tt=tt: e.matmul(
                ps_h[:, 0:16], g.ones_col_f[:], oh[:], start=(tt == 0), stop=(tt == NTT - 1)),
                wait=[(dv, oh_done)], inc=hmm)
            s.op("vector", lambda e: e.tensor_tensor(cnd[:], oh[:], ps_r[:, 0:16], ALU.mult),
                 wait=[(rmm, r_done)])
            s.op("vector", lambda e: e.reduce_sum(rk[:], cnd[:], axis=AX.X))
            s.op("tensor", lambda e: e.matmul(ps_r[0:1, 16:32], g.ones_col_f[:], oh[:],
                                              start=True, stop=True),
                 wait=[(dv, dv.n)], inc=rmm)
            t_done = rmm.n
            s.op("vector", lambda e: e.tensor_tensor(run_row[:], run_row[:],
                                                     ps_r[0:1, 16:32], ALU.add),
                 wait=[(rmm, t_done)])
            run_user = dv.n
            s.op("vector", lambda e, tt=tt: e.tensor_copy(w_sb[:, tt:tt + 1], rz[:]))
            s.op("vector", lambda e, tt=tt: e.tensor_copy(sp_sb[:, tt:tt + 1], rk[:]))
            s.op("vector", lambda e, tt=tt: e.scalar_tensor_tensor(
                sp_sb[:, tt:tt + 1], am[:], 65536.0, sp_sb[:, tt:tt + 1],
                op0=ALU.mult, op1=ALU.add))
        hist_done = hmm.n
        s.op("vector", lambda e: e.tensor_copy(run_row[:], ps_h[0:1, 0:16]),
             wait=[(hmm, hist_done)])
        hcopy = dv.n
        s.op("sync", lambda e: e.dma_start(hist_in.ap(), run_row[:]),
             wait=[(dv, hcopy)], inc=ld, inck=16, dma=True)
        hst = ld.n
        s.op("gpsimd", lambda e: e.collective_compute(
            "AllGather", ALU.bypass, replica_groups=rg,
            ins=[hist_in.ap()], outs=[hist_all.ap()]),
            wait=[(ld, hst)], inc=cc, dma=True)
        s.op("gpsimd", lambda e: e.wait_ge(cc.sem, cc.n), dma=True)
        s.flush()

    with ExitStack() as c2:
        E2 = c2.enter_context
        hists = E2(nc.sbuf_tensor(f"{tag}_hists", [8, 16], F32))
        offs = E2(nc.sbuf_tensor(f"{tag}_offs", [1, 16], F32))
        offs_bc = E2(nc.sbuf_tensor(f"{tag}_offsbc", [128, 16], F32))
        cmk = E2(nc.sbuf_tensor(f"{tag}_cmk", [8, 1], F32))
        zero_big = E2(nc.sbuf_tensor(f"{tag}_zb0", [128, CAP0 * NE0 // 128], F32))
        oh = E2(nc.sbuf_tensor(f"{tag}_oh2", [128, 16], F32))
        am = E2(nc.sbuf_tensor(f"{tag}_am2", [128, 1], F32))
        rk = E2(nc.sbuf_tensor(f"{tag}_rk2", [128, 1], F32))
        offsel = E2(nc.sbuf_tensor(f"{tag}_ofsel", [128, 1], F32))
        slot = E2(nc.sbuf_tensor(f"{tag}_slot", [128, 1], F32))
        keep = E2(nc.sbuf_tensor(f"{tag}_keep", [128, 1], F32))
        ids_f = E2(nc.sbuf_tensor(f"{tag}_idsf", [128, 1], F32))
        slot_i = E2(nc.sbuf_tensor(f"{tag}_sloti", [128, 1], I32))
        bb = E2(nc.sbuf_tensor(f"{tag}_bb", [128, 1], F32))
        ps_b = E2(nc.psum_tensor(f"{tag}_pb", [128, 512], F32))

        s = Sched(nc)
        s.bnd(NE0 * CAP0 - 1)
        dv, ac = s.v, s.a
        ld = Ctr(nc, "ld")
        mm = Ctr(nc, "mm")
        cc = Ctr(nc, "cc")
        sc = Ctr(nc, "sc")

        s.op("sync", lambda e: e.dma_start(hists[:], hist_all.ap()), inc=ld, inck=16, dma=True)
        s.op("sync", lambda e: e.dma_start(cmk[:], coremask.ap()), inc=ld, inck=16, dma=True)
        s.op("sync", lambda e: e.dma_start(base_sb[:], g.dr_base.ap()), inc=ld, inck=16, dma=True)
        s.op("vector", lambda e: e.memset(zero_big[:], 0.0))
        zb_done = dv.n
        s.op("sync", lambda e: e.dma_start(
            idt_loc.ap().rearrange("(p f) o -> p (f o)", p=128), zero_big[:]),
            wait=[(dv, zb_done)], inc=ld, inck=16, dma=True)
        zero_done = ld.n
        ld_done = ld.n
        s.op("tensor", lambda e: e.matmul(ps_b[0:1, 0:16], cmk[:], hists[:],
                                          start=True, stop=True),
             wait=[(ld, ld_done)], inc=mm)
        s.op("vector", lambda e: e.tensor_copy(offs[:], ps_b[0:1, 0:16]), wait=[(mm, mm.n)])
        off_done = dv.n
        s.op("tensor", lambda e: e.matmul(ps_b[:, 16:32], g.ones_row_f[:], offs[:],
                                          start=True, stop=True),
             wait=[(dv, off_done)], inc=mm)
        s.op("vector", lambda e: e.tensor_copy(offs_bc[:], ps_b[:, 16:32]), wait=[(mm, mm.n)])
        s.op("tensor", lambda e: e.matmul(ps_b[:, 32:33], g.ones_row_f[:], base_sb[:],
                                          start=True, stop=True), inc=mm)
        s.op("vector", lambda e: e.tensor_copy(bb[:], ps_b[:, 32:33]), wait=[(mm, mm.n)])
        prep_done = dv.n

        sc_prev = 0
        for tt in range(NTT):
            s.op("vector", lambda e, tt=tt: e.tensor_scalar(
                am[:], sp_sb[:, tt:tt + 1], 1.0 / 65536.0, None, op0=ALU.mult),
                wait=[(dv, prep_done), (sc, sc_prev)])
            am_w = dv.n
            s.op("gpsimd", lambda e: e.tensor_copy(slot_i[:], am[:]),
                 wait=[(dv, am_w)])
            s.op("vector", lambda e: e.tensor_copy(am[:], slot_i[:]),
                 wait=[(s.g, s.g.n)])
            s.op("vector", lambda e, tt=tt: e.scalar_tensor_tensor(
                rk[:], am[:], -65536.0, sp_sb[:, tt:tt + 1], op0=ALU.mult, op1=ALU.add))
            s.op("vector", lambda e: e.tensor_scalar(oh[:], g.iota16[:], am[:, 0:1], None,
                                                     op0=ALU.is_equal))
            s.op("vector", lambda e: e.tensor_tensor(oh[:], oh[:], offs_bc[:], ALU.mult))
            s.op("vector", lambda e: e.reduce_sum(offsel[:], oh[:], axis=AX.X))
            s.op("vector", lambda e: e.tensor_tensor(slot[:], offsel[:], rk[:], ALU.add))
            s.op("vector", lambda e: e.tensor_scalar(keep[:], slot[:], float(CAP0), None,
                                                     op0=ALU.is_le))
            s.op("vector", lambda e, tt=tt: e.tensor_tensor(
                w_sb[:, tt:tt + 1], w_sb[:, tt:tt + 1], keep[:], ALU.mult))
            s.op("vector", lambda e: e.scalar_tensor_tensor(
                slot[:], am[:], float(CAP0), slot[:], op0=ALU.mult, op1=ALU.add))
            s.op("vector", lambda e: e.tensor_scalar(slot[:], slot[:], -1.0, None, op0=ALU.add))
            s.op("vector", lambda e: e.tensor_scalar(keep[:], keep[:], -524288.0, 524288.0,
                                                     op0=ALU.mult, op1=ALU.add))
            s.op("vector", lambda e: e.tensor_tensor(slot[:], slot[:], keep[:], ALU.add))
            s.op("vector", lambda e, tt=tt: e.tensor_copy(sp_sb[:, tt:tt + 1], slot[:]))
            s.op("vector", lambda e, tt=tt: e.tensor_scalar(
                ids_f[:], g.iotacol[:], float(tt * 128 + 1), None, op0=ALU.add))
            s.op("vector", lambda e: e.tensor_tensor(ids_f[:], ids_f[:], bb[:], ALU.add))
            ids_done = dv.n
            s.op("gpsimd", lambda e: e.tensor_copy(slot_i[:], slot[:]),
                 wait=[(dv, ids_done)])
            cast_done = s.g.n
            s.op("gpsimd", lambda e: e.indirect_dma_start(
                idt_loc.ap(), bass.IndirectOffsetOnAxis(slot_i[:], 0),
                ids_f[:], None, bounds_check=s._breg[s.bnd(NE0 * CAP0 - 1)],
                oob_is_err=False),
                wait=[(ld, zero_done), (s.g, cast_done)], inc=sc, inck=16, dma=True)
            sc_prev = sc.n
        s.op("gpsimd", lambda e: e.collective_compute(
            "AllReduce", ALU.add, replica_groups=rg,
            ins=[idt_loc.ap()], outs=[idt_all.ap()]),
            wait=[(sc, sc.n)], inc=cc, dma=True)
        s.op("gpsimd", lambda e: e.wait_ge(cc.sem, cc.n), dma=True)
        s.flush()

    with ExitStack() as c3:
        E3 = c3.enter_context
        w1_sb = E3(nc.sbuf_tensor(f"{tag}_w1", [128, NDC * DFF0], BF16))
        w2_sb = E3(nc.sbuf_tensor(f"{tag}_w2", [128, (DFF0 // 128) * D], BF16))
        hT = E3(nc.sbuf_tensor(f"{tag}_hT", [128, (DFF0 // 128) * 512], BF16))
        disp = E3(nc.sbuf_tensor(f"{tag}_disp", [128, 4 * D], BF16))
        dispT = E3(nc.sbuf_tensor(f"{tag}_dispT", [128, NDC * 512], BF16))
        idcol_f = E3(nc.sbuf_tensor(f"{tag}_idcf", [128, 1], F32))
        idcol_i = E3(nc.sbuf_tensor(f"{tag}_idci", [128, 1], I32))
        negm = E3(nc.sbuf_tensor(f"{tag}_negm", [128, 1], F32))
        slotbase = E3(nc.sbuf_tensor(f"{tag}_slb", [1, 1], F32))
        slotbase_c = E3(nc.sbuf_tensor(f"{tag}_slbc", [128, 1], F32))
        ystage = E3(nc.sbuf_tensor(f"{tag}_yst", [128, D], BF16))
        ps_f = [E3(nc.psum_tensor(f"{tag}_pf{i}", [128, 512], F32)) for i in range(2)]
        ps_y = [E3(nc.psum_tensor(f"{tag}_py{i}", [128, 512], F32)) for i in range(2)]
        ps_tr = [E3(nc.psum_tensor(f"{tag}_ptr{i}", [128, 512], BF16)) for i in range(2)]

        s = Sched(nc)
        s.bnd(NE0 * CAP0 - 1)
        s.bnd(NCORES * T - 1)
        dv, ac = s.v, s.a
        wld = Ctr(nc, "wld")
        mm = Ctr(nc, "mm")
        st = Ctr(nc, "st")
        gg = Ctr(nc, "gg")
        cc = Ctr(nc, "cc")

        sbl = Ctr(nc, "sbl")
        s.op("sync", lambda e: e.dma_start(slotbase[:], g.dr_base.ap()), inc=sbl, inck=16, dma=True)
        s.op("vector", lambda e: e.tensor_scalar(slotbase[:], slotbase[:],
                                                 float(EPC0 * CAP0) / T, None, op0=ALU.mult),
             wait=[(sbl, sbl.n)])
        s.op("tensor", lambda e: e.matmul(ps_f[0][:, 0:1], g.ones_row_f[:], slotbase[:],
                                          start=True, stop=True), wait=[(dv, dv.n)], inc=mm)
        s.op("vector", lambda e: e.tensor_copy(slotbase_c[:], ps_f[0][:, 0:1]),
             wait=[(mm, mm.n)])
        sb_done = dv.n

        gdone_prev = 0
        idc_free = 0
        for se in range(EPC0):
            w1ld = Ctr(nc, f"w1ld{se}")
            wwar = [(mm, mm.n)]
            for i in range(NDC):
                s.op("gpsimd", lambda e, i=i, se=se: e.dma_start(
                    w1_sb[:, i * DFF0:(i + 1) * DFF0],
                    w1.ap()[se, i * 128:(i + 1) * 128, :]), wait=wwar,
                    inc=w1ld, inck=16, dma=True)
            for i in range(DFF0 // 128):
                s.op("gpsimd", lambda e, i=i, se=se: e.dma_start(
                    w2_sb[:, i * D:(i + 1) * D],
                    w2.ap()[se, i * 128:(i + 1) * 128, :]), wait=wwar,
                    inc=w1ld, inck=16, dma=True)
            w_ready = w1ld.n

            for cchunk in range(CAP0 // 512):
                g_done = {}
                for tj in range(4):
                    cslot = cchunk * 512 + tj * 128
                    s.op("vector", lambda e, cslot=cslot, se=se: e.tensor_scalar(
                        idcol_f[:], g.iotacol[:], float(se * CAP0 + cslot), None,
                        op0=ALU.add),
                        wait=[(dv, sb_done), (mm, mm.n), (s.g, idc_free)])
                    s.op("vector", lambda e: e.tensor_tensor(idcol_f[:], idcol_f[:],
                                                             slotbase_c[:], ALU.add))
                    idsl = dv.n
                    s.op("gpsimd", lambda e: e.tensor_copy(idcol_i[:], idcol_f[:]),
                         wait=[(dv, idsl), (gg, gdone_prev)])
                    cast1 = s.g.n
                    s.op("gpsimd", lambda e: e.indirect_dma_start(
                        idcol_f[:], None,
                        idt_all.ap(), bass.IndirectOffsetOnAxis(idcol_i[:], 0),
                        bounds_check=s._breg[s.bnd(NE0 * CAP0 - 1)], oob_is_err=False),
                        wait=[(s.g, cast1)], inc=gg, inck=16, dma=True)
                    gat1 = gg.n
                    s.op("gpsimd", lambda e, v=gg.n: e.wait_ge(gg.sem, v), dma=True)
                    s.op("vector", lambda e: e.tensor_scalar(idcol_f[:], idcol_f[:],
                                                             -1.0, None, op0=ALU.add),
                         wait=[(s.g, s.g.n), (gg, gat1)])
                    s.op("vector", lambda e: e.tensor_scalar(negm[:], idcol_f[:], 0.0, None,
                                                             op0=ALU.is_lt))
                    s.op("vector", lambda e: e.scalar_tensor_tensor(
                        idcol_f[:], negm[:], 524288.0, idcol_f[:], op0=ALU.mult, op1=ALU.add))
                    neg_done = dv.n
                    s.op("gpsimd", lambda e: e.tensor_copy(idcol_i[:], idcol_f[:]),
                         wait=[(dv, neg_done)])
                    cast2 = s.g.n
                    s.op("vector", lambda e, tj=tj: e.memset(disp[:, tj * D:(tj + 1) * D], 0.0),
                         wait=[(mm, mm.n)])
                    mz = dv.n
                    s.op("gpsimd", lambda e, tj=tj: e.indirect_dma_start(
                        disp[:, tj * D:(tj + 1) * D], None,
                        x1bf_all.ap(), bass.IndirectOffsetOnAxis(idcol_i[:], 0),
                        bounds_check=s._breg[s.bnd(NCORES * T - 1)], oob_is_err=False),
                        wait=[(dv, mz), (s.g, cast2)], inc=gg, inck=16, dma=True)
                    g_done[tj] = gg.n
                    gdone_prev = gg.n
                    idc_free = s.g.n
                    s.op("gpsimd", lambda e, v=gg.n: e.wait_ge(gg.sem, v), dma=True)
                for tj in range(4):
                    for i in range(NDC):
                        pslab = ps_tr[(tj * NDC + i) % 2]
                        s.op("tensor", lambda e, tj=tj, i=i, pslab=pslab: e.transpose(
                            pslab[:, 0:128], disp[:, tj * D + i * 128: tj * D + (i + 1) * 128],
                            g.idb[:]), wait=[(gg, g_done[tj]), (ac, max(0, s.a.n - 1))], inc=mm)
                        mval = mm.n
                        s.op("scalar", lambda e, tj=tj, i=i, pslab=pslab: e.copy(
                            dispT[:, i * 512 + tj * 128: i * 512 + (tj + 1) * 128],
                            pslab[:, 0:128]), wait=[(mm, mval)])
                tr_done = s.a.n
                for fc in range(DFF0 // 128):
                    pslab = ps_f[fc % 2]

                    def hgrp(e, fc=fc, pslab=pslab):
                        for i in range(NDC):
                            inst = e.matmul(pslab[:],
                                            w1_sb[:, i * DFF0 + fc * 128: i * DFF0 + (fc + 1) * 128],
                                            dispT[:, i * 512:(i + 1) * 512],
                                            start=(i == 0), stop=(i == NDC - 1))
                        return inst
                    s.op("tensor", hgrp, wait=[(w1ld, w_ready), (ac, max(0, s.a.n - 1 + 0)),
                                               (ac, tr_done)], inc=mm)
                    mval = mm.n
                    s.op("scalar", lambda e, fc=fc, pslab=pslab: e.activation(
                        hT[:, fc * 512:(fc + 1) * 512], pslab[:], AF.Gelu_apprx_tanh),
                        wait=[(mm, mval)])
                h_done = s.a.n
                for tj in range(4):
                    for dc2 in range(2):
                        pslab = ps_y[(tj * 2 + dc2) % 2]

                        def ygrp(e, tj=tj, dc2=dc2, pslab=pslab):
                            for fc in range(DFF0 // 128):
                                inst = e.matmul(
                                    pslab[:],
                                    hT[:, fc * 512 + tj * 128: fc * 512 + (tj + 1) * 128],
                                    w2_sb[:, fc * D + dc2 * 512: fc * D + (dc2 + 1) * 512],
                                    start=(fc == 0), stop=(fc == DFF0 // 128 - 1))
                            return inst
                        s.op("tensor", ygrp, wait=[(ac, h_done), (dv, max(0, s.v.n))], inc=mm)
                        mval = mm.n
                        s.op("vector", lambda e, dc2=dc2, pslab=pslab: e.tensor_copy(
                            ystage[:, dc2 * 512:(dc2 + 1) * 512], pslab[:]),
                            wait=[(mm, mval), (st, st.n)])
                    yst = dv.n
                    s.op("sync", lambda e, se=se, cchunk=cchunk, tj=tj: e.dma_start(
                        yl.ap()[se * CAP0 + cchunk * 512 + tj * 128:
                                se * CAP0 + cchunk * 512 + (tj + 1) * 128, :], ystage[:]),
                        wait=[(dv, yst)], inc=st, inck=16, dma=True)
        s.op("gpsimd", lambda e: e.collective_compute(
            "AllGather", ALU.bypass, replica_groups=rg,
            ins=[yl.ap()], outs=[yall.ap()]),
            wait=[(st, st.n)], inc=cc, dma=True)
        s.op("gpsimd", lambda e: e.wait_ge(cc.sem, cc.n), dma=True)
        s.flush()

    with ExitStack() as c4:
        E4 = c4.enter_context
        ytile = [E4(nc.sbuf_tensor(f"{tag}_ytl{i}", [128, D], BF16)) for i in range(2)]
        x1t = [E4(nc.sbuf_tensor(f"{tag}_x1t{i}", [128, D], F32)) for i in range(2)]
        hb2 = [E4(nc.sbuf_tensor(f"{tag}_hb{i}", [128, D], F32)) for i in range(2)]
        ob2 = [E4(nc.sbuf_tensor(f"{tag}_ob{i}", [128, D], F32)) for i in range(2)]
        slot_i = E4(nc.sbuf_tensor(f"{tag}_sloti4", [128, 1], I32))
        tmp = {"sqs": E4(nc.sbuf_tensor(f"{tag}_sqs", [128, D], F32)),
               "s1": E4(nc.sbuf_tensor(f"{tag}_s1", [128, 1], F32)),
               "s2": E4(nc.sbuf_tensor(f"{tag}_s2", [128, 1], F32)),
               "mmean": E4(nc.sbuf_tensor(f"{tag}_mmean", [128, 1], F32)),
               "vvar": E4(nc.sbuf_tensor(f"{tag}_vvar", [128, 1], F32)),
               "rstd": E4(nc.sbuf_tensor(f"{tag}_rstd", [128, 1], F32))}
        epsb = E4(nc.sbuf_tensor(f"{tag}_epsb", [128, 1], F32))

        s = Sched(nc)
        s.bnd(NE0 * CAP0 - 1)
        dv, ac = s.v, s.a
        xld = Ctr(nc, "xld")
        gg = Ctr(nc, "gg")
        stp = [Ctr(nc, "stp0"), Ctr(nc, "stp1")]

        s.op("vector", lambda e: e.memset(epsb[:], 1e-5))
        h_user = {}
        y_prev = 0
        for tt in range(NTT):
            yb = ytile[tt % 2]
            xb = x1t[tt % 2]
            hb = hb2[tt % 2]
            ob = ob2[tt % 2]
            s.op("sync", lambda e, tt=tt, xb=xb: e.dma_start(
                xb[:], x1.ap()[tt * 128:(tt + 1) * 128, :]),
                wait=[(dv, h_user.get(tt - 2, 0))], inc=xld, inck=16, dma=True)
            xrd = xld.n
            s.op("gpsimd", lambda e, tt=tt: e.tensor_copy(slot_i[:], sp_sb[:, tt:tt + 1]),
                 wait=[(dv, h_user.get(tt - 2, 0)), (gg, y_prev)])
            cast_done = s.g.n
            s.op("vector", lambda e, yb=yb: e.memset(yb[:], 0.0),
                 wait=[(dv, h_user.get(tt - 2, 0))])
            ym = dv.n
            s.op("gpsimd", lambda e, yb=yb: e.indirect_dma_start(
                yb[:], None, yall.ap(), bass.IndirectOffsetOnAxis(slot_i[:], 0),
                bounds_check=s._breg[s.bnd(NE0 * CAP0 - 1)], oob_is_err=False),
                wait=[(dv, ym), (s.g, cast_done)], inc=gg, inck=16, dma=True)
            y_done = gg.n
            y_prev = gg.n
            s.op("vector", lambda e, tt=tt, yb=yb, xb=xb, hb=hb: e.scalar_tensor_tensor(
                hb[:], yb[:], w_sb[:, tt:tt + 1], xb[:],
                op0=ALU.mult, op1=ALU.add),
                wait=[(gg, y_done), (xld, xrd)])
            h_ready = dv.n
            s.op("vector", lambda e: e.memset(tmp["s1"][0:1, 0:1], 0.0),
                 wait=[(stp[tt % 2], stp[tt % 2].n)])
            out_done = emit_ln_tile(s, tmp, hb, ob, epsb, h_ready)
            h_user[tt] = dv.n
            s.op("sync", lambda e, tt=tt, ob=ob: e.dma_start(
                x2.ap()[tt * 128:(tt + 1) * 128, :], ob[:]),
                wait=[(dv, out_done)], inc=stp[tt % 2], inck=16, dma=True)
        s.op("sync", lambda e, a=stp[0].n: e.wait_ge(stp[0].sem, a))
        s.op("sync", lambda e, a=stp[1].n: e.wait_ge(stp[1].sem, a))
        s.flush()



NH1 = 8
DH1 = 128
E1 = 2048
LAMBDA_INIT = 0.8 - 0.6 * 0.7408182206817179


def emit_phase_c(nc, ctx, g, x2, wqT, wkT, wvT, woT, lamq1T, lamk1T, lamq2T, lamk2T,
                 rms_bc, x3, tag="c"):
    E = ctx.enter_context
    NCH = T // 512
    xbf = nc.dram_tensor(f"{tag}_xbf", [T, D], BF16, kind="Internal")
    qpark = nc.dram_tensor(f"{tag}_qpark", [E1, T], BF16, kind="Internal")
    kpark = nc.dram_tensor(f"{tag}_kpark", [E1, T], BF16, kind="Internal")
    vpark = nc.dram_tensor(f"{tag}_vpark", [T, E1], BF16, kind="Internal")
    oT = nc.dram_tensor(f"{tag}_oT", [E1, T], BF16, kind="Internal")

    nqrow = E(nc.sbuf_tensor(f"{tag}_nq", [1, 16], F32))
    nkrow = E(nc.sbuf_tensor(f"{tag}_nk", [1, 16], F32))
    lamrow = E(nc.sbuf_tensor(f"{tag}_lam", [1, NH1], F32))
    nbrow = E(nc.sbuf_tensor(f"{tag}_nb", [1, NH1], F32))
    lamcs = E(nc.sbuf_tensor(f"{tag}_lamcs", [128, NH1], F32))
    nbcs = E(nc.sbuf_tensor(f"{tag}_nbcs", [128, NH1], F32))

    s = Sched(nc)
    ld = Ctr(nc, "ld")
    s.op("gpsimd", lambda e: e.dma_start(xbf.ap(), x2.ap()), inc=ld, inck=16, dma=True)
    s.op("gpsimd", lambda e: e.wait_ge(ld.sem, ld.n), dma=True)
    s.flush()

    with ExitStack() as c1:
        E1s = c1.enter_context
        wq_sb = E1s(nc.sbuf_tensor(f"{tag}_wq", [128, NDC * E1], BF16))
        wk_sb = E1s(nc.sbuf_tensor(f"{tag}_wk", [128, NDC * E1], BF16))
        wv_sb = E1s(nc.sbuf_tensor(f"{tag}_wv", [128, NDC * E1], BF16))
        xTc = [E1s(nc.sbuf_tensor(f"{tag}_xTc{i}", [128, NDC * 512], BF16)) for i in range(2)]
        qstage = [E1s(nc.sbuf_tensor(f"{tag}_qs{i}", [128, 512], BF16)) for i in range(4)]
        sq = E1s(nc.sbuf_tensor(f"{tag}_sq", [128, 512], F32))
        vstage = [E1s(nc.sbuf_tensor(f"{tag}_vs{i}", [128, 512], BF16)) for i in range(2)]
        mrow = E1s(nc.sbuf_tensor(f"{tag}_mrow", [1, 1], F32))
        ps_p = [E1s(nc.psum_tensor(f"{tag}_pp{i}", [128, 512], F32)) for i in range(2)]
        ps_v = [E1s(nc.psum_tensor(f"{tag}_pv{i}", [128, 512], F32)) for i in range(2)]
        ps_n = E1s(nc.psum_tensor(f"{tag}_pn", [1, 512], F32))

        s = Sched(nc)
        dv, ac = s.v, s.a
        wld = Ctr(nc, "wld")
        xldp = [Ctr(nc, "xld0"), Ctr(nc, "xld1")]
        mm = Ctr(nc, "mm")
        nmm = Ctr(nc, "nmm")
        stq = [Ctr(nc, "stq0"), Ctr(nc, "stq1"), Ctr(nc, "stq2"), Ctr(nc, "stq3")]
        stv = [Ctr(nc, "stv0"), Ctr(nc, "stv1")]

        s.op("vector", lambda e: e.memset(nqrow[:], 0.0))
        s.op("vector", lambda e: e.memset(nkrow[:], 0.0))
        for i in range(NDC):
            s.op("gpsimd", lambda e, i=i: e.dma_start(
                wq_sb[:, i * E1:(i + 1) * E1], wqT.ap()[i * 128:(i + 1) * 128, :]),
                inc=wld, inck=16, dma=True)
            s.op("gpsimd", lambda e, i=i: e.dma_start(
                wk_sb[:, i * E1:(i + 1) * E1], wkT.ap()[i * 128:(i + 1) * 128, :]),
                inc=wld, inck=16, dma=True)
            s.op("gpsimd", lambda e, i=i: e.dma_start(
                wv_sb[:, i * E1:(i + 1) * E1], wvT.ap()[i * 128:(i + 1) * 128, :]),
                inc=wld, inck=16, dma=True)
        wready = wld.n

        mm_after_tc = {}
        evac_qk = {}
        vevac = {}
        for tc in range(NCH):
            xb = xTc[tc % 2]
            xld = xldp[tc % 2]
            w = [(mm, mm_after_tc.get(tc - 2, 0))]
            for i in range(NDC):
                s.op("sync", lambda e, i=i, xb=xb, tc=tc: e.dma_start(
                    xb[:, i * 512:(i + 1) * 512],
                    xbf.ap()[tc * 512:(tc + 1) * 512, i * 128:(i + 1) * 128],
                    transpose=True), wait=w, inc=xld, inck=16, dma=True)
            xready = xld.n
            for et in range(E1 // 128):
                for which, wsb, park, nrow in [(0, wq_sb, qpark, nqrow), (1, wk_sb, kpark, nkrow)]:
                    pp = ps_p[which]

                    def pgrp(e, et=et, xb=xb, wsb=wsb, pp=pp):
                        for i in range(NDC):
                            inst = e.matmul(pp[:],
                                            wsb[:, i * E1 + et * 128: i * E1 + (et + 1) * 128],
                                            xb[:, i * 512:(i + 1) * 512],
                                            start=(i == 0), stop=(i == NDC - 1))
                        return inst
                    qs = qstage[which * 2 + (et % 2)]
                    s.op("tensor", pgrp, wait=[(wld, wready), (xld, xready),
                                               (ac, evac_qk.get(which, 0))], inc=mm)
                    pd = mm.n
                    s.op("scalar", lambda e, qs=qs, pp=pp: e.copy(qs[:], pp[:]),
                         wait=[(mm, pd), (stq[which * 2 + (et % 2)],
                                          stq[which * 2 + (et % 2)].n)])
                    evac_qk[which] = ac.n
                    evd = ac.n
                    s.op("scalar", lambda e, qs=qs: e.activation(sq[:], qs[:], AF.Square),
                         wait=[(nmm, nmm.n)])
                    sqd = ac.n
                    s.op("sync", lambda e, et=et, tc=tc, qs=qs, park=park: e.dma_start(
                        park.ap()[et * 128:(et + 1) * 128, tc * 512:(tc + 1) * 512], qs[:]),
                        wait=[(ac, sqd)], inc=stq[which * 2 + (et % 2)], inck=16, dma=True)
                    s.op("tensor", lambda e: e.matmul(ps_n[:], g.ones_col_f[:], sq[:],
                                                      start=True, stop=True),
                         wait=[(ac, sqd), (dv, dv.n)], inc=nmm)
                    nd = nmm.n
                    s.op("vector", lambda e: e.reduce_max(mrow[:], ps_n[0:1, :], axis=AX.X),
                         wait=[(nmm, nd)])
                    s.op("vector", lambda e, et=et, nrow=nrow: e.tensor_tensor(
                        nrow[:, et:et + 1], nrow[:, et:et + 1], mrow[:], ALU.max))
            for tj in range(4):
                tt = tc * 4 + tj
                for vc in range(E1 // 512):
                    pv = ps_v[(tj * 4 + vc) % 2]

                    def vgrp(e, tj=tj, vc=vc, xb=xb, pv=pv):
                        for i in range(NDC):
                            inst = e.matmul(pv[:],
                                            xb[:, i * 512 + tj * 128: i * 512 + (tj + 1) * 128],
                                            wv_sb[:, i * E1 + vc * 512: i * E1 + (vc + 1) * 512],
                                            start=(i == 0), stop=(i == NDC - 1))
                        return inst
                    vs = vstage[(tj * 4 + vc) % 2]
                    s.op("tensor", vgrp, wait=[(wld, wready), (xld, xready),
                                               (ac, vevac.get((tj * 4 + vc) % 2, 0))], inc=mm)
                    mval = mm.n
                    s.op("scalar", lambda e, vs=vs, pv=pv: e.copy(vs[:], pv[:]),
                         wait=[(mm, mval), (stv[(tj * 4 + vc) % 2], stv[(tj * 4 + vc) % 2].n)])
                    vevac[(tj * 4 + vc) % 2] = ac.n
                    vd = ac.n
                    s.op("sync", lambda e, tt=tt, vc=vc, vs=vs: e.dma_start(
                        vpark.ap()[tt * 128:(tt + 1) * 128, vc * 512:(vc + 1) * 512], vs[:]),
                        wait=[(ac, vd)], inc=stv[(tj * 4 + vc) % 2], inck=16, dma=True)
            mm_after_tc[tc] = mm.n
        for c in stq + stv:
            s.op("sync", lambda e, c=c: e.wait_ge(c.sem, c.n))
        s.flush()

    with ExitStack() as c2:
        E2 = c2.enter_context
        lt = {k: E2(nc.sbuf_tensor(f"{tag}_l{k}", [128, NH1], F32)) for k in "abcd"}
        prod = E2(nc.sbuf_tensor(f"{tag}_prod", [128, NH1], F32))
        l1r = E2(nc.sbuf_tensor(f"{tag}_l1r", [1, NH1], F32))
        l2r = E2(nc.sbuf_tensor(f"{tag}_l2r", [1, NH1], F32))
        nqh = E2(nc.sbuf_tensor(f"{tag}_nqh", [1, NH1], F32))
        nkh = E2(nc.sbuf_tensor(f"{tag}_nkh", [1, NH1], F32))
        ps_l = E2(nc.psum_tensor(f"{tag}_pl", [1, 512], F32))

        s = Sched(nc)
        dv, ac = s.v, s.a
        ld = Ctr(nc, "ld")
        mm = Ctr(nc, "mm")
        for k, t in [("a", lamq1T), ("b", lamk1T), ("c", lamq2T), ("d", lamk2T)]:
            s.op("sync", lambda e, k=k, t=t: e.dma_start(lt[k][:], t.ap()),
                 inc=ld, inck=16, dma=True)
        ldd = ld.n
        for (ka, kb, out) in [("a", "b", l1r), ("c", "d", l2r)]:
            s.op("vector", lambda e, ka=ka, kb=kb: e.tensor_tensor(
                prod[:], lt[ka][:], lt[kb][:], ALU.mult),
                wait=[(ld, ldd), (mm, mm.n)])
            pd = dv.n
            s.op("tensor", lambda e: e.matmul(ps_l[0:1, 0:NH1], g.ones_col_f[:], prod[:],
                                              start=True, stop=True),
                 wait=[(dv, pd), (ac, ac.n)], inc=mm)
            md = mm.n
            s.op("scalar", lambda e, out=out: e.activation(out[:], ps_l[0:1, 0:NH1], AF.Exp),
                 wait=[(mm, md)])
        exps = ac.n
        s.op("vector", lambda e: e.tensor_tensor(lamrow[:], l1r[:], l2r[:], ALU.subtract),
             wait=[(ac, exps)])
        s.op("vector", lambda e: e.tensor_scalar(lamrow[:], lamrow[:], LAMBDA_INIT, None,
                                                 op0=ALU.add))
        s.op("vector", lambda e: e.tensor_tensor(
            nqh[:], nqrow[0:1, 0:16:2], nqrow[0:1, 1:16:2], ALU.max))
        s.op("vector", lambda e: e.tensor_tensor(
            nkh[:], nkrow[0:1, 0:16:2], nkrow[0:1, 1:16:2], ALU.max))
        s.op("vector", lambda e: e.tensor_tensor(nbrow[:], nqh[:], nkh[:], ALU.mult))
        s.op("scalar", lambda e: e.activation(nbrow[:], nbrow[:], AF.Sqrt),
             wait=[(dv, dv.n)])
        s.op("vector", lambda e: e.tensor_scalar(
            nbrow[:], nbrow[:], -1.0 / 11.313708498984761, None, op0=ALU.mult),
            wait=[(ac, ac.n)])
        ps_bc = E2(nc.psum_tensor(f"{tag}_pbc", [128, 2 * NH1], F32))
        s.op("tensor", lambda e: e.matmul(ps_bc[:, 0:NH1], g.ones_row_f[:],
                                          lamrow[0:1, :], start=True, stop=True),
             wait=[(dv, dv.n), (ac, ac.n)], inc=mm)
        s.op("tensor", lambda e: e.matmul(ps_bc[:, NH1:2 * NH1], g.ones_row_f[:],
                                          nbrow[0:1, :], start=True, stop=True), inc=mm)
        bcd = mm.n
        s.op("vector", lambda e: e.tensor_copy(lamcs[:], ps_bc[:, 0:NH1]),
             wait=[(mm, bcd)])
        s.op("vector", lambda e: e.tensor_copy(nbcs[:], ps_bc[:, NH1:2 * NH1]))
        s.flush()

    with ExitStack() as c3:
        E3 = c3.enter_context
        Qh = E3(nc.sbuf_tensor(f"{tag}_Qh", [128, 2 * T], BF16))
        Kh = E3(nc.sbuf_tensor(f"{tag}_Kh", [128, 2 * T], BF16))
        Vh = E3(nc.sbuf_tensor(f"{tag}_Vh", [128, NT * 257], BF16))
        PTD = [E3(nc.sbuf_tensor(f"{tag}_PT{i}", [128, NT * 1024], BF16)) for i in range(2)]
        rz1 = E3(nc.sbuf_tensor(f"{tag}_rz1", [128, 1], F32))
        rz2 = E3(nc.sbuf_tensor(f"{tag}_rz2", [128, 1], F32))
        oc4 = E3(nc.sbuf_tensor(f"{tag}_oc4", [128, 1024], F32))
        oc2 = E3(nc.sbuf_tensor(f"{tag}_oc2", [128, 256], F32))
        osq = E3(nc.sbuf_tensor(f"{tag}_osq", [128, 256], F32))
        ms4 = E3(nc.sbuf_tensor(f"{tag}_ms4", [128, 4], F32))
        rstd4 = E3(nc.sbuf_tensor(f"{tag}_rstd4", [128, 4], F32))
        epsb = E3(nc.sbuf_tensor(f"{tag}_epsb3", [128, 1], F32))
        rmsb = E3(nc.sbuf_tensor(f"{tag}_rmsb", [128, 256], F32))
        obf4 = E3(nc.sbuf_tensor(f"{tag}_obf4", [128, 1024], BF16))
        ots = E3(nc.sbuf_tensor(f"{tag}_ots", [128, 256], BF16))
        ps_s = [E3(nc.psum_tensor(f"{tag}_ps{i}", [128, 512], F32)) for i in range(3)]
        ps_o = [E3(nc.psum_tensor(f"{tag}_po{i}", [128, 1024], F32)) for i in range(2)]
        ps_tr = E3(nc.psum_tensor(f"{tag}_ptr", [128, 256], BF16))

        s = Sched(nc)
        dv, ac = s.v, s.a
        qkl = [Ctr(nc, "qkl0"), Ctr(nc, "qkl1")]
        vll = [Ctr(nc, "vll0"), Ctr(nc, "vll1")]
        mm = Ctr(nc, "mm")
        pvm = Ctr(nc, "pvm")
        trm = Ctr(nc, "trm")
        sto = Ctr(nc, "sto")
        rmc = Ctr(nc, "rmc")

        s.op("vector", lambda e: e.memset(epsb[:], 1e-5))
        s.op("sync", lambda e: e.dma_start(rmsb[:], rms_bc.ap()), inc=rmc, inck=16, dma=True)
        s.op("vector", lambda e: e.memset(
            Vh.ap().rearrange("p (t c) -> p t c", c=257)[:, :, 256:257], 1.0))
        ones_done = dv.n

        NQC = T // 512
        G = NH1 * NQC
        mm_free = {0: 0}
        qk_ready = {}
        vready = {}
        chunk_exp = {}
        slab_free = {0: 0, 1: 0, 2: 0}
        ot_user = {0: ones_done, 1: ones_done}
        tr_user = {0: 0, 1: 0}
        tr_evac = 0
        st_prev = 0
        pending = None

        def load_head(h):
            qk = qkl[h % 2]
            vl = vll[h % 2]
            pvf = pvm.n + (8 if h > 0 else 0)
            for half in range(2):
                et = 2 * h + half
                s.op("sync", lambda e, half=half, et=et: e.dma_start(
                    Qh[:, half * T:(half + 1) * T], qpark.ap()[et * 128:(et + 1) * 128, :]),
                    wait=[(mm, mm_free[h])], inc=qk, inck=16, dma=True)
                s.op("sync", lambda e, half=half, et=et: e.dma_start(
                    Kh[:, half * T:(half + 1) * T], kpark.ap()[et * 128:(et + 1) * 128, :]),
                    inc=qk, inck=16, dma=True)
            qk_ready[h] = qk.n
            for kt in range(NT):
                s.op("sync", lambda e, kt=kt, h=h: e.dma_start(
                    Vh[:, kt * 257: kt * 257 + 256],
                    vpark.ap()[kt * 128:(kt + 1) * 128, h * 256:(h + 1) * 256]),
                    wait=([(pvm, pvf)] if kt == 0 else []), inc=vl, inck=16, dma=True)
            vready[h] = vl.n

        def make_pv_emitters(g2):
            out = []
            h2, qc2 = divmod(g2, NQC)
            for jj in range(8):
                q, wi = divmod(jj, 2)
                gq = g2 * 4 + q
                PT = PTD[g2 % 2]
                po = ps_o[gq % 2]
                col = wi * 512
                for kt2 in range(NT):
                    def one(e, kt2=kt2, q=q, wi=wi, PT=PT, po=po, col=col):
                        return e.matmul(
                            po[:, col:col + 257],
                            PT[:, kt2 * 1024 + wi * 512 + q * 128:
                               kt2 * 1024 + wi * 512 + (q + 1) * 128],
                            Vh[:, kt2 * 257:(kt2 + 1) * 257],
                            start=(kt2 == 0), stop=(kt2 == NT - 1))
                    out.append((one, kt2 == 0, kt2 == NT - 1, g2, gq, h2, jj))
            return out

        def emit_pv_item(item):
            one, first, last, g2, gq, h2, jj = item
            w = []
            if first:
                w = [(ac, chunk_exp[g2]), (dv, ot_user[gq % 2]),
                     (vll[h2 % 2], vready[h2])]
            s.op("tensor", one, wait=w, inc=(pvm if last else None))
            if last and jj % 2 == 1:
                emit_combine(g2, jj // 2)
                if jj == 7:
                    emit_rms(g2)

        def emit_combine(g2, q):
            h2, qc2 = divmod(g2, NQC)
            gq = g2 * 4 + q
            po = ps_o[gq % 2]
            pv_done = pvm.n
            s.op("vector", lambda e, po=po: e.reciprocal(rz1[:], po[:, 256:257]),
                 wait=[(pvm, pv_done)])
            s.op("vector", lambda e, po=po: e.reciprocal(rz2[:], po[:, 768:769]))
            s.op("vector", lambda e, po=po, q=q: e.tensor_scalar(
                oc4[:, q * 256:(q + 1) * 256], po[:, 0:256], rz1[:, 0:1], None,
                op0=ALU.mult))
            s.op("vector", lambda e, h2=h2: e.tensor_tensor(
                rz2[:], rz2[:], lamcs[:, h2:h2 + 1], ALU.mult))
            s.op("vector", lambda e, po=po: e.tensor_scalar(
                oc2[:], po[:, 512:768], rz2[:, 0:1], None, op0=ALU.mult))
            s.op("vector", lambda e, q=q: e.tensor_tensor(
                oc4[:, q * 256:(q + 1) * 256], oc4[:, q * 256:(q + 1) * 256],
                oc2[:], ALU.subtract))
            ot_user[gq % 2] = dv.n

        def emit_rms(g2):
            nonlocal pending
            h2, qc2 = divmod(g2, NQC)
            for q in range(4):
                s.op("vector", lambda e, q=q: e.tensor_tensor(
                    osq[:], oc4[:, q * 256:(q + 1) * 256],
                    oc4[:, q * 256:(q + 1) * 256], ALU.mult))
                s.op("vector", lambda e, q=q: e.reduce_sum(
                    ms4[:, q:q + 1], osq[:], axis=AX.X))
            s.op("vector", lambda e: e.tensor_scalar(ms4[:], ms4[:], 1.0 / 256.0, None,
                                                     op0=ALU.mult))
            s.op("scalar", lambda e: e.activation(ms4[:], ms4[:], AF.Sqrt,
                                                  bias=epsb[:, 0:1]),
                 wait=[(dv, dv.n)])
            s.op("vector", lambda e: e.reciprocal(rstd4[:], ms4[:]), wait=[(ac, ac.n)])
            for q in range(4):
                s.op("vector", lambda e, q=q: e.tensor_scalar(
                    oc4[:, q * 256:(q + 1) * 256], oc4[:, q * 256:(q + 1) * 256],
                    rstd4[:, q:q + 1], 1.0 - LAMBDA_INIT, op0=ALU.mult, op1=ALU.mult))
            for q in range(4):
                s.op("vector", lambda e, q=q: e.tensor_tensor(
                    obf4[:, q * 256:(q + 1) * 256], oc4[:, q * 256:(q + 1) * 256],
                    rmsb[:], ALU.mult),
                    wait=[(trm, tr_user[0]), (rmc, 16)])
            pending = (h2, qc2, dv.n)

        def flush_transpose():
            nonlocal tr_evac, st_prev, pending
            if pending is None:
                return
            h2, qc2, obd = pending
            pending = None
            for q in range(4):
                qt = qc2 * 4 + q
                for half in range(2):
                    s.op("tensor", lambda e, q=q, half=half: e.transpose(
                        ps_tr[:, 0:128] if half == 0 else ps_tr[:, 128:256],
                        obf4[:, q * 256 + half * 128: q * 256 + (half + 1) * 128],
                        g.idb[:]),
                        wait=[(dv, obd), (ac, tr_evac)], inc=trm)
                    trd = trm.n
                    s.op("scalar", lambda e, half=half: e.copy(
                        ots[:, half * 128:(half + 1) * 128],
                        ps_tr[:, 0:128] if half == 0 else ps_tr[:, 128:256]),
                        wait=[(trm, trd), (sto, st_prev)])
                tr_evac = ac.n
                for half in range(2):
                    s.op("gpsimd", lambda e, h2=h2, qt=qt, half=half: e.dma_start(
                        oT.ap()[(2 * h2 + half) * 128:(2 * h2 + half + 1) * 128,
                                qt * 128:(qt + 1) * 128],
                        ots[:, half * 128:(half + 1) * 128]),
                        wait=[(ac, tr_evac)], inc=sto, inck=16, dma=True)
                st_prev = sto.n
            tr_user[0] = trm.n

        pvq = []
        for gg in range(G):
            h, qc = divmod(gg, NQC)
            if qc == 0:
                if h > 0:
                    mm_free[h] = mm.n
                load_head(h)
            PT = PTD[gg % 2]
            pt_free = pvm.n
            if gg > 0:
                assert not pvq
                pvq = make_pv_emitters(gg - 1)
            for kt in range(NT):
                for wi in range(2):
                    j = (2 * kt + wi) % 3
                    def smm(e, kt=kt, qc=qc, wi=wi, j=j):
                        return e.matmul(ps_s[j][:],
                                        Kh[:, wi * T + kt * 128: wi * T + (kt + 1) * 128],
                                        Qh[:, wi * T + qc * 512: wi * T + (qc + 1) * 512],
                                        start=True, stop=True)
                    s.op("tensor", smm,
                         wait=[(qkl[h % 2], qk_ready[h]), (ac, slab_free[j])], inc=mm)
                    sd = mm.n
                    s.op("scalar", lambda e, kt=kt, wi=wi, j=j, PT=PT, h=h: e.activation(
                        PT[:, kt * 1024 + wi * 512: kt * 1024 + (wi + 1) * 512],
                        ps_s[j][:], AF.Exp,
                        bias=nbcs[:, h:h + 1], scale=1.0 / 11.313708498984761),
                        wait=[(mm, sd), (pvm, pt_free)])
                    slab_free[j] = ac.n
                for _ in range(8):
                    if pvq:
                        emit_pv_item(pvq.pop(0))
                if kt == NT // 2 - 1:
                    flush_transpose()
            chunk_exp[gg] = ac.n
        flush_transpose()
        for item in make_pv_emitters(G - 1):
            emit_pv_item(item)
        flush_transpose()
        s.op("sync", lambda e, a=sto.n: e.wait_ge(sto.sem, a))
        s.flush()

    with ExitStack() as c4:
        E4 = c4.enter_context
        wo_sb = E4(nc.sbuf_tensor(f"{tag}_wo", [128, (E1 // 128) * D], BF16))
        otc = [E4(nc.sbuf_tensor(f"{tag}_otc{i}", [128, (E1 // 128) * 128], BF16))
               for i in range(2)]
        x2t = [E4(nc.sbuf_tensor(f"{tag}_x2t{i}", [128, D], F32)) for i in range(2)]
        hb2 = [E4(nc.sbuf_tensor(f"{tag}_hb{i}", [128, D], F32)) for i in range(2)]
        ob2 = [E4(nc.sbuf_tensor(f"{tag}_ob{i}", [128, D], F32)) for i in range(2)]
        tmp = {"sqs": E4(nc.sbuf_tensor(f"{tag}_sqs", [128, D], F32)),
               "s1": E4(nc.sbuf_tensor(f"{tag}_s1", [128, 1], F32)),
               "s2": E4(nc.sbuf_tensor(f"{tag}_s2", [128, 1], F32)),
               "mmean": E4(nc.sbuf_tensor(f"{tag}_mmean", [128, 1], F32)),
               "vvar": E4(nc.sbuf_tensor(f"{tag}_vvar", [128, 1], F32)),
               "rstd": E4(nc.sbuf_tensor(f"{tag}_rstd", [128, 1], F32))}
        epsb = E4(nc.sbuf_tensor(f"{tag}_epsb4", [128, 1], F32))
        ps_o4 = [E4(nc.psum_tensor(f"{tag}_po4{i}", [128, 512], F32)) for i in range(4)]

        s = Sched(nc)
        dv, ac = s.v, s.a
        wld = Ctr(nc, "wld")
        oldp = [Ctr(nc, "old0"), Ctr(nc, "old1")]
        xldr = Ctr(nc, "xldr")
        mm = Ctr(nc, "mm")
        stp = [Ctr(nc, "stp0"), Ctr(nc, "stp1")]

        s.op("vector", lambda e: e.memset(epsb[:], 1e-5))
        for i in range(E1 // 128):
            s.op("gpsimd", lambda e, i=i: e.dma_start(
                wo_sb[:, i * D:(i + 1) * D], woT.ap()[i * 128:(i + 1) * 128, :]),
                inc=wld, inck=16, dma=True)
        wready = wld.n

        ot_done = {}
        for i in range(E1 // 128):
            s.op("sync", lambda e, i=i: e.dma_start(
                otc[0][:, i * 128:(i + 1) * 128],
                oT.ap()[i * 128:(i + 1) * 128, 0:128]),
                inc=oldp[0], inck=16, dma=True)
        ot_done[0] = oldp[0].n

        mm_after_tt = {}
        x2_user = {}
        ps_user = {}
        for tt in range(NT):
            cb = otc[tt % 2]
            xb = x2t[tt % 2]
            hb = hb2[tt % 2]
            ob = ob2[tt % 2]
            s.op("sync", lambda e, tt=tt, xb=xb: e.dma_start(
                xb[:], x2.ap()[tt * 128:(tt + 1) * 128, :]),
                wait=[(dv, x2_user.get(tt - 2, 0))], inc=xldr, inck=16, dma=True)
            xrd = xldr.n
            for dc in range(2):
                pslab = (tt % 2) * 2 + dc
                pp = ps_o4[pslab]

                def ogrp(e, dc=dc, cb=cb, pp=pp):
                    for i in range(E1 // 128):
                        inst = e.matmul(pp[:],
                                        cb[:, i * 128:(i + 1) * 128],
                                        wo_sb[:, i * D + dc * 512: i * D + (dc + 1) * 512],
                                        start=(i == 0), stop=(i == E1 // 128 - 1))
                    return inst
                s.op("tensor", ogrp, wait=[(wld, wready), (oldp[tt % 2], ot_done[tt]),
                                           (dv, ps_user.get(pslab, 0))], inc=mm)
                mval = mm.n
                s.op("vector", lambda e, dc=dc, xb=xb, hb=hb, pp=pp: e.tensor_tensor(
                    hb[:, dc * 512:(dc + 1) * 512], xb[:, dc * 512:(dc + 1) * 512],
                    pp[:], ALU.add),
                    wait=[(mm, mval), (xldr, xrd), (stp[tt % 2], stp[tt % 2].n)])
                ps_user[pslab] = dv.n
            x2_user[tt] = dv.n
            h_ready = dv.n
            mm_after_tt[tt] = mm.n
            if tt + 1 < NT:
                for i in range(E1 // 128):
                    s.op("sync", lambda e, i=i, q=tt + 1: e.dma_start(
                        otc[q % 2][:, i * 128:(i + 1) * 128],
                        oT.ap()[i * 128:(i + 1) * 128, q * 128:(q + 1) * 128]),
                        wait=[(mm, mm_after_tt.get(tt - 1, 0))],
                        inc=oldp[(tt + 1) % 2], inck=16, dma=True)
                ot_done[tt + 1] = oldp[(tt + 1) % 2].n
            out_done = emit_ln_tile(s, tmp, hb, ob, epsb, h_ready)
            s.op("sync", lambda e, tt=tt, ob=ob: e.dma_start(
                x3.ap()[tt * 128:(tt + 1) * 128, :], ob[:]),
                wait=[(dv, out_done)], inc=stp[tt % 2], inck=16, dma=True)
        s.op("sync", lambda e, a=stp[0].n: e.wait_ge(stp[0].sem, a))
        s.op("sync", lambda e, a=stp[1].n: e.wait_ge(stp[1].sem, a))
        s.flush()



HQ2, HKV2, DK2 = 16, 4, 64


def emit_phase_d_attn(nc, ctx, g, x3, wq, wk, wv, wo, g1bc, g2bc, cosq, sinq,
                      h_out, xm_out, xmbf, tag="d", dbg=None):
    E = ctx.enter_context
    xnbf = nc.dram_tensor(f"{tag}_xnbf", [T, D], BF16, kind="Internal")
    qpark = nc.dram_tensor(f"{tag}_qpark", [HQ2 * DK2, T], BF16, kind="Internal")
    kpark = nc.dram_tensor(f"{tag}_kpark", [HKV2 * DK2, T], BF16, kind="Internal")
    vpark = nc.dram_tensor(f"{tag}_vpark", [T, HKV2 * DK2], BF16, kind="Internal")
    oT = nc.dram_tensor(f"{tag}_oT", [HQ2 * DK2, T], BF16, kind="Internal")

    nqc = E(nc.sbuf_tensor(f"{tag}_nqc", [128, 1], F32))
    nkc = E(nc.sbuf_tensor(f"{tag}_nkc", [128, 1], F32))
    nbc = E(nc.sbuf_tensor(f"{tag}_nbc", [128, 1], F32))

    with ExitStack() as c0:
        E0 = c0.enter_context
        x3t = [E0(nc.sbuf_tensor(f"{tag}_x3t{i}", [128, D], F32)) for i in range(2)]
        xnt = [E0(nc.sbuf_tensor(f"{tag}_xnt{i}", [128, D], BF16)) for i in range(2)]
        g1t = E0(nc.sbuf_tensor(f"{tag}_g1t", [128, D], F32))
        sqs = E0(nc.sbuf_tensor(f"{tag}_sqs0", [128, D], F32))
        ms = E0(nc.sbuf_tensor(f"{tag}_ms0", [128, 1], F32))
        rstd = E0(nc.sbuf_tensor(f"{tag}_rstd0", [128, 1], F32))
        epsb = E0(nc.sbuf_tensor(f"{tag}_epsb0", [128, 1], F32))
        tmpf = E0(nc.sbuf_tensor(f"{tag}_tmpf0", [128, D], F32))

        s = Sched(nc)
        dv, ac = s.v, s.a
        ld = Ctr(nc, "ld")
        xldp = [Ctr(nc, "xl0"), Ctr(nc, "xl1")]
        stp = [Ctr(nc, "st0"), Ctr(nc, "st1")]
        s.op("vector", lambda e: e.memset(epsb[:], 1e-6))
        s.op("sync", lambda e: e.dma_start(g1t[:], g1bc.ap()), inc=ld, inck=16, dma=True)
        g1d = ld.n
        xn_user = {}
        for tt in range(NT):
            xb = x3t[tt % 2]
            nb = xnt[tt % 2]
            s.op("sync", lambda e, tt=tt, xb=xb: e.dma_start(
                xb[:], x3.ap()[tt * 128:(tt + 1) * 128, :]),
                wait=[(dv, xn_user.get(tt - 2, 0))], inc=xldp[tt % 2], inck=16, dma=True)
            xrd = xldp[tt % 2].n
            s.op("scalar", lambda e, xb=xb: e.activation(sqs[:], xb[:], AF.Square),
                 wait=[(xldp[tt % 2], xrd), (dv, dv.n)])
            s.op("vector", lambda e: e.reduce_sum(ms[:], sqs[:], axis=AX.X),
                 wait=[(ac, ac.n)])
            s.op("vector", lambda e: e.tensor_scalar(ms[:], ms[:], 1.0 / D, None, op0=ALU.mult))
            s.op("scalar", lambda e: e.activation(ms[:], ms[:], AF.Sqrt, bias=epsb[:, 0:1]),
                 wait=[(dv, dv.n)])
            s.op("vector", lambda e: e.reciprocal(rstd[:], ms[:]), wait=[(ac, ac.n)])
            s.op("vector", lambda e, xb=xb: e.tensor_scalar(
                tmpf[:], xb[:], rstd[:, 0:1], None, op0=ALU.mult), wait=[(ld, g1d)])
            s.op("vector", lambda e, nb=nb: e.tensor_tensor(nb[:], tmpf[:], g1t[:], ALU.mult),
                 wait=[(stp[tt % 2], stp[tt % 2].n)])
            xn_user[tt] = dv.n
            s.op("sync", lambda e, tt=tt, nb=nb: e.dma_start(
                xnbf.ap()[tt * 128:(tt + 1) * 128, :], nb[:]),
                wait=[(dv, dv.n)], inc=stp[tt % 2], inck=16, dma=True)
        s.op("sync", lambda e, a=stp[0].n: e.wait_ge(stp[0].sem, a))
        s.op("sync", lambda e, a=stp[1].n: e.wait_ge(stp[1].sem, a))
        s.flush()

    with ExitStack() as c1:
        E1s = c1.enter_context
        wq_sb = E1s(nc.sbuf_tensor(f"{tag}_wq", [128, NDC * 1024], BF16))
        wk_sb = E1s(nc.sbuf_tensor(f"{tag}_wk", [128, NDC * 256], BF16))
        wv_sb = E1s(nc.sbuf_tensor(f"{tag}_wv", [128, NDC * 256], BF16))
        xTc = [E1s(nc.sbuf_tensor(f"{tag}_xTc{i}", [128, NDC * 128], BF16)) for i in range(2)]
        cost = [E1s(nc.sbuf_tensor(f"{tag}_cost{i}", [128, 1024], F32)) for i in range(2)]
        sint = [E1s(nc.sbuf_tensor(f"{tag}_sint{i}", [128, 1024], F32)) for i in range(2)]
        qsw = E1s(nc.sbuf_tensor(f"{tag}_qsw", [128, 1024], F32))
        qrot = E1s(nc.sbuf_tensor(f"{tag}_qrot", [128, 1024], F32))
        qbf = E1s(nc.sbuf_tensor(f"{tag}_qbf", [128, 1024], BF16))
        ksw = E1s(nc.sbuf_tensor(f"{tag}_ksw", [128, 256], F32))
        krot = E1s(nc.sbuf_tensor(f"{tag}_krot", [128, 256], F32))
        kbf = E1s(nc.sbuf_tensor(f"{tag}_kbf", [128, 256], BF16))
        vbf = E1s(nc.sbuf_tensor(f"{tag}_vbf", [128, 256], BF16))
        sq = E1s(nc.sbuf_tensor(f"{tag}_sq1", [128, 1024], F32))
        nrm = E1s(nc.sbuf_tensor(f"{tag}_nrm1", [128, 1], F32))
        tstage = E1s(nc.sbuf_tensor(f"{tag}_tst", [128, 1024 + 256], BF16))
        ps_q = [E1s(nc.psum_tensor(f"{tag}_pq{i}", [128, 512], F32)) for i in range(2)]
        ps_kv = E1s(nc.psum_tensor(f"{tag}_pkv", [128, 512], F32))
        ps_tr = [E1s(nc.psum_tensor(f"{tag}_ptr{i}", [128, 512], BF16)) for i in range(2)]

        s = Sched(nc)
        dv, ac = s.v, s.a
        wld = Ctr(nc, "wld")
        xldp = [Ctr(nc, "xl0"), Ctr(nc, "xl1")]
        mm = Ctr(nc, "mm")
        trm = Ctr(nc, "trm")
        stq = Ctr(nc, "stq")
        stv = Ctr(nc, "stv")

        s.op("vector", lambda e: e.memset(nqc[:], 0.0))
        s.op("vector", lambda e: e.memset(nkc[:], 0.0))
        for i in range(NDC):
            s.op("gpsimd", lambda e, i=i: e.dma_start(
                wq_sb[:, i * 1024:(i + 1) * 1024], wq.ap()[i * 128:(i + 1) * 128, :]),
                inc=wld, inck=16, dma=True)
            s.op("gpsimd", lambda e, i=i: e.dma_start(
                wk_sb[:, i * 256:(i + 1) * 256], wk.ap()[i * 128:(i + 1) * 128, :]),
                inc=wld, inck=16, dma=True)
            s.op("gpsimd", lambda e, i=i: e.dma_start(
                wv_sb[:, i * 256:(i + 1) * 256], wv.ap()[i * 128:(i + 1) * 128, :]),
                inc=wld, inck=16, dma=True)
        wready = wld.n

        def hv(t, a, b):
            return t.ap().rearrange("p (h c) -> p h c", c=64)[:, :, a:b]

        mm_after = {}
        rotq_user = 0
        rotk_user = 0
        v_user = 0
        tr_done = {}
        for tt in range(NT):
            xb = xTc[tt % 2]
            xld = xldp[tt % 2]
            w = [(mm, mm_after.get(tt - 2, 0))]
            for i in range(NDC):
                s.op("sync", lambda e, i=i, tt=tt, xb=xb: e.dma_start(
                    xb[:, i * 128:(i + 1) * 128],
                    xnbf.ap()[tt * 128:(tt + 1) * 128, i * 128:(i + 1) * 128],
                    transpose=True), wait=w, inc=xld, inck=16, dma=True)
            s.op("sync", lambda e, tt=tt: e.dma_start(
                cost[tt % 2][:], cosq.ap()[tt * 128:(tt + 1) * 128, :]),
                wait=[(dv, rotq_user)], inc=xld, inck=16, dma=True)
            s.op("sync", lambda e, tt=tt: e.dma_start(
                sint[tt % 2][:], sinq.ap()[tt * 128:(tt + 1) * 128, :]),
                inc=xld, inck=16, dma=True)
            xready = xld.n

            for dc in range(2):
                def qgrp(e, dc=dc, xb=xb):
                    for i in range(NDC):
                        inst = e.matmul(ps_q[dc][:],
                                        xb[:, i * 128:(i + 1) * 128],
                                        wq_sb[:, i * 1024 + dc * 512: i * 1024 + (dc + 1) * 512],
                                        start=(i == 0), stop=(i == NDC - 1))
                    return inst
                s.op("tensor", qgrp, wait=[(wld, wready), (xld, xready), (dv, rotq_user)],
                     inc=mm)
            q_mm = mm.n
            for dc in range(2):
                s.op("vector", lambda e, dc=dc: e.tensor_scalar(
                    hv(qsw, 0, 32)[:, 8 * dc:8 * (dc + 1), :],
                    ps_q[dc].ap().rearrange("p (h c) -> p h c", c=64)[:, :, 32:64],
                    -1.0, None, op0=ALU.mult),
                    wait=[(mm, q_mm), (xld, xready)])
                s.op("vector", lambda e, dc=dc: e.tensor_copy(
                    hv(qsw, 32, 64)[:, 8 * dc:8 * (dc + 1), :],
                    ps_q[dc].ap().rearrange("p (h c) -> p h c", c=64)[:, :, 0:32]))
                s.op("vector", lambda e, dc=dc, tt=tt: e.tensor_tensor(
                    qrot[:, dc * 512:(dc + 1) * 512], ps_q[dc][:],
                    cost[tt % 2][:, dc * 512:(dc + 1) * 512], ALU.mult))
            s.op("vector", lambda e, tt=tt: e.tensor_tensor(qsw[:], qsw[:], sint[tt % 2][:],
                                                            ALU.mult))
            s.op("vector", lambda e: e.tensor_tensor(qrot[:], qrot[:], qsw[:], ALU.add))
            s.op("vector", lambda e: e.tensor_copy(qbf[:], qrot[:]),
                 wait=[(trm, tr_done.get("qbf", 0))])
            rot_q = dv.n
            s.op("scalar", lambda e: e.activation(sq[:], qrot[:], AF.Square),
                 wait=[(dv, rot_q)])
            s.op("vector", lambda e: e.reduce_sum(nrm[:], sq[:], axis=AX.X),
                 wait=[(ac, ac.n)])
            s.op("vector", lambda e: e.tensor_tensor(nqc[:], nqc[:], nrm[:], ALU.max))
            rotq_user = dv.n

            def kgrp(e, xb=xb):
                for i in range(NDC):
                    inst = e.matmul(ps_kv[:, 0:256],
                                    xb[:, i * 128:(i + 1) * 128],
                                    wk_sb[:, i * 256:(i + 1) * 256],
                                    start=(i == 0), stop=(i == NDC - 1))
                return inst
            s.op("tensor", kgrp, wait=[(xld, xready), (dv, rotk_user)], inc=mm)
            k_mm = mm.n
            s.op("vector", lambda e: e.tensor_scalar(
                hv(ksw, 0, 32),
                ps_kv.ap().rearrange("p (h c) -> p h c", c=64)[:, 0:4, 32:64],
                -1.0, None, op0=ALU.mult), wait=[(mm, k_mm)])
            s.op("vector", lambda e: e.tensor_copy(
                hv(ksw, 32, 64),
                ps_kv.ap().rearrange("p (h c) -> p h c", c=64)[:, 0:4, 0:32]))
            s.op("vector", lambda e, tt=tt: e.tensor_tensor(
                krot[:], ps_kv[:, 0:256], cost[tt % 2][:, 0:256], ALU.mult))
            s.op("vector", lambda e, tt=tt: e.tensor_tensor(
                ksw[:], ksw[:], sint[tt % 2][:, 0:256], ALU.mult))
            s.op("vector", lambda e: e.tensor_tensor(krot[:], krot[:], ksw[:], ALU.add))
            s.op("vector", lambda e: e.tensor_copy(kbf[:], krot[:]),
                 wait=[(trm, tr_done.get("kbf", 0))])
            rot_k = dv.n
            s.op("scalar", lambda e: e.activation(sq[:, 0:256], krot[:], AF.Square),
                 wait=[(dv, rot_k)])
            s.op("vector", lambda e: e.reduce_sum(nrm[:], sq[:, 0:256], axis=AX.X),
                 wait=[(ac, ac.n)])
            s.op("vector", lambda e: e.tensor_tensor(nkc[:], nkc[:], nrm[:], ALU.max))
            rotk_user = dv.n

            def vgrp(e, xb=xb):
                for i in range(NDC):
                    inst = e.matmul(ps_kv[:, 256:512],
                                    xb[:, i * 128:(i + 1) * 128],
                                    wv_sb[:, i * 256:(i + 1) * 256],
                                    start=(i == 0), stop=(i == NDC - 1))
                return inst
            s.op("tensor", vgrp, wait=[(xld, xready), (dv, v_user)], inc=mm)
            v_mm = mm.n
            s.op("vector", lambda e: e.tensor_copy(vbf[:], ps_kv[:, 256:512]),
                 wait=[(mm, v_mm), (stv, stv.n)])
            v_user = dv.n
            s.op("sync", lambda e, tt=tt: e.dma_start(
                vpark.ap()[tt * 128:(tt + 1) * 128, :], vbf[:]),
                wait=[(dv, v_user)], inc=stv, inck=16, dma=True)

            kq_done = {}
            for i in range(10):
                ptr = ps_tr[i % 2]
                if i < 8:
                    src_t, col = qbf, i * 128
                    dep = rot_q
                else:
                    src_t, col = kbf, (i - 8) * 128
                    dep = rot_k
                s.op("tensor", lambda e, ptr=ptr, src_t=src_t, col=col: e.transpose(
                    ptr[:, 0:128], src_t[:, col:col + 128], g.idb[:]),
                    wait=[(dv, dep), (ac, tr_done.get(("ev", i % 2), 0))], inc=trm)
                td = trm.n
                s.op("scalar", lambda e, i=i, ptr=ptr: e.copy(
                    tstage[:, i * 128:(i + 1) * 128], ptr[:, 0:128]),
                    wait=[(trm, td), (stq, stq.n)])
                tr_done[("ev", i % 2)] = ac.n
            tr_done["qbf"] = trm.n
            tr_done["kbf"] = trm.n
            ts_done = ac.n
            for i in range(8):
                s.op("sync", lambda e, i=i, tt=tt: e.dma_start(
                    qpark.ap()[i * 128:(i + 1) * 128, tt * 128:(tt + 1) * 128],
                    tstage[:, i * 128:(i + 1) * 128]),
                    wait=[(ac, ts_done)], inc=stq, inck=16, dma=True)
            for i in range(2):
                s.op("sync", lambda e, i=i, tt=tt: e.dma_start(
                    kpark.ap()[i * 128:(i + 1) * 128, tt * 128:(tt + 1) * 128],
                    tstage[:, (8 + i) * 128:(9 + i) * 128]),
                    inc=stq, inck=16, dma=True)
            mm_after[tt] = mm.n
        for c in [stq, stv]:
            s.op("sync", lambda e, c=c: e.wait_ge(c.sem, c.n))
        s.flush()

    if dbg is not None:
        s = Sched(nc)
        dst = Ctr(nc, "dst")
        for k, t in [("qpark", qpark), ("kpark", kpark), ("vpark", vpark)]:
            if k in dbg:
                s.op("gpsimd", lambda e, k=k, t=t: e.dma_start(dbg[k].ap(), t.ap()),
                     inc=dst, inck=16, dma=True)
        s.op("gpsimd", lambda e: e.wait_ge(dst.sem, dst.n), dma=True)
        s.flush()

    with ExitStack() as c2:
        E2 = c2.enter_context
        KT = E2(nc.sbuf_tensor(f"{tag}_KT", [64, 4 * T], BF16))
        VTp = E2(nc.sbuf_tensor(f"{tag}_VTp", [128, NT * 256], BF16))
        VT = E2(nc.sbuf_tensor(f"{tag}_VT", [128, NT * 260], BF16))
        QT = [E2(nc.sbuf_tensor(f"{tag}_QT{i}", [64, T], BF16)) for i in range(2)]
        PT2 = [E2(nc.sbuf_tensor(f"{tag}_PT{i}", [128, NT * 512], BF16)) for i in range(2)]
        rzrow = E2(nc.sbuf_tensor(f"{tag}_rzrow", [1, 512], F32))
        rzb = E2(nc.sbuf_tensor(f"{tag}_rzb", [64, 512], F32))
        ocb = [E2(nc.sbuf_tensor(f"{tag}_oc{i}", [64, 512], BF16)) for i in range(2)]
        mqv = E2(nc.sbuf_tensor(f"{tag}_mqv", [1, 1], F32))
        mkv = E2(nc.sbuf_tensor(f"{tag}_mkv", [1, 1], F32))
        ps_s = [E2(nc.psum_tensor(f"{tag}_ps{i}", [128, 1024], F32)) for i in range(2)]
        ps_v = [E2(nc.psum_tensor(f"{tag}_pv{i}", [65, 512], F32)) for i in range(2)]
        ps_z = E2(nc.psum_tensor(f"{tag}_pz", [64, 512], F32))
        ps_t2 = E2(nc.psum_tensor(f"{tag}_pt2", [128, 512], F32))

        s = Sched(nc)
        dv, ac = s.v, s.a
        kld = Ctr(nc, "kld")
        qld = [Ctr(nc, "ql0"), Ctr(nc, "ql1")]
        mm = Ctr(nc, "mm")
        pvm = Ctr(nc, "pvm")
        rpm = Ctr(nc, "rpm")
        sto2 = [Ctr(nc, "sto0"), Ctr(nc, "sto1")]

        s.op("tensor", lambda e: e.transpose(ps_t2[0:1, 0:128], nqc[:], g.idf[:]), inc=mm)
        s.op("tensor", lambda e: e.transpose(ps_t2[0:1, 128:256], nkc[:], g.idf[:]), inc=mm)
        tmm = mm.n
        s.op("vector", lambda e: e.reduce_max(mqv[:], ps_t2[0:1, 0:128], axis=AX.X),
             wait=[(mm, tmm)])
        s.op("vector", lambda e: e.reduce_max(mkv[:], ps_t2[0:1, 128:256], axis=AX.X))
        s.op("vector", lambda e: e.tensor_tensor(mqv[:], mqv[:], mkv[:], ALU.mult))
        s.op("scalar", lambda e: e.activation(mqv[:], mqv[:], AF.Sqrt), wait=[(dv, dv.n)])
        s.op("vector", lambda e: e.tensor_scalar(mqv[:], mqv[:], -1.0 / 8.0, None,
                                                 op0=ALU.mult), wait=[(ac, ac.n)])
        nb_done = dv.n
        s.op("tensor", lambda e: e.matmul(ps_t2[:, 256:257], g.ones_row_f[:], mqv[:],
                                          start=True, stop=True),
             wait=[(dv, nb_done)], inc=mm)
        s.op("vector", lambda e: e.tensor_copy(nbc[:], ps_t2[:, 256:257]), wait=[(mm, mm.n)])
        nbc_dv = dv.n

        for gi in range(4):
            s.op("sync", lambda e, gi=gi: e.dma_start(
                KT[0:64, gi * T:(gi + 1) * T], kpark.ap()[gi * 64:(gi + 1) * 64, :]),
                inc=kld, inck=16, dma=True)
        for kt in range(NT):
            s.op("sync", lambda e, kt=kt: e.dma_start(
                VTp[:, kt * 256:(kt + 1) * 256], vpark.ap()[kt * 128:(kt + 1) * 128, :]),
                inc=kld, inck=16, dma=True)
        kv_ready = kld.n
        s.op("vector", lambda e: e.tensor_copy(
            VT.ap().rearrange("p (q x) -> p q x", x=65)[:, :, 0:64],
            VTp.ap().rearrange("p (q x) -> p q x", x=64)[:, :, 0:64]),
            wait=[(kld, kv_ready)])
        s.op("vector", lambda e: e.memset(
            VT.ap().rearrange("p (q x) -> p q x", x=65)[:, :, 64:65], 1.0))
        ones_dv = dv.n

        mm_free = {0: 0}
        NQC2 = T // 512
        GG = HQ2 * NQC2
        q_ready = {}
        chunk_exp = {}
        slab_free = {0: 0, 1: 0}
        ot_user = {0: ones_dv, 1: ones_dv}
        oc_st = {0: 0, 1: 0}
        pz_user = 0
        rz_user = 0

        def make_pv_items(g2):
            h2, qc2 = divmod(g2, NQC2)
            gi2 = h2 % HKV2
            pov = ps_v[g2 % 2]
            PTb = PT2[g2 % 2]
            nk = 4 * qc2 + 4
            items = []
            for kt2 in range(nk):
                def one(e, kt2=kt2, gi2=gi2, pov=pov, PTb=PTb, nk=nk):
                    return e.matmul(
                        pov[:, :],
                        VT[:, kt2 * 260 + gi2 * 65: kt2 * 260 + (gi2 + 1) * 65],
                        PTb[:, kt2 * 512:(kt2 + 1) * 512],
                        start=(kt2 == 0), stop=(kt2 == nk - 1))
                w = []
                if kt2 == 0:
                    w = [(ac, chunk_exp[g2]), (dv, ot_user[g2 % 2])]
                items.append(lambda one=one, w=w, last=(kt2 == nk - 1):
                             s.op("tensor", one, wait=w, inc=(pvm if last else None)))

            def comb_a():
                pvd = pvm.n
                s.op("vector", lambda e, pov=pov: e.reciprocal(
                    rzrow[:], pov[64:65, :]), wait=[(pvm, pvd), (rpm, rz_user)])
            items.append(comb_a)

            def comb_b():
                nonlocal pz_user, rz_user
                s.op("tensor", lambda e: e.matmul(
                    ps_z[:], g.ones_row_f[0:1, 0:64], rzrow[:],
                    start=True, stop=True),
                    wait=[(dv, dv.n), (dv, pz_user)], inc=rpm)
                rz_user = rpm.n
                rpd = rpm.n
                oc = ocb[g2 % 2]
                s.op("vector", lambda e: e.tensor_copy(rzb[:], ps_z[:]),
                     wait=[(rpm, rpd)])
                pz_user = dv.n
                s.op("vector", lambda e, pov=pov, oc=oc: e.tensor_tensor(
                    oc[:], pov[0:64, :], rzb[:], ALU.mult),
                    wait=[(sto2[g2 % 2], oc_st[g2 % 2])])
                ot_user[g2 % 2] = dv.n
                ocd = dv.n
                s.op("sync", lambda e, h2=h2, qc2=qc2, oc=oc: e.dma_start(
                    oT.ap()[h2 * 64:(h2 + 1) * 64, qc2 * 512:(qc2 + 1) * 512], oc[:]),
                    wait=[(dv, ocd)], inc=sto2[g2 % 2], inck=16, dma=True)
                oc_st[g2 % 2] = sto2[g2 % 2].n
            items.append(comb_b)
            return items

        pvq = []
        for gg2 in range(GG):
            h, qc = divmod(gg2, NQC2)
            gi = h % HKV2
            qb = QT[h % 2]
            if qc == 0:
                if h > 0:
                    mm_free[h] = mm.n
                hl = qld[h % 2]
                s.op("sync", lambda e, h=h, qb=qb: e.dma_start(
                    qb[0:64, :], qpark.ap()[h * 64:(h + 1) * 64, :]),
                    wait=[(mm, mm_free[h])], inc=hl, inck=16, dma=True)
                q_ready[h] = hl.n
            PTb = PT2[gg2 % 2]
            pt_free = pvm.n
            if gg2 > 0:
                assert not pvq
                pvq = make_pv_items(gg2 - 1)
            npair = 2 * qc + 2
            for p in range(npair):
                psl = ps_s[p % 2]
                for half in range(2):
                    kt = 2 * p + half

                    def sgrp(e, kt=kt, qc=qc, gi=gi, qb=qb, psl=psl, half=half):
                        masks = []
                        for qs in range(4):
                            qt = qc * 4 + qs
                            if qt == kt:
                                masks.append((qs, g.cmb))
                            elif qt < kt:
                                masks.append((qs, g.nfb))
                        inst = e.matmul(
                            psl[:, half * 512:(half + 1) * 512],
                            KT[0:64, gi * T + kt * 128: gi * T + (kt + 1) * 128],
                            qb[0:64, qc * 512:(qc + 1) * 512],
                            start=True, stop=(not masks))
                        for j, (qs, mt) in enumerate(masks):
                            inst = e.matmul(
                                psl[:, half * 512 + qs * 128: half * 512 + (qs + 1) * 128],
                                g.idb[:], mt[:], start=False, stop=(j == len(masks) - 1))
                        return inst
                    s.op("tensor", sgrp,
                         wait=[(kld, kv_ready), (qld[h % 2], q_ready[h]),
                               (ac, slab_free[p % 2]), (dv, nbc_dv)], inc=mm)
                sd = mm.n
                s.op("scalar", lambda e, p=p, psl=psl, PTb=PTb: e.activation(
                    PTb[:, 2 * p * 512: 2 * p * 512 + 1024], psl[:], AF.Exp,
                    bias=nbc[:, 0:1], scale=1.0 / 8.0),
                    wait=[(mm, sd), (pvm, pt_free)])
                slab_free[p % 2] = ac.n
                if pvq:
                    rem_slots = npair - p
                    take = (len(pvq) + rem_slots - 1) // rem_slots
                    for _ in range(take):
                        if pvq:
                            pvq.pop(0)()
            chunk_exp[gg2] = ac.n
        for it in make_pv_items(GG - 1):
            it()
        s.op("sync", lambda e, a=sto2[0].n: e.wait_ge(sto2[0].sem, a))
        s.op("sync", lambda e, a=sto2[1].n: e.wait_ge(sto2[1].sem, a))
        if dbg is not None and "oT" in dbg:
            dst2 = Ctr(nc, "dst2")
            s.op("gpsimd", lambda e: e.dma_start(dbg["oT"].ap(), oT.ap()),
                 inc=dst2, inck=16, dma=True)
            s.op("gpsimd", lambda e, a=dst2: e.wait_ge(a.sem, a.n), dma=True)
        s.flush()

    with ExitStack() as c3:
        E3 = c3.enter_context
        wo_sb = E3(nc.sbuf_tensor(f"{tag}_wo", [128, NDC * D], BF16))
        otc = [E3(nc.sbuf_tensor(f"{tag}_otc{i}", [128, NDC * 128], BF16)) for i in range(2)]
        x3t = [E3(nc.sbuf_tensor(f"{tag}_x3b{i}", [128, D], F32)) for i in range(2)]
        hb3 = [E3(nc.sbuf_tensor(f"{tag}_hb{i}", [128, D], F32)) for i in range(2)]
        xmt = [E3(nc.sbuf_tensor(f"{tag}_xm{i}", [128, D], F32)) for i in range(2)]
        xmb = [E3(nc.sbuf_tensor(f"{tag}_xmb{i}", [128, D], BF16)) for i in range(2)]
        g2t = E3(nc.sbuf_tensor(f"{tag}_g2t", [128, D], F32))
        sqs = E3(nc.sbuf_tensor(f"{tag}_sqs3", [128, D], F32))
        ms = E3(nc.sbuf_tensor(f"{tag}_ms3", [128, 1], F32))
        rstd = E3(nc.sbuf_tensor(f"{tag}_rstd3", [128, 1], F32))
        epsb = E3(nc.sbuf_tensor(f"{tag}_epsb3", [128, 1], F32))
        ps_o3 = [E3(nc.psum_tensor(f"{tag}_po3{i}", [128, 512], F32)) for i in range(4)]

        s = Sched(nc)
        dv, ac = s.v, s.a
        wld = Ctr(nc, "wld")
        oldp = [Ctr(nc, "ol0"), Ctr(nc, "ol1")]
        xldr = Ctr(nc, "xlr")
        mm = Ctr(nc, "mm")
        sth = [Ctr(nc, "sh0"), Ctr(nc, "sh1")]
        stm = [Ctr(nc, "sm0"), Ctr(nc, "sm1")]

        s.op("vector", lambda e: e.memset(epsb[:], 1e-6))
        g2l = Ctr(nc, "g2l")
        s.op("sync", lambda e: e.dma_start(g2t[:], g2bc.ap()), inc=g2l, inck=16, dma=True)
        g2d = g2l.n
        for i in range(NDC):
            s.op("gpsimd", lambda e, i=i: e.dma_start(
                wo_sb[:, i * D:(i + 1) * D], wo.ap()[i * 128:(i + 1) * 128, :]),
                inc=wld, inck=16, dma=True)
        wready = wld.n

        ot_done = {}
        for i in range(NDC):
            s.op("sync", lambda e, i=i: e.dma_start(
                otc[0][:, i * 128:(i + 1) * 128], oT.ap()[i * 128:(i + 1) * 128, 0:128]),
                inc=oldp[0], inck=16, dma=True)
        ot_done[0] = oldp[0].n

        mm_after_tt = {}
        x3_user = {}
        ps_user = {}
        for tt in range(NT):
            cb = otc[tt % 2]
            xb = x3t[tt % 2]
            hb = hb3[tt % 2]
            mt = xmt[tt % 2]
            mb = xmb[tt % 2]
            s.op("sync", lambda e, tt=tt, xb=xb: e.dma_start(
                xb[:], x3.ap()[tt * 128:(tt + 1) * 128, :]),
                wait=[(dv, x3_user.get(tt - 2, 0))], inc=xldr, inck=16, dma=True)
            xrd = xldr.n
            for dc in range(2):
                pslab = (tt % 2) * 2 + dc
                pp = ps_o3[pslab]

                def ogrp(e, dc=dc, cb=cb, pp=pp):
                    for i in range(NDC):
                        inst = e.matmul(pp[:],
                                        cb[:, i * 128:(i + 1) * 128],
                                        wo_sb[:, i * D + dc * 512: i * D + (dc + 1) * 512],
                                        start=(i == 0), stop=(i == NDC - 1))
                    return inst
                s.op("tensor", ogrp, wait=[(wld, wready), (oldp[tt % 2], ot_done[tt]),
                                           (dv, ps_user.get(pslab, 0))], inc=mm)
                mval = mm.n
                s.op("vector", lambda e, dc=dc, xb=xb, hb=hb, pp=pp: e.tensor_tensor(
                    hb[:, dc * 512:(dc + 1) * 512], xb[:, dc * 512:(dc + 1) * 512],
                    pp[:], ALU.add),
                    wait=[(mm, mval), (xldr, xrd), (sth[tt % 2], sth[tt % 2].n)])
                ps_user[pslab] = dv.n
            x3_user[tt] = dv.n
            h_ready = dv.n
            mm_after_tt[tt] = mm.n
            if tt + 1 < NT:
                for i in range(NDC):
                    s.op("sync", lambda e, i=i, q=tt + 1: e.dma_start(
                        otc[q % 2][:, i * 128:(i + 1) * 128],
                        oT.ap()[i * 128:(i + 1) * 128, q * 128:(q + 1) * 128]),
                        wait=[(mm, mm_after_tt.get(tt - 1, 0))],
                        inc=oldp[(tt + 1) % 2], inck=16, dma=True)
                ot_done[tt + 1] = oldp[(tt + 1) % 2].n
            s.op("sync", lambda e, tt=tt, hb=hb: e.dma_start(
                h_out.ap()[tt * 128:(tt + 1) * 128, :], hb[:]),
                wait=[(dv, h_ready)], inc=sth[tt % 2], inck=16, dma=True)
            s.op("scalar", lambda e, hb=hb: e.activation(sqs[:], hb[:], AF.Square),
                 wait=[(dv, h_ready), (dv, dv.n)])
            s.op("vector", lambda e: e.reduce_sum(ms[:], sqs[:], axis=AX.X), wait=[(ac, ac.n)])
            s.op("vector", lambda e: e.tensor_scalar(ms[:], ms[:], 1.0 / D, None, op0=ALU.mult))
            s.op("scalar", lambda e: e.activation(ms[:], ms[:], AF.Sqrt, bias=epsb[:, 0:1]),
                 wait=[(dv, dv.n)])
            s.op("vector", lambda e: e.reciprocal(rstd[:], ms[:]), wait=[(ac, ac.n)])
            s.op("vector", lambda e, hb=hb, mt=mt: e.tensor_scalar(
                mt[:], hb[:], rstd[:, 0:1], None, op0=ALU.mult),
                wait=[(stm[tt % 2], stm[tt % 2].n)])
            s.op("vector", lambda e, mt=mt: e.tensor_tensor(mt[:], mt[:], g2t[:], ALU.mult),
                 wait=[(g2l, g2d)])
            s.op("vector", lambda e, mt=mt, mb=mb: e.tensor_copy(mb[:], mt[:]))
            xm_dv = dv.n
            s.op("sync", lambda e, tt=tt, mt=mt: e.dma_start(
                xm_out.ap()[tt * 128:(tt + 1) * 128, :], mt[:]),
                wait=[(dv, xm_dv)], inc=stm[tt % 2], inck=16, dma=True)
            s.op("sync", lambda e, tt=tt, mb=mb: e.dma_start(
                xmbf.ap()[tt * 128:(tt + 1) * 128, :], mb[:]),
                inc=stm[tt % 2], inck=16, dma=True)
        for c in [sth[0], sth[1], stm[0], stm[1]]:
            s.op("sync", lambda e, c=c: e.wait_ge(c.sem, c.n))
        s.flush()



NE2 = 8
CAP2 = 4096
PAD2 = 8192
HID2 = 2048
SENT = 524288.0


def emit_phase_d_moe(nc, ctx, g, x3, h_in, xm_in, xmbf, rwT, rb_bc, w1, w2, w3,
                     coremask, out, tag="e"):
    E = ctx.enter_context
    NTT = NT
    xm_all = nc.dram_tensor(f"{tag}_xmall", [NCORES * T, D], BF16, kind="Internal")
    hist_in = nc.dram_tensor(f"{tag}_histin", [1, 16], F32, kind="Internal")
    hist_all = nc.dram_tensor(f"{tag}_histall", [NCORES, 16], F32, kind="Internal")
    histm_in = nc.dram_tensor(f"{tag}_histmin", [1, 8], F32, kind="Internal")
    histm_all = nc.dram_tensor(f"{tag}_histmall", [NCORES, 8], F32, kind="Internal")
    idt_loc = nc.dram_tensor(f"{tag}_idtloc", [NE2 * PAD2, 1], F32, kind="Internal")
    idt_all = nc.dram_tensor(f"{tag}_idtall", [NE2 * PAD2, 1], F32, kind="Internal")
    yl = nc.dram_tensor(f"{tag}_yl", [PAD2, D], BF16, kind="Internal")
    yall = nc.dram_tensor(f"{tag}_yall", [NE2 * PAD2, D], BF16, kind="Internal")
    yq4 = [nc.dram_tensor(f"{tag}_yq{i}", [NE2 * (PAD2 // 4), D], BF16, kind="Internal")
           for i in range(4)]

    i1c = E(nc.sbuf_tensor(f"{tag}_i1c", [128, NTT], F32))
    i2c = E(nc.sbuf_tensor(f"{tag}_i2c", [128, NTT], F32))
    p0c = E(nc.sbuf_tensor(f"{tag}_p0c", [128, NTT], F32))
    p1c = E(nc.sbuf_tensor(f"{tag}_p1c", [128, NTT], F32))
    s1c = E(nc.sbuf_tensor(f"{tag}_s1c", [128, NTT], F32))
    wyc = E(nc.sbuf_tensor(f"{tag}_wyc", [128, NTT], F32))
    spc = E(nc.sbuf_tensor(f"{tag}_spc", [128, NTT], F32))
    selc = E(nc.sbuf_tensor(f"{tag}_selc", [128, NTT], F32))
    base_sb = E(nc.sbuf_tensor(f"{tag}_base", [1, 1], F32))
    rg = [list(range(NCORES))]

    with ExitStack() as c1:
        E1 = c1.enter_context
        rw_sb = E1(nc.sbuf_tensor(f"{tag}_rw", [128, NDC * 8], F32))
        rbt = E1(nc.sbuf_tensor(f"{tag}_rbt", [128, 8], F32))
        xt = [E1(nc.sbuf_tensor(f"{tag}_xt{i}", [128, D], F32)) for i in range(2)]
        xT = E1(nc.sbuf_tensor(f"{tag}_xT", [128, NDC * 128], F32))
        lg = E1(nc.sbuf_tensor(f"{tag}_lg", [128, 8], F32))
        lgm = E1(nc.sbuf_tensor(f"{tag}_lgm", [128, 8], F32))
        eq = E1(nc.sbuf_tensor(f"{tag}_eq", [128, 8], F32))
        cnd = E1(nc.sbuf_tensor(f"{tag}_cnd", [128, 8], F32))
        ohsum = E1(nc.sbuf_tensor(f"{tag}_ohs", [128, 8], F32))
        oh1 = E1(nc.sbuf_tensor(f"{tag}_oh1", [128, 8], F32))
        oh2 = E1(nc.sbuf_tensor(f"{tag}_oh2", [128, 8], F32))
        mx1 = E1(nc.sbuf_tensor(f"{tag}_mx1", [128, 1], F32))
        mx2 = E1(nc.sbuf_tensor(f"{tag}_mx2", [128, 1], F32))
        tcol = E1(nc.sbuf_tensor(f"{tag}_tcol", [128, 1], F32))
        run0 = E1(nc.sbuf_tensor(f"{tag}_run0", [1, 8], F32))
        run01 = E1(nc.sbuf_tensor(f"{tag}_run01", [1, 8], F32))
        hrow = E1(nc.sbuf_tensor(f"{tag}_hrow", [1, 16], F32))
        ps_t = [E1(nc.psum_tensor(f"{tag}_pt{i}", [128, 512], F32)) for i in range(2)]
        ps_l = E1(nc.psum_tensor(f"{tag}_pl", [128, 512], F32))
        ps_r = E1(nc.psum_tensor(f"{tag}_pr", [128, 512], F32))
        ps_h = E1(nc.psum_tensor(f"{tag}_ph", [1, 512], F32))

        s = Sched(nc)
        dv, ac = s.v, s.a
        ld = Ctr(nc, "ld")
        castc = Ctr(nc, "castc")
        xldp = [Ctr(nc, "xl0"), Ctr(nc, "xl1")]
        mm = Ctr(nc, "mm")
        rmm = Ctr(nc, "rmm")
        hmm = Ctr(nc, "hmm")
        cc = Ctr(nc, "cc")

        s.op("gpsimd", lambda e: e.collective_compute(
            "AllGather", ALU.bypass, replica_groups=rg,
            ins=[xmbf.ap()], outs=[xm_all.ap()]), inc=cc, dma=True)

        for i in range(NDC):
            s.op("sync", lambda e, i=i: e.dma_start(
                rw_sb[:, i * 8:(i + 1) * 8], rwT.ap()[i * 128:(i + 1) * 128, :]),
                inc=ld, inck=16, dma=True)
        s.op("sync", lambda e: e.dma_start(rbt[:], rb_bc.ap()), inc=ld, inck=16, dma=True)
        rw_ready = ld.n
        s.op("vector", lambda e: e.memset(run0[:], 0.0))
        s.op("vector", lambda e: e.memset(run01[:], 0.0))

        tr_evac = 0
        lg_user = 0
        run_user = 0
        for tt in range(NTT):
            xb = xt[tt % 2]
            xld = xldp[tt % 2]
            s.op("sync", lambda e, tt=tt, xb=xb: e.dma_start(
                xb[:], xm_in.ap()[tt * 128:(tt + 1) * 128, :]),
                wait=[(mm, max(0, mm.n - 10))], inc=xld, inck=16, dma=True)
            xrd = xld.n
            for i in range(NDC):
                pslab = ps_t[i % 2]
                s.op("tensor", lambda e, i=i, xb=xb, pslab=pslab: e.transpose(
                    pslab[:, 0:128], xb[:, i * 128:(i + 1) * 128], g.idf[:]),
                    wait=[(xld, xrd), (dv, max(0, tr_evac - 1))], inc=mm)
                mval = mm.n
                s.op("vector", lambda e, i=i, pslab=pslab: e.tensor_copy(
                    xT[:, i * 128:(i + 1) * 128], pslab[:, 0:128]),
                    wait=[(mm, mval)])
                tr_evac = dv.n

            def lgrp(e):
                for i in range(NDC):
                    inst = e.matmul(ps_l[:, 0:8], xT[:, i * 128:(i + 1) * 128],
                                    rw_sb[:, i * 8:(i + 1) * 8],
                                    start=(i == 0), stop=(i == NDC - 1))
                return inst
            s.op("tensor", lgrp, wait=[(ld, rw_ready), (dv, tr_evac), (dv, lg_user)], inc=mm)
            lg_mm = mm.n
            s.op("vector", lambda e: e.tensor_tensor(lg[:], ps_l[:, 0:8], rbt[:], ALU.add),
                 wait=[(mm, lg_mm)])
            s.op("vector", lambda e: e.reduce_max(mx1[:], lg[:], axis=AX.X))
            s.op("vector", lambda e: e.tensor_scalar(eq[:], lg[:], mx1[:, 0:1], None,
                                                     op0=ALU.is_ge))
            s.op("vector", lambda e: e.scalar_tensor_tensor(
                cnd[:], eq[:], -1e6, g.iota16b[:, 0:8], op0=ALU.mult, op1=ALU.add))
            s.op("vector", lambda e: e.tensor_reduce(tcol[:], cnd[:], axis=AX.X, op=ALU.min))
            s.op("vector", lambda e, tt=tt: e.tensor_copy(i1c[:, tt:tt + 1], tcol[:]))
            s.op("vector", lambda e: e.tensor_scalar(oh1[:], g.iota16[:, 0:8], tcol[:, 0:1],
                                                     None, op0=ALU.is_equal))
            s.op("vector", lambda e: e.scalar_tensor_tensor(
                lgm[:], oh1[:], -1e9, lg[:], op0=ALU.mult, op1=ALU.add))
            s.op("vector", lambda e: e.reduce_max(mx2[:], lgm[:], axis=AX.X))
            s.op("vector", lambda e: e.tensor_scalar(eq[:], lgm[:], mx2[:, 0:1], None,
                                                     op0=ALU.is_ge))
            s.op("vector", lambda e: e.scalar_tensor_tensor(
                cnd[:], eq[:], -1e6, g.iota16b[:, 0:8], op0=ALU.mult, op1=ALU.add))
            s.op("vector", lambda e: e.tensor_reduce(tcol[:], cnd[:], axis=AX.X, op=ALU.min))
            s.op("vector", lambda e, tt=tt: e.tensor_copy(i2c[:, tt:tt + 1], tcol[:]))
            s.op("vector", lambda e: e.tensor_scalar(oh2[:], g.iota16[:, 0:8], tcol[:, 0:1],
                                                     None, op0=ALU.is_equal))
            s.op("vector", lambda e: e.tensor_tensor(ohsum[:], oh1[:], oh2[:], ALU.add))
            s.op("vector", lambda e: e.tensor_tensor(mx1[:], mx1[:], mx2[:], ALU.subtract))
            s.op("scalar", lambda e, tt=tt: e.activation(s1c[:, tt:tt + 1], mx1[:],
                                                         AF.Sigmoid), wait=[(dv, dv.n)])
            oh_done = dv.n
            lg_user = dv.n

            def rgrp(e):
                e.matmul(ps_r[:, 0:8], g.uincl[:], oh1[:], start=True, stop=False)
                e.matmul(ps_r[:, 0:8], g.ones_row_f[:], run0[:], start=False, stop=True)
                e.matmul(ps_r[:, 8:16], g.uincl[:], ohsum[:], start=True, stop=False)
                return e.matmul(ps_r[:, 8:16], g.ones_row_f[:], run01[:],
                                start=False, stop=True)
            s.op("tensor", rgrp, wait=[(dv, oh_done), (dv, run_user), (ac, ac.n)], inc=rmm)
            r_done = rmm.n
            def hgrp(e, tt=tt):
                e.matmul(ps_h[:, 0:8], g.ones_col_f[:], oh1[:],
                         start=(tt == 0), stop=False)
                return e.matmul(ps_h[:, 8:16], g.ones_col_f[:], ohsum[:],
                                start=False, stop=(tt == NTT - 1))
            s.op("tensor", hgrp, wait=[(dv, oh_done)], inc=hmm)
            s.op("vector", lambda e: e.tensor_tensor(cnd[:], oh1[:], ps_r[:, 0:8], ALU.mult),
                 wait=[(rmm, r_done)])
            s.op("vector", lambda e, tt=tt: e.reduce_sum(p0c[:, tt:tt + 1], cnd[:], axis=AX.X))
            s.op("vector", lambda e: e.tensor_tensor(cnd[:], oh2[:], ps_r[:, 8:16], ALU.mult))
            s.op("vector", lambda e, tt=tt: e.reduce_sum(p1c[:, tt:tt + 1], cnd[:], axis=AX.X))
            def tgrp(e):
                e.matmul(ps_r[0:1, 16:24], g.ones_col_f[:], oh1[:], start=True, stop=True)
                return e.matmul(ps_r[0:1, 24:32], g.ones_col_f[:], ohsum[:],
                                start=True, stop=True)
            s.op("tensor", tgrp, wait=[(dv, dv.n)], inc=rmm)
            t_done = rmm.n
            s.op("vector", lambda e: e.tensor_tensor(run0[:], run0[:], ps_r[0:1, 16:24],
                                                     ALU.add), wait=[(rmm, t_done)])
            s.op("vector", lambda e: e.tensor_tensor(run01[:], run01[:], ps_r[0:1, 24:32],
                                                     ALU.add))
            run_user = dv.n
        hist_done = hmm.n
        s.op("vector", lambda e: e.tensor_copy(hrow[:], ps_h[0:1, 0:16]),
             wait=[(hmm, hist_done)])
        hcopy = dv.n
        s.op("sync", lambda e: e.dma_start(hist_in.ap(), hrow[:]),
             wait=[(dv, hcopy)], inc=ld, inck=16, dma=True)
        hst = ld.n
        s.op("gpsimd", lambda e: e.collective_compute(
            "AllGather", ALU.bypass, replica_groups=rg,
            ins=[hist_in.ap()], outs=[hist_all.ap()]),
            wait=[(ld, hst)], inc=cc, dma=True)
        s.op("gpsimd", lambda e: e.wait_ge(cc.sem, cc.n), dma=True)
        s.flush()

    with ExitStack() as c2:
        E2 = c2.enter_context
        hists = E2(nc.sbuf_tensor(f"{tag}_hists", [8, 16], F32))
        cmk = E2(nc.sbuf_tensor(f"{tag}_cmk", [8, 1], F32))
        off0b = E2(nc.sbuf_tensor(f"{tag}_off0b", [128, 8], F32))
        off01b = E2(nc.sbuf_tensor(f"{tag}_off01b", [128, 8], F32))
        oh1 = E2(nc.sbuf_tensor(f"{tag}_m2oh1", [128, 8], F32))
        oh2 = E2(nc.sbuf_tensor(f"{tag}_m2oh2", [128, 8], F32))
        ohm = E2(nc.sbuf_tensor(f"{tag}_m2ohm", [128, 8], F32))
        tmp8 = E2(nc.sbuf_tensor(f"{tag}_m2t8", [128, 8], F32))
        p0g = E2(nc.sbuf_tensor(f"{tag}_p0g", [128, 1], F32))
        p1g = E2(nc.sbuf_tensor(f"{tag}_p1g", [128, 1], F32))
        k0 = E2(nc.sbuf_tensor(f"{tag}_k0", [128, 1], F32))
        k1 = E2(nc.sbuf_tensor(f"{tag}_k1", [128, 1], F32))
        mcol = E2(nc.sbuf_tensor(f"{tag}_mcol", [128, 1], F32))
        tcol = E2(nc.sbuf_tensor(f"{tag}_m2tc", [128, 1], F32))
        runm = E2(nc.sbuf_tensor(f"{tag}_runm", [1, 8], F32))
        hmrow = E2(nc.sbuf_tensor(f"{tag}_hmrow", [1, 8], F32))
        ps_b = E2(nc.psum_tensor(f"{tag}_m2pb", [128, 512], F32))
        ps_h2 = E2(nc.psum_tensor(f"{tag}_m2ph", [1, 512], F32))

        s = Sched(nc)
        dv, ac = s.v, s.a
        ld = Ctr(nc, "ld")
        mm = Ctr(nc, "mm")
        hmm = Ctr(nc, "hmm")
        cc = Ctr(nc, "cc")

        s.op("sync", lambda e: e.dma_start(hists[:], hist_all.ap()), inc=ld, inck=16, dma=True)
        s.op("sync", lambda e: e.dma_start(cmk[:], coremask.ap()), inc=ld, inck=16, dma=True)
        s.op("sync", lambda e: e.wait_ge(ld.sem, ld.n))
        s.op("vector", lambda e: e.memset(runm[:], 0.0))
        s.flush()
        s = Sched(nc)
        dv, ac = s.v, s.a
        mm = Ctr(nc, "mm")
        hmm = Ctr(nc, "hmm")
        rmm = Ctr(nc, "rmm")
        cc = Ctr(nc, "cc")
        ld = Ctr(nc, "ld")
        off0r = E2(nc.sbuf_tensor(f"{tag}_off0r", [1, 8], F32))
        off01r = E2(nc.sbuf_tensor(f"{tag}_off01r", [1, 8], F32))
        s.op("tensor", lambda e: e.matmul(ps_b[0:1, 0:16], cmk[:], hists[:],
                                          start=True, stop=True), inc=mm)
        omm = mm.n
        s.op("vector", lambda e: e.tensor_copy(off0r[:], ps_b[0:1, 0:8]), wait=[(mm, omm)])
        s.op("vector", lambda e: e.tensor_copy(off01r[:], ps_b[0:1, 8:16]))
        orow_dv = dv.n
        s.op("tensor", lambda e: e.matmul(ps_b[:, 16:24], g.ones_row_f[:], off0r[:],
                                          start=True, stop=True),
             wait=[(dv, orow_dv)], inc=mm)
        s.op("tensor", lambda e: e.matmul(ps_b[:, 24:32], g.ones_row_f[:], off01r[:],
                                          start=True, stop=True), inc=mm)
        obc_mm = mm.n
        s.op("vector", lambda e: e.tensor_copy(off0b[:], ps_b[:, 16:24]), wait=[(mm, obc_mm)])
        s.op("vector", lambda e: e.tensor_copy(off01b[:], ps_b[:, 24:32]))
        prep_done = dv.n

        run_user = 0
        for tt in range(NTT):
            s.op("vector", lambda e, tt=tt: e.tensor_scalar(
                oh1[:], g.iota16[:, 0:8], i1c[:, tt:tt + 1], None, op0=ALU.is_equal),
                wait=[(dv, prep_done)])
            s.op("vector", lambda e, tt=tt: e.tensor_scalar(
                oh2[:], g.iota16[:, 0:8], i2c[:, tt:tt + 1], None, op0=ALU.is_equal))
            s.op("vector", lambda e: e.tensor_tensor(tmp8[:], oh1[:], off0b[:], ALU.mult))
            s.op("vector", lambda e: e.reduce_sum(p0g[:], tmp8[:], axis=AX.X))
            s.op("vector", lambda e, tt=tt: e.tensor_tensor(p0g[:], p0g[:], p0c[:, tt:tt + 1],
                                                            ALU.add))
            s.op("vector", lambda e: e.tensor_tensor(tmp8[:], oh2[:], off01b[:], ALU.mult))
            s.op("vector", lambda e: e.reduce_sum(p1g[:], tmp8[:], axis=AX.X))
            s.op("vector", lambda e, tt=tt: e.tensor_tensor(p1g[:], p1g[:], p1c[:, tt:tt + 1],
                                                            ALU.add))
            s.op("vector", lambda e: e.tensor_scalar(k0[:], p0g[:], float(CAP2), None,
                                                     op0=ALU.is_lt))
            s.op("vector", lambda e: e.tensor_scalar(k1[:], p1g[:], float(CAP2), None,
                                                     op0=ALU.is_lt))
            s.op("vector", lambda e, tt=tt: e.tensor_tensor(mcol[:], i1c[:, tt:tt + 1], k0[:],
                                                            ALU.mult))
            s.op("vector", lambda e, tt=tt: e.tensor_tensor(tcol[:], i2c[:, tt:tt + 1], k1[:],
                                                            ALU.mult))
            s.op("vector", lambda e: e.tensor_tensor(mcol[:], mcol[:], tcol[:], ALU.max))
            s.op("vector", lambda e, tt=tt: e.tensor_tensor(selc[:, tt:tt + 1], k0[:], k1[:],
                                                            ALU.max))
            s.op("vector", lambda e: e.tensor_scalar(ohm[:], g.iota16[:, 0:8], mcol[:, 0:1],
                                                     None, op0=ALU.is_equal))
            s.op("vector", lambda e, tt=tt: e.tensor_scalar(
                ohm[:], ohm[:], selc[:, tt:tt + 1], None, op0=ALU.mult))
            s.op("vector", lambda e, tt=tt: e.tensor_tensor(tcol[:], s1c[:, tt:tt + 1], k0[:],
                                                            ALU.mult))
            s.op("vector", lambda e, tt=tt: e.tensor_scalar(
                p0g[:], s1c[:, tt:tt + 1], -1.0, 1.0, op0=ALU.mult, op1=ALU.add))
            s.op("vector", lambda e: e.tensor_tensor(k1[:], k1[:], p0g[:], ALU.mult))
            s.op("vector", lambda e, tt=tt: e.tensor_tensor(wyc[:, tt:tt + 1], tcol[:], k1[:],
                                                            ALU.add))
            oh_done = dv.n
            def rgrp(e):
                e.matmul(ps_b[:, 40:48], g.uincl[:], ohm[:], start=True, stop=False)
                return e.matmul(ps_b[:, 40:48], g.ones_row_f[:], runm[:],
                                start=False, stop=True)
            s.op("tensor", rgrp, wait=[(dv, oh_done), (dv, run_user), (ac, ac.n)], inc=rmm)
            r_done = rmm.n
            def hgrp(e, tt=tt):
                return e.matmul(ps_h2[:, 0:8], g.ones_col_f[:], ohm[:],
                                start=(tt == 0), stop=(tt == NTT - 1))
            s.op("tensor", hgrp, wait=[(dv, oh_done)], inc=hmm)
            s.op("vector", lambda e: e.tensor_tensor(tmp8[:], ohm[:], ps_b[:, 40:48],
                                                     ALU.mult), wait=[(rmm, r_done)])
            s.op("vector", lambda e: e.reduce_sum(tcol[:], tmp8[:], axis=AX.X))
            s.op("vector", lambda e, tt=tt: e.scalar_tensor_tensor(
                spc[:, tt:tt + 1], mcol[:], 65536.0, tcol[:], op0=ALU.mult, op1=ALU.add))
            def tgrp(e):
                return e.matmul(ps_b[0:1, 48:56], g.ones_col_f[:], ohm[:],
                                start=True, stop=True)
            s.op("tensor", tgrp, wait=[(dv, dv.n)], inc=rmm)
            t_done = rmm.n
            s.op("vector", lambda e: e.tensor_tensor(runm[:], runm[:], ps_b[0:1, 48:56],
                                                     ALU.add), wait=[(rmm, t_done)])
            run_user = dv.n
        histm_done = hmm.n
        s.op("vector", lambda e: e.tensor_copy(hmrow[:], ps_h2[0:1, 0:8]),
             wait=[(hmm, histm_done)])
        hcopy = dv.n
        s.op("sync", lambda e: e.dma_start(histm_in.ap(), hmrow[:]),
             wait=[(dv, hcopy)], inc=ld, inck=16, dma=True)
        hst = ld.n
        s.op("gpsimd", lambda e: e.collective_compute(
            "AllGather", ALU.bypass, replica_groups=rg,
            ins=[histm_in.ap()], outs=[histm_all.ap()]),
            wait=[(ld, hst)], inc=cc, dma=True)
        s.op("gpsimd", lambda e: e.wait_ge(cc.sem, cc.n), dma=True)
        s.flush()

    with ExitStack() as c3:
        E3 = c3.enter_context
        histm = E3(nc.sbuf_tensor(f"{tag}_histm", [8, 8], F32))
        cmk = E3(nc.sbuf_tensor(f"{tag}_m3cmk", [8, 1], F32))
        offmb = E3(nc.sbuf_tensor(f"{tag}_offmb", [128, 8], F32))
        offmr = E3(nc.sbuf_tensor(f"{tag}_offmr", [1, 8], F32))
        ohm = E3(nc.sbuf_tensor(f"{tag}_m3ohm", [128, 8], F32))
        mcol = E3(nc.sbuf_tensor(f"{tag}_m3m", [128, 1], F32))
        rkm = E3(nc.sbuf_tensor(f"{tag}_m3rk", [128, 1], F32))
        offsel = E3(nc.sbuf_tensor(f"{tag}_m3of", [128, 1], F32))
        slot = E3(nc.sbuf_tensor(f"{tag}_m3sl", [128, 1], F32))
        ids_f = E3(nc.sbuf_tensor(f"{tag}_m3id", [128, 1], F32))
        slot_i = E3(nc.sbuf_tensor(f"{tag}_m3si", [128, 1], I32))
        bb = E3(nc.sbuf_tensor(f"{tag}_m3bb", [128, 1], F32))
        zero_big = E3(nc.sbuf_tensor(f"{tag}_m3z", [128, NE2 * PAD2 // 128], F32))
        ps_b = E3(nc.psum_tensor(f"{tag}_m3pb", [128, 512], F32))

        s = Sched(nc)
        s.bnd(NE2 * PAD2 - 1)
        dv, ac = s.v, s.a
        ld = Ctr(nc, "ld")
        mm = Ctr(nc, "mm")
        sc = Ctr(nc, "sc")
        cc = Ctr(nc, "cc")

        s.op("sync", lambda e: e.dma_start(histm[:], histm_all.ap()), inc=ld, inck=16, dma=True)
        s.op("sync", lambda e: e.dma_start(cmk[:], coremask.ap()), inc=ld, inck=16, dma=True)
        s.op("sync", lambda e: e.dma_start(base_sb[:], g.dr_base.ap()), inc=ld, inck=16, dma=True)
        s.op("vector", lambda e: e.memset(zero_big[:], 0.0))
        zb_done = dv.n
        s.op("sync", lambda e: e.dma_start(
            idt_loc.ap().rearrange("(p f) o -> p (f o)", p=128), zero_big[:]),
            wait=[(dv, zb_done)], inc=ld, inck=16, dma=True)
        zero_done = ld.n
        ld_done = ld.n
        s.op("tensor", lambda e: e.matmul(ps_b[0:1, 0:8], cmk[:], histm[:],
                                          start=True, stop=True),
             wait=[(ld, ld_done)], inc=mm)
        omm = mm.n
        s.op("vector", lambda e: e.tensor_copy(offmr[:], ps_b[0:1, 0:8]), wait=[(mm, omm)])
        od = dv.n
        s.op("tensor", lambda e: e.matmul(ps_b[:, 8:16], g.ones_row_f[:], offmr[:],
                                          start=True, stop=True), wait=[(dv, od)], inc=mm)
        s.op("tensor", lambda e: e.matmul(ps_b[:, 16:17], g.ones_row_f[:], base_sb[:],
                                          start=True, stop=True), inc=mm)
        bmm = mm.n
        s.op("vector", lambda e: e.tensor_copy(offmb[:], ps_b[:, 8:16]), wait=[(mm, bmm)])
        s.op("vector", lambda e: e.tensor_copy(bb[:], ps_b[:, 16:17]))
        prep_done = dv.n

        sc_prev = 0
        for tt in range(NTT):
            s.op("vector", lambda e, tt=tt: e.tensor_scalar(
                mcol[:], spc[:, tt:tt + 1], 1.0 / 65536.0, None, op0=ALU.mult),
                wait=[(dv, prep_done), (sc, sc_prev)])
            am_w = dv.n
            s.op("gpsimd", lambda e: e.tensor_copy(slot_i[:], mcol[:]), wait=[(dv, am_w)])
            s.op("vector", lambda e: e.tensor_copy(mcol[:], slot_i[:]), wait=[(s.g, s.g.n)])
            s.op("vector", lambda e, tt=tt: e.scalar_tensor_tensor(
                rkm[:], mcol[:], -65536.0, spc[:, tt:tt + 1], op0=ALU.mult, op1=ALU.add))
            s.op("vector", lambda e: e.tensor_scalar(ohm[:], g.iota16[:, 0:8], mcol[:, 0:1],
                                                     None, op0=ALU.is_equal))
            s.op("vector", lambda e: e.tensor_tensor(ohm[:], ohm[:], offmb[:], ALU.mult))
            s.op("vector", lambda e: e.reduce_sum(offsel[:], ohm[:], axis=AX.X))
            s.op("vector", lambda e: e.scalar_tensor_tensor(
                slot[:], mcol[:], float(PAD2), offsel[:], op0=ALU.mult, op1=ALU.add))
            s.op("vector", lambda e: e.tensor_tensor(slot[:], slot[:], rkm[:], ALU.add))
            s.op("vector", lambda e: e.tensor_scalar(slot[:], slot[:], -1.0, None, op0=ALU.add))
            s.op("vector", lambda e, tt=tt: e.tensor_scalar(
                offsel[:], selc[:, tt:tt + 1], -SENT, SENT, op0=ALU.mult, op1=ALU.add))
            s.op("vector", lambda e: e.tensor_tensor(slot[:], slot[:], offsel[:], ALU.add))
            s.op("vector", lambda e, tt=tt: e.tensor_copy(spc[:, tt:tt + 1], slot[:]))
            s.op("vector", lambda e, tt=tt: e.tensor_scalar(
                ids_f[:], g.iotacol[:], float(tt * 128 + 1), None, op0=ALU.add))
            s.op("vector", lambda e: e.tensor_tensor(ids_f[:], ids_f[:], bb[:], ALU.add))
            ids_done = dv.n
            s.op("gpsimd", lambda e: e.tensor_copy(slot_i[:], slot[:]), wait=[(dv, ids_done)])
            cast_done = s.g.n
            s.op("gpsimd", lambda e: e.indirect_dma_start(
                idt_loc.ap(), bass.IndirectOffsetOnAxis(slot_i[:], 0),
                ids_f[:], None, bounds_check=s._breg[s.bnd(NE2 * PAD2 - 1)],
                oob_is_err=False),
                wait=[(ld, zero_done), (s.g, cast_done)], inc=sc, inck=16, dma=True)
            sc_prev = sc.n
        s.op("gpsimd", lambda e: e.collective_compute(
            "AllReduce", ALU.add, replica_groups=rg,
            ins=[idt_loc.ap()], outs=[idt_all.ap()]),
            wait=[(sc, sc.n)], inc=cc, dma=True)
        s.op("gpsimd", lambda e: e.wait_ge(cc.sem, cc.n), dma=True)
        s.flush()

    with ExitStack() as c4:
        E4 = c4.enter_context
        w1_sb = E4(nc.sbuf_tensor(f"{tag}_w1", [128, NDC * HID2], BF16))
        w2_sb = E4(nc.sbuf_tensor(f"{tag}_w2", [128, NDC * HID2], BF16))
        w3_sb = E4(nc.sbuf_tensor(f"{tag}_w3", [128, (HID2 // 128) * D], BF16))
        H1 = E4(nc.sbuf_tensor(f"{tag}_H1", [128, (HID2 // 128) * 512], BF16))
        hT = E4(nc.sbuf_tensor(f"{tag}_hT", [128, (HID2 // 128) * 512], BF16))
        prodf2 = [E4(nc.sbuf_tensor(f"{tag}_prodf{i}", [128, 512], F32)) for i in range(2)]
        disp2 = [E4(nc.sbuf_tensor(f"{tag}_disp{i}", [128, 4 * D], BF16)) for i in range(2)]
        dispT = E4(nc.sbuf_tensor(f"{tag}_dispT", [128, NDC * 512], BF16))
        idcol_f = E4(nc.sbuf_tensor(f"{tag}_idcf", [128, 1], F32))
        idcol_i = E4(nc.sbuf_tensor(f"{tag}_idci", [128, 1], I32))
        negm = E4(nc.sbuf_tensor(f"{tag}_negm", [128, 1], F32))
        slotbase = E4(nc.sbuf_tensor(f"{tag}_slb", [1, 1], F32))
        slotbase_c = E4(nc.sbuf_tensor(f"{tag}_slbc", [128, 1], F32))
        ystage2 = [E4(nc.sbuf_tensor(f"{tag}_yst{i}", [128, D], BF16)) for i in range(2)]
        ps_f = [E4(nc.psum_tensor(f"{tag}_pf{i}", [128, 512], F32)) for i in range(2)]
        ps_h = [E4(nc.psum_tensor(f"{tag}_ph{i}", [128, 512], F32)) for i in range(2)]
        ps_y = [E4(nc.psum_tensor(f"{tag}_py{i}", [128, 512], F32)) for i in range(2)]
        ps_tr = [E4(nc.psum_tensor(f"{tag}_ptr{i}", [128, 512], BF16)) for i in range(2)]

        s = Sched(nc)
        s.bnd(NE2 * PAD2 - 1)
        s.bnd(NCORES * T - 1)
        dv, ac = s.v, s.a
        wld = Ctr(nc, "wld")
        mm = Ctr(nc, "mm")
        st2 = [Ctr(nc, "st0"), Ctr(nc, "st1")]
        ybc = Ctr(nc, "ybc")
        gg = Ctr(nc, "gg")
        yst_free = {0: 0, 1: 0}

        sbl = Ctr(nc, "sbl")
        s.op("sync", lambda e: e.dma_start(slotbase[:], g.dr_base.ap()), inc=sbl, inck=16,
             dma=True)
        s.op("vector", lambda e: e.tensor_scalar(slotbase[:], slotbase[:],
                                                 float(PAD2) / T, None, op0=ALU.mult),
             wait=[(sbl, sbl.n)])
        s.op("tensor", lambda e: e.matmul(ps_f[0][:, 0:1], g.ones_row_f[:], slotbase[:],
                                          start=True, stop=True), wait=[(dv, dv.n)], inc=mm)
        s.op("vector", lambda e: e.tensor_copy(slotbase_c[:], ps_f[0][:, 0:1]),
             wait=[(mm, mm.n)])
        sb_done = dv.n

        for i in range(NDC):
            s.op("gpsimd", lambda e, i=i: e.dma_start(
                w1_sb[:, i * HID2:(i + 1) * HID2], w1.ap()[0, i * 128:(i + 1) * 128, :]),
                inc=wld, inck=16, dma=True)
            s.op("gpsimd", lambda e, i=i: e.dma_start(
                w2_sb[:, i * HID2:(i + 1) * HID2], w2.ap()[0, i * 128:(i + 1) * 128, :]),
                inc=wld, inck=16, dma=True)
        for i in range(HID2 // 128):
            s.op("gpsimd", lambda e, i=i: e.dma_start(
                w3_sb[:, i * D:(i + 1) * D], w3.ap()[0, i * 128:(i + 1) * 128, :]),
                inc=wld, inck=16, dma=True)
        w_ready = wld.n

        gdone_prev = 0
        idc_free = 0
        NCH = PAD2 // 512
        g_done = {}
        trmm_done = {-1: 0, -2: 0}
        h2mul_last = {-1: 0}
        ymm_done = {-1: 0}
        gel = {}
        h1ev = {}
        h2mul = {}
        agws = []

        def emit_gathers(c):
            nonlocal gdone_prev, idc_free
            dp = disp2[c % 2]
            for tj in range(4):
                cslot = c * 512 + tj * 128
                s.op("vector", lambda e, cslot=cslot: e.tensor_scalar(
                    idcol_f[:], g.iotacol[:], float(cslot), None, op0=ALU.add),
                    wait=[(dv, sb_done), (s.g, idc_free)])
                s.op("vector", lambda e: e.tensor_tensor(idcol_f[:], idcol_f[:],
                                                         slotbase_c[:], ALU.add))
                idsl = dv.n
                s.op("gpsimd", lambda e: e.tensor_copy(idcol_i[:], idcol_f[:]),
                     wait=[(dv, idsl), (gg, gdone_prev)])
                cast1 = s.g.n
                s.op("gpsimd", lambda e: e.indirect_dma_start(
                    idcol_f[:], None,
                    idt_all.ap(), bass.IndirectOffsetOnAxis(idcol_i[:], 0),
                    bounds_check=s._breg[s.bnd(NE2 * PAD2 - 1)], oob_is_err=False),
                    wait=[(s.g, cast1)], inc=gg, inck=16, dma=True)
                gat1 = gg.n
                s.op("gpsimd", lambda e, v=gg.n: e.wait_ge(gg.sem, v), dma=True)
                s.op("vector", lambda e: e.tensor_scalar(idcol_f[:], idcol_f[:],
                                                         -1.0, None, op0=ALU.add),
                     wait=[(s.g, s.g.n), (gg, gat1)])
                s.op("vector", lambda e: e.tensor_scalar(negm[:], idcol_f[:], 0.0, None,
                                                         op0=ALU.is_lt))
                s.op("vector", lambda e: e.scalar_tensor_tensor(
                    idcol_f[:], negm[:], SENT, idcol_f[:], op0=ALU.mult, op1=ALU.add))
                neg_done = dv.n
                s.op("gpsimd", lambda e: e.tensor_copy(idcol_i[:], idcol_f[:]),
                     wait=[(dv, neg_done)])
                cast2 = s.g.n
                s.op("vector", lambda e, tj=tj, dp=dp: e.memset(
                    dp[:, tj * D:(tj + 1) * D], 0.0),
                    wait=[(mm, trmm_done[c - 2])])
                mz = dv.n
                s.op("gpsimd", lambda e, tj=tj, dp=dp: e.indirect_dma_start(
                    dp[:, tj * D:(tj + 1) * D], None,
                    xm_all.ap(), bass.IndirectOffsetOnAxis(idcol_i[:], 0),
                    bounds_check=s._breg[s.bnd(NCORES * T - 1)], oob_is_err=False),
                    wait=[(dv, mz), (s.g, cast2)], inc=gg, inck=16, dma=True)
                g_done[(c, tj)] = gg.n
                gdone_prev = gg.n
                idc_free = s.g.n
                s.op("gpsimd", lambda e, v=gg.n: e.wait_ge(gg.sem, v), dma=True)

        emit_gathers(0)
        for cchunk in range(NCH):
            dp = disp2[cchunk % 2]
            for tj in range(4):
                for i in range(NDC):
                    pslab = ps_tr[(tj * NDC + i) % 2]
                    s.op("tensor", lambda e, tj=tj, i=i, pslab=pslab, dp=dp: e.transpose(
                        pslab[:, 0:128], dp[:, tj * D + i * 128: tj * D + (i + 1) * 128],
                        g.idb[:]),
                        wait=[(gg, g_done[(cchunk, tj)]), (ac, max(0, s.a.n - 1)),
                              (dv, h2mul_last[cchunk - 1])], inc=mm)
                    mval = mm.n
                    s.op("scalar", lambda e, tj=tj, i=i, pslab=pslab: e.copy(
                        dispT[:, i * 512 + tj * 128: i * 512 + (tj + 1) * 128],
                        pslab[:, 0:128]), wait=[(mm, mval)])
            trmm_done[cchunk] = mm.n
            tr_done = s.a.n
            if cchunk + 1 < NCH:
                emit_gathers(cchunk + 1)
            for fc in range(HID2 // 128):
                pfa = ps_f[fc % 2]

                def h1grp(e, fc=fc, pfa=pfa):
                    for i in range(NDC):
                        inst = e.matmul(pfa[:],
                                        w1_sb[:, i * HID2 + fc * 128: i * HID2 + (fc + 1) * 128],
                                        dispT[:, i * 512:(i + 1) * 512],
                                        start=(i == 0), stop=(i == NDC - 1))
                    return inst
                s.op("tensor", h1grp, wait=[(wld, w_ready), (ac, tr_done),
                                            (ac, h1ev.get((cchunk, fc - 2), 0))], inc=mm)
                mval = mm.n
                s.op("scalar", lambda e, fc=fc, pfa=pfa: e.copy(
                    H1[:, fc * 512:(fc + 1) * 512], pfa[:]),
                    wait=[(mm, mval), (dv, h2mul_last[cchunk - 1])])
                h1ev[(cchunk, fc)] = s.a.n
                phb = ps_h[fc % 2]

                def h2grp(e, fc=fc, phb=phb):
                    for i in range(NDC):
                        inst = e.matmul(phb[:],
                                        w2_sb[:, i * HID2 + fc * 128: i * HID2 + (fc + 1) * 128],
                                        dispT[:, i * 512:(i + 1) * 512],
                                        start=(i == 0), stop=(i == NDC - 1))
                    return inst
                s.op("tensor", h2grp, wait=[(dv, h2mul.get((cchunk, fc - 2), 0))], inc=mm)
                mval2 = mm.n
                s.op("vector", lambda e, fc=fc, phb=phb: e.tensor_tensor(
                    prodf2[fc % 2][:], phb[:], H1[:, fc * 512:(fc + 1) * 512], ALU.mult),
                    wait=[(mm, mval2), (ac, h1ev[(cchunk, fc)]),
                          (ac, gel.get((cchunk, fc - 2), 0))])
                h2mul[(cchunk, fc)] = dv.n
                s.op("scalar", lambda e, fc=fc: e.activation(
                    hT[:, fc * 512:(fc + 1) * 512], prodf2[fc % 2][:], AF.Gelu),
                    wait=[(dv, h2mul[(cchunk, fc)]), (mm, ymm_done[cchunk - 1])])
                gel[(cchunk, fc)] = s.a.n
            h2mul_last[cchunk] = dv.n
            h_done = gel[(cchunk, HID2 // 128 - 1)]
            for tj in range(4):
                for dc2 in range(2):
                    pslab = ps_y[(tj * 2 + dc2) % 2]

                    def ygrp(e, tj=tj, dc2=dc2, pslab=pslab):
                        for fc in range(HID2 // 128):
                            inst = e.matmul(
                                pslab[:],
                                hT[:, fc * 512 + tj * 128: fc * 512 + (tj + 1) * 128],
                                w3_sb[:, fc * D + dc2 * 512: fc * D + (dc2 + 1) * 512],
                                start=(fc == 0), stop=(fc == HID2 // 128 - 1))
                        return inst
                    s.op("tensor", ygrp, wait=[(ac, h_done), (dv, max(0, s.v.n))], inc=mm)
                    mval = mm.n
                    s.op("vector", lambda e, dc2=dc2, pslab=pslab, tj=tj: e.tensor_copy(
                        ystage2[tj % 2][:, dc2 * 512:(dc2 + 1) * 512], pslab[:]),
                        wait=[(mm, mval), (st2[tj % 2], yst_free[tj % 2])])
                yst = dv.n
                s.op("sync", lambda e, cchunk=cchunk, tj=tj: e.dma_start(
                    yl.ap()[cchunk * 512 + tj * 128: cchunk * 512 + (tj + 1) * 128, :],
                    ystage2[tj % 2][:]),
                    wait=[(dv, yst)], inc=st2[tj % 2], inck=16, dma=True)
                yst_free[tj % 2] = st2[tj % 2].n
            ymm_done[cchunk] = mm.n
            if (cchunk + 1) % (NCH // 4) == 0:
                gi4 = (cchunk + 1) // (NCH // 4) - 1
                r0 = gi4 * (PAD2 // 4)
                r1 = (gi4 + 1) * (PAD2 // 4)
                ccx = Ctr(nc, f"ccy{gi4}")
                s.op("gpsimd", lambda e, r0=r0, r1=r1, gi4=gi4: e.collective_compute(
                    "AllGather", ALU.bypass, replica_groups=rg,
                    ins=[yl.ap()[r0:r1, :]], outs=[yq4[gi4].ap()]),
                    wait=[(st2[0], st2[0].n), (st2[1], st2[1].n)], inc=ccx, dma=True)
                yview = yall.ap().rearrange("(e p) d -> e p d", p=PAD2)[:, r0:r1, :]
                s.op("gpsimd", lambda e, gi4=gi4, yview=yview: e.dma_start(
                    yview, yq4[gi4].ap().rearrange("(e p) d -> e p d", p=PAD2 // 4)),
                    wait=[(ccx, ccx.n)], inc=ybc, inck=16, dma=True)
                agws.append(ccx)
        s.op("gpsimd", lambda e, a=ybc.n: e.wait_ge(ybc.sem, a), dma=True)
        s.flush()

    with ExitStack() as c5:
        E5 = c5.enter_context
        ytile = [E5(nc.sbuf_tensor(f"{tag}_ytl{i}", [128, D], BF16)) for i in range(2)]
        xmt = [E5(nc.sbuf_tensor(f"{tag}_xmt{i}", [128, D], F32)) for i in range(2)]
        x3t = [E5(nc.sbuf_tensor(f"{tag}_x3t{i}", [128, D], F32)) for i in range(2)]
        ht = [E5(nc.sbuf_tensor(f"{tag}_ht{i}", [128, D], F32)) for i in range(2)]
        fb = [E5(nc.sbuf_tensor(f"{tag}_fb{i}", [128, D], F32)) for i in range(2)]
        ob5 = [E5(nc.sbuf_tensor(f"{tag}_ob{i}", [128, D], F32)) for i in range(2)]
        wxcol = E5(nc.sbuf_tensor(f"{tag}_wx", [128, 1], F32))
        slot_i = E5(nc.sbuf_tensor(f"{tag}_m5si", [128, 1], I32))
        tmp = {"sqs": E5(nc.sbuf_tensor(f"{tag}_sqs5", [128, D], F32)),
               "s1": E5(nc.sbuf_tensor(f"{tag}_s15", [128, 1], F32)),
               "s2": E5(nc.sbuf_tensor(f"{tag}_s25", [128, 1], F32)),
               "mmean": E5(nc.sbuf_tensor(f"{tag}_mm5", [128, 1], F32)),
               "vvar": E5(nc.sbuf_tensor(f"{tag}_vv5", [128, 1], F32)),
               "rstd": E5(nc.sbuf_tensor(f"{tag}_rs5", [128, 1], F32))}
        epsb = E5(nc.sbuf_tensor(f"{tag}_eps5", [128, 1], F32))

        s = Sched(nc)
        s.bnd(NE2 * PAD2 - 1)
        dv, ac = s.v, s.a
        xld = Ctr(nc, "xld")
        gg = Ctr(nc, "gg")
        stp = [Ctr(nc, "sp0"), Ctr(nc, "sp1")]

        s.op("vector", lambda e: e.memset(epsb[:], 1e-5))
        h_user = {}
        y_prev = 0
        for tt in range(NTT):
            yb = ytile[tt % 2]
            mb = xmt[tt % 2]
            xb = x3t[tt % 2]
            hb = ht[tt % 2]
            f = fb[tt % 2]
            ob = ob5[tt % 2]
            s.op("sync", lambda e, tt=tt, mb=mb: e.dma_start(
                mb[:], xm_in.ap()[tt * 128:(tt + 1) * 128, :]),
                wait=[(dv, h_user.get(tt - 2, 0))], inc=xld, inck=16, dma=True)
            s.op("sync", lambda e, tt=tt, xb=xb: e.dma_start(
                xb[:], x3.ap()[tt * 128:(tt + 1) * 128, :]), inc=xld, inck=16, dma=True)
            s.op("sync", lambda e, tt=tt, hb=hb: e.dma_start(
                hb[:], h_in.ap()[tt * 128:(tt + 1) * 128, :]), inc=xld, inck=16, dma=True)
            xrd = xld.n
            s.op("gpsimd", lambda e, tt=tt: e.tensor_copy(slot_i[:], spc[:, tt:tt + 1]),
                 wait=[(dv, h_user.get(tt - 2, 0)), (gg, y_prev)])
            cast_done = s.g.n
            s.op("vector", lambda e, yb=yb: e.memset(yb[:], 0.0),
                 wait=[(dv, h_user.get(tt - 2, 0))])
            ym = dv.n
            s.op("gpsimd", lambda e, yb=yb: e.indirect_dma_start(
                yb[:], None, yall.ap(), bass.IndirectOffsetOnAxis(slot_i[:], 0),
                bounds_check=s._breg[s.bnd(NE2 * PAD2 - 1)], oob_is_err=False),
                wait=[(dv, ym), (s.g, cast_done)], inc=gg, inck=16, dma=True)
            y_done = gg.n
            y_prev = gg.n
            s.op("vector", lambda e, tt=tt: e.tensor_scalar(
                wxcol[:], wyc[:, tt:tt + 1], -1.0, 1.0, op0=ALU.mult, op1=ALU.add),
                wait=[(xld, xrd), (gg, y_done)])
            s.op("vector", lambda e, tt=tt, f=f, yb=yb: e.tensor_scalar(
                f[:], yb[:], wyc[:, tt:tt + 1], None, op0=ALU.mult))
            s.op("vector", lambda e, f=f, mb=mb: e.scalar_tensor_tensor(
                f[:], mb[:], wxcol[:, 0:1], f[:], op0=ALU.mult, op1=ALU.add))
            s.op("vector", lambda e, f=f, xb=xb: e.tensor_tensor(f[:], f[:], xb[:], ALU.add))
            s.op("vector", lambda e, f=f, hb=hb: e.tensor_tensor(f[:], f[:], hb[:], ALU.add))
            s.op("vector", lambda e: e.memset(tmp["s1"][0:1, 0:1], 0.0),
                 wait=[(stp[tt % 2], stp[tt % 2].n)])
            f_ready = dv.n
            out_done = emit_ln_tile(s, tmp, f, ob, epsb, f_ready)
            h_user[tt] = dv.n
            s.op("sync", lambda e, tt=tt, ob=ob: e.dma_start(
                out.ap()[tt * 128:(tt + 1) * 128, :], ob[:]),
                wait=[(dv, out_done)], inc=stp[tt % 2], inck=16, dma=True)
        s.op("sync", lambda e, a=stp[0].n: e.wait_ge(stp[0].sem, a))
        s.op("sync", lambda e, a=stp[1].n: e.wait_ge(stp[1].sem, a))
        s.flush()


def rotary_tables():
    t, dim = T, 64
    thetas = 10000.0 ** (-np.arange(0, dim, 2, dtype=np.float32) / dim)
    ticks = np.outer(np.arange(t, dtype=np.float32), thetas)
    ticks = np.tile(ticks, (1, 2))
    cos = np.cos(ticks).astype(np.float32)
    sin = np.sin(ticks).astype(np.float32)
    cosq = np.tile(cos, (1, 16)).copy()
    sinq = np.tile(sin, (1, 16)).copy()
    return cosq, sinq


def build_phase_d_nc(parts=("attn", "moe"), dbg_keys=()):
    dbg = None
    nc = bass.Bass("TRN2", target_bir_lowering=False)
    x3 = nc.dram_tensor("x3", [T, D], F32, kind="ExternalInput")
    wqd = nc.dram_tensor("lf_wq", [D, 1024], F32, kind="ExternalInput")
    wkd = nc.dram_tensor("lf_wk", [D, 256], F32, kind="ExternalInput")
    wvd = nc.dram_tensor("lf_wv", [D, 256], F32, kind="ExternalInput")
    wod = nc.dram_tensor("lf_wo", [1024, D], F32, kind="ExternalInput")
    g1bc = nc.dram_tensor("lf_g1bc", [128, D], F32, kind="ExternalInput")
    g2bc = nc.dram_tensor("lf_g2bc", [128, D], F32, kind="ExternalInput")
    cosq = nc.dram_tensor("c_cosq", [T, 1024], F32, kind="ExternalInput")
    sinq = nc.dram_tensor("c_sinq", [T, 1024], F32, kind="ExternalInput")
    rwT = nc.dram_tensor("lf_rw", [D, 8], F32, kind="ExternalInput")
    rbbc = nc.dram_tensor("lf_rbbc", [128, 8], F32, kind="ExternalInput")
    w1 = nc.dram_tensor("lf_w1", [1, D, HID2], F32, kind="ExternalInput")
    w2 = nc.dram_tensor("lf_w2", [1, D, HID2], F32, kind="ExternalInput")
    w3 = nc.dram_tensor("lf_w3", [1, HID2, D], F32, kind="ExternalInput")
    cmask = nc.dram_tensor("c_coremask", [8, 1], F32, kind="ExternalInput")
    if dbg_keys:
        shapes = {"qpark": [1024, T], "kpark": [256, T], "vpark": [T, 256],
                  "oT": [1024, T]}
        dbg = {k: nc.dram_tensor("dbg_" + k, shapes[k], BF16, kind="ExternalOutput")
               for k in dbg_keys}
    if "moe" in parts:
        out = nc.dram_tensor("out", [T, D], F32, kind="ExternalOutput")
    if "attn" in parts and "moe" in parts:
        h_out = nc.dram_tensor("h", [T, D], F32, kind="Internal")
        xm_out = nc.dram_tensor("xm", [T, D], F32, kind="Internal")
        xmbf = nc.dram_tensor("xmbf", [T, D], BF16, kind="Internal")
    elif "attn" in parts:
        h_out = nc.dram_tensor("h", [T, D], F32, kind="ExternalOutput")
        xm_out = nc.dram_tensor("xm", [T, D], F32, kind="ExternalOutput")
        xmbf = nc.dram_tensor("xmbf", [T, D], BF16, kind="Internal")
    else:
        h_out = nc.dram_tensor("h", [T, D], F32, kind="ExternalInput")
        xm_out = nc.dram_tensor("xm", [T, D], F32, kind="ExternalInput")
        xmbf_in = nc.dram_tensor("xmbf_ext", [T, D], F32, kind="ExternalInput")
        xmbf = nc.dram_tensor("xmbf", [T, D], BF16, kind="Internal")
    with ExitStack() as ctx:
        g = Common(nc, ctx)
        g.load(nc)
        if "attn" in parts:
            emit_phase_d_attn(nc, ctx, g, x3, wqd, wkd, wvd, wod, g1bc, g2bc, cosq, sinq,
                              h_out, xm_out, xmbf, tag="d", dbg=dbg)
        if "moe" in parts:
            if "attn" not in parts:
                s = Sched(nc)
                cst = Ctr(nc, "cst")
                s.op("gpsimd", lambda e: e.dma_start(xmbf.ap(), xmbf_in.ap()),
                     inc=cst, inck=16, dma=True)
                s.op("gpsimd", lambda e: e.wait_ge(cst.sem, cst.n), dma=True)
                s.flush()
            emit_phase_d_moe(nc, ctx, g, x3, h_out, xm_out, xmbf, rwT, rbbc, w1, w2, w3,
                             cmask, out, tag="e")
    return nc


def phase_d_inmaps(x3_full, z, parts=("attn", "moe"), extra=None):
    consts = host_consts()
    cosq, sinq = rotary_tables()
    in_maps = []
    for c in range(NCORES):
        m = dict(consts)
        m["x3"] = np.ascontiguousarray(x3_full[c])
        m["lf_wq"] = np.ascontiguousarray(z["L1_ffn_wq"])
        m["lf_wk"] = np.ascontiguousarray(z["L1_ffn_wk"])
        m["lf_wv"] = np.ascontiguousarray(z["L1_ffn_wv"])
        m["lf_wo"] = np.ascontiguousarray(z["L1_ffn_wo"])
        m["lf_g1bc"] = np.ascontiguousarray(
            np.broadcast_to(z["L1_ffn_g1"], (128, D)).astype(np.float32))
        m["lf_g2bc"] = np.ascontiguousarray(
            np.broadcast_to(z["L1_ffn_g2"], (128, D)).astype(np.float32))
        m["c_cosq"] = cosq
        m["c_sinq"] = sinq
        m["lf_rw"] = np.ascontiguousarray(z["L1_ffn_router_w"])
        m["lf_rbbc"] = np.ascontiguousarray(
            np.broadcast_to(z["L1_ffn_router_b"], (128, 8)).astype(np.float32))
        m["lf_w1"] = np.ascontiguousarray(z["L1_ffn_w1"][c:c + 1])
        m["lf_w2"] = np.ascontiguousarray(z["L1_ffn_w2"][c:c + 1])
        m["lf_w3"] = np.ascontiguousarray(z["L1_ffn_w3"][c:c + 1])
        m["c_coremask"] = (np.arange(8) < c).astype(np.float32).reshape(8, 1)
        m["c_base"] = np.full((1, 1), c * T, np.float32)
        if extra is not None:
            for k, v in extra.items():
                m[k] = np.ascontiguousarray(v[c])
        in_maps.append(m)
    return in_maps


def run_phase_d(x3_full, z, parts=("attn", "moe"), extra=None):
    nc = build_phase_d_nc(parts)
    in_maps = phase_d_inmaps(x3_full, z, parts, extra)
    res = run_bass_kernel_spmd(nc, in_maps, core_ids=list(range(NCORES)))
    outs = {}
    names = (["out"] if "moe" in parts else []) +             (["h", "xm"] if ("attn" in parts and "moe" not in parts) else [])
    for n in names:
        outs[n] = np.stack([res.results[c][n] for c in range(NCORES)])
    return outs




def build_phase_b_nc():
    nc = bass.Bass("TRN2", target_bir_lowering=False)
    x1 = nc.dram_tensor("x1", [T, D], F32, kind="ExternalInput")
    gwT = nc.dram_tensor("l0_gwT", [D, 16], F32, kind="ExternalInput")
    w1 = nc.dram_tensor("l0_w1", [EPC0, D, DFF0], F32, kind="ExternalInput")
    w2 = nc.dram_tensor("l0_w2", [EPC0, DFF0, D], F32, kind="ExternalInput")
    cmask = nc.dram_tensor("c_coremask", [8, 1], F32, kind="ExternalInput")
    x2 = nc.dram_tensor("x2", [T, D], F32, kind="ExternalOutput")
    with ExitStack() as ctx:
        g = Common(nc, ctx)
        g.load(nc)
        emit_phase_b(nc, ctx, g, x1, gwT, w1, w2, cmask, x2, tag="b")
    return nc


def phase_b_inmaps(x1_full, gate_w, w1_full, w2_full):
    consts = host_consts()
    in_maps = []
    for c in range(NCORES):
        m = dict(consts)
        m["x1"] = np.ascontiguousarray(x1_full[c])
        m["l0_gwT"] = np.ascontiguousarray(gate_w.T)
        m["l0_w1"] = np.ascontiguousarray(w1_full[EPC0 * c:EPC0 * (c + 1)])
        m["l0_w2"] = np.ascontiguousarray(w2_full[EPC0 * c:EPC0 * (c + 1)])
        m["c_coremask"] = (np.arange(8) < c).astype(np.float32).reshape(8, 1)
        m["c_base"] = np.full((1, 1), c * T, np.float32)
        in_maps.append(m)
    return in_maps


def run_phase_b(x1_full, gate_w, w1_full, w2_full):
    nc = build_phase_b_nc()
    in_maps = phase_b_inmaps(x1_full, gate_w, w1_full, w2_full)
    res = run_bass_kernel_spmd(nc, in_maps, core_ids=list(range(NCORES)))
    return np.stack([res.results[c]["x2"] for c in range(NCORES)])


def build_phase_c_nc():
    nc = bass.Bass("TRN2", target_bir_lowering=False)
    x2 = nc.dram_tensor("x2", [T, D], F32, kind="ExternalInput")
    wqT = nc.dram_tensor("l1_wqT", [D, E1], F32, kind="ExternalInput")
    wkT = nc.dram_tensor("l1_wkT", [D, E1], F32, kind="ExternalInput")
    wvT = nc.dram_tensor("l1_wvT", [D, E1], F32, kind="ExternalInput")
    woT = nc.dram_tensor("l1_woT", [E1, D], F32, kind="ExternalInput")
    lq1 = nc.dram_tensor("l1_lq1", [128, NH1], F32, kind="ExternalInput")
    lk1 = nc.dram_tensor("l1_lk1", [128, NH1], F32, kind="ExternalInput")
    lq2 = nc.dram_tensor("l1_lq2", [128, NH1], F32, kind="ExternalInput")
    lk2 = nc.dram_tensor("l1_lk2", [128, NH1], F32, kind="ExternalInput")
    rms_bc = nc.dram_tensor("l1_rmsbc", [128, 256], F32, kind="ExternalInput")
    x3 = nc.dram_tensor("x3", [T, D], F32, kind="ExternalOutput")
    with ExitStack() as ctx:
        g = Common(nc, ctx)
        g.load(nc)
        emit_phase_c(nc, ctx, g, x2, wqT, wkT, wvT, woT, lq1, lk1, lq2, lk2, rms_bc,
                     x3, tag="c")
    return nc


def phase_c_inmaps(x2_full, Wq, Wk, Wv, Wo, lq1, lk1, lq2, lk2, rms_scale):
    consts = host_consts()
    in_maps = []
    for c in range(NCORES):
        m = dict(consts)
        m["x2"] = np.ascontiguousarray(x2_full[c])
        m["l1_wqT"] = np.ascontiguousarray(Wq.T)
        m["l1_wkT"] = np.ascontiguousarray(Wk.T)
        m["l1_wvT"] = np.ascontiguousarray(Wv.T)
        m["l1_woT"] = np.ascontiguousarray(Wo.T)
        m["l1_lq1"] = np.ascontiguousarray(lq1.T)
        m["l1_lk1"] = np.ascontiguousarray(lk1.T)
        m["l1_lq2"] = np.ascontiguousarray(lq2.T)
        m["l1_lk2"] = np.ascontiguousarray(lk2.T)
        m["l1_rmsbc"] = np.ascontiguousarray(np.broadcast_to(rms_scale, (128, 256)).astype(np.float32))
        m["c_base"] = np.full((1, 1), c * T, np.float32)
        in_maps.append(m)
    return in_maps


def run_phase_c(x2_full, Wq, Wk, Wv, Wo, lq1, lk1, lq2, lk2, rms_scale):
    nc = build_phase_c_nc()
    in_maps = phase_c_inmaps(x2_full, Wq, Wk, Wv, Wo, lq1, lk1, lq2, lk2, rms_scale)
    res = run_bass_kernel_spmd(nc, in_maps, core_ids=list(range(NCORES)))
    return np.stack([res.results[c]["x3"] for c in range(NCORES)])


def build_phase_a_nc():
    nc = bass.Bass("TRN2", target_bir_lowering=False)
    x0 = nc.dram_tensor("x0", [T, D], F32, kind="ExternalInput")
    wqT = nc.dram_tensor("l0_wqT", [D, D], F32, kind="ExternalInput")
    wkT = nc.dram_tensor("l0_wkT", [D, D], F32, kind="ExternalInput")
    wvT = nc.dram_tensor("l0_wvT", [D, D], F32, kind="ExternalInput")
    woT = nc.dram_tensor("l0_woT", [D, D], F32, kind="ExternalInput")
    x1 = nc.dram_tensor("x1", [T, D], F32, kind="ExternalOutput")
    with ExitStack() as ctx:
        g = Common(nc, ctx)
        g.load(nc)
        emit_phase_a(nc, ctx, g, x0, wqT, wkT, wvT, woT, x1, tag="a")
    return nc


def run_phase_a(x_full, wq, wk, wv, wo):
    nc = build_phase_a_nc()
    consts = host_consts()
    in_maps = []
    for c in range(NCORES):
        m = dict(consts)
        m["x0"] = np.ascontiguousarray(x_full[c])
        m["l0_wqT"] = np.ascontiguousarray(wq.T)
        m["l0_wkT"] = np.ascontiguousarray(wk.T)
        m["l0_wvT"] = np.ascontiguousarray(wv.T)
        m["l0_woT"] = np.ascontiguousarray(wo.T)
        m["c_base"] = np.full((1, 1), c * T, np.float32)
        in_maps.append(m)
    res = run_bass_kernel_spmd(nc, in_maps, core_ids=list(range(NCORES)))
    return np.stack([res.results[c]["x1"] for c in range(NCORES)])


def build_fused_nc():
    nc = bass.Bass("TRN2", target_bir_lowering=False)
    x0 = nc.dram_tensor("x0", [T, D], F32, kind="ExternalInput")
    wqT = nc.dram_tensor("l0_wqT", [D, D], F32, kind="ExternalInput")
    wkT = nc.dram_tensor("l0_wkT", [D, D], F32, kind="ExternalInput")
    wvT = nc.dram_tensor("l0_wvT", [D, D], F32, kind="ExternalInput")
    woT = nc.dram_tensor("l0_woT", [D, D], F32, kind="ExternalInput")
    gwT = nc.dram_tensor("l0_gwT", [D, 16], F32, kind="ExternalInput")
    w1a = nc.dram_tensor("l0_w1", [EPC0, D, DFF0], F32, kind="ExternalInput")
    w2a = nc.dram_tensor("l0_w2", [EPC0, DFF0, D], F32, kind="ExternalInput")
    cwqT = nc.dram_tensor("l1_wqT", [D, E1], F32, kind="ExternalInput")
    cwkT = nc.dram_tensor("l1_wkT", [D, E1], F32, kind="ExternalInput")
    cwvT = nc.dram_tensor("l1_wvT", [D, E1], F32, kind="ExternalInput")
    cwoT = nc.dram_tensor("l1_woT", [E1, D], F32, kind="ExternalInput")
    lq1 = nc.dram_tensor("l1_lq1", [128, NH1], F32, kind="ExternalInput")
    lk1 = nc.dram_tensor("l1_lk1", [128, NH1], F32, kind="ExternalInput")
    lq2 = nc.dram_tensor("l1_lq2", [128, NH1], F32, kind="ExternalInput")
    lk2 = nc.dram_tensor("l1_lk2", [128, NH1], F32, kind="ExternalInput")
    rms_bc = nc.dram_tensor("l1_rmsbc", [128, 256], F32, kind="ExternalInput")
    wqd = nc.dram_tensor("lf_wq", [D, 1024], F32, kind="ExternalInput")
    wkd = nc.dram_tensor("lf_wk", [D, 256], F32, kind="ExternalInput")
    wvd = nc.dram_tensor("lf_wv", [D, 256], F32, kind="ExternalInput")
    wod = nc.dram_tensor("lf_wo", [1024, D], F32, kind="ExternalInput")
    g1bc = nc.dram_tensor("lf_g1bc", [128, D], F32, kind="ExternalInput")
    g2bc = nc.dram_tensor("lf_g2bc", [128, D], F32, kind="ExternalInput")
    cosq = nc.dram_tensor("c_cosq", [T, 1024], F32, kind="ExternalInput")
    sinq = nc.dram_tensor("c_sinq", [T, 1024], F32, kind="ExternalInput")
    rwT = nc.dram_tensor("lf_rw", [D, 8], F32, kind="ExternalInput")
    rbbc = nc.dram_tensor("lf_rbbc", [128, 8], F32, kind="ExternalInput")
    w1f = nc.dram_tensor("lf_w1", [1, D, HID2], F32, kind="ExternalInput")
    w2f = nc.dram_tensor("lf_w2", [1, D, HID2], F32, kind="ExternalInput")
    w3f = nc.dram_tensor("lf_w3", [1, HID2, D], F32, kind="ExternalInput")
    cmask = nc.dram_tensor("c_coremask", [8, 1], F32, kind="ExternalInput")
    x1 = nc.dram_tensor("x1", [T, D], F32, kind="Internal")
    x2 = nc.dram_tensor("x2", [T, D], F32, kind="Internal")
    x3 = nc.dram_tensor("x3", [T, D], F32, kind="Internal")
    h_out = nc.dram_tensor("h", [T, D], F32, kind="Internal")
    xm_out = nc.dram_tensor("xm", [T, D], F32, kind="Internal")
    xmbf = nc.dram_tensor("xmbf", [T, D], BF16, kind="Internal")
    out = nc.dram_tensor("out", [T, D], F32, kind="ExternalOutput")

    with ExitStack() as cg:
        g = Common(nc, cg)
        g.load(nc)
        mark = len(Ctr.created)
        with ExitStack() as c1:
            emit_phase_a(nc, c1, g, x0, wqT, wkT, wvT, woT, x1, tag="a")
        recycle_sems(nc, mark)
        mark = len(Ctr.created)
        with ExitStack() as c2:
            emit_phase_b(nc, c2, g, x1, gwT, w1a, w2a, cmask, x2, tag="b")
        recycle_sems(nc, mark)
        mark = len(Ctr.created)
        with ExitStack() as c3:
            emit_phase_c(nc, c3, g, x2, cwqT, cwkT, cwvT, cwoT, lq1, lk1, lq2, lk2,
                         rms_bc, x3, tag="c")
        recycle_sems(nc, mark)
        mark = len(Ctr.created)
        with ExitStack() as c4:
            emit_phase_d_attn(nc, c4, g, x3, wqd, wkd, wvd, wod, g1bc, g2bc, cosq, sinq,
                              h_out, xm_out, xmbf, tag="d")
        recycle_sems(nc, mark)
        mark = len(Ctr.created)
        with ExitStack() as c5:
            emit_phase_d_moe(nc, c5, g, x3, h_out, xm_out, xmbf, rwT, rbbc, w1f, w2f, w3f,
                             cmask, out, tag="e")
        recycle_sems(nc, mark)
    return nc


def fused_inmaps(inputs):
    consts = host_consts()
    cosq, sinq = rotary_tables()
    z = inputs
    in_maps = []
    for c in range(NCORES):
        m = dict(consts)
        m["x0"] = np.ascontiguousarray(np.asarray(z["L0_attn_in_features"][c], np.float32))
        m["l0_wqT"] = np.ascontiguousarray(np.asarray(z["L0_attn_w_q"]).T)
        m["l0_wkT"] = np.ascontiguousarray(np.asarray(z["L0_attn_w_k"]).T)
        m["l0_wvT"] = np.ascontiguousarray(np.asarray(z["L0_attn_w_v"]).T)
        m["l0_woT"] = np.ascontiguousarray(np.asarray(z["L0_attn_w_o"]).T)
        m["l0_gwT"] = np.ascontiguousarray(np.asarray(z["L0_ffn_gate_w"]).T)
        m["l0_w1"] = np.ascontiguousarray(z["L0_ffn_w1"][EPC0 * c:EPC0 * (c + 1)])
        m["l0_w2"] = np.ascontiguousarray(z["L0_ffn_w2"][EPC0 * c:EPC0 * (c + 1)])
        m["l1_wqT"] = np.ascontiguousarray(np.asarray(z["L1_attn_Wq"]).T)
        m["l1_wkT"] = np.ascontiguousarray(np.asarray(z["L1_attn_Wk"]).T)
        m["l1_wvT"] = np.ascontiguousarray(np.asarray(z["L1_attn_Wv"]).T)
        m["l1_woT"] = np.ascontiguousarray(np.asarray(z["L1_attn_Wo"]).T)
        m["l1_lq1"] = np.ascontiguousarray(np.asarray(z["L1_attn_lambda_q1"]).T)
        m["l1_lk1"] = np.ascontiguousarray(np.asarray(z["L1_attn_lambda_k1"]).T)
        m["l1_lq2"] = np.ascontiguousarray(np.asarray(z["L1_attn_lambda_q2"]).T)
        m["l1_lk2"] = np.ascontiguousarray(np.asarray(z["L1_attn_lambda_k2"]).T)
        m["l1_rmsbc"] = np.ascontiguousarray(
            np.broadcast_to(np.asarray(z["L1_attn_rms_scale"]), (128, 256)).astype(np.float32))
        m["lf_wq"] = np.ascontiguousarray(z["L1_ffn_wq"])
        m["lf_wk"] = np.ascontiguousarray(z["L1_ffn_wk"])
        m["lf_wv"] = np.ascontiguousarray(z["L1_ffn_wv"])
        m["lf_wo"] = np.ascontiguousarray(z["L1_ffn_wo"])
        m["lf_g1bc"] = np.ascontiguousarray(
            np.broadcast_to(np.asarray(z["L1_ffn_g1"]), (128, D)).astype(np.float32))
        m["lf_g2bc"] = np.ascontiguousarray(
            np.broadcast_to(np.asarray(z["L1_ffn_g2"]), (128, D)).astype(np.float32))
        m["c_cosq"] = cosq
        m["c_sinq"] = sinq
        m["lf_rw"] = np.ascontiguousarray(z["L1_ffn_router_w"])
        m["lf_rbbc"] = np.ascontiguousarray(
            np.broadcast_to(np.asarray(z["L1_ffn_router_b"]), (128, 8)).astype(np.float32))
        m["lf_w1"] = np.ascontiguousarray(z["L1_ffn_w1"][c:c + 1])
        m["lf_w2"] = np.ascontiguousarray(z["L1_ffn_w2"][c:c + 1])
        m["lf_w3"] = np.ascontiguousarray(z["L1_ffn_w3"][c:c + 1])
        m["c_coremask"] = (np.arange(8) < c).astype(np.float32).reshape(8, 1)
        m["c_base"] = np.full((1, 1), c * T, np.float32)
        in_maps.append(m)
    return in_maps


def kernel(**inputs):
    nc = build_fused_nc()
    in_maps = fused_inmaps(inputs)
    res = run_bass_kernel_spmd(nc, in_maps, core_ids=list(range(NCORES)))
    return np.ascontiguousarray(
        np.stack([res.results[c]["out"] for c in range(NCORES)]).astype(np.float32))


def kernel_unfused(**inputs):
    x = np.asarray(inputs["L0_attn_in_features"], dtype=np.float32)
    x1 = run_phase_a(x, inputs["L0_attn_w_q"], inputs["L0_attn_w_k"],
                     inputs["L0_attn_w_v"], inputs["L0_attn_w_o"])
    x2 = run_phase_b(x1, inputs["L0_ffn_gate_w"], inputs["L0_ffn_w1"],
                     inputs["L0_ffn_w2"])
    x3 = run_phase_c(x2, inputs["L1_attn_Wq"], inputs["L1_attn_Wk"],
                     inputs["L1_attn_Wv"], inputs["L1_attn_Wo"],
                     inputs["L1_attn_lambda_q1"], inputs["L1_attn_lambda_k1"],
                     inputs["L1_attn_lambda_q2"], inputs["L1_attn_lambda_k2"],
                     inputs["L1_attn_rms_scale"])
    outs = run_phase_d(x3, inputs, parts=("attn", "moe"))
    return np.ascontiguousarray(outs["out"].astype(np.float32))


# version 12
# speedup vs baseline: 1.0026x; 1.0026x over previous
import os
import numpy as np
import ml_dtypes
import concourse.bass as bass
import concourse.mybir as mybir
from concourse.bass_utils import run_bass_kernel_spmd
from contextlib import ExitStack

F32 = mybir.dt.float32
BF16 = mybir.dt.bfloat16
I32 = mybir.dt.int32
AF = mybir.ActivationFunctionType
ALU = mybir.AluOpType
AX = mybir.AxisListType
BF = ml_dtypes.bfloat16

B, T, D = 8, 4096, 1024
NT = T // 128
NDC = D // 128
NEG = -1.0e30
NCORES = 8



class Ctr:
    _uid = 0
    created = []

    def __init__(self, nc, name=None):
        Ctr._uid += 1
        self._cm = nc.semaphore(f"{name or 'c'}_{Ctr._uid}")
        self.sem = self._cm.__enter__()
        self.n = 0
        Ctr.created.append(self)


def recycle_sems(nc, mark):
    ctrs = Ctr.created[mark:]
    if not ctrs:
        return
    nc.clear_and_free_semaphores([c.sem for c in ctrs])
    for c in ctrs:
        c._cm = None
    del Ctr.created[mark:]


ENGS = ("tensor", "vector", "scalar", "gpsimd", "sync")


class Sched:
    _breg_uid = 0
    """Collects per-engine thunks with schedule-time semaphore values; flush()
    emits them inside one nc.Block() (all-engine barrier at entry and exit).

    vector / scalar / gpsimd-compute ops are auto-serialized with a per-engine
    semaphore chain (wait for the previous op's WRITEBACK before issuing): the
    device does not order same-engine RAW/WAW through the queue alone.
    Completion value of the last op on engine e: self.v/.a/.g Ctr .n."""

    PARANOID = bool(os.environ.get("SCHED_PARANOID"))

    def __init__(self, nc):
        self.nc = nc
        self.ops = {e: [] for e in ENGS}
        self.lastw = {e: {} for e in ENGS}
        self.v = Ctr(nc, "sv")
        self.a = Ctr(nc, "sa")
        self.g = Ctr(nc, "sg")
        self._auto = {"vector": self.v, "scalar": self.a, "gpsimd": self.g}
        self._breg_vals = []
        self._breg = {}
        self._allctrs = [self.v, self.a, self.g]

    def bnd(self, v):
        v = int(v)
        if v not in self._breg_vals:
            self._breg_vals.append(v)
        return v

    def op(self, eng, emit, wait=(), inc=None, inck=1, dma=False):
        wl = []
        auto = self._auto.get(eng)
        if inc is not None and inc not in self._allctrs:
            self._allctrs.append(inc)
        if self.PARANOID:
            wait = list(wait) + [(c, c.n) for c in self._allctrs]
        elif auto is not None and not dma:
            wait = list(wait) + [(auto, auto.n)]
        for (c, v) in wait:
            if v > 0 and v > self.lastw[eng].get(id(c), -1):
                wl.append((c.sem, v))
                self.lastw[eng][id(c)] = v
        incs = []
        if inc is not None:
            inc.n += inck
            incs.append((inc.sem, inck))
        if auto is not None and not (dma and inc is not None):
            k = 16 if dma else 1
            auto.n += k
            incs.append((auto.sem, k))

        def f(e, emit=emit, wl=tuple(wl), incs=tuple(incs)):
            for (s, v) in wl:
                e.wait_ge(s, v)
            inst = emit(e)
            for (s, k) in incs:
                r = inst.then_inc(s, k)
                inst = r if r is not None else inst

        self.ops[eng].append(f)

    def flush(self):
        nc = self.nc
        if self._breg_vals:
            vals = list(self._breg_vals)
            cms = []

            def setup(e, vals=vals, cms=cms):
                for v in vals:
                    Sched._breg_uid += 1
                    cm = e.register(f"bnd{v}_{Sched._breg_uid}")
                    rb = cm.__enter__()
                    cms.append(cm)
                    e.reg_mov(rb, v)
                    self._breg[v] = rb

            def teardown(e, cms=cms):
                for cm in cms:
                    cm.__exit__(None, None, None)

            self.ops["gpsimd"] = ([lambda e: setup(e)] + self.ops["gpsimd"]
                                  + [lambda e: teardown(e)])
        with nc.Block() as blk:
            for eng in ENGS:
                fns = self.ops[eng]
                if not fns:
                    continue

                def runner(e, fns=tuple(fns)):
                    for f in fns:
                        f(e)

                getattr(blk, eng)(runner)
        self.ops = {e: [] for e in ENGS}
        self.lastw = {e: {} for e in ENGS}
        self._breg_vals = []
        self._breg = {}




def host_consts():
    i = np.arange(128)
    cmaskT = np.where(i[:, None] > i[None, :], NEG, 0.0).astype(BF)
    negfull = np.full((128, 128), NEG, dtype=BF)
    ident = np.eye(128, dtype=BF)
    identf = np.eye(128, dtype=np.float32)
    uincl = np.where(i[:, None] <= i[None, :], 1.0, 0.0).astype(np.float32)
    iota16 = np.broadcast_to(np.arange(16, dtype=np.float32), (128, 16)).copy()
    iota16b = iota16 + 1e6
    iotacol = np.arange(128, dtype=np.float32).reshape(128, 1)
    return {"c_ident_bf": ident, "c_cmaskT_bf": cmaskT, "c_negfull_bf": negfull,
            "c_ident_f": identf, "c_uincl_f": uincl, "c_iota16": iota16,
            "c_iota16b": iota16b, "c_iotacol": iotacol}


CONST_SHAPES = {"c_ident_bf": ([128, 128], "bf"), "c_cmaskT_bf": ([128, 128], "bf"),
                "c_negfull_bf": ([128, 128], "bf"), "c_ident_f": ([128, 128], "f"),
                "c_uincl_f": ([128, 128], "f"), "c_iota16": ([128, 16], "f"),
                "c_iota16b": ([128, 16], "f"), "c_iotacol": ([128, 1], "f")}


class Common:

    def __init__(self, nc, ctx):
        E = ctx.enter_context
        self.dr = {k: nc.dram_tensor(k, v[0], BF16 if v[1] == "bf" else F32,
                                     kind="ExternalInput")
                   for k, v in CONST_SHAPES.items()}
        self.idb = E(nc.sbuf_tensor("g_idb", [128, 128], BF16))
        self.cmb = E(nc.sbuf_tensor("g_cmb", [128, 128], BF16))
        self.nfb = E(nc.sbuf_tensor("g_nfb", [128, 128], BF16))
        self.idf = E(nc.sbuf_tensor("g_idf", [128, 128], F32))
        self.uincl = E(nc.sbuf_tensor("g_uincl", [128, 128], F32))
        self.iota16 = E(nc.sbuf_tensor("g_iota16", [128, 16], F32))
        self.iota16b = E(nc.sbuf_tensor("g_iota16b", [128, 16], F32))
        self.iotacol = E(nc.sbuf_tensor("g_iotacol", [128, 1], F32))
        self.ones_col_bf = E(nc.sbuf_tensor("g_ones_bf", [128, 1], BF16))
        self.ones_col_f = E(nc.sbuf_tensor("g_ones_f", [128, 1], F32))
        self.ones_row_f = E(nc.sbuf_tensor("g_onesrow_f", [1, 128], F32))
        self.dr_base = nc.dram_tensor("c_base", [1, 1], F32, kind="ExternalInput")

    def load(self, nc):
        s = Sched(nc)
        ld = Ctr(nc, "gld")
        for name, tile in [("c_ident_bf", self.idb), ("c_cmaskT_bf", self.cmb),
                           ("c_negfull_bf", self.nfb), ("c_ident_f", self.idf),
                           ("c_uincl_f", self.uincl), ("c_iota16", self.iota16),
                           ("c_iota16b", self.iota16b), ("c_iotacol", self.iotacol)]:
            s.op("sync", lambda e, n=name, t=tile: e.dma_start(t[:], self.dr[n].ap()),
                 inc=ld, inck=16, dma=True)
        s.op("vector", lambda e: e.memset(self.ones_col_bf[:], 1.0))
        s.op("vector", lambda e: e.memset(self.ones_col_f[:], 1.0))
        s.op("vector", lambda e: e.memset(self.ones_row_f[:], 1.0))
        s.op("sync", lambda e: e.wait_ge(ld.sem, ld.n))
        s.flush()


def emit_ln_tile(s, tmp, hb, ob, eps_tile, h_ready_v):
    dv, ac = s.v, s.a
    s.op("vector", lambda e: e.reduce_sum(tmp["s1"][:], hb[:], axis=AX.X),
         wait=[(dv, h_ready_v)])
    s1_done = dv.n
    s.op("scalar", lambda e: e.activation(tmp["sqs"][:], hb[:], AF.Square),
         wait=[(dv, h_ready_v)])
    sq_done = ac.n
    s.op("vector", lambda e: e.reduce_sum(tmp["s2"][:], tmp["sqs"][:], axis=AX.X),
         wait=[(ac, sq_done)])
    s2_done = dv.n
    s.op("vector", lambda e: e.tensor_scalar(tmp["mmean"][:], tmp["s1"][:], 1.0 / D, None,
                                             op0=ALU.mult))
    s.op("vector", lambda e: e.tensor_scalar(tmp["vvar"][:], tmp["s2"][:], 1.0 / D, None,
                                             op0=ALU.mult))
    s.op("vector", lambda e: e.tensor_tensor(tmp["s1"][:], tmp["mmean"][:], tmp["mmean"][:],
                                             ALU.mult))
    s.op("vector", lambda e: e.tensor_tensor(tmp["vvar"][:], tmp["vvar"][:], tmp["s1"][:],
                                             ALU.subtract))
    var_done = dv.n
    s.op("scalar", lambda e: e.activation(tmp["vvar"][:], tmp["vvar"][:], AF.Sqrt,
                                          bias=eps_tile[:, 0:1]),
         wait=[(dv, var_done)])
    sd_done = ac.n
    s.op("vector", lambda e: e.reciprocal(tmp["rstd"][:], tmp["vvar"][:]),
         wait=[(ac, sd_done)])
    s.op("vector", lambda e: e.tensor_scalar(ob[:], hb[:], tmp["mmean"][:, 0:1],
                                             tmp["rstd"][:, 0:1],
                                             op0=ALU.subtract, op1=ALU.mult))
    return dv.n




def emit_phase_a(nc, ctx, g, x0, wqT, wkT, wvT, woT, x1, tag="a", dbg=None):
    E = ctx.enter_context
    xbf = nc.dram_tensor(f"{tag}_xbf", [T, D], BF16, kind="Internal")
    qpark = nc.dram_tensor(f"{tag}_qpark", [D, T], BF16, kind="Internal")
    ctxT = nc.dram_tensor(f"{tag}_ctxT", [D, T], BF16, kind="Internal")

    K_sb = E(nc.sbuf_tensor(f"{tag}_K", [128, NDC * T], BF16))
    V_sb = E(nc.sbuf_tensor(f"{tag}_V", [128, NT * D], BF16))
    mq2 = E(nc.sbuf_tensor(f"{tag}_mq2", [1, 1], F32))
    mk2 = E(nc.sbuf_tensor(f"{tag}_mk2", [1, 1], F32))
    mqt = E(nc.sbuf_tensor(f"{tag}_mqt", [1, 1], F32))
    mkt = E(nc.sbuf_tensor(f"{tag}_mkt", [1, 1], F32))

    s = Sched(nc)
    ld = Ctr(nc, "ld")
    s.op("gpsimd", lambda e: e.dma_start(xbf.ap(), x0.ap()), inc=ld, inck=16, dma=True)
    s.op("gpsimd", lambda e: e.wait_ge(ld.sem, ld.n), dma=True, inc=ld, inck=0) if False else None
    s.op("sync", lambda e: e.wait_ge(ld.sem, ld.n))
    s.flush()

    with ExitStack() as c1:
        E1 = c1.enter_context
        wq_sb = E1(nc.sbuf_tensor(f"{tag}_wq", [128, NDC * D], BF16))
        wk_sb = E1(nc.sbuf_tensor(f"{tag}_wk", [128, NDC * D], BF16))
        xTc = [E1(nc.sbuf_tensor(f"{tag}_xTc{i}", [128, NDC * 512], BF16)) for i in range(2)]
        qstage = [E1(nc.sbuf_tensor(f"{tag}_qs{i}", [128, 512], BF16)) for i in range(2)]
        sqq = E1(nc.sbuf_tensor(f"{tag}_sqq", [128, 512], F32))
        sqk = E1(nc.sbuf_tensor(f"{tag}_sqk", [128, 512], F32))
        ps_k = E1(nc.psum_tensor(f"{tag}_pk", [128, 512], F32))
        ps_q = E1(nc.psum_tensor(f"{tag}_pq", [128, 512], F32))
        ps_nq = E1(nc.psum_tensor(f"{tag}_pnq", [1, 512], F32))
        ps_nk = E1(nc.psum_tensor(f"{tag}_pnk", [1, 512], F32))

        s = Sched(nc)
        dv, ac = s.v, s.a
        wld = Ctr(nc, "wld")
        xldp = [Ctr(nc, "xld0"), Ctr(nc, "xld1")]
        mm = Ctr(nc, "mm")
        nmm = Ctr(nc, "nmm")
        stq = [Ctr(nc, "stq0"), Ctr(nc, "stq1")]

        s.op("vector", lambda e: e.memset(mq2[:], 0.0))
        s.op("vector", lambda e: e.memset(mk2[:], 0.0))
        for i in range(NDC):
            s.op("gpsimd", lambda e, i=i: e.dma_start(
                wq_sb[:, i * D:(i + 1) * D], wqT.ap()[i * 128:(i + 1) * 128, :]),
                inc=wld, inck=16, dma=True)
            s.op("gpsimd", lambda e, i=i: e.dma_start(
                wk_sb[:, i * D:(i + 1) * D], wkT.ap()[i * 128:(i + 1) * 128, :]),
                inc=wld, inck=16, dma=True)
        wready = wld.n

        mm_after_tc = {}
        kevac = {}
        qevac = {}
        sqk_user = {}
        sqq_user = {}
        norm_free = 0
        for tc in range(T // 512):
            xb = xTc[tc % 2]
            xld = xldp[tc % 2]
            w = [(mm, mm_after_tc.get(tc - 2, 0))]
            for i in range(NDC):
                s.op("sync", lambda e, i=i, xb=xb, tc=tc: e.dma_start(
                    xb[:, i * 512:(i + 1) * 512],
                    xbf.ap()[tc * 512:(tc + 1) * 512, i * 128:(i + 1) * 128],
                    transpose=True), wait=w, inc=xld, inck=16, dma=True)
            xready = xld.n
            for ot in range(NDC):
                def kgrp(e, ot=ot, xb=xb):
                    for i in range(NDC):
                        inst = e.matmul(ps_k[:],
                                        wk_sb[:, i * D + ot * 128: i * D + (ot + 1) * 128],
                                        xb[:, i * 512:(i + 1) * 512],
                                        start=(i == 0), stop=(i == NDC - 1))
                    return inst
                s.op("tensor", kgrp,
                     wait=[(wld, wready), (xld, xready),
                           (ac, kevac.get((tc, ot - 1), kevac.get((tc - 1, NDC - 1), 0)))],
                     inc=mm)
                kd = mm.n
                s.op("scalar", lambda e, tc=tc, ot=ot: e.copy(
                    K_sb[:, ot * T + tc * 512: ot * T + (tc + 1) * 512], ps_k[:]),
                    wait=[(mm, kd)])
                kevac[(tc, ot)] = ac.n
                s.op("scalar", lambda e, tc=tc, ot=ot: e.activation(
                    sqk[:], K_sb[:, ot * T + tc * 512: ot * T + (tc + 1) * 512], AF.Square),
                    wait=[(nmm, sqk_user.get(ot - 1, 0))])
                sqkd = ac.n
                s.op("tensor", lambda e, ot=ot: e.matmul(
                    ps_nk[:], g.ones_col_f[:], sqk[:], start=(ot == 0), stop=(ot == NDC - 1)),
                    wait=[(ac, sqkd), (dv, norm_free)], inc=nmm)
                sqk_user[ot] = nmm.n

                def qgrp(e, ot=ot, xb=xb):
                    for i in range(NDC):
                        inst = e.matmul(ps_q[:],
                                        wq_sb[:, i * D + ot * 128: i * D + (ot + 1) * 128],
                                        xb[:, i * 512:(i + 1) * 512],
                                        start=(i == 0), stop=(i == NDC - 1))
                    return inst
                s.op("tensor", qgrp,
                     wait=[(ac, qevac.get((tc, ot - 1), qevac.get((tc - 1, NDC - 1), 0)))],
                     inc=mm)
                qd = mm.n
                qs = qstage[ot % 2]
                s.op("scalar", lambda e, qs=qs: e.copy(qs[:], ps_q[:]),
                     wait=[(mm, qd), (stq[ot % 2], stq[ot % 2].n)])
                qevac[(tc, ot)] = ac.n
                s.op("scalar", lambda e, qs=qs: e.activation(sqq[:], qs[:], AF.Square),
                     wait=[(nmm, sqq_user.get(ot - 1, 0))])
                sqqd = ac.n
                s.op("sync", lambda e, tc=tc, ot=ot, qs=qs: e.dma_start(
                    qpark.ap()[ot * 128:(ot + 1) * 128, tc * 512:(tc + 1) * 512], qs[:]),
                    wait=[(ac, sqqd)], inc=stq[ot % 2], inck=16, dma=True)
                s.op("tensor", lambda e, ot=ot: e.matmul(
                    ps_nq[:], g.ones_col_f[:], sqq[:], start=(ot == 0), stop=(ot == NDC - 1)),
                    wait=[(ac, sqqd), (dv, norm_free)], inc=nmm)
                sqq_user[ot] = nmm.n
            mm_after_tc[tc] = mm.n
            nval = nmm.n
            s.op("vector", lambda e: e.reduce_max(mkt[:], ps_nk[0:1, :], axis=AX.X),
                 wait=[(nmm, nval)])
            s.op("vector", lambda e: e.tensor_tensor(mk2[:], mk2[:], mkt[:], ALU.max))
            s.op("vector", lambda e: e.reduce_max(mqt[:], ps_nq[0:1, :], axis=AX.X))
            s.op("vector", lambda e: e.tensor_tensor(mq2[:], mq2[:], mqt[:], ALU.max))
            norm_free = dv.n
        s.op("sync", lambda e, a=stq[0].n: e.wait_ge(stq[0].sem, a))
        s.op("sync", lambda e, a=stq[1].n: e.wait_ge(stq[1].sem, a))
        s.flush()

    with ExitStack() as c1:
        E1 = c1.enter_context
        wv_sb = E1(nc.sbuf_tensor(f"{tag}_wv", [128, NDC * D], BF16))
        xTc = [E1(nc.sbuf_tensor(f"{tag}_bxTc{i}", [128, NDC * 512], BF16)) for i in range(2)]
        ps_v = [E1(nc.psum_tensor(f"{tag}_pv{i}", [128, 512], F32)) for i in range(2)]

        s = Sched(nc)
        ac = s.a
        wld = Ctr(nc, "wld")
        xldp = [Ctr(nc, "xld0"), Ctr(nc, "xld1")]
        mm = Ctr(nc, "mm")
        for i in range(NDC):
            s.op("gpsimd", lambda e, i=i: e.dma_start(
                wv_sb[:, i * D:(i + 1) * D], wvT.ap()[i * 128:(i + 1) * 128, :]),
                inc=wld, inck=16, dma=True)
        wready = wld.n
        mm_after_tc = {}
        evac = {}
        for tc in range(T // 512):
            xb = xTc[tc % 2]
            xld = xldp[tc % 2]
            w = [(mm, mm_after_tc.get(tc - 2, 0))]
            for i in range(NDC):
                s.op("sync", lambda e, i=i, xb=xb, tc=tc: e.dma_start(
                    xb[:, i * 512:(i + 1) * 512],
                    xbf.ap()[tc * 512:(tc + 1) * 512, i * 128:(i + 1) * 128],
                    transpose=True), wait=w, inc=xld, inck=16, dma=True)
            xready = xld.n
            for tj in range(4):
                tt = tc * 4 + tj
                for vc in range(2):
                    k = (tj * 2 + vc) % 2
                    pv = ps_v[k]

                    def vgrp(e, tj=tj, vc=vc, xb=xb, pv=pv):
                        for i in range(NDC):
                            inst = e.matmul(pv[:],
                                            xb[:, i * 512 + tj * 128: i * 512 + (tj + 1) * 128],
                                            wv_sb[:, i * D + vc * 512: i * D + (vc + 1) * 512],
                                            start=(i == 0), stop=(i == NDC - 1))
                        return inst
                    s.op("tensor", vgrp,
                         wait=[(wld, wready), (xld, xready), (ac, evac.get(k, 0))], inc=mm)
                    mval = mm.n
                    s.op("scalar", lambda e, tt=tt, vc=vc, pv=pv: e.copy(
                        V_sb[:, tt * D + vc * 512: tt * D + (vc + 1) * 512], pv[:]),
                        wait=[(mm, mval)])
                    evac[k] = ac.n
            mm_after_tc[tc] = mm.n
        s.flush()

    if dbg is not None:
        s = Sched(nc)
        st = Ctr(nc, "dbst")
        if "K" in dbg:
            s.op("sync", lambda e: e.dma_start(dbg["K"].ap(), K_sb[:]), inc=st, inck=16, dma=True)
        if "V" in dbg:
            s.op("sync", lambda e: e.dma_start(dbg["V"].ap(), V_sb[:]), inc=st, inck=16, dma=True)
        if "mq" in dbg:
            s.op("sync", lambda e: e.dma_start(dbg["mq"].ap()[0:1, 0:1], mq2[:]), inc=st, inck=16, dma=True)
            s.op("sync", lambda e: e.dma_start(dbg["mq"].ap()[0:1, 1:2], mk2[:]), inc=st, inck=16, dma=True)
        if "q" in dbg:
            s.op("gpsimd", lambda e: e.dma_start(dbg["q"].ap(), qpark.ap()), inc=st, inck=16, dma=True)
        s.op("sync", lambda e: e.wait_ge(st.sem, st.n))
        s.op("gpsimd", lambda e: e.wait_ge(st.sem, st.n), dma=True, inc=None)
        s.flush()

    with ExitStack() as c2:
        E2 = c2.enter_context
        PT = E2(nc.sbuf_tensor(f"{tag}_PT", [128, NT * 512], BF16))
        Qc = [E2(nc.sbuf_tensor(f"{tag}_Qc{i}", [128, NDC * 512], BF16)) for i in range(2)]
        zr = E2(nc.sbuf_tensor(f"{tag}_zr", [1, 512], F32))
        zb = E2(nc.sbuf_tensor(f"{tag}_zb", [128, 512], F32))
        negB = E2(nc.sbuf_tensor(f"{tag}_negB", [1, 1], F32))
        nbc = E2(nc.sbuf_tensor(f"{tag}_negBc", [128, 1], F32))
        sc1 = E2(nc.sbuf_tensor(f"{tag}_sc1", [1, 1], F32))
        sc2 = E2(nc.sbuf_tensor(f"{tag}_sc2", [1, 1], F32))
        ctstage = [E2(nc.sbuf_tensor(f"{tag}_cts{i}", [128, NDC * 512], BF16)) for i in range(2)]
        ps_s = [E2(nc.psum_tensor(f"{tag}_ps{i}", [128, 512], F32)) for i in range(2)]
        ps_z = E2(nc.psum_tensor(f"{tag}_pz", [1, 512], F32))
        ps_bc = E2(nc.psum_tensor(f"{tag}_pbc", [128, 512], F32))
        ps_ot = [E2(nc.psum_tensor(f"{tag}_pot{i}", [128, 512], F32)) for i in range(4)]

        s = Sched(nc)
        dv, ac = s.v, s.a
        qldp = [Ctr(nc, "qld0"), Ctr(nc, "qld1")]
        mm = Ctr(nc, "mm")
        zmm = Ctr(nc, "zmm")
        bmm = Ctr(nc, "bmm")
        pvm = Ctr(nc, "pvm")
        stc = [Ctr(nc, "stc0"), Ctr(nc, "stc1")]

        s.op("scalar", lambda e: e.activation(sc1[:], mq2[:], AF.Sqrt))
        s.op("scalar", lambda e: e.activation(sc2[:], mk2[:], AF.Sqrt))
        sq_done = ac.n
        s.op("vector", lambda e: e.tensor_tensor(negB[:], sc1[:], sc2[:], ALU.mult),
             wait=[(ac, sq_done)])
        s.op("vector", lambda e: e.tensor_scalar(negB[:], negB[:], -1.0 / 32.0, None,
                                                 op0=ALU.mult))
        nb_done = dv.n
        s.op("tensor", lambda e: e.matmul(ps_bc[0:128, 0:1], g.ones_row_f[:], negB[:],
                                          start=True, stop=True), wait=[(dv, nb_done)], inc=bmm)
        s.op("vector", lambda e: e.tensor_copy(nbc[:], ps_bc[0:128, 0:1]),
             wait=[(bmm, bmm.n)])
        nbc_done = dv.n

        qload_done = {}
        for i in range(NDC):
            s.op("sync", lambda e, i=i: e.dma_start(
                Qc[0][:, i * 512:(i + 1) * 512], qpark.ap()[i * 128:(i + 1) * 128, 0:512]),
                inc=qldp[0], inck=16, dma=True)
        qload_done[0] = qldp[0].n

        s_after_qc = {}
        pv_done_prev = 0
        ot_free = 0
        zrow_free = 0
        bc_free = nbc_done

        for qc in range(T // 512):
            nk_t = 4 * qc + 4
            qb = Qc[qc % 2]
            exp_after = {}
            s_after = {}

            def emit_s(kt, qc=qc, qb=qb):
                psl = ps_s[kt % 2]

                def sgrp(e, kt=kt, qc=qc, qb=qb, psl=psl):
                    masks = []
                    for qs in range(4):
                        qt = qc * 4 + qs
                        if qt == kt:
                            masks.append((qs, g.cmb))
                        elif qt < kt:
                            masks.append((qs, g.nfb))
                    for i in range(NDC):
                        inst = e.matmul(psl[:],
                                        K_sb[:, i * T + kt * 128: i * T + (kt + 1) * 128],
                                        qb[:, i * 512:(i + 1) * 512],
                                        start=(i == 0), stop=(i == NDC - 1 and not masks))
                    for j, (qs, mt) in enumerate(masks):
                        inst = e.matmul(psl[:, qs * 128:(qs + 1) * 128], g.idb[:], mt[:],
                                        start=False, stop=(j == len(masks) - 1))
                    return inst
                s.op("tensor", sgrp,
                     wait=[(qldp[qc % 2], qload_done[qc]), (ac, exp_after.get(kt - 2, 0)),
                           (dv, nbc_done)], inc=mm)
                s_after[kt] = mm.n

            def emit_exp(kt, qc=qc):
                psl = ps_s[kt % 2]
                s.op("scalar", lambda e, kt=kt, psl=psl: e.activation(
                    PT[:, kt * 512:(kt + 1) * 512], psl[:], AF.Exp,
                    bias=nbc[:, 0:1], scale=1.0 / 32.0),
                    wait=[(mm, s_after[kt]), (pvm, pv_done_prev)])
                exp_after[kt] = ac.n

            def emit_z(kt, nk_t=nk_t):
                s.op("tensor", lambda e, kt=kt: e.matmul(
                    ps_z[:], g.ones_col_bf[:], PT[:, kt * 512:(kt + 1) * 512],
                    start=(kt == 0), stop=(kt == nk_t - 1)),
                    wait=[(ac, exp_after[kt]), (dv, zrow_free)], inc=zmm)

            for kt in range(nk_t):
                emit_s(kt)
                emit_exp(kt)
                if kt >= 1:
                    emit_z(kt - 1)
            emit_z(nk_t - 1)
            s_after_qc[qc] = mm.n
            zdone = zmm.n
            if qc + 1 < T // 512:
                for i in range(NDC):
                    s.op("sync", lambda e, i=i, q=qc + 1: e.dma_start(
                        Qc[q % 2][:, i * 512:(i + 1) * 512],
                        qpark.ap()[i * 128:(i + 1) * 128, q * 512:(q + 1) * 512]),
                        wait=[(mm, s_after_qc[qc])], inc=qldp[(qc + 1) % 2], inck=16, dma=True)
                qload_done[qc + 1] = qldp[(qc + 1) % 2].n

            s.op("vector", lambda e: e.reciprocal(zr[:], ps_z[0:1, :]), wait=[(zmm, zdone)])
            zrow_free = dv.n
            s.op("tensor", lambda e: e.matmul(ps_bc[:], g.ones_row_f[:], zr[:],
                                              start=True, stop=True),
                 wait=[(dv, max(zrow_free, bc_free))], inc=bmm)
            s.op("vector", lambda e: e.tensor_copy(zb[:], ps_bc[:]), wait=[(bmm, bmm.n)])
            zb_ready = dv.n
            bc_free = dv.n

            stg = ctstage[qc % 2]
            for half in range(2):
                for vs in range(4):
                    vv = half * 4 + vs

                    def pvgrp(e, vv=vv, nk_t=nk_t, vs=vs):
                        for kt in range(nk_t):
                            inst = e.matmul(ps_ot[vs][:],
                                            V_sb[:, kt * D + vv * 128: kt * D + (vv + 1) * 128],
                                            PT[:, kt * 512:(kt + 1) * 512],
                                            start=(kt == 0), stop=(kt == nk_t - 1))
                        return inst
                    s.op("tensor", pvgrp,
                         wait=[(ac, exp_after[nk_t - 1]), (dv, ot_free)], inc=pvm)
                    pvv = pvm.n
                    s.op("vector", lambda e, vv=vv, vs=vs, stg=stg: e.tensor_tensor(
                        stg[:, vv * 512:(vv + 1) * 512], ps_ot[vs][:], zb[:], ALU.mult),
                        wait=[(pvm, pvv), (dv, zb_ready),
                              (stc[qc % 2], stc[qc % 2].n)])
                ot_free = dv.n
            pv_done_prev = pvm.n
            evac_all = dv.n
            for i in range(NDC):
                s.op("sync", lambda e, i=i, qc=qc, stg=stg: e.dma_start(
                    ctxT.ap()[i * 128:(i + 1) * 128, qc * 512:(qc + 1) * 512],
                    stg[:, i * 512:(i + 1) * 512]), wait=[(dv, evac_all)],
                    inc=stc[qc % 2], inck=16, dma=True)
        s.op("sync", lambda e, a=stc[0].n: e.wait_ge(stc[0].sem, a))
        s.op("sync", lambda e, a=stc[1].n: e.wait_ge(stc[1].sem, a))
        if dbg is not None and "ctx" in dbg:
            dbc = Ctr(nc, "dbc")
            s.op("gpsimd", lambda e: e.dma_start(dbg["ctx"].ap(), ctxT.ap()),
                 inc=dbc, inck=16, dma=True)
            s.op("gpsimd", lambda e, a=dbc: e.wait_ge(a.sem, a.n), dma=True)
        if dbg is not None and "PT" in dbg:
            dbp = Ctr(nc, "dbp")
            s.op("sync", lambda e: e.dma_start(dbg["PT"].ap(), PT[:]), inc=dbp, inck=16, dma=True)
            s.op("sync", lambda e, a=dbp: e.wait_ge(a.sem, a.n))
        s.flush()

    with ExitStack() as c3:
        E3 = c3.enter_context
        wo_sb = E3(nc.sbuf_tensor(f"{tag}_wo", [128, NDC * D], BF16))
        ctc = [E3(nc.sbuf_tensor(f"{tag}_ctc{i}", [128, NDC * 512], BF16)) for i in range(2)]
        x0t = [E3(nc.sbuf_tensor(f"{tag}_x0t{i}", [128, D], F32)) for i in range(2)]
        h_sb = [E3(nc.sbuf_tensor(f"{tag}_h{i}", [128, D], F32)) for i in range(2)]
        x1t = [E3(nc.sbuf_tensor(f"{tag}_x1t{i}", [128, D], F32)) for i in range(2)]
        tmp = {"sqs": E3(nc.sbuf_tensor(f"{tag}_sqs", [128, D], F32)),
               "s1": E3(nc.sbuf_tensor(f"{tag}_s1", [128, 1], F32)),
               "s2": E3(nc.sbuf_tensor(f"{tag}_s2", [128, 1], F32)),
               "mmean": E3(nc.sbuf_tensor(f"{tag}_mmean", [128, 1], F32)),
               "vvar": E3(nc.sbuf_tensor(f"{tag}_vvar", [128, 1], F32)),
               "rstd": E3(nc.sbuf_tensor(f"{tag}_rstd", [128, 1], F32))}
        epsb = E3(nc.sbuf_tensor(f"{tag}_epsb", [128, 1], F32))
        ps_o = [E3(nc.psum_tensor(f"{tag}_po{i}", [128, 512], F32)) for i in range(4)]

        s = Sched(nc)
        dv, ac = s.v, s.a
        wld = Ctr(nc, "wld")
        cldp = [Ctr(nc, "cld0"), Ctr(nc, "cld1")]
        xldr = Ctr(nc, "xldr")
        mm = Ctr(nc, "mm")
        stp = [Ctr(nc, "stp0"), Ctr(nc, "stp1")]

        s.op("vector", lambda e: e.memset(epsb[:], 1e-5))
        for i in range(NDC):
            s.op("gpsimd", lambda e, i=i: e.dma_start(
                wo_sb[:, i * D:(i + 1) * D], woT.ap()[i * 128:(i + 1) * 128, :]),
                inc=wld, inck=16, dma=True)
        wready = wld.n

        ct_done = {}
        for i in range(NDC):
            s.op("sync", lambda e, i=i: e.dma_start(
                ctc[0][:, i * 512:(i + 1) * 512], ctxT.ap()[i * 128:(i + 1) * 128, 0:512]),
                inc=cldp[0], inck=16, dma=True)
        ct_done[0] = cldp[0].n

        mm_after_qc = {}
        x0_user = {}
        ps_user = {}
        for qc in range(T // 512):
            cb = ctc[qc % 2]
            for tj in range(4):
                tt = qc * 4 + tj
                xb = x0t[tt % 2]
                hb = h_sb[tt % 2]
                ob = x1t[tt % 2]
                s.op("sync", lambda e, tt=tt, xb=xb: e.dma_start(
                    xb[:], x0.ap()[tt * 128:(tt + 1) * 128, :]),
                    wait=[(dv, x0_user.get(tt - 2, 0))], inc=xldr, inck=16, dma=True)
                xrd = xldr.n
                for dc in range(2):
                    pslab = (tj * 2 + dc) % 4
                    pp = ps_o[pslab]

                    def ogrp(e, tj=tj, dc=dc, cb=cb, pp=pp):
                        for i in range(NDC):
                            inst = e.matmul(pp[:],
                                            cb[:, i * 512 + tj * 128: i * 512 + (tj + 1) * 128],
                                            wo_sb[:, i * D + dc * 512: i * D + (dc + 1) * 512],
                                            start=(i == 0), stop=(i == NDC - 1))
                        return inst
                    s.op("tensor", ogrp, wait=[(wld, wready), (cldp[qc % 2], ct_done[qc]),
                                               (dv, ps_user.get(pslab, 0))], inc=mm)
                    mval = mm.n
                    s.op("vector", lambda e, dc=dc, xb=xb, hb=hb, pp=pp: e.tensor_tensor(
                        hb[:, dc * 512:(dc + 1) * 512], xb[:, dc * 512:(dc + 1) * 512],
                        pp[:], ALU.add),
                        wait=[(mm, mval), (xldr, xrd), (stp[tt % 2], stp[tt % 2].n)])
                    ps_user[pslab] = dv.n
                x0_user[tt] = dv.n
                h_ready = dv.n
                out_done = emit_ln_tile(s, tmp, hb, ob, epsb, h_ready)
                s.op("sync", lambda e, tt=tt, ob=ob: e.dma_start(
                    x1.ap()[tt * 128:(tt + 1) * 128, :], ob[:]),
                    wait=[(dv, out_done)], inc=stp[tt % 2], inck=16, dma=True)
            mm_after_qc[qc] = mm.n
            if qc + 1 < T // 512:
                for i in range(NDC):
                    s.op("sync", lambda e, i=i, q=qc + 1: e.dma_start(
                        ctc[q % 2][:, i * 512:(i + 1) * 512],
                        ctxT.ap()[i * 128:(i + 1) * 128, q * 512:(q + 1) * 512]),
                        wait=[(mm, mm_after_qc[qc])], inc=cldp[(qc + 1) % 2], inck=16, dma=True)
                ct_done[qc + 1] = cldp[(qc + 1) % 2].n
        s.op("sync", lambda e, a=stp[0].n: e.wait_ge(stp[0].sem, a))
        s.op("sync", lambda e, a=stp[1].n: e.wait_ge(stp[1].sem, a))
        s.flush()



CAP0 = 2560
NE0 = 16
EPC0 = 2
DFF0 = 4096


def emit_phase_b(nc, ctx, g, x1, gwT, w1, w2, coremask, x2, tag="b"):
    E = ctx.enter_context
    NTT = T // 128
    NCH = T // 512
    x1bf_in = nc.dram_tensor(f"{tag}_x1bf_in", [T, D], BF16, kind="Internal")
    x1bf_all = nc.dram_tensor(f"{tag}_x1bf_all", [NCORES * T, D], BF16, kind="Internal")
    hist_in = nc.dram_tensor(f"{tag}_hist_in", [1, 16], F32, kind="Internal")
    hist_all = nc.dram_tensor(f"{tag}_hist_all", [NCORES, 16], F32, kind="Internal")
    idt_loc = nc.dram_tensor(f"{tag}_idt_loc", [NE0 * CAP0, 1], F32, kind="Internal")
    idt_all = nc.dram_tensor(f"{tag}_idt_all", [NE0 * CAP0, 1], F32, kind="Internal")
    yl = nc.dram_tensor(f"{tag}_yl", [EPC0 * CAP0, D], BF16, kind="Internal")
    yall = nc.dram_tensor(f"{tag}_yall", [NE0 * CAP0, D], BF16, kind="Internal")

    sp_sb = E(nc.sbuf_tensor(f"{tag}_sp", [128, NTT], F32))
    w_sb = E(nc.sbuf_tensor(f"{tag}_w", [128, NTT], F32))
    cm_sb = E(nc.sbuf_tensor(f"{tag}_cm", [8, 16], F32))
    base_sb = E(nc.sbuf_tensor(f"{tag}_base", [1, 1], F32))

    rg = [list(range(NCORES))]

    with ExitStack() as c1:
        E1 = c1.enter_context
        gw_sb = E1(nc.sbuf_tensor(f"{tag}_gw", [128, NDC * 16], F32))
        xt = [E1(nc.sbuf_tensor(f"{tag}_xt{i}", [128, D], F32)) for i in range(2)]
        xT = E1(nc.sbuf_tensor(f"{tag}_xT", [128, NDC * 128], F32))
        lg = E1(nc.sbuf_tensor(f"{tag}_lg", [128, 16], F32))
        exl = E1(nc.sbuf_tensor(f"{tag}_exl", [128, 16], F32))
        oh = E1(nc.sbuf_tensor(f"{tag}_oh", [128, 16], F32))
        eq = E1(nc.sbuf_tensor(f"{tag}_eq", [128, 16], F32))
        cnd = E1(nc.sbuf_tensor(f"{tag}_cnd", [128, 16], F32))
        mx = E1(nc.sbuf_tensor(f"{tag}_mx", [128, 1], F32))
        nmx = E1(nc.sbuf_tensor(f"{tag}_nmx", [128, 1], F32))
        zz = E1(nc.sbuf_tensor(f"{tag}_zz", [128, 1], F32))
        rz = E1(nc.sbuf_tensor(f"{tag}_rz", [128, 1], F32))
        am = E1(nc.sbuf_tensor(f"{tag}_am", [128, 1], F32))
        rk = E1(nc.sbuf_tensor(f"{tag}_rk", [128, 1], F32))
        run_row = E1(nc.sbuf_tensor(f"{tag}_run", [1, 16], F32))
        ps_t = [E1(nc.psum_tensor(f"{tag}_pt{i}", [128, 512], F32)) for i in range(2)]
        ps_l = E1(nc.psum_tensor(f"{tag}_pl", [128, 512], F32))
        ps_r = E1(nc.psum_tensor(f"{tag}_pr", [128, 512], F32))
        ps_h = E1(nc.psum_tensor(f"{tag}_ph", [1, 512], F32))

        s = Sched(nc)
        dv, ac = s.v, s.a
        ld = Ctr(nc, "ld")
        xldp = [Ctr(nc, "xld0"), Ctr(nc, "xld1")]
        mm = Ctr(nc, "mm")
        rmm = Ctr(nc, "rmm")
        hmm = Ctr(nc, "hmm")
        cc = Ctr(nc, "cc")

        castc = Ctr(nc, "castc")
        s.op("gpsimd", lambda e: e.dma_start(x1bf_in.ap(), x1.ap()), inc=castc, inck=16, dma=True)
        s.op("gpsimd", lambda e: e.wait_ge(castc.sem, 16), dma=True)
        s.op("gpsimd", lambda e: e.collective_compute(
            "AllGather", ALU.bypass, replica_groups=rg,
            ins=[x1bf_in.ap()], outs=[x1bf_all.ap()]), inc=cc, dma=True)

        for i in range(NDC):
            s.op("sync", lambda e, i=i: e.dma_start(
                gw_sb[:, i * 16:(i + 1) * 16], gwT.ap()[i * 128:(i + 1) * 128, :]),
                inc=ld, inck=16, dma=True)
        gw_ready = ld.n
        s.op("vector", lambda e: e.memset(run_row[:], 0.0))

        tr_evac = 0
        lg_user = 0
        run_user = 0
        for tt in range(NTT):
            xb = xt[tt % 2]
            xld = xldp[tt % 2]
            s.op("sync", lambda e, tt=tt, xb=xb: e.dma_start(
                xb[:], x1.ap()[tt * 128:(tt + 1) * 128, :]),
                wait=[(mm, max(0, mm.n - 9))], inc=xld, inck=16, dma=True)
            xrd = xld.n
            def trg(e, xb=xb):
                for i in range(NDC):
                    pslab = ps_t[i % 2]
                    inst = e.transpose(pslab[:, 0:128], xb[:, i * 128:(i + 1) * 128], g.idf[:])
                return inst
            for i in range(NDC):
                pslab = ps_t[i % 2]
                s.op("tensor", lambda e, i=i, xb=xb, pslab=pslab: e.transpose(
                    pslab[:, 0:128], xb[:, i * 128:(i + 1) * 128], g.idf[:]),
                    wait=[(xld, xrd), (dv, max(0, tr_evac - 1))], inc=mm)
                mval = mm.n
                s.op("vector", lambda e, i=i, pslab=pslab: e.tensor_copy(
                    xT[:, i * 128:(i + 1) * 128], pslab[:, 0:128]),
                    wait=[(mm, mval)])
                tr_evac = dv.n
            def lgrp(e):
                for i in range(NDC):
                    inst = e.matmul(ps_l[:, 0:16], xT[:, i * 128:(i + 1) * 128],
                                    gw_sb[:, i * 16:(i + 1) * 16],
                                    start=(i == 0), stop=(i == NDC - 1))
                return inst
            s.op("tensor", lgrp, wait=[(ld, gw_ready), (dv, tr_evac), (dv, lg_user)], inc=mm)
            lg_mm = mm.n
            s.op("vector", lambda e: e.tensor_copy(lg[:], ps_l[:, 0:16]), wait=[(mm, lg_mm)])
            s.op("vector", lambda e: e.reduce_max(mx[:], lg[:], axis=AX.X))
            s.op("vector", lambda e: e.reduce_max(nmx[:], lg[:], axis=AX.X, negate=True))
            mx_done = dv.n
            s.op("scalar", lambda e: e.activation(exl[:], lg[:], AF.Exp, bias=nmx[:, 0:1]),
                 wait=[(dv, mx_done)])
            ex_done = ac.n
            s.op("vector", lambda e: e.reduce_sum(zz[:], exl[:], axis=AX.X),
                 wait=[(ac, ex_done)])
            s.op("vector", lambda e: e.reciprocal(rz[:], zz[:]))
            s.op("vector", lambda e: e.tensor_scalar(eq[:], lg[:], mx[:, 0:1], None,
                                                     op0=ALU.is_ge))
            s.op("vector", lambda e: e.scalar_tensor_tensor(
                cnd[:], eq[:], -1e6, g.iota16b[:], op0=ALU.mult, op1=ALU.add))
            s.op("vector", lambda e: e.tensor_reduce(am[:], cnd[:], axis=AX.X, op=ALU.min))
            s.op("vector", lambda e: e.tensor_scalar(oh[:], g.iota16[:], am[:, 0:1], None,
                                                     op0=ALU.is_equal))
            oh_done = dv.n
            lg_user = dv.n
            def rgrp(e):
                e.matmul(ps_r[:, 0:16], g.uincl[:], oh[:], start=True, stop=False)
                return e.matmul(ps_r[:, 0:16], g.ones_row_f[:], run_row[:],
                                start=False, stop=True)
            s.op("tensor", rgrp, wait=[(dv, oh_done), (dv, run_user)], inc=rmm)
            r_done = rmm.n
            s.op("tensor", lambda e, tt=tt: e.matmul(
                ps_h[:, 0:16], g.ones_col_f[:], oh[:], start=(tt == 0), stop=(tt == NTT - 1)),
                wait=[(dv, oh_done)], inc=hmm)
            s.op("vector", lambda e: e.tensor_tensor(cnd[:], oh[:], ps_r[:, 0:16], ALU.mult),
                 wait=[(rmm, r_done)])
            s.op("vector", lambda e: e.reduce_sum(rk[:], cnd[:], axis=AX.X))
            s.op("tensor", lambda e: e.matmul(ps_r[0:1, 16:32], g.ones_col_f[:], oh[:],
                                              start=True, stop=True),
                 wait=[(dv, dv.n)], inc=rmm)
            t_done = rmm.n
            s.op("vector", lambda e: e.tensor_tensor(run_row[:], run_row[:],
                                                     ps_r[0:1, 16:32], ALU.add),
                 wait=[(rmm, t_done)])
            run_user = dv.n
            s.op("vector", lambda e, tt=tt: e.tensor_copy(w_sb[:, tt:tt + 1], rz[:]))
            s.op("vector", lambda e, tt=tt: e.tensor_copy(sp_sb[:, tt:tt + 1], rk[:]))
            s.op("vector", lambda e, tt=tt: e.scalar_tensor_tensor(
                sp_sb[:, tt:tt + 1], am[:], 65536.0, sp_sb[:, tt:tt + 1],
                op0=ALU.mult, op1=ALU.add))
        hist_done = hmm.n
        s.op("vector", lambda e: e.tensor_copy(run_row[:], ps_h[0:1, 0:16]),
             wait=[(hmm, hist_done)])
        hcopy = dv.n
        s.op("sync", lambda e: e.dma_start(hist_in.ap(), run_row[:]),
             wait=[(dv, hcopy)], inc=ld, inck=16, dma=True)
        hst = ld.n
        s.op("gpsimd", lambda e: e.collective_compute(
            "AllGather", ALU.bypass, replica_groups=rg,
            ins=[hist_in.ap()], outs=[hist_all.ap()]),
            wait=[(ld, hst)], inc=cc, dma=True)
        s.op("gpsimd", lambda e: e.wait_ge(cc.sem, cc.n), dma=True)
        s.flush()

    with ExitStack() as c2:
        E2 = c2.enter_context
        hists = E2(nc.sbuf_tensor(f"{tag}_hists", [8, 16], F32))
        offs = E2(nc.sbuf_tensor(f"{tag}_offs", [1, 16], F32))
        offs_bc = E2(nc.sbuf_tensor(f"{tag}_offsbc", [128, 16], F32))
        cmk = E2(nc.sbuf_tensor(f"{tag}_cmk", [8, 1], F32))
        zero_big = E2(nc.sbuf_tensor(f"{tag}_zb0", [128, CAP0 * NE0 // 128], F32))
        oh = E2(nc.sbuf_tensor(f"{tag}_oh2", [128, 16], F32))
        am = E2(nc.sbuf_tensor(f"{tag}_am2", [128, 1], F32))
        rk = E2(nc.sbuf_tensor(f"{tag}_rk2", [128, 1], F32))
        offsel = E2(nc.sbuf_tensor(f"{tag}_ofsel", [128, 1], F32))
        slot = E2(nc.sbuf_tensor(f"{tag}_slot", [128, 1], F32))
        keep = E2(nc.sbuf_tensor(f"{tag}_keep", [128, 1], F32))
        ids_f = E2(nc.sbuf_tensor(f"{tag}_idsf", [128, 1], F32))
        slot_i = E2(nc.sbuf_tensor(f"{tag}_sloti", [128, 1], I32))
        bb = E2(nc.sbuf_tensor(f"{tag}_bb", [128, 1], F32))
        ps_b = E2(nc.psum_tensor(f"{tag}_pb", [128, 512], F32))

        s = Sched(nc)
        s.bnd(NE0 * CAP0 - 1)
        dv, ac = s.v, s.a
        ld = Ctr(nc, "ld")
        mm = Ctr(nc, "mm")
        cc = Ctr(nc, "cc")
        sc = Ctr(nc, "sc")

        s.op("sync", lambda e: e.dma_start(hists[:], hist_all.ap()), inc=ld, inck=16, dma=True)
        s.op("sync", lambda e: e.dma_start(cmk[:], coremask.ap()), inc=ld, inck=16, dma=True)
        s.op("sync", lambda e: e.dma_start(base_sb[:], g.dr_base.ap()), inc=ld, inck=16, dma=True)
        s.op("vector", lambda e: e.memset(zero_big[:], 0.0))
        zb_done = dv.n
        s.op("sync", lambda e: e.dma_start(
            idt_loc.ap().rearrange("(p f) o -> p (f o)", p=128), zero_big[:]),
            wait=[(dv, zb_done)], inc=ld, inck=16, dma=True)
        zero_done = ld.n
        ld_done = ld.n
        s.op("tensor", lambda e: e.matmul(ps_b[0:1, 0:16], cmk[:], hists[:],
                                          start=True, stop=True),
             wait=[(ld, ld_done)], inc=mm)
        s.op("vector", lambda e: e.tensor_copy(offs[:], ps_b[0:1, 0:16]), wait=[(mm, mm.n)])
        off_done = dv.n
        s.op("tensor", lambda e: e.matmul(ps_b[:, 16:32], g.ones_row_f[:], offs[:],
                                          start=True, stop=True),
             wait=[(dv, off_done)], inc=mm)
        s.op("vector", lambda e: e.tensor_copy(offs_bc[:], ps_b[:, 16:32]), wait=[(mm, mm.n)])
        s.op("tensor", lambda e: e.matmul(ps_b[:, 32:33], g.ones_row_f[:], base_sb[:],
                                          start=True, stop=True), inc=mm)
        s.op("vector", lambda e: e.tensor_copy(bb[:], ps_b[:, 32:33]), wait=[(mm, mm.n)])
        prep_done = dv.n

        sc_prev = 0
        for tt in range(NTT):
            s.op("vector", lambda e, tt=tt: e.tensor_scalar(
                am[:], sp_sb[:, tt:tt + 1], 1.0 / 65536.0, None, op0=ALU.mult),
                wait=[(dv, prep_done), (sc, sc_prev)])
            am_w = dv.n
            s.op("gpsimd", lambda e: e.tensor_copy(slot_i[:], am[:]),
                 wait=[(dv, am_w)])
            s.op("vector", lambda e: e.tensor_copy(am[:], slot_i[:]),
                 wait=[(s.g, s.g.n)])
            s.op("vector", lambda e, tt=tt: e.scalar_tensor_tensor(
                rk[:], am[:], -65536.0, sp_sb[:, tt:tt + 1], op0=ALU.mult, op1=ALU.add))
            s.op("vector", lambda e: e.tensor_scalar(oh[:], g.iota16[:], am[:, 0:1], None,
                                                     op0=ALU.is_equal))
            s.op("vector", lambda e: e.tensor_tensor(oh[:], oh[:], offs_bc[:], ALU.mult))
            s.op("vector", lambda e: e.reduce_sum(offsel[:], oh[:], axis=AX.X))
            s.op("vector", lambda e: e.tensor_tensor(slot[:], offsel[:], rk[:], ALU.add))
            s.op("vector", lambda e: e.tensor_scalar(keep[:], slot[:], float(CAP0), None,
                                                     op0=ALU.is_le))
            s.op("vector", lambda e, tt=tt: e.tensor_tensor(
                w_sb[:, tt:tt + 1], w_sb[:, tt:tt + 1], keep[:], ALU.mult))
            s.op("vector", lambda e: e.scalar_tensor_tensor(
                slot[:], am[:], float(CAP0), slot[:], op0=ALU.mult, op1=ALU.add))
            s.op("vector", lambda e: e.tensor_scalar(slot[:], slot[:], -1.0, None, op0=ALU.add))
            s.op("vector", lambda e: e.tensor_scalar(keep[:], keep[:], -524288.0, 524288.0,
                                                     op0=ALU.mult, op1=ALU.add))
            s.op("vector", lambda e: e.tensor_tensor(slot[:], slot[:], keep[:], ALU.add))
            s.op("vector", lambda e, tt=tt: e.tensor_copy(sp_sb[:, tt:tt + 1], slot[:]))
            s.op("vector", lambda e, tt=tt: e.tensor_scalar(
                ids_f[:], g.iotacol[:], float(tt * 128 + 1), None, op0=ALU.add))
            s.op("vector", lambda e: e.tensor_tensor(ids_f[:], ids_f[:], bb[:], ALU.add))
            ids_done = dv.n
            s.op("gpsimd", lambda e: e.tensor_copy(slot_i[:], slot[:]),
                 wait=[(dv, ids_done)])
            cast_done = s.g.n
            s.op("gpsimd", lambda e: e.indirect_dma_start(
                idt_loc.ap(), bass.IndirectOffsetOnAxis(slot_i[:], 0),
                ids_f[:], None, bounds_check=s._breg[s.bnd(NE0 * CAP0 - 1)],
                oob_is_err=False),
                wait=[(ld, zero_done), (s.g, cast_done)], inc=sc, inck=16, dma=True)
            sc_prev = sc.n
        s.op("gpsimd", lambda e: e.collective_compute(
            "AllReduce", ALU.add, replica_groups=rg,
            ins=[idt_loc.ap()], outs=[idt_all.ap()]),
            wait=[(sc, sc.n)], inc=cc, dma=True)
        s.op("gpsimd", lambda e: e.wait_ge(cc.sem, cc.n), dma=True)
        s.flush()

    with ExitStack() as c3:
        E3 = c3.enter_context
        w1_sb = E3(nc.sbuf_tensor(f"{tag}_w1", [128, NDC * DFF0], BF16))
        w2_sb = E3(nc.sbuf_tensor(f"{tag}_w2", [128, (DFF0 // 128) * D], BF16))
        hT = E3(nc.sbuf_tensor(f"{tag}_hT", [128, (DFF0 // 128) * 512], BF16))
        disp = E3(nc.sbuf_tensor(f"{tag}_disp", [128, 4 * D], BF16))
        dispT = E3(nc.sbuf_tensor(f"{tag}_dispT", [128, NDC * 512], BF16))
        idcol_f = E3(nc.sbuf_tensor(f"{tag}_idcf", [128, 1], F32))
        idcol_i = E3(nc.sbuf_tensor(f"{tag}_idci", [128, 1], I32))
        negm = E3(nc.sbuf_tensor(f"{tag}_negm", [128, 1], F32))
        slotbase = E3(nc.sbuf_tensor(f"{tag}_slb", [1, 1], F32))
        slotbase_c = E3(nc.sbuf_tensor(f"{tag}_slbc", [128, 1], F32))
        ystage = E3(nc.sbuf_tensor(f"{tag}_yst", [128, D], BF16))
        ps_f = [E3(nc.psum_tensor(f"{tag}_pf{i}", [128, 512], F32)) for i in range(2)]
        ps_y = [E3(nc.psum_tensor(f"{tag}_py{i}", [128, 512], F32)) for i in range(2)]
        ps_tr = [E3(nc.psum_tensor(f"{tag}_ptr{i}", [128, 512], BF16)) for i in range(2)]

        s = Sched(nc)
        s.bnd(NE0 * CAP0 - 1)
        s.bnd(NCORES * T - 1)
        dv, ac = s.v, s.a
        wld = Ctr(nc, "wld")
        mm = Ctr(nc, "mm")
        st = Ctr(nc, "st")
        gg = Ctr(nc, "gg")
        cc = Ctr(nc, "cc")

        sbl = Ctr(nc, "sbl")
        s.op("sync", lambda e: e.dma_start(slotbase[:], g.dr_base.ap()), inc=sbl, inck=16, dma=True)
        s.op("vector", lambda e: e.tensor_scalar(slotbase[:], slotbase[:],
                                                 float(EPC0 * CAP0) / T, None, op0=ALU.mult),
             wait=[(sbl, sbl.n)])
        s.op("tensor", lambda e: e.matmul(ps_f[0][:, 0:1], g.ones_row_f[:], slotbase[:],
                                          start=True, stop=True), wait=[(dv, dv.n)], inc=mm)
        s.op("vector", lambda e: e.tensor_copy(slotbase_c[:], ps_f[0][:, 0:1]),
             wait=[(mm, mm.n)])
        sb_done = dv.n

        gdone_prev = 0
        idc_free = 0
        for se in range(EPC0):
            w1ld = Ctr(nc, f"w1ld{se}")
            wwar = [(mm, mm.n)]
            for i in range(NDC):
                s.op("gpsimd", lambda e, i=i, se=se: e.dma_start(
                    w1_sb[:, i * DFF0:(i + 1) * DFF0],
                    w1.ap()[se, i * 128:(i + 1) * 128, :]), wait=wwar,
                    inc=w1ld, inck=16, dma=True)
            for i in range(DFF0 // 128):
                s.op("gpsimd", lambda e, i=i, se=se: e.dma_start(
                    w2_sb[:, i * D:(i + 1) * D],
                    w2.ap()[se, i * 128:(i + 1) * 128, :]), wait=wwar,
                    inc=w1ld, inck=16, dma=True)
            w_ready = w1ld.n

            for cchunk in range(CAP0 // 512):
                g_done = {}
                for tj in range(4):
                    cslot = cchunk * 512 + tj * 128
                    s.op("vector", lambda e, cslot=cslot, se=se: e.tensor_scalar(
                        idcol_f[:], g.iotacol[:], float(se * CAP0 + cslot), None,
                        op0=ALU.add),
                        wait=[(dv, sb_done), (mm, mm.n), (s.g, idc_free)])
                    s.op("vector", lambda e: e.tensor_tensor(idcol_f[:], idcol_f[:],
                                                             slotbase_c[:], ALU.add))
                    idsl = dv.n
                    s.op("gpsimd", lambda e: e.tensor_copy(idcol_i[:], idcol_f[:]),
                         wait=[(dv, idsl), (gg, gdone_prev)])
                    cast1 = s.g.n
                    s.op("gpsimd", lambda e: e.indirect_dma_start(
                        idcol_f[:], None,
                        idt_all.ap(), bass.IndirectOffsetOnAxis(idcol_i[:], 0),
                        bounds_check=s._breg[s.bnd(NE0 * CAP0 - 1)], oob_is_err=False),
                        wait=[(s.g, cast1)], inc=gg, inck=16, dma=True)
                    gat1 = gg.n
                    s.op("gpsimd", lambda e, v=gg.n: e.wait_ge(gg.sem, v), dma=True)
                    s.op("vector", lambda e: e.tensor_scalar(idcol_f[:], idcol_f[:],
                                                             -1.0, None, op0=ALU.add),
                         wait=[(s.g, s.g.n), (gg, gat1)])
                    s.op("vector", lambda e: e.tensor_scalar(negm[:], idcol_f[:], 0.0, None,
                                                             op0=ALU.is_lt))
                    s.op("vector", lambda e: e.scalar_tensor_tensor(
                        idcol_f[:], negm[:], 524288.0, idcol_f[:], op0=ALU.mult, op1=ALU.add))
                    neg_done = dv.n
                    s.op("gpsimd", lambda e: e.tensor_copy(idcol_i[:], idcol_f[:]),
                         wait=[(dv, neg_done)])
                    cast2 = s.g.n
                    s.op("vector", lambda e, tj=tj: e.memset(disp[:, tj * D:(tj + 1) * D], 0.0),
                         wait=[(mm, mm.n)])
                    mz = dv.n
                    s.op("gpsimd", lambda e, tj=tj: e.indirect_dma_start(
                        disp[:, tj * D:(tj + 1) * D], None,
                        x1bf_all.ap(), bass.IndirectOffsetOnAxis(idcol_i[:], 0),
                        bounds_check=s._breg[s.bnd(NCORES * T - 1)], oob_is_err=False),
                        wait=[(dv, mz), (s.g, cast2)], inc=gg, inck=16, dma=True)
                    g_done[tj] = gg.n
                    gdone_prev = gg.n
                    idc_free = s.g.n
                    s.op("gpsimd", lambda e, v=gg.n: e.wait_ge(gg.sem, v), dma=True)
                for tj in range(4):
                    for i in range(NDC):
                        pslab = ps_tr[(tj * NDC + i) % 2]
                        s.op("tensor", lambda e, tj=tj, i=i, pslab=pslab: e.transpose(
                            pslab[:, 0:128], disp[:, tj * D + i * 128: tj * D + (i + 1) * 128],
                            g.idb[:]), wait=[(gg, g_done[tj]), (ac, max(0, s.a.n - 1))], inc=mm)
                        mval = mm.n
                        s.op("scalar", lambda e, tj=tj, i=i, pslab=pslab: e.copy(
                            dispT[:, i * 512 + tj * 128: i * 512 + (tj + 1) * 128],
                            pslab[:, 0:128]), wait=[(mm, mval)])
                tr_done = s.a.n
                for fc in range(DFF0 // 128):
                    pslab = ps_f[fc % 2]

                    def hgrp(e, fc=fc, pslab=pslab):
                        for i in range(NDC):
                            inst = e.matmul(pslab[:],
                                            w1_sb[:, i * DFF0 + fc * 128: i * DFF0 + (fc + 1) * 128],
                                            dispT[:, i * 512:(i + 1) * 512],
                                            start=(i == 0), stop=(i == NDC - 1))
                        return inst
                    s.op("tensor", hgrp, wait=[(w1ld, w_ready), (ac, max(0, s.a.n - 1 + 0)),
                                               (ac, tr_done)], inc=mm)
                    mval = mm.n
                    s.op("scalar", lambda e, fc=fc, pslab=pslab: e.activation(
                        hT[:, fc * 512:(fc + 1) * 512], pslab[:], AF.Gelu_apprx_tanh),
                        wait=[(mm, mval)])
                h_done = s.a.n
                for tj in range(4):
                    for dc2 in range(2):
                        pslab = ps_y[(tj * 2 + dc2) % 2]

                        def ygrp(e, tj=tj, dc2=dc2, pslab=pslab):
                            for fc in range(DFF0 // 128):
                                inst = e.matmul(
                                    pslab[:],
                                    hT[:, fc * 512 + tj * 128: fc * 512 + (tj + 1) * 128],
                                    w2_sb[:, fc * D + dc2 * 512: fc * D + (dc2 + 1) * 512],
                                    start=(fc == 0), stop=(fc == DFF0 // 128 - 1))
                            return inst
                        s.op("tensor", ygrp, wait=[(ac, h_done), (dv, max(0, s.v.n))], inc=mm)
                        mval = mm.n
                        s.op("vector", lambda e, dc2=dc2, pslab=pslab: e.tensor_copy(
                            ystage[:, dc2 * 512:(dc2 + 1) * 512], pslab[:]),
                            wait=[(mm, mval), (st, st.n)])
                    yst = dv.n
                    s.op("sync", lambda e, se=se, cchunk=cchunk, tj=tj: e.dma_start(
                        yl.ap()[se * CAP0 + cchunk * 512 + tj * 128:
                                se * CAP0 + cchunk * 512 + (tj + 1) * 128, :], ystage[:]),
                        wait=[(dv, yst)], inc=st, inck=16, dma=True)
        s.op("gpsimd", lambda e: e.collective_compute(
            "AllGather", ALU.bypass, replica_groups=rg,
            ins=[yl.ap()], outs=[yall.ap()]),
            wait=[(st, st.n)], inc=cc, dma=True)
        s.op("gpsimd", lambda e: e.wait_ge(cc.sem, cc.n), dma=True)
        s.flush()

    with ExitStack() as c4:
        E4 = c4.enter_context
        ytile = [E4(nc.sbuf_tensor(f"{tag}_ytl{i}", [128, D], BF16)) for i in range(2)]
        x1t = [E4(nc.sbuf_tensor(f"{tag}_x1t{i}", [128, D], F32)) for i in range(2)]
        hb2 = [E4(nc.sbuf_tensor(f"{tag}_hb{i}", [128, D], F32)) for i in range(2)]
        ob2 = [E4(nc.sbuf_tensor(f"{tag}_ob{i}", [128, D], F32)) for i in range(2)]
        slot_i = E4(nc.sbuf_tensor(f"{tag}_sloti4", [128, 1], I32))
        tmp = {"sqs": E4(nc.sbuf_tensor(f"{tag}_sqs", [128, D], F32)),
               "s1": E4(nc.sbuf_tensor(f"{tag}_s1", [128, 1], F32)),
               "s2": E4(nc.sbuf_tensor(f"{tag}_s2", [128, 1], F32)),
               "mmean": E4(nc.sbuf_tensor(f"{tag}_mmean", [128, 1], F32)),
               "vvar": E4(nc.sbuf_tensor(f"{tag}_vvar", [128, 1], F32)),
               "rstd": E4(nc.sbuf_tensor(f"{tag}_rstd", [128, 1], F32))}
        epsb = E4(nc.sbuf_tensor(f"{tag}_epsb", [128, 1], F32))

        s = Sched(nc)
        s.bnd(NE0 * CAP0 - 1)
        dv, ac = s.v, s.a
        xld = Ctr(nc, "xld")
        gg = Ctr(nc, "gg")
        stp = [Ctr(nc, "stp0"), Ctr(nc, "stp1")]

        s.op("vector", lambda e: e.memset(epsb[:], 1e-5))
        h_user = {}
        y_prev = 0
        for tt in range(NTT):
            yb = ytile[tt % 2]
            xb = x1t[tt % 2]
            hb = hb2[tt % 2]
            ob = ob2[tt % 2]
            s.op("sync", lambda e, tt=tt, xb=xb: e.dma_start(
                xb[:], x1.ap()[tt * 128:(tt + 1) * 128, :]),
                wait=[(dv, h_user.get(tt - 2, 0))], inc=xld, inck=16, dma=True)
            xrd = xld.n
            s.op("gpsimd", lambda e, tt=tt: e.tensor_copy(slot_i[:], sp_sb[:, tt:tt + 1]),
                 wait=[(dv, h_user.get(tt - 2, 0)), (gg, y_prev)])
            cast_done = s.g.n
            s.op("vector", lambda e, yb=yb: e.memset(yb[:], 0.0),
                 wait=[(dv, h_user.get(tt - 2, 0))])
            ym = dv.n
            s.op("gpsimd", lambda e, yb=yb: e.indirect_dma_start(
                yb[:], None, yall.ap(), bass.IndirectOffsetOnAxis(slot_i[:], 0),
                bounds_check=s._breg[s.bnd(NE0 * CAP0 - 1)], oob_is_err=False),
                wait=[(dv, ym), (s.g, cast_done)], inc=gg, inck=16, dma=True)
            y_done = gg.n
            y_prev = gg.n
            s.op("vector", lambda e, tt=tt, yb=yb, xb=xb, hb=hb: e.scalar_tensor_tensor(
                hb[:], yb[:], w_sb[:, tt:tt + 1], xb[:],
                op0=ALU.mult, op1=ALU.add),
                wait=[(gg, y_done), (xld, xrd)])
            h_ready = dv.n
            s.op("vector", lambda e: e.memset(tmp["s1"][0:1, 0:1], 0.0),
                 wait=[(stp[tt % 2], stp[tt % 2].n)])
            out_done = emit_ln_tile(s, tmp, hb, ob, epsb, h_ready)
            h_user[tt] = dv.n
            s.op("sync", lambda e, tt=tt, ob=ob: e.dma_start(
                x2.ap()[tt * 128:(tt + 1) * 128, :], ob[:]),
                wait=[(dv, out_done)], inc=stp[tt % 2], inck=16, dma=True)
        s.op("sync", lambda e, a=stp[0].n: e.wait_ge(stp[0].sem, a))
        s.op("sync", lambda e, a=stp[1].n: e.wait_ge(stp[1].sem, a))
        s.flush()



NH1 = 8
DH1 = 128
E1 = 2048
LAMBDA_INIT = 0.8 - 0.6 * 0.7408182206817179


def emit_phase_c(nc, ctx, g, x2, wqT, wkT, wvT, woT, lamq1T, lamk1T, lamq2T, lamk2T,
                 rms_bc, x3, tag="c"):
    E = ctx.enter_context
    NCH = T // 512
    xbf = nc.dram_tensor(f"{tag}_xbf", [T, D], BF16, kind="Internal")
    qpark = nc.dram_tensor(f"{tag}_qpark", [E1, T], BF16, kind="Internal")
    kpark = nc.dram_tensor(f"{tag}_kpark", [E1, T], BF16, kind="Internal")
    vpark = nc.dram_tensor(f"{tag}_vpark", [T, E1], BF16, kind="Internal")
    oT = nc.dram_tensor(f"{tag}_oT", [E1, T], BF16, kind="Internal")

    nqrow = E(nc.sbuf_tensor(f"{tag}_nq", [1, 16], F32))
    nkrow = E(nc.sbuf_tensor(f"{tag}_nk", [1, 16], F32))
    lamrow = E(nc.sbuf_tensor(f"{tag}_lam", [1, NH1], F32))
    nbrow = E(nc.sbuf_tensor(f"{tag}_nb", [1, NH1], F32))
    lamcs = E(nc.sbuf_tensor(f"{tag}_lamcs", [128, NH1], F32))
    nbcs = E(nc.sbuf_tensor(f"{tag}_nbcs", [128, NH1], F32))

    s = Sched(nc)
    ld = Ctr(nc, "ld")
    s.op("gpsimd", lambda e: e.dma_start(xbf.ap(), x2.ap()), inc=ld, inck=16, dma=True)
    s.op("gpsimd", lambda e: e.wait_ge(ld.sem, ld.n), dma=True)
    s.flush()

    with ExitStack() as c1:
        E1s = c1.enter_context
        wq_sb = E1s(nc.sbuf_tensor(f"{tag}_wq", [128, NDC * E1], BF16))
        wk_sb = E1s(nc.sbuf_tensor(f"{tag}_wk", [128, NDC * E1], BF16))
        wv_sb = E1s(nc.sbuf_tensor(f"{tag}_wv", [128, NDC * E1], BF16))
        xTc = [E1s(nc.sbuf_tensor(f"{tag}_xTc{i}", [128, NDC * 512], BF16)) for i in range(2)]
        qstage = [E1s(nc.sbuf_tensor(f"{tag}_qs{i}", [128, 512], BF16)) for i in range(4)]
        sq = E1s(nc.sbuf_tensor(f"{tag}_sq", [128, 512], F32))
        vstage = [E1s(nc.sbuf_tensor(f"{tag}_vs{i}", [128, 512], BF16)) for i in range(2)]
        mrow = E1s(nc.sbuf_tensor(f"{tag}_mrow", [1, 1], F32))
        ps_p = [E1s(nc.psum_tensor(f"{tag}_pp{i}", [128, 512], F32)) for i in range(2)]
        ps_v = [E1s(nc.psum_tensor(f"{tag}_pv{i}", [128, 512], F32)) for i in range(2)]
        ps_n = E1s(nc.psum_tensor(f"{tag}_pn", [1, 512], F32))

        s = Sched(nc)
        dv, ac = s.v, s.a
        wld = Ctr(nc, "wld")
        xldp = [Ctr(nc, "xld0"), Ctr(nc, "xld1")]
        mm = Ctr(nc, "mm")
        nmm = Ctr(nc, "nmm")
        stq = [Ctr(nc, "stq0"), Ctr(nc, "stq1"), Ctr(nc, "stq2"), Ctr(nc, "stq3")]
        stv = [Ctr(nc, "stv0"), Ctr(nc, "stv1")]

        s.op("vector", lambda e: e.memset(nqrow[:], 0.0))
        s.op("vector", lambda e: e.memset(nkrow[:], 0.0))
        for i in range(NDC):
            s.op("gpsimd", lambda e, i=i: e.dma_start(
                wq_sb[:, i * E1:(i + 1) * E1], wqT.ap()[i * 128:(i + 1) * 128, :]),
                inc=wld, inck=16, dma=True)
            s.op("gpsimd", lambda e, i=i: e.dma_start(
                wk_sb[:, i * E1:(i + 1) * E1], wkT.ap()[i * 128:(i + 1) * 128, :]),
                inc=wld, inck=16, dma=True)
            s.op("gpsimd", lambda e, i=i: e.dma_start(
                wv_sb[:, i * E1:(i + 1) * E1], wvT.ap()[i * 128:(i + 1) * 128, :]),
                inc=wld, inck=16, dma=True)
        wready = wld.n

        mm_after_tc = {}
        evac_qk = {}
        vevac = {}
        for tc in range(NCH):
            xb = xTc[tc % 2]
            xld = xldp[tc % 2]
            w = [(mm, mm_after_tc.get(tc - 2, 0))]
            for i in range(NDC):
                s.op("sync", lambda e, i=i, xb=xb, tc=tc: e.dma_start(
                    xb[:, i * 512:(i + 1) * 512],
                    xbf.ap()[tc * 512:(tc + 1) * 512, i * 128:(i + 1) * 128],
                    transpose=True), wait=w, inc=xld, inck=16, dma=True)
            xready = xld.n
            for et in range(E1 // 128):
                for which, wsb, park, nrow in [(0, wq_sb, qpark, nqrow), (1, wk_sb, kpark, nkrow)]:
                    pp = ps_p[which]

                    def pgrp(e, et=et, xb=xb, wsb=wsb, pp=pp):
                        for i in range(NDC):
                            inst = e.matmul(pp[:],
                                            wsb[:, i * E1 + et * 128: i * E1 + (et + 1) * 128],
                                            xb[:, i * 512:(i + 1) * 512],
                                            start=(i == 0), stop=(i == NDC - 1))
                        return inst
                    qs = qstage[which * 2 + (et % 2)]
                    s.op("tensor", pgrp, wait=[(wld, wready), (xld, xready),
                                               (ac, evac_qk.get(which, 0))], inc=mm)
                    pd = mm.n
                    s.op("scalar", lambda e, qs=qs, pp=pp: e.copy(qs[:], pp[:]),
                         wait=[(mm, pd), (stq[which * 2 + (et % 2)],
                                          stq[which * 2 + (et % 2)].n)])
                    evac_qk[which] = ac.n
                    evd = ac.n
                    s.op("scalar", lambda e, qs=qs: e.activation(sq[:], qs[:], AF.Square),
                         wait=[(nmm, nmm.n)])
                    sqd = ac.n
                    s.op("sync", lambda e, et=et, tc=tc, qs=qs, park=park: e.dma_start(
                        park.ap()[et * 128:(et + 1) * 128, tc * 512:(tc + 1) * 512], qs[:]),
                        wait=[(ac, sqd)], inc=stq[which * 2 + (et % 2)], inck=16, dma=True)
                    s.op("tensor", lambda e: e.matmul(ps_n[:], g.ones_col_f[:], sq[:],
                                                      start=True, stop=True),
                         wait=[(ac, sqd), (dv, dv.n)], inc=nmm)
                    nd = nmm.n
                    s.op("vector", lambda e: e.reduce_max(mrow[:], ps_n[0:1, :], axis=AX.X),
                         wait=[(nmm, nd)])
                    s.op("vector", lambda e, et=et, nrow=nrow: e.tensor_tensor(
                        nrow[:, et:et + 1], nrow[:, et:et + 1], mrow[:], ALU.max))
            for tj in range(4):
                tt = tc * 4 + tj
                for vc in range(E1 // 512):
                    pv = ps_v[(tj * 4 + vc) % 2]

                    def vgrp(e, tj=tj, vc=vc, xb=xb, pv=pv):
                        for i in range(NDC):
                            inst = e.matmul(pv[:],
                                            xb[:, i * 512 + tj * 128: i * 512 + (tj + 1) * 128],
                                            wv_sb[:, i * E1 + vc * 512: i * E1 + (vc + 1) * 512],
                                            start=(i == 0), stop=(i == NDC - 1))
                        return inst
                    vs = vstage[(tj * 4 + vc) % 2]
                    s.op("tensor", vgrp, wait=[(wld, wready), (xld, xready),
                                               (ac, vevac.get((tj * 4 + vc) % 2, 0))], inc=mm)
                    mval = mm.n
                    s.op("scalar", lambda e, vs=vs, pv=pv: e.copy(vs[:], pv[:]),
                         wait=[(mm, mval), (stv[(tj * 4 + vc) % 2], stv[(tj * 4 + vc) % 2].n)])
                    vevac[(tj * 4 + vc) % 2] = ac.n
                    vd = ac.n
                    s.op("sync", lambda e, tt=tt, vc=vc, vs=vs: e.dma_start(
                        vpark.ap()[tt * 128:(tt + 1) * 128, vc * 512:(vc + 1) * 512], vs[:]),
                        wait=[(ac, vd)], inc=stv[(tj * 4 + vc) % 2], inck=16, dma=True)
            mm_after_tc[tc] = mm.n
        for c in stq + stv:
            s.op("sync", lambda e, c=c: e.wait_ge(c.sem, c.n))
        s.flush()

    with ExitStack() as c2:
        E2 = c2.enter_context
        lt = {k: E2(nc.sbuf_tensor(f"{tag}_l{k}", [128, NH1], F32)) for k in "abcd"}
        prod = E2(nc.sbuf_tensor(f"{tag}_prod", [128, NH1], F32))
        l1r = E2(nc.sbuf_tensor(f"{tag}_l1r", [1, NH1], F32))
        l2r = E2(nc.sbuf_tensor(f"{tag}_l2r", [1, NH1], F32))
        nqh = E2(nc.sbuf_tensor(f"{tag}_nqh", [1, NH1], F32))
        nkh = E2(nc.sbuf_tensor(f"{tag}_nkh", [1, NH1], F32))
        ps_l = E2(nc.psum_tensor(f"{tag}_pl", [1, 512], F32))

        s = Sched(nc)
        dv, ac = s.v, s.a
        ld = Ctr(nc, "ld")
        mm = Ctr(nc, "mm")
        for k, t in [("a", lamq1T), ("b", lamk1T), ("c", lamq2T), ("d", lamk2T)]:
            s.op("sync", lambda e, k=k, t=t: e.dma_start(lt[k][:], t.ap()),
                 inc=ld, inck=16, dma=True)
        ldd = ld.n
        for (ka, kb, out) in [("a", "b", l1r), ("c", "d", l2r)]:
            s.op("vector", lambda e, ka=ka, kb=kb: e.tensor_tensor(
                prod[:], lt[ka][:], lt[kb][:], ALU.mult),
                wait=[(ld, ldd), (mm, mm.n)])
            pd = dv.n
            s.op("tensor", lambda e: e.matmul(ps_l[0:1, 0:NH1], g.ones_col_f[:], prod[:],
                                              start=True, stop=True),
                 wait=[(dv, pd), (ac, ac.n)], inc=mm)
            md = mm.n
            s.op("scalar", lambda e, out=out: e.activation(out[:], ps_l[0:1, 0:NH1], AF.Exp),
                 wait=[(mm, md)])
        exps = ac.n
        s.op("vector", lambda e: e.tensor_tensor(lamrow[:], l1r[:], l2r[:], ALU.subtract),
             wait=[(ac, exps)])
        s.op("vector", lambda e: e.tensor_scalar(lamrow[:], lamrow[:], LAMBDA_INIT, None,
                                                 op0=ALU.add))
        s.op("vector", lambda e: e.tensor_tensor(
            nqh[:], nqrow[0:1, 0:16:2], nqrow[0:1, 1:16:2], ALU.max))
        s.op("vector", lambda e: e.tensor_tensor(
            nkh[:], nkrow[0:1, 0:16:2], nkrow[0:1, 1:16:2], ALU.max))
        s.op("vector", lambda e: e.tensor_tensor(nbrow[:], nqh[:], nkh[:], ALU.mult))
        s.op("scalar", lambda e: e.activation(nbrow[:], nbrow[:], AF.Sqrt),
             wait=[(dv, dv.n)])
        s.op("vector", lambda e: e.tensor_scalar(
            nbrow[:], nbrow[:], -1.0 / 11.313708498984761, None, op0=ALU.mult),
            wait=[(ac, ac.n)])
        ps_bc = E2(nc.psum_tensor(f"{tag}_pbc", [128, 2 * NH1], F32))
        s.op("tensor", lambda e: e.matmul(ps_bc[:, 0:NH1], g.ones_row_f[:],
                                          lamrow[0:1, :], start=True, stop=True),
             wait=[(dv, dv.n), (ac, ac.n)], inc=mm)
        s.op("tensor", lambda e: e.matmul(ps_bc[:, NH1:2 * NH1], g.ones_row_f[:],
                                          nbrow[0:1, :], start=True, stop=True), inc=mm)
        bcd = mm.n
        s.op("vector", lambda e: e.tensor_copy(lamcs[:], ps_bc[:, 0:NH1]),
             wait=[(mm, bcd)])
        s.op("vector", lambda e: e.tensor_copy(nbcs[:], ps_bc[:, NH1:2 * NH1]))
        s.flush()

    with ExitStack() as c3:
        E3 = c3.enter_context
        Qh = E3(nc.sbuf_tensor(f"{tag}_Qh", [128, 2 * T], BF16))
        Kh = E3(nc.sbuf_tensor(f"{tag}_Kh", [128, 2 * T], BF16))
        Vh = E3(nc.sbuf_tensor(f"{tag}_Vh", [128, NT * 257], BF16))
        PTD = [E3(nc.sbuf_tensor(f"{tag}_PT{i}", [128, NT * 1024], BF16)) for i in range(2)]
        rz1 = E3(nc.sbuf_tensor(f"{tag}_rz1", [128, 1], F32))
        rz2 = E3(nc.sbuf_tensor(f"{tag}_rz2", [128, 1], F32))
        oc4 = E3(nc.sbuf_tensor(f"{tag}_oc4", [128, 1024], F32))
        oc2 = E3(nc.sbuf_tensor(f"{tag}_oc2", [128, 256], F32))
        osq = E3(nc.sbuf_tensor(f"{tag}_osq", [128, 256], F32))
        ms4 = E3(nc.sbuf_tensor(f"{tag}_ms4", [128, 4], F32))
        rstd4 = E3(nc.sbuf_tensor(f"{tag}_rstd4", [128, 4], F32))
        epsb = E3(nc.sbuf_tensor(f"{tag}_epsb3", [128, 1], F32))
        rmsb = E3(nc.sbuf_tensor(f"{tag}_rmsb", [128, 256], F32))
        obf4 = E3(nc.sbuf_tensor(f"{tag}_obf4", [128, 1024], BF16))
        ots = E3(nc.sbuf_tensor(f"{tag}_ots", [128, 256], BF16))
        ps_s = [E3(nc.psum_tensor(f"{tag}_ps{i}", [128, 512], F32)) for i in range(3)]
        ps_o = [E3(nc.psum_tensor(f"{tag}_po{i}", [128, 1024], F32)) for i in range(2)]
        ps_tr = E3(nc.psum_tensor(f"{tag}_ptr", [128, 256], BF16))

        s = Sched(nc)
        dv, ac = s.v, s.a
        qkl = [Ctr(nc, "qkl0"), Ctr(nc, "qkl1")]
        vll = [Ctr(nc, "vll0"), Ctr(nc, "vll1")]
        mm = Ctr(nc, "mm")
        pvm = Ctr(nc, "pvm")
        trm = Ctr(nc, "trm")
        sto = Ctr(nc, "sto")
        rmc = Ctr(nc, "rmc")

        s.op("vector", lambda e: e.memset(epsb[:], 1e-5))
        s.op("sync", lambda e: e.dma_start(rmsb[:], rms_bc.ap()), inc=rmc, inck=16, dma=True)
        s.op("vector", lambda e: e.memset(
            Vh.ap().rearrange("p (t c) -> p t c", c=257)[:, :, 256:257], 1.0))
        ones_done = dv.n

        NQC = T // 512
        G = NH1 * NQC
        mm_free = {0: 0}
        qk_ready = {}
        vready = {}
        chunk_exp = {}
        slab_free = {0: 0, 1: 0, 2: 0}
        ot_user = {0: ones_done, 1: ones_done}
        tr_user = {0: 0, 1: 0}
        tr_evac = 0
        st_prev = 0
        pending = None

        def load_head(h):
            qk = qkl[h % 2]
            vl = vll[h % 2]
            pvf = pvm.n + (8 if h > 0 else 0)
            for half in range(2):
                et = 2 * h + half
                s.op("sync", lambda e, half=half, et=et: e.dma_start(
                    Qh[:, half * T:(half + 1) * T], qpark.ap()[et * 128:(et + 1) * 128, :]),
                    wait=[(mm, mm_free[h])], inc=qk, inck=16, dma=True)
                s.op("sync", lambda e, half=half, et=et: e.dma_start(
                    Kh[:, half * T:(half + 1) * T], kpark.ap()[et * 128:(et + 1) * 128, :]),
                    inc=qk, inck=16, dma=True)
            qk_ready[h] = qk.n
            for kt in range(NT):
                s.op("sync", lambda e, kt=kt, h=h: e.dma_start(
                    Vh[:, kt * 257: kt * 257 + 256],
                    vpark.ap()[kt * 128:(kt + 1) * 128, h * 256:(h + 1) * 256]),
                    wait=([(pvm, pvf)] if kt == 0 else []), inc=vl, inck=16, dma=True)
            vready[h] = vl.n

        def make_pv_emitters(g2):
            out = []
            h2, qc2 = divmod(g2, NQC)
            for jj in range(8):
                q, wi = divmod(jj, 2)
                gq = g2 * 4 + q
                PT = PTD[g2 % 2]
                po = ps_o[gq % 2]
                col = wi * 512
                for kt2 in range(NT):
                    def one(e, kt2=kt2, q=q, wi=wi, PT=PT, po=po, col=col):
                        return e.matmul(
                            po[:, col:col + 257],
                            PT[:, kt2 * 1024 + wi * 512 + q * 128:
                               kt2 * 1024 + wi * 512 + (q + 1) * 128],
                            Vh[:, kt2 * 257:(kt2 + 1) * 257],
                            start=(kt2 == 0), stop=(kt2 == NT - 1))
                    out.append((one, kt2 == 0, kt2 == NT - 1, g2, gq, h2, jj))
            return out

        def emit_pv_item(item):
            one, first, last, g2, gq, h2, jj = item
            w = []
            if first:
                w = [(ac, chunk_exp[g2]), (dv, ot_user[gq % 2]),
                     (vll[h2 % 2], vready[h2])]
            s.op("tensor", one, wait=w, inc=(pvm if last else None))
            if last and jj % 2 == 1:
                emit_combine(g2, jj // 2)
                if jj == 7:
                    emit_rms(g2)

        def emit_combine(g2, q):
            h2, qc2 = divmod(g2, NQC)
            gq = g2 * 4 + q
            po = ps_o[gq % 2]
            pv_done = pvm.n
            s.op("vector", lambda e, po=po: e.reciprocal(rz1[:], po[:, 256:257]),
                 wait=[(pvm, pv_done)])
            s.op("vector", lambda e, po=po: e.reciprocal(rz2[:], po[:, 768:769]))
            s.op("vector", lambda e, po=po, q=q: e.tensor_scalar(
                oc4[:, q * 256:(q + 1) * 256], po[:, 0:256], rz1[:, 0:1], None,
                op0=ALU.mult))
            s.op("vector", lambda e, h2=h2: e.tensor_tensor(
                rz2[:], rz2[:], lamcs[:, h2:h2 + 1], ALU.mult))
            s.op("vector", lambda e, po=po: e.tensor_scalar(
                oc2[:], po[:, 512:768], rz2[:, 0:1], None, op0=ALU.mult))
            s.op("vector", lambda e, q=q: e.tensor_tensor(
                oc4[:, q * 256:(q + 1) * 256], oc4[:, q * 256:(q + 1) * 256],
                oc2[:], ALU.subtract))
            ot_user[gq % 2] = dv.n

        def emit_rms(g2):
            nonlocal pending
            h2, qc2 = divmod(g2, NQC)
            for q in range(4):
                s.op("vector", lambda e, q=q: e.tensor_tensor(
                    osq[:], oc4[:, q * 256:(q + 1) * 256],
                    oc4[:, q * 256:(q + 1) * 256], ALU.mult))
                s.op("vector", lambda e, q=q: e.reduce_sum(
                    ms4[:, q:q + 1], osq[:], axis=AX.X))
            s.op("vector", lambda e: e.tensor_scalar(ms4[:], ms4[:], 1.0 / 256.0, None,
                                                     op0=ALU.mult))
            s.op("scalar", lambda e: e.activation(ms4[:], ms4[:], AF.Sqrt,
                                                  bias=epsb[:, 0:1]),
                 wait=[(dv, dv.n)])
            s.op("vector", lambda e: e.reciprocal(rstd4[:], ms4[:]), wait=[(ac, ac.n)])
            for q in range(4):
                s.op("vector", lambda e, q=q: e.tensor_scalar(
                    oc4[:, q * 256:(q + 1) * 256], oc4[:, q * 256:(q + 1) * 256],
                    rstd4[:, q:q + 1], 1.0 - LAMBDA_INIT, op0=ALU.mult, op1=ALU.mult))
            for q in range(4):
                s.op("vector", lambda e, q=q: e.tensor_tensor(
                    obf4[:, q * 256:(q + 1) * 256], oc4[:, q * 256:(q + 1) * 256],
                    rmsb[:], ALU.mult),
                    wait=[(trm, tr_user[0]), (rmc, 16)])
            pending = (h2, qc2, dv.n)

        def flush_transpose():
            nonlocal tr_evac, st_prev, pending
            if pending is None:
                return
            h2, qc2, obd = pending
            pending = None
            for q in range(4):
                qt = qc2 * 4 + q
                for half in range(2):
                    s.op("tensor", lambda e, q=q, half=half: e.transpose(
                        ps_tr[:, 0:128] if half == 0 else ps_tr[:, 128:256],
                        obf4[:, q * 256 + half * 128: q * 256 + (half + 1) * 128],
                        g.idb[:]),
                        wait=[(dv, obd), (ac, tr_evac)], inc=trm)
                    trd = trm.n
                    s.op("scalar", lambda e, half=half: e.copy(
                        ots[:, half * 128:(half + 1) * 128],
                        ps_tr[:, 0:128] if half == 0 else ps_tr[:, 128:256]),
                        wait=[(trm, trd), (sto, st_prev)])
                tr_evac = ac.n
                for half in range(2):
                    s.op("gpsimd", lambda e, h2=h2, qt=qt, half=half: e.dma_start(
                        oT.ap()[(2 * h2 + half) * 128:(2 * h2 + half + 1) * 128,
                                qt * 128:(qt + 1) * 128],
                        ots[:, half * 128:(half + 1) * 128]),
                        wait=[(ac, tr_evac)], inc=sto, inck=16, dma=True)
                st_prev = sto.n
            tr_user[0] = trm.n

        pvq = []
        for gg in range(G):
            h, qc = divmod(gg, NQC)
            if qc == 0:
                if h > 0:
                    mm_free[h] = mm.n
                load_head(h)
            PT = PTD[gg % 2]
            pt_free = pvm.n
            if gg > 0:
                assert not pvq
                pvq = make_pv_emitters(gg - 1)
            for kt in range(NT):
                for wi in range(2):
                    j = (2 * kt + wi) % 3
                    def smm(e, kt=kt, qc=qc, wi=wi, j=j):
                        return e.matmul(ps_s[j][:],
                                        Kh[:, wi * T + kt * 128: wi * T + (kt + 1) * 128],
                                        Qh[:, wi * T + qc * 512: wi * T + (qc + 1) * 512],
                                        start=True, stop=True)
                    s.op("tensor", smm,
                         wait=[(qkl[h % 2], qk_ready[h]), (ac, slab_free[j])], inc=mm)
                    sd = mm.n
                    s.op("scalar", lambda e, kt=kt, wi=wi, j=j, PT=PT, h=h: e.activation(
                        PT[:, kt * 1024 + wi * 512: kt * 1024 + (wi + 1) * 512],
                        ps_s[j][:], AF.Exp,
                        bias=nbcs[:, h:h + 1], scale=1.0 / 11.313708498984761),
                        wait=[(mm, sd), (pvm, pt_free)])
                    slab_free[j] = ac.n
                for _ in range(8):
                    if pvq:
                        emit_pv_item(pvq.pop(0))
                if kt == NT // 2 - 1:
                    flush_transpose()
            chunk_exp[gg] = ac.n
        flush_transpose()
        for item in make_pv_emitters(G - 1):
            emit_pv_item(item)
        flush_transpose()
        s.op("sync", lambda e, a=sto.n: e.wait_ge(sto.sem, a))
        s.flush()

    with ExitStack() as c4:
        E4 = c4.enter_context
        wo_sb = E4(nc.sbuf_tensor(f"{tag}_wo", [128, (E1 // 128) * D], BF16))
        otc = [E4(nc.sbuf_tensor(f"{tag}_otc{i}", [128, (E1 // 128) * 128], BF16))
               for i in range(2)]
        x2t = [E4(nc.sbuf_tensor(f"{tag}_x2t{i}", [128, D], F32)) for i in range(2)]
        hb2 = [E4(nc.sbuf_tensor(f"{tag}_hb{i}", [128, D], F32)) for i in range(2)]
        ob2 = [E4(nc.sbuf_tensor(f"{tag}_ob{i}", [128, D], F32)) for i in range(2)]
        tmp = {"sqs": E4(nc.sbuf_tensor(f"{tag}_sqs", [128, D], F32)),
               "s1": E4(nc.sbuf_tensor(f"{tag}_s1", [128, 1], F32)),
               "s2": E4(nc.sbuf_tensor(f"{tag}_s2", [128, 1], F32)),
               "mmean": E4(nc.sbuf_tensor(f"{tag}_mmean", [128, 1], F32)),
               "vvar": E4(nc.sbuf_tensor(f"{tag}_vvar", [128, 1], F32)),
               "rstd": E4(nc.sbuf_tensor(f"{tag}_rstd", [128, 1], F32))}
        epsb = E4(nc.sbuf_tensor(f"{tag}_epsb4", [128, 1], F32))
        ps_o4 = [E4(nc.psum_tensor(f"{tag}_po4{i}", [128, 512], F32)) for i in range(4)]

        s = Sched(nc)
        dv, ac = s.v, s.a
        wld = Ctr(nc, "wld")
        oldp = [Ctr(nc, "old0"), Ctr(nc, "old1")]
        xldr = Ctr(nc, "xldr")
        mm = Ctr(nc, "mm")
        stp = [Ctr(nc, "stp0"), Ctr(nc, "stp1")]

        s.op("vector", lambda e: e.memset(epsb[:], 1e-5))
        for i in range(E1 // 128):
            s.op("gpsimd", lambda e, i=i: e.dma_start(
                wo_sb[:, i * D:(i + 1) * D], woT.ap()[i * 128:(i + 1) * 128, :]),
                inc=wld, inck=16, dma=True)
        wready = wld.n

        ot_done = {}
        for i in range(E1 // 128):
            s.op("sync", lambda e, i=i: e.dma_start(
                otc[0][:, i * 128:(i + 1) * 128],
                oT.ap()[i * 128:(i + 1) * 128, 0:128]),
                inc=oldp[0], inck=16, dma=True)
        ot_done[0] = oldp[0].n

        mm_after_tt = {}
        x2_user = {}
        ps_user = {}
        for tt in range(NT):
            cb = otc[tt % 2]
            xb = x2t[tt % 2]
            hb = hb2[tt % 2]
            ob = ob2[tt % 2]
            s.op("sync", lambda e, tt=tt, xb=xb: e.dma_start(
                xb[:], x2.ap()[tt * 128:(tt + 1) * 128, :]),
                wait=[(dv, x2_user.get(tt - 2, 0))], inc=xldr, inck=16, dma=True)
            xrd = xldr.n
            for dc in range(2):
                pslab = (tt % 2) * 2 + dc
                pp = ps_o4[pslab]

                def ogrp(e, dc=dc, cb=cb, pp=pp):
                    for i in range(E1 // 128):
                        inst = e.matmul(pp[:],
                                        cb[:, i * 128:(i + 1) * 128],
                                        wo_sb[:, i * D + dc * 512: i * D + (dc + 1) * 512],
                                        start=(i == 0), stop=(i == E1 // 128 - 1))
                    return inst
                s.op("tensor", ogrp, wait=[(wld, wready), (oldp[tt % 2], ot_done[tt]),
                                           (dv, ps_user.get(pslab, 0))], inc=mm)
                mval = mm.n
                s.op("vector", lambda e, dc=dc, xb=xb, hb=hb, pp=pp: e.tensor_tensor(
                    hb[:, dc * 512:(dc + 1) * 512], xb[:, dc * 512:(dc + 1) * 512],
                    pp[:], ALU.add),
                    wait=[(mm, mval), (xldr, xrd), (stp[tt % 2], stp[tt % 2].n)])
                ps_user[pslab] = dv.n
            x2_user[tt] = dv.n
            h_ready = dv.n
            mm_after_tt[tt] = mm.n
            if tt + 1 < NT:
                for i in range(E1 // 128):
                    s.op("sync", lambda e, i=i, q=tt + 1: e.dma_start(
                        otc[q % 2][:, i * 128:(i + 1) * 128],
                        oT.ap()[i * 128:(i + 1) * 128, q * 128:(q + 1) * 128]),
                        wait=[(mm, mm_after_tt.get(tt - 1, 0))],
                        inc=oldp[(tt + 1) % 2], inck=16, dma=True)
                ot_done[tt + 1] = oldp[(tt + 1) % 2].n
            out_done = emit_ln_tile(s, tmp, hb, ob, epsb, h_ready)
            s.op("sync", lambda e, tt=tt, ob=ob: e.dma_start(
                x3.ap()[tt * 128:(tt + 1) * 128, :], ob[:]),
                wait=[(dv, out_done)], inc=stp[tt % 2], inck=16, dma=True)
        s.op("sync", lambda e, a=stp[0].n: e.wait_ge(stp[0].sem, a))
        s.op("sync", lambda e, a=stp[1].n: e.wait_ge(stp[1].sem, a))
        s.flush()



HQ2, HKV2, DK2 = 16, 4, 64


def emit_phase_d_attn(nc, ctx, g, x3, wq, wk, wv, wo, g1bc, g2bc, cosq, sinq,
                      h_out, xm_out, xmbf, tag="d", dbg=None):
    E = ctx.enter_context
    xnbf = nc.dram_tensor(f"{tag}_xnbf", [T, D], BF16, kind="Internal")
    qpark = nc.dram_tensor(f"{tag}_qpark", [HQ2 * DK2, T], BF16, kind="Internal")
    kpark = nc.dram_tensor(f"{tag}_kpark", [HKV2 * DK2, T], BF16, kind="Internal")
    vpark = nc.dram_tensor(f"{tag}_vpark", [T, HKV2 * DK2], BF16, kind="Internal")
    oT = nc.dram_tensor(f"{tag}_oT", [HQ2 * DK2, T], BF16, kind="Internal")

    nqc = E(nc.sbuf_tensor(f"{tag}_nqc", [128, 1], F32))
    nkc = E(nc.sbuf_tensor(f"{tag}_nkc", [128, 1], F32))
    nbc = E(nc.sbuf_tensor(f"{tag}_nbc", [128, 1], F32))

    with ExitStack() as c0:
        E0 = c0.enter_context
        x3t = [E0(nc.sbuf_tensor(f"{tag}_x3t{i}", [128, D], F32)) for i in range(2)]
        xnt = [E0(nc.sbuf_tensor(f"{tag}_xnt{i}", [128, D], BF16)) for i in range(2)]
        g1t = E0(nc.sbuf_tensor(f"{tag}_g1t", [128, D], F32))
        sqs = E0(nc.sbuf_tensor(f"{tag}_sqs0", [128, D], F32))
        ms = E0(nc.sbuf_tensor(f"{tag}_ms0", [128, 1], F32))
        rstd = E0(nc.sbuf_tensor(f"{tag}_rstd0", [128, 1], F32))
        epsb = E0(nc.sbuf_tensor(f"{tag}_epsb0", [128, 1], F32))
        tmpf = E0(nc.sbuf_tensor(f"{tag}_tmpf0", [128, D], F32))

        s = Sched(nc)
        dv, ac = s.v, s.a
        ld = Ctr(nc, "ld")
        xldp = [Ctr(nc, "xl0"), Ctr(nc, "xl1")]
        stp = [Ctr(nc, "st0"), Ctr(nc, "st1")]
        s.op("vector", lambda e: e.memset(epsb[:], 1e-6))
        s.op("sync", lambda e: e.dma_start(g1t[:], g1bc.ap()), inc=ld, inck=16, dma=True)
        g1d = ld.n
        xn_user = {}
        for tt in range(NT):
            xb = x3t[tt % 2]
            nb = xnt[tt % 2]
            s.op("sync", lambda e, tt=tt, xb=xb: e.dma_start(
                xb[:], x3.ap()[tt * 128:(tt + 1) * 128, :]),
                wait=[(dv, xn_user.get(tt - 2, 0))], inc=xldp[tt % 2], inck=16, dma=True)
            xrd = xldp[tt % 2].n
            s.op("scalar", lambda e, xb=xb: e.activation(sqs[:], xb[:], AF.Square),
                 wait=[(xldp[tt % 2], xrd), (dv, dv.n)])
            s.op("vector", lambda e: e.reduce_sum(ms[:], sqs[:], axis=AX.X),
                 wait=[(ac, ac.n)])
            s.op("vector", lambda e: e.tensor_scalar(ms[:], ms[:], 1.0 / D, None, op0=ALU.mult))
            s.op("scalar", lambda e: e.activation(ms[:], ms[:], AF.Sqrt, bias=epsb[:, 0:1]),
                 wait=[(dv, dv.n)])
            s.op("vector", lambda e: e.reciprocal(rstd[:], ms[:]), wait=[(ac, ac.n)])
            s.op("vector", lambda e, xb=xb: e.tensor_scalar(
                tmpf[:], xb[:], rstd[:, 0:1], None, op0=ALU.mult), wait=[(ld, g1d)])
            s.op("vector", lambda e, nb=nb: e.tensor_tensor(nb[:], tmpf[:], g1t[:], ALU.mult),
                 wait=[(stp[tt % 2], stp[tt % 2].n)])
            xn_user[tt] = dv.n
            s.op("sync", lambda e, tt=tt, nb=nb: e.dma_start(
                xnbf.ap()[tt * 128:(tt + 1) * 128, :], nb[:]),
                wait=[(dv, dv.n)], inc=stp[tt % 2], inck=16, dma=True)
        s.op("sync", lambda e, a=stp[0].n: e.wait_ge(stp[0].sem, a))
        s.op("sync", lambda e, a=stp[1].n: e.wait_ge(stp[1].sem, a))
        s.flush()

    with ExitStack() as c1:
        E1s = c1.enter_context
        wq_sb = E1s(nc.sbuf_tensor(f"{tag}_wq", [128, NDC * 1024], BF16))
        wk_sb = E1s(nc.sbuf_tensor(f"{tag}_wk", [128, NDC * 256], BF16))
        wv_sb = E1s(nc.sbuf_tensor(f"{tag}_wv", [128, NDC * 256], BF16))
        xTc = [E1s(nc.sbuf_tensor(f"{tag}_xTc{i}", [128, NDC * 128], BF16)) for i in range(2)]
        cost = [E1s(nc.sbuf_tensor(f"{tag}_cost{i}", [128, 1024], F32)) for i in range(2)]
        sint = [E1s(nc.sbuf_tensor(f"{tag}_sint{i}", [128, 1024], F32)) for i in range(2)]
        qsw = E1s(nc.sbuf_tensor(f"{tag}_qsw", [128, 1024], F32))
        qrot = E1s(nc.sbuf_tensor(f"{tag}_qrot", [128, 1024], F32))
        qbf = E1s(nc.sbuf_tensor(f"{tag}_qbf", [128, 1024], BF16))
        ksw = E1s(nc.sbuf_tensor(f"{tag}_ksw", [128, 256], F32))
        krot = E1s(nc.sbuf_tensor(f"{tag}_krot", [128, 256], F32))
        kbf = E1s(nc.sbuf_tensor(f"{tag}_kbf", [128, 256], BF16))
        vbf = E1s(nc.sbuf_tensor(f"{tag}_vbf", [128, 256], BF16))
        sq = E1s(nc.sbuf_tensor(f"{tag}_sq1", [128, 1024], F32))
        nrm = E1s(nc.sbuf_tensor(f"{tag}_nrm1", [128, 1], F32))
        tstage = E1s(nc.sbuf_tensor(f"{tag}_tst", [128, 1024 + 256], BF16))
        ps_q = [E1s(nc.psum_tensor(f"{tag}_pq{i}", [128, 512], F32)) for i in range(2)]
        ps_kv = E1s(nc.psum_tensor(f"{tag}_pkv", [128, 512], F32))
        ps_tr = [E1s(nc.psum_tensor(f"{tag}_ptr{i}", [128, 512], BF16)) for i in range(2)]

        s = Sched(nc)
        dv, ac = s.v, s.a
        wld = Ctr(nc, "wld")
        xldp = [Ctr(nc, "xl0"), Ctr(nc, "xl1")]
        mm = Ctr(nc, "mm")
        trm = Ctr(nc, "trm")
        stq = Ctr(nc, "stq")
        stv = Ctr(nc, "stv")

        s.op("vector", lambda e: e.memset(nqc[:], 0.0))
        s.op("vector", lambda e: e.memset(nkc[:], 0.0))
        for i in range(NDC):
            s.op("gpsimd", lambda e, i=i: e.dma_start(
                wq_sb[:, i * 1024:(i + 1) * 1024], wq.ap()[i * 128:(i + 1) * 128, :]),
                inc=wld, inck=16, dma=True)
            s.op("gpsimd", lambda e, i=i: e.dma_start(
                wk_sb[:, i * 256:(i + 1) * 256], wk.ap()[i * 128:(i + 1) * 128, :]),
                inc=wld, inck=16, dma=True)
            s.op("gpsimd", lambda e, i=i: e.dma_start(
                wv_sb[:, i * 256:(i + 1) * 256], wv.ap()[i * 128:(i + 1) * 128, :]),
                inc=wld, inck=16, dma=True)
        wready = wld.n

        def hv(t, a, b):
            return t.ap().rearrange("p (h c) -> p h c", c=64)[:, :, a:b]

        mm_after = {}
        rotq_user = 0
        rotk_user = 0
        v_user = 0
        tr_done = {}
        for tt in range(NT):
            xb = xTc[tt % 2]
            xld = xldp[tt % 2]
            w = [(mm, mm_after.get(tt - 2, 0))]
            for i in range(NDC):
                s.op("sync", lambda e, i=i, tt=tt, xb=xb: e.dma_start(
                    xb[:, i * 128:(i + 1) * 128],
                    xnbf.ap()[tt * 128:(tt + 1) * 128, i * 128:(i + 1) * 128],
                    transpose=True), wait=w, inc=xld, inck=16, dma=True)
            s.op("sync", lambda e, tt=tt: e.dma_start(
                cost[tt % 2][:], cosq.ap()[tt * 128:(tt + 1) * 128, :]),
                wait=[(dv, rotq_user)], inc=xld, inck=16, dma=True)
            s.op("sync", lambda e, tt=tt: e.dma_start(
                sint[tt % 2][:], sinq.ap()[tt * 128:(tt + 1) * 128, :]),
                inc=xld, inck=16, dma=True)
            xready = xld.n

            for dc in range(2):
                def qgrp(e, dc=dc, xb=xb):
                    for i in range(NDC):
                        inst = e.matmul(ps_q[dc][:],
                                        xb[:, i * 128:(i + 1) * 128],
                                        wq_sb[:, i * 1024 + dc * 512: i * 1024 + (dc + 1) * 512],
                                        start=(i == 0), stop=(i == NDC - 1))
                    return inst
                s.op("tensor", qgrp, wait=[(wld, wready), (xld, xready), (dv, rotq_user)],
                     inc=mm)
            q_mm = mm.n
            for dc in range(2):
                s.op("vector", lambda e, dc=dc: e.tensor_scalar(
                    hv(qsw, 0, 32)[:, 8 * dc:8 * (dc + 1), :],
                    ps_q[dc].ap().rearrange("p (h c) -> p h c", c=64)[:, :, 32:64],
                    -1.0, None, op0=ALU.mult),
                    wait=[(mm, q_mm), (xld, xready)])
                s.op("vector", lambda e, dc=dc: e.tensor_copy(
                    hv(qsw, 32, 64)[:, 8 * dc:8 * (dc + 1), :],
                    ps_q[dc].ap().rearrange("p (h c) -> p h c", c=64)[:, :, 0:32]))
                s.op("vector", lambda e, dc=dc, tt=tt: e.tensor_tensor(
                    qrot[:, dc * 512:(dc + 1) * 512], ps_q[dc][:],
                    cost[tt % 2][:, dc * 512:(dc + 1) * 512], ALU.mult))
            s.op("vector", lambda e, tt=tt: e.tensor_tensor(qsw[:], qsw[:], sint[tt % 2][:],
                                                            ALU.mult))
            s.op("vector", lambda e: e.tensor_tensor(qrot[:], qrot[:], qsw[:], ALU.add))
            s.op("vector", lambda e: e.tensor_copy(qbf[:], qrot[:]),
                 wait=[(trm, tr_done.get("qbf", 0))])
            rot_q = dv.n
            s.op("scalar", lambda e: e.activation(sq[:], qrot[:], AF.Square),
                 wait=[(dv, rot_q)])
            s.op("vector", lambda e: e.reduce_sum(nrm[:], sq[:], axis=AX.X),
                 wait=[(ac, ac.n)])
            s.op("vector", lambda e: e.tensor_tensor(nqc[:], nqc[:], nrm[:], ALU.max))
            rotq_user = dv.n

            def kgrp(e, xb=xb):
                for i in range(NDC):
                    inst = e.matmul(ps_kv[:, 0:256],
                                    xb[:, i * 128:(i + 1) * 128],
                                    wk_sb[:, i * 256:(i + 1) * 256],
                                    start=(i == 0), stop=(i == NDC - 1))
                return inst
            s.op("tensor", kgrp, wait=[(xld, xready), (dv, rotk_user)], inc=mm)
            k_mm = mm.n
            s.op("vector", lambda e: e.tensor_scalar(
                hv(ksw, 0, 32),
                ps_kv.ap().rearrange("p (h c) -> p h c", c=64)[:, 0:4, 32:64],
                -1.0, None, op0=ALU.mult), wait=[(mm, k_mm)])
            s.op("vector", lambda e: e.tensor_copy(
                hv(ksw, 32, 64),
                ps_kv.ap().rearrange("p (h c) -> p h c", c=64)[:, 0:4, 0:32]))
            s.op("vector", lambda e, tt=tt: e.tensor_tensor(
                krot[:], ps_kv[:, 0:256], cost[tt % 2][:, 0:256], ALU.mult))
            s.op("vector", lambda e, tt=tt: e.tensor_tensor(
                ksw[:], ksw[:], sint[tt % 2][:, 0:256], ALU.mult))
            s.op("vector", lambda e: e.tensor_tensor(krot[:], krot[:], ksw[:], ALU.add))
            s.op("vector", lambda e: e.tensor_copy(kbf[:], krot[:]),
                 wait=[(trm, tr_done.get("kbf", 0))])
            rot_k = dv.n
            s.op("scalar", lambda e: e.activation(sq[:, 0:256], krot[:], AF.Square),
                 wait=[(dv, rot_k)])
            s.op("vector", lambda e: e.reduce_sum(nrm[:], sq[:, 0:256], axis=AX.X),
                 wait=[(ac, ac.n)])
            s.op("vector", lambda e: e.tensor_tensor(nkc[:], nkc[:], nrm[:], ALU.max))
            rotk_user = dv.n

            def vgrp(e, xb=xb):
                for i in range(NDC):
                    inst = e.matmul(ps_kv[:, 256:512],
                                    xb[:, i * 128:(i + 1) * 128],
                                    wv_sb[:, i * 256:(i + 1) * 256],
                                    start=(i == 0), stop=(i == NDC - 1))
                return inst
            s.op("tensor", vgrp, wait=[(xld, xready), (dv, v_user)], inc=mm)
            v_mm = mm.n
            s.op("vector", lambda e: e.tensor_copy(vbf[:], ps_kv[:, 256:512]),
                 wait=[(mm, v_mm), (stv, stv.n)])
            v_user = dv.n
            s.op("sync", lambda e, tt=tt: e.dma_start(
                vpark.ap()[tt * 128:(tt + 1) * 128, :], vbf[:]),
                wait=[(dv, v_user)], inc=stv, inck=16, dma=True)

            kq_done = {}
            for i in range(10):
                ptr = ps_tr[i % 2]
                if i < 8:
                    src_t, col = qbf, i * 128
                    dep = rot_q
                else:
                    src_t, col = kbf, (i - 8) * 128
                    dep = rot_k
                s.op("tensor", lambda e, ptr=ptr, src_t=src_t, col=col: e.transpose(
                    ptr[:, 0:128], src_t[:, col:col + 128], g.idb[:]),
                    wait=[(dv, dep), (ac, tr_done.get(("ev", i % 2), 0))], inc=trm)
                td = trm.n
                s.op("scalar", lambda e, i=i, ptr=ptr: e.copy(
                    tstage[:, i * 128:(i + 1) * 128], ptr[:, 0:128]),
                    wait=[(trm, td), (stq, stq.n)])
                tr_done[("ev", i % 2)] = ac.n
            tr_done["qbf"] = trm.n
            tr_done["kbf"] = trm.n
            ts_done = ac.n
            for i in range(8):
                s.op("sync", lambda e, i=i, tt=tt: e.dma_start(
                    qpark.ap()[i * 128:(i + 1) * 128, tt * 128:(tt + 1) * 128],
                    tstage[:, i * 128:(i + 1) * 128]),
                    wait=[(ac, ts_done)], inc=stq, inck=16, dma=True)
            for i in range(2):
                s.op("sync", lambda e, i=i, tt=tt: e.dma_start(
                    kpark.ap()[i * 128:(i + 1) * 128, tt * 128:(tt + 1) * 128],
                    tstage[:, (8 + i) * 128:(9 + i) * 128]),
                    inc=stq, inck=16, dma=True)
            mm_after[tt] = mm.n
        for c in [stq, stv]:
            s.op("sync", lambda e, c=c: e.wait_ge(c.sem, c.n))
        s.flush()

    if dbg is not None:
        s = Sched(nc)
        dst = Ctr(nc, "dst")
        for k, t in [("qpark", qpark), ("kpark", kpark), ("vpark", vpark)]:
            if k in dbg:
                s.op("gpsimd", lambda e, k=k, t=t: e.dma_start(dbg[k].ap(), t.ap()),
                     inc=dst, inck=16, dma=True)
        s.op("gpsimd", lambda e: e.wait_ge(dst.sem, dst.n), dma=True)
        s.flush()

    with ExitStack() as c2:
        E2 = c2.enter_context
        KT = E2(nc.sbuf_tensor(f"{tag}_KT", [64, 4 * T], BF16))
        VTp = E2(nc.sbuf_tensor(f"{tag}_VTp", [128, NT * 256], BF16))
        VT = E2(nc.sbuf_tensor(f"{tag}_VT", [128, NT * 260], BF16))
        QT = [E2(nc.sbuf_tensor(f"{tag}_QT{i}", [64, T], BF16)) for i in range(2)]
        PT2 = [E2(nc.sbuf_tensor(f"{tag}_PT{i}", [128, NT * 512], BF16)) for i in range(2)]
        rzrow = E2(nc.sbuf_tensor(f"{tag}_rzrow", [1, 512], F32))
        rzb = E2(nc.sbuf_tensor(f"{tag}_rzb", [64, 512], F32))
        ocb = [E2(nc.sbuf_tensor(f"{tag}_oc{i}", [64, 512], BF16)) for i in range(2)]
        mqv = E2(nc.sbuf_tensor(f"{tag}_mqv", [1, 1], F32))
        mkv = E2(nc.sbuf_tensor(f"{tag}_mkv", [1, 1], F32))
        ps_s = [E2(nc.psum_tensor(f"{tag}_ps{i}", [128, 1024], F32)) for i in range(2)]
        ps_v = [E2(nc.psum_tensor(f"{tag}_pv{i}", [65, 512], F32)) for i in range(2)]
        ps_z = E2(nc.psum_tensor(f"{tag}_pz", [64, 512], F32))
        ps_t2 = E2(nc.psum_tensor(f"{tag}_pt2", [128, 512], F32))

        s = Sched(nc)
        dv, ac = s.v, s.a
        kld = Ctr(nc, "kld")
        qld = [Ctr(nc, "ql0"), Ctr(nc, "ql1")]
        mm = Ctr(nc, "mm")
        pvm = Ctr(nc, "pvm")
        rpm = Ctr(nc, "rpm")
        sto2 = [Ctr(nc, "sto0"), Ctr(nc, "sto1")]

        s.op("tensor", lambda e: e.transpose(ps_t2[0:1, 0:128], nqc[:], g.idf[:]), inc=mm)
        s.op("tensor", lambda e: e.transpose(ps_t2[0:1, 128:256], nkc[:], g.idf[:]), inc=mm)
        tmm = mm.n
        s.op("vector", lambda e: e.reduce_max(mqv[:], ps_t2[0:1, 0:128], axis=AX.X),
             wait=[(mm, tmm)])
        s.op("vector", lambda e: e.reduce_max(mkv[:], ps_t2[0:1, 128:256], axis=AX.X))
        s.op("vector", lambda e: e.tensor_tensor(mqv[:], mqv[:], mkv[:], ALU.mult))
        s.op("scalar", lambda e: e.activation(mqv[:], mqv[:], AF.Sqrt), wait=[(dv, dv.n)])
        s.op("vector", lambda e: e.tensor_scalar(mqv[:], mqv[:], -1.0 / 8.0, None,
                                                 op0=ALU.mult), wait=[(ac, ac.n)])
        nb_done = dv.n
        s.op("tensor", lambda e: e.matmul(ps_t2[:, 256:257], g.ones_row_f[:], mqv[:],
                                          start=True, stop=True),
             wait=[(dv, nb_done)], inc=mm)
        s.op("vector", lambda e: e.tensor_copy(nbc[:], ps_t2[:, 256:257]), wait=[(mm, mm.n)])
        nbc_dv = dv.n

        for gi in range(4):
            s.op("sync", lambda e, gi=gi: e.dma_start(
                KT[0:64, gi * T:(gi + 1) * T], kpark.ap()[gi * 64:(gi + 1) * 64, :]),
                inc=kld, inck=16, dma=True)
        for kt in range(NT):
            s.op("sync", lambda e, kt=kt: e.dma_start(
                VTp[:, kt * 256:(kt + 1) * 256], vpark.ap()[kt * 128:(kt + 1) * 128, :]),
                inc=kld, inck=16, dma=True)
        kv_ready = kld.n
        s.op("vector", lambda e: e.tensor_copy(
            VT.ap().rearrange("p (q x) -> p q x", x=65)[:, :, 0:64],
            VTp.ap().rearrange("p (q x) -> p q x", x=64)[:, :, 0:64]),
            wait=[(kld, kv_ready)])
        s.op("vector", lambda e: e.memset(
            VT.ap().rearrange("p (q x) -> p q x", x=65)[:, :, 64:65], 1.0))
        ones_dv = dv.n

        mm_free = {0: 0}
        NQC2 = T // 512
        GG = HQ2 * NQC2
        q_ready = {}
        chunk_exp = {}
        slab_free = {0: 0, 1: 0}
        ot_user = {0: ones_dv, 1: ones_dv}
        oc_st = {0: 0, 1: 0}
        pz_user = 0
        rz_user = 0

        def make_pv_items(g2):
            h2, qc2 = divmod(g2, NQC2)
            gi2 = h2 % HKV2
            pov = ps_v[g2 % 2]
            PTb = PT2[g2 % 2]
            nk = 4 * qc2 + 4
            items = []
            for kt2 in range(nk):
                def one(e, kt2=kt2, gi2=gi2, pov=pov, PTb=PTb, nk=nk):
                    return e.matmul(
                        pov[:, :],
                        VT[:, kt2 * 260 + gi2 * 65: kt2 * 260 + (gi2 + 1) * 65],
                        PTb[:, kt2 * 512:(kt2 + 1) * 512],
                        start=(kt2 == 0), stop=(kt2 == nk - 1))
                w = []
                if kt2 == 0:
                    w = [(ac, chunk_exp[g2]), (dv, ot_user[g2 % 2])]
                items.append(lambda one=one, w=w, last=(kt2 == nk - 1):
                             s.op("tensor", one, wait=w, inc=(pvm if last else None)))

            def comb_a():
                pvd = pvm.n
                s.op("vector", lambda e, pov=pov: e.reciprocal(
                    rzrow[:], pov[64:65, :]), wait=[(pvm, pvd), (rpm, rz_user)])
            items.append(comb_a)

            def comb_b():
                nonlocal pz_user, rz_user
                s.op("tensor", lambda e: e.matmul(
                    ps_z[:], g.ones_row_f[0:1, 0:64], rzrow[:],
                    start=True, stop=True),
                    wait=[(dv, dv.n), (dv, pz_user)], inc=rpm)
                rz_user = rpm.n
                rpd = rpm.n
                oc = ocb[g2 % 2]
                s.op("vector", lambda e: e.tensor_copy(rzb[:], ps_z[:]),
                     wait=[(rpm, rpd)])
                pz_user = dv.n
                s.op("vector", lambda e, pov=pov, oc=oc: e.tensor_tensor(
                    oc[:], pov[0:64, :], rzb[:], ALU.mult),
                    wait=[(sto2[g2 % 2], oc_st[g2 % 2])])
                ot_user[g2 % 2] = dv.n
                ocd = dv.n
                s.op("sync", lambda e, h2=h2, qc2=qc2, oc=oc: e.dma_start(
                    oT.ap()[h2 * 64:(h2 + 1) * 64, qc2 * 512:(qc2 + 1) * 512], oc[:]),
                    wait=[(dv, ocd)], inc=sto2[g2 % 2], inck=16, dma=True)
                oc_st[g2 % 2] = sto2[g2 % 2].n
            items.append(comb_b)
            return items

        pvq = []
        for gg2 in range(GG):
            h, qc = divmod(gg2, NQC2)
            gi = h % HKV2
            qb = QT[h % 2]
            if qc == 0:
                if h > 0:
                    mm_free[h] = mm.n
                hl = qld[h % 2]
                s.op("sync", lambda e, h=h, qb=qb: e.dma_start(
                    qb[0:64, :], qpark.ap()[h * 64:(h + 1) * 64, :]),
                    wait=[(mm, mm_free[h])], inc=hl, inck=16, dma=True)
                q_ready[h] = hl.n
            PTb = PT2[gg2 % 2]
            pt_free = pvm.n
            if gg2 > 0:
                assert not pvq
                pvq = make_pv_items(gg2 - 1)
            npair = 2 * qc + 2
            for p in range(npair):
                psl = ps_s[p % 2]
                for half in range(2):
                    kt = 2 * p + half

                    def sgrp(e, kt=kt, qc=qc, gi=gi, qb=qb, psl=psl, half=half):
                        masks = []
                        for qs in range(4):
                            qt = qc * 4 + qs
                            if qt == kt:
                                masks.append((qs, g.cmb))
                            elif qt < kt:
                                masks.append((qs, g.nfb))
                        inst = e.matmul(
                            psl[:, half * 512:(half + 1) * 512],
                            KT[0:64, gi * T + kt * 128: gi * T + (kt + 1) * 128],
                            qb[0:64, qc * 512:(qc + 1) * 512],
                            start=True, stop=(not masks))
                        for j, (qs, mt) in enumerate(masks):
                            inst = e.matmul(
                                psl[:, half * 512 + qs * 128: half * 512 + (qs + 1) * 128],
                                g.idb[:], mt[:], start=False, stop=(j == len(masks) - 1))
                        return inst
                    s.op("tensor", sgrp,
                         wait=[(kld, kv_ready), (qld[h % 2], q_ready[h]),
                               (ac, slab_free[p % 2]), (dv, nbc_dv)], inc=mm)
                sd = mm.n
                s.op("scalar", lambda e, p=p, psl=psl, PTb=PTb: e.activation(
                    PTb[:, 2 * p * 512: 2 * p * 512 + 1024], psl[:], AF.Exp,
                    bias=nbc[:, 0:1], scale=1.0 / 8.0),
                    wait=[(mm, sd), (pvm, pt_free)])
                slab_free[p % 2] = ac.n
                if pvq:
                    rem_slots = npair - p
                    take = (len(pvq) + rem_slots - 1) // rem_slots
                    for _ in range(take):
                        if pvq:
                            pvq.pop(0)()
            chunk_exp[gg2] = ac.n
        for it in make_pv_items(GG - 1):
            it()
        s.op("sync", lambda e, a=sto2[0].n: e.wait_ge(sto2[0].sem, a))
        s.op("sync", lambda e, a=sto2[1].n: e.wait_ge(sto2[1].sem, a))
        if dbg is not None and "oT" in dbg:
            dst2 = Ctr(nc, "dst2")
            s.op("gpsimd", lambda e: e.dma_start(dbg["oT"].ap(), oT.ap()),
                 inc=dst2, inck=16, dma=True)
            s.op("gpsimd", lambda e, a=dst2: e.wait_ge(a.sem, a.n), dma=True)
        s.flush()

    with ExitStack() as c3:
        E3 = c3.enter_context
        wo_sb = E3(nc.sbuf_tensor(f"{tag}_wo", [128, NDC * D], BF16))
        otc = [E3(nc.sbuf_tensor(f"{tag}_otc{i}", [128, NDC * 128], BF16)) for i in range(2)]
        x3t = [E3(nc.sbuf_tensor(f"{tag}_x3b{i}", [128, D], F32)) for i in range(2)]
        hb3 = [E3(nc.sbuf_tensor(f"{tag}_hb{i}", [128, D], F32)) for i in range(2)]
        xmt = [E3(nc.sbuf_tensor(f"{tag}_xm{i}", [128, D], F32)) for i in range(2)]
        xmb = [E3(nc.sbuf_tensor(f"{tag}_xmb{i}", [128, D], BF16)) for i in range(2)]
        g2t = E3(nc.sbuf_tensor(f"{tag}_g2t", [128, D], F32))
        sqs = E3(nc.sbuf_tensor(f"{tag}_sqs3", [128, D], F32))
        ms = E3(nc.sbuf_tensor(f"{tag}_ms3", [128, 1], F32))
        rstd = E3(nc.sbuf_tensor(f"{tag}_rstd3", [128, 1], F32))
        epsb = E3(nc.sbuf_tensor(f"{tag}_epsb3", [128, 1], F32))
        ps_o3 = [E3(nc.psum_tensor(f"{tag}_po3{i}", [128, 512], F32)) for i in range(4)]

        s = Sched(nc)
        dv, ac = s.v, s.a
        wld = Ctr(nc, "wld")
        oldp = [Ctr(nc, "ol0"), Ctr(nc, "ol1")]
        xldr = Ctr(nc, "xlr")
        mm = Ctr(nc, "mm")
        sth = [Ctr(nc, "sh0"), Ctr(nc, "sh1")]
        stm = [Ctr(nc, "sm0"), Ctr(nc, "sm1")]

        s.op("vector", lambda e: e.memset(epsb[:], 1e-6))
        g2l = Ctr(nc, "g2l")
        s.op("sync", lambda e: e.dma_start(g2t[:], g2bc.ap()), inc=g2l, inck=16, dma=True)
        g2d = g2l.n
        for i in range(NDC):
            s.op("gpsimd", lambda e, i=i: e.dma_start(
                wo_sb[:, i * D:(i + 1) * D], wo.ap()[i * 128:(i + 1) * 128, :]),
                inc=wld, inck=16, dma=True)
        wready = wld.n

        ot_done = {}
        for i in range(NDC):
            s.op("sync", lambda e, i=i: e.dma_start(
                otc[0][:, i * 128:(i + 1) * 128], oT.ap()[i * 128:(i + 1) * 128, 0:128]),
                inc=oldp[0], inck=16, dma=True)
        ot_done[0] = oldp[0].n

        mm_after_tt = {}
        x3_user = {}
        ps_user = {}
        for tt in range(NT):
            cb = otc[tt % 2]
            xb = x3t[tt % 2]
            hb = hb3[tt % 2]
            mt = xmt[tt % 2]
            mb = xmb[tt % 2]
            s.op("sync", lambda e, tt=tt, xb=xb: e.dma_start(
                xb[:], x3.ap()[tt * 128:(tt + 1) * 128, :]),
                wait=[(dv, x3_user.get(tt - 2, 0))], inc=xldr, inck=16, dma=True)
            xrd = xldr.n
            for dc in range(2):
                pslab = (tt % 2) * 2 + dc
                pp = ps_o3[pslab]

                def ogrp(e, dc=dc, cb=cb, pp=pp):
                    for i in range(NDC):
                        inst = e.matmul(pp[:],
                                        cb[:, i * 128:(i + 1) * 128],
                                        wo_sb[:, i * D + dc * 512: i * D + (dc + 1) * 512],
                                        start=(i == 0), stop=(i == NDC - 1))
                    return inst
                s.op("tensor", ogrp, wait=[(wld, wready), (oldp[tt % 2], ot_done[tt]),
                                           (dv, ps_user.get(pslab, 0))], inc=mm)
                mval = mm.n
                s.op("vector", lambda e, dc=dc, xb=xb, hb=hb, pp=pp: e.tensor_tensor(
                    hb[:, dc * 512:(dc + 1) * 512], xb[:, dc * 512:(dc + 1) * 512],
                    pp[:], ALU.add),
                    wait=[(mm, mval), (xldr, xrd), (sth[tt % 2], sth[tt % 2].n)])
                ps_user[pslab] = dv.n
            x3_user[tt] = dv.n
            h_ready = dv.n
            mm_after_tt[tt] = mm.n
            if tt + 1 < NT:
                for i in range(NDC):
                    s.op("sync", lambda e, i=i, q=tt + 1: e.dma_start(
                        otc[q % 2][:, i * 128:(i + 1) * 128],
                        oT.ap()[i * 128:(i + 1) * 128, q * 128:(q + 1) * 128]),
                        wait=[(mm, mm_after_tt.get(tt - 1, 0))],
                        inc=oldp[(tt + 1) % 2], inck=16, dma=True)
                ot_done[tt + 1] = oldp[(tt + 1) % 2].n
            s.op("sync", lambda e, tt=tt, hb=hb: e.dma_start(
                h_out.ap()[tt * 128:(tt + 1) * 128, :], hb[:]),
                wait=[(dv, h_ready)], inc=sth[tt % 2], inck=16, dma=True)
            s.op("scalar", lambda e, hb=hb: e.activation(sqs[:], hb[:], AF.Square),
                 wait=[(dv, h_ready), (dv, dv.n)])
            s.op("vector", lambda e: e.reduce_sum(ms[:], sqs[:], axis=AX.X), wait=[(ac, ac.n)])
            s.op("vector", lambda e: e.tensor_scalar(ms[:], ms[:], 1.0 / D, None, op0=ALU.mult))
            s.op("scalar", lambda e: e.activation(ms[:], ms[:], AF.Sqrt, bias=epsb[:, 0:1]),
                 wait=[(dv, dv.n)])
            s.op("vector", lambda e: e.reciprocal(rstd[:], ms[:]), wait=[(ac, ac.n)])
            s.op("vector", lambda e, hb=hb, mt=mt: e.tensor_scalar(
                mt[:], hb[:], rstd[:, 0:1], None, op0=ALU.mult),
                wait=[(stm[tt % 2], stm[tt % 2].n)])
            s.op("vector", lambda e, mt=mt: e.tensor_tensor(mt[:], mt[:], g2t[:], ALU.mult),
                 wait=[(g2l, g2d)])
            s.op("vector", lambda e, mt=mt, mb=mb: e.tensor_copy(mb[:], mt[:]))
            xm_dv = dv.n
            s.op("sync", lambda e, tt=tt, mt=mt: e.dma_start(
                xm_out.ap()[tt * 128:(tt + 1) * 128, :], mt[:]),
                wait=[(dv, xm_dv)], inc=stm[tt % 2], inck=16, dma=True)
            s.op("sync", lambda e, tt=tt, mb=mb: e.dma_start(
                xmbf.ap()[tt * 128:(tt + 1) * 128, :], mb[:]),
                inc=stm[tt % 2], inck=16, dma=True)
        for c in [sth[0], sth[1], stm[0], stm[1]]:
            s.op("sync", lambda e, c=c: e.wait_ge(c.sem, c.n))
        s.flush()



NE2 = 8
CAP2 = 4096
PAD2 = 8192
HID2 = 2048
SENT = 524288.0


def emit_phase_d_moe(nc, ctx, g, x3, h_in, xm_in, xmbf, rwT, rb_bc, w1, w2, w3,
                     coremask, out, tag="e"):
    E = ctx.enter_context
    NTT = NT
    xm_all = nc.dram_tensor(f"{tag}_xmall", [NCORES * T, D], BF16, kind="Internal")
    hist_in = nc.dram_tensor(f"{tag}_histin", [1, 16], F32, kind="Internal")
    hist_all = nc.dram_tensor(f"{tag}_histall", [NCORES, 16], F32, kind="Internal")
    histm_in = nc.dram_tensor(f"{tag}_histmin", [1, 8], F32, kind="Internal")
    histm_all = nc.dram_tensor(f"{tag}_histmall", [NCORES, 8], F32, kind="Internal")
    idt_loc = nc.dram_tensor(f"{tag}_idtloc", [NE2 * PAD2, 1], F32, kind="Internal")
    idt_all = nc.dram_tensor(f"{tag}_idtall", [NE2 * PAD2, 1], F32, kind="Internal")
    yl = nc.dram_tensor(f"{tag}_yl", [PAD2, D], BF16, kind="Internal")
    yall = nc.dram_tensor(f"{tag}_yall", [NE2 * PAD2, D], BF16, kind="Internal")
    yq4 = [nc.dram_tensor(f"{tag}_yq{i}", [NE2 * (PAD2 // 4), D], BF16, kind="Internal")
           for i in range(4)]

    i1c = E(nc.sbuf_tensor(f"{tag}_i1c", [128, NTT], F32))
    i2c = E(nc.sbuf_tensor(f"{tag}_i2c", [128, NTT], F32))
    p0c = E(nc.sbuf_tensor(f"{tag}_p0c", [128, NTT], F32))
    p1c = E(nc.sbuf_tensor(f"{tag}_p1c", [128, NTT], F32))
    s1c = E(nc.sbuf_tensor(f"{tag}_s1c", [128, NTT], F32))
    wyc = E(nc.sbuf_tensor(f"{tag}_wyc", [128, NTT], F32))
    spc = E(nc.sbuf_tensor(f"{tag}_spc", [128, NTT], F32))
    selc = E(nc.sbuf_tensor(f"{tag}_selc", [128, NTT], F32))
    base_sb = E(nc.sbuf_tensor(f"{tag}_base", [1, 1], F32))
    rg = [list(range(NCORES))]

    with ExitStack() as c1:
        E1 = c1.enter_context
        rw_sb = E1(nc.sbuf_tensor(f"{tag}_rw", [128, NDC * 8], F32))
        rbt = E1(nc.sbuf_tensor(f"{tag}_rbt", [128, 8], F32))
        xt = [E1(nc.sbuf_tensor(f"{tag}_xt{i}", [128, D], F32)) for i in range(2)]
        xT = E1(nc.sbuf_tensor(f"{tag}_xT", [128, NDC * 128], F32))
        lg = E1(nc.sbuf_tensor(f"{tag}_lg", [128, 8], F32))
        lgm = E1(nc.sbuf_tensor(f"{tag}_lgm", [128, 8], F32))
        eq = E1(nc.sbuf_tensor(f"{tag}_eq", [128, 8], F32))
        cnd = E1(nc.sbuf_tensor(f"{tag}_cnd", [128, 8], F32))
        ohsum = E1(nc.sbuf_tensor(f"{tag}_ohs", [128, 8], F32))
        oh1 = E1(nc.sbuf_tensor(f"{tag}_oh1", [128, 8], F32))
        oh2 = E1(nc.sbuf_tensor(f"{tag}_oh2", [128, 8], F32))
        mx1 = E1(nc.sbuf_tensor(f"{tag}_mx1", [128, 1], F32))
        mx2 = E1(nc.sbuf_tensor(f"{tag}_mx2", [128, 1], F32))
        tcol = E1(nc.sbuf_tensor(f"{tag}_tcol", [128, 1], F32))
        run0 = E1(nc.sbuf_tensor(f"{tag}_run0", [1, 8], F32))
        run01 = E1(nc.sbuf_tensor(f"{tag}_run01", [1, 8], F32))
        hrow = E1(nc.sbuf_tensor(f"{tag}_hrow", [1, 16], F32))
        ps_t = [E1(nc.psum_tensor(f"{tag}_pt{i}", [128, 512], F32)) for i in range(2)]
        ps_l = E1(nc.psum_tensor(f"{tag}_pl", [128, 512], F32))
        ps_r = E1(nc.psum_tensor(f"{tag}_pr", [128, 512], F32))
        ps_h = E1(nc.psum_tensor(f"{tag}_ph", [1, 512], F32))

        s = Sched(nc)
        dv, ac = s.v, s.a
        ld = Ctr(nc, "ld")
        castc = Ctr(nc, "castc")
        xldp = [Ctr(nc, "xl0"), Ctr(nc, "xl1")]
        mm = Ctr(nc, "mm")
        rmm = Ctr(nc, "rmm")
        hmm = Ctr(nc, "hmm")
        cc = Ctr(nc, "cc")

        s.op("gpsimd", lambda e: e.collective_compute(
            "AllGather", ALU.bypass, replica_groups=rg,
            ins=[xmbf.ap()], outs=[xm_all.ap()]), inc=cc, dma=True)

        for i in range(NDC):
            s.op("sync", lambda e, i=i: e.dma_start(
                rw_sb[:, i * 8:(i + 1) * 8], rwT.ap()[i * 128:(i + 1) * 128, :]),
                inc=ld, inck=16, dma=True)
        s.op("sync", lambda e: e.dma_start(rbt[:], rb_bc.ap()), inc=ld, inck=16, dma=True)
        rw_ready = ld.n
        s.op("vector", lambda e: e.memset(run0[:], 0.0))
        s.op("vector", lambda e: e.memset(run01[:], 0.0))

        tr_evac = 0
        lg_user = 0
        run_user = 0
        for tt in range(NTT):
            xb = xt[tt % 2]
            xld = xldp[tt % 2]
            s.op("sync", lambda e, tt=tt, xb=xb: e.dma_start(
                xb[:], xm_in.ap()[tt * 128:(tt + 1) * 128, :]),
                wait=[(mm, max(0, mm.n - 10))], inc=xld, inck=16, dma=True)
            xrd = xld.n
            for i in range(NDC):
                pslab = ps_t[i % 2]
                s.op("tensor", lambda e, i=i, xb=xb, pslab=pslab: e.transpose(
                    pslab[:, 0:128], xb[:, i * 128:(i + 1) * 128], g.idf[:]),
                    wait=[(xld, xrd), (dv, max(0, tr_evac - 1))], inc=mm)
                mval = mm.n
                s.op("vector", lambda e, i=i, pslab=pslab: e.tensor_copy(
                    xT[:, i * 128:(i + 1) * 128], pslab[:, 0:128]),
                    wait=[(mm, mval)])
                tr_evac = dv.n

            def lgrp(e):
                for i in range(NDC):
                    inst = e.matmul(ps_l[:, 0:8], xT[:, i * 128:(i + 1) * 128],
                                    rw_sb[:, i * 8:(i + 1) * 8],
                                    start=(i == 0), stop=(i == NDC - 1))
                return inst
            s.op("tensor", lgrp, wait=[(ld, rw_ready), (dv, tr_evac), (dv, lg_user)], inc=mm)
            lg_mm = mm.n
            s.op("vector", lambda e: e.tensor_tensor(lg[:], ps_l[:, 0:8], rbt[:], ALU.add),
                 wait=[(mm, lg_mm)])
            s.op("vector", lambda e: e.reduce_max(mx1[:], lg[:], axis=AX.X))
            s.op("vector", lambda e: e.tensor_scalar(eq[:], lg[:], mx1[:, 0:1], None,
                                                     op0=ALU.is_ge))
            s.op("vector", lambda e: e.scalar_tensor_tensor(
                cnd[:], eq[:], -1e6, g.iota16b[:, 0:8], op0=ALU.mult, op1=ALU.add))
            s.op("vector", lambda e: e.tensor_reduce(tcol[:], cnd[:], axis=AX.X, op=ALU.min))
            s.op("vector", lambda e, tt=tt: e.tensor_copy(i1c[:, tt:tt + 1], tcol[:]))
            s.op("vector", lambda e: e.tensor_scalar(oh1[:], g.iota16[:, 0:8], tcol[:, 0:1],
                                                     None, op0=ALU.is_equal))
            s.op("vector", lambda e: e.scalar_tensor_tensor(
                lgm[:], oh1[:], -1e9, lg[:], op0=ALU.mult, op1=ALU.add))
            s.op("vector", lambda e: e.reduce_max(mx2[:], lgm[:], axis=AX.X))
            s.op("vector", lambda e: e.tensor_scalar(eq[:], lgm[:], mx2[:, 0:1], None,
                                                     op0=ALU.is_ge))
            s.op("vector", lambda e: e.scalar_tensor_tensor(
                cnd[:], eq[:], -1e6, g.iota16b[:, 0:8], op0=ALU.mult, op1=ALU.add))
            s.op("vector", lambda e: e.tensor_reduce(tcol[:], cnd[:], axis=AX.X, op=ALU.min))
            s.op("vector", lambda e, tt=tt: e.tensor_copy(i2c[:, tt:tt + 1], tcol[:]))
            s.op("vector", lambda e: e.tensor_scalar(oh2[:], g.iota16[:, 0:8], tcol[:, 0:1],
                                                     None, op0=ALU.is_equal))
            s.op("vector", lambda e: e.tensor_tensor(ohsum[:], oh1[:], oh2[:], ALU.add))
            s.op("vector", lambda e: e.tensor_tensor(mx1[:], mx1[:], mx2[:], ALU.subtract))
            s.op("scalar", lambda e, tt=tt: e.activation(s1c[:, tt:tt + 1], mx1[:],
                                                         AF.Sigmoid), wait=[(dv, dv.n)])
            oh_done = dv.n
            lg_user = dv.n

            def rgrp(e):
                e.matmul(ps_r[:, 0:8], g.uincl[:], oh1[:], start=True, stop=False)
                e.matmul(ps_r[:, 0:8], g.ones_row_f[:], run0[:], start=False, stop=True)
                e.matmul(ps_r[:, 8:16], g.uincl[:], ohsum[:], start=True, stop=False)
                return e.matmul(ps_r[:, 8:16], g.ones_row_f[:], run01[:],
                                start=False, stop=True)
            s.op("tensor", rgrp, wait=[(dv, oh_done), (dv, run_user), (ac, ac.n)], inc=rmm)
            r_done = rmm.n
            def hgrp(e, tt=tt):
                e.matmul(ps_h[:, 0:8], g.ones_col_f[:], oh1[:],
                         start=(tt == 0), stop=False)
                return e.matmul(ps_h[:, 8:16], g.ones_col_f[:], ohsum[:],
                                start=False, stop=(tt == NTT - 1))
            s.op("tensor", hgrp, wait=[(dv, oh_done)], inc=hmm)
            s.op("vector", lambda e: e.tensor_tensor(cnd[:], oh1[:], ps_r[:, 0:8], ALU.mult),
                 wait=[(rmm, r_done)])
            s.op("vector", lambda e, tt=tt: e.reduce_sum(p0c[:, tt:tt + 1], cnd[:], axis=AX.X))
            s.op("vector", lambda e: e.tensor_tensor(cnd[:], oh2[:], ps_r[:, 8:16], ALU.mult))
            s.op("vector", lambda e, tt=tt: e.reduce_sum(p1c[:, tt:tt + 1], cnd[:], axis=AX.X))
            def tgrp(e):
                e.matmul(ps_r[0:1, 16:24], g.ones_col_f[:], oh1[:], start=True, stop=True)
                return e.matmul(ps_r[0:1, 24:32], g.ones_col_f[:], ohsum[:],
                                start=True, stop=True)
            s.op("tensor", tgrp, wait=[(dv, dv.n)], inc=rmm)
            t_done = rmm.n
            s.op("vector", lambda e: e.tensor_tensor(run0[:], run0[:], ps_r[0:1, 16:24],
                                                     ALU.add), wait=[(rmm, t_done)])
            s.op("vector", lambda e: e.tensor_tensor(run01[:], run01[:], ps_r[0:1, 24:32],
                                                     ALU.add))
            run_user = dv.n
        hist_done = hmm.n
        s.op("vector", lambda e: e.tensor_copy(hrow[:], ps_h[0:1, 0:16]),
             wait=[(hmm, hist_done)])
        hcopy = dv.n
        s.op("sync", lambda e: e.dma_start(hist_in.ap(), hrow[:]),
             wait=[(dv, hcopy)], inc=ld, inck=16, dma=True)
        hst = ld.n
        s.op("gpsimd", lambda e: e.collective_compute(
            "AllGather", ALU.bypass, replica_groups=rg,
            ins=[hist_in.ap()], outs=[hist_all.ap()]),
            wait=[(ld, hst)], inc=cc, dma=True)
        s.op("gpsimd", lambda e: e.wait_ge(cc.sem, cc.n), dma=True)
        s.flush()

    with ExitStack() as c2:
        E2 = c2.enter_context
        hists = E2(nc.sbuf_tensor(f"{tag}_hists", [8, 16], F32))
        cmk = E2(nc.sbuf_tensor(f"{tag}_cmk", [8, 1], F32))
        off0b = E2(nc.sbuf_tensor(f"{tag}_off0b", [128, 8], F32))
        off01b = E2(nc.sbuf_tensor(f"{tag}_off01b", [128, 8], F32))
        oh1 = E2(nc.sbuf_tensor(f"{tag}_m2oh1", [128, 8], F32))
        oh2 = E2(nc.sbuf_tensor(f"{tag}_m2oh2", [128, 8], F32))
        ohm = E2(nc.sbuf_tensor(f"{tag}_m2ohm", [128, 8], F32))
        tmp8 = E2(nc.sbuf_tensor(f"{tag}_m2t8", [128, 8], F32))
        p0g = E2(nc.sbuf_tensor(f"{tag}_p0g", [128, 1], F32))
        p1g = E2(nc.sbuf_tensor(f"{tag}_p1g", [128, 1], F32))
        k0 = E2(nc.sbuf_tensor(f"{tag}_k0", [128, 1], F32))
        k1 = E2(nc.sbuf_tensor(f"{tag}_k1", [128, 1], F32))
        mcol = E2(nc.sbuf_tensor(f"{tag}_mcol", [128, 1], F32))
        tcol = E2(nc.sbuf_tensor(f"{tag}_m2tc", [128, 1], F32))
        runm = E2(nc.sbuf_tensor(f"{tag}_runm", [1, 8], F32))
        hmrow = E2(nc.sbuf_tensor(f"{tag}_hmrow", [1, 8], F32))
        ps_b = E2(nc.psum_tensor(f"{tag}_m2pb", [128, 512], F32))
        ps_h2 = E2(nc.psum_tensor(f"{tag}_m2ph", [1, 512], F32))

        s = Sched(nc)
        dv, ac = s.v, s.a
        ld = Ctr(nc, "ld")
        mm = Ctr(nc, "mm")
        hmm = Ctr(nc, "hmm")
        cc = Ctr(nc, "cc")

        s.op("sync", lambda e: e.dma_start(hists[:], hist_all.ap()), inc=ld, inck=16, dma=True)
        s.op("sync", lambda e: e.dma_start(cmk[:], coremask.ap()), inc=ld, inck=16, dma=True)
        s.op("sync", lambda e: e.wait_ge(ld.sem, ld.n))
        s.op("vector", lambda e: e.memset(runm[:], 0.0))
        s.flush()
        s = Sched(nc)
        dv, ac = s.v, s.a
        mm = Ctr(nc, "mm")
        hmm = Ctr(nc, "hmm")
        rmm = Ctr(nc, "rmm")
        cc = Ctr(nc, "cc")
        ld = Ctr(nc, "ld")
        off0r = E2(nc.sbuf_tensor(f"{tag}_off0r", [1, 8], F32))
        off01r = E2(nc.sbuf_tensor(f"{tag}_off01r", [1, 8], F32))
        s.op("tensor", lambda e: e.matmul(ps_b[0:1, 0:16], cmk[:], hists[:],
                                          start=True, stop=True), inc=mm)
        omm = mm.n
        s.op("vector", lambda e: e.tensor_copy(off0r[:], ps_b[0:1, 0:8]), wait=[(mm, omm)])
        s.op("vector", lambda e: e.tensor_copy(off01r[:], ps_b[0:1, 8:16]))
        orow_dv = dv.n
        s.op("tensor", lambda e: e.matmul(ps_b[:, 16:24], g.ones_row_f[:], off0r[:],
                                          start=True, stop=True),
             wait=[(dv, orow_dv)], inc=mm)
        s.op("tensor", lambda e: e.matmul(ps_b[:, 24:32], g.ones_row_f[:], off01r[:],
                                          start=True, stop=True), inc=mm)
        obc_mm = mm.n
        s.op("vector", lambda e: e.tensor_copy(off0b[:], ps_b[:, 16:24]), wait=[(mm, obc_mm)])
        s.op("vector", lambda e: e.tensor_copy(off01b[:], ps_b[:, 24:32]))
        prep_done = dv.n

        run_user = 0
        for tt in range(NTT):
            s.op("vector", lambda e, tt=tt: e.tensor_scalar(
                oh1[:], g.iota16[:, 0:8], i1c[:, tt:tt + 1], None, op0=ALU.is_equal),
                wait=[(dv, prep_done)])
            s.op("vector", lambda e, tt=tt: e.tensor_scalar(
                oh2[:], g.iota16[:, 0:8], i2c[:, tt:tt + 1], None, op0=ALU.is_equal))
            s.op("vector", lambda e: e.tensor_tensor(tmp8[:], oh1[:], off0b[:], ALU.mult))
            s.op("vector", lambda e: e.reduce_sum(p0g[:], tmp8[:], axis=AX.X))
            s.op("vector", lambda e, tt=tt: e.tensor_tensor(p0g[:], p0g[:], p0c[:, tt:tt + 1],
                                                            ALU.add))
            s.op("vector", lambda e: e.tensor_tensor(tmp8[:], oh2[:], off01b[:], ALU.mult))
            s.op("vector", lambda e: e.reduce_sum(p1g[:], tmp8[:], axis=AX.X))
            s.op("vector", lambda e, tt=tt: e.tensor_tensor(p1g[:], p1g[:], p1c[:, tt:tt + 1],
                                                            ALU.add))
            s.op("vector", lambda e: e.tensor_scalar(k0[:], p0g[:], float(CAP2), None,
                                                     op0=ALU.is_lt))
            s.op("vector", lambda e: e.tensor_scalar(k1[:], p1g[:], float(CAP2), None,
                                                     op0=ALU.is_lt))
            s.op("vector", lambda e, tt=tt: e.tensor_tensor(mcol[:], i1c[:, tt:tt + 1], k0[:],
                                                            ALU.mult))
            s.op("vector", lambda e, tt=tt: e.tensor_tensor(tcol[:], i2c[:, tt:tt + 1], k1[:],
                                                            ALU.mult))
            s.op("vector", lambda e: e.tensor_tensor(mcol[:], mcol[:], tcol[:], ALU.max))
            s.op("vector", lambda e, tt=tt: e.tensor_tensor(selc[:, tt:tt + 1], k0[:], k1[:],
                                                            ALU.max))
            s.op("vector", lambda e: e.tensor_scalar(ohm[:], g.iota16[:, 0:8], mcol[:, 0:1],
                                                     None, op0=ALU.is_equal))
            s.op("vector", lambda e, tt=tt: e.tensor_scalar(
                ohm[:], ohm[:], selc[:, tt:tt + 1], None, op0=ALU.mult))
            s.op("vector", lambda e, tt=tt: e.tensor_tensor(tcol[:], s1c[:, tt:tt + 1], k0[:],
                                                            ALU.mult))
            s.op("vector", lambda e, tt=tt: e.tensor_scalar(
                p0g[:], s1c[:, tt:tt + 1], -1.0, 1.0, op0=ALU.mult, op1=ALU.add))
            s.op("vector", lambda e: e.tensor_tensor(k1[:], k1[:], p0g[:], ALU.mult))
            s.op("vector", lambda e, tt=tt: e.tensor_tensor(wyc[:, tt:tt + 1], tcol[:], k1[:],
                                                            ALU.add))
            oh_done = dv.n
            def rgrp(e):
                e.matmul(ps_b[:, 40:48], g.uincl[:], ohm[:], start=True, stop=False)
                return e.matmul(ps_b[:, 40:48], g.ones_row_f[:], runm[:],
                                start=False, stop=True)
            s.op("tensor", rgrp, wait=[(dv, oh_done), (dv, run_user), (ac, ac.n)], inc=rmm)
            r_done = rmm.n
            def hgrp(e, tt=tt):
                return e.matmul(ps_h2[:, 0:8], g.ones_col_f[:], ohm[:],
                                start=(tt == 0), stop=(tt == NTT - 1))
            s.op("tensor", hgrp, wait=[(dv, oh_done)], inc=hmm)
            s.op("vector", lambda e: e.tensor_tensor(tmp8[:], ohm[:], ps_b[:, 40:48],
                                                     ALU.mult), wait=[(rmm, r_done)])
            s.op("vector", lambda e: e.reduce_sum(tcol[:], tmp8[:], axis=AX.X))
            s.op("vector", lambda e, tt=tt: e.scalar_tensor_tensor(
                spc[:, tt:tt + 1], mcol[:], 65536.0, tcol[:], op0=ALU.mult, op1=ALU.add))
            def tgrp(e):
                return e.matmul(ps_b[0:1, 48:56], g.ones_col_f[:], ohm[:],
                                start=True, stop=True)
            s.op("tensor", tgrp, wait=[(dv, dv.n)], inc=rmm)
            t_done = rmm.n
            s.op("vector", lambda e: e.tensor_tensor(runm[:], runm[:], ps_b[0:1, 48:56],
                                                     ALU.add), wait=[(rmm, t_done)])
            run_user = dv.n
        histm_done = hmm.n
        s.op("vector", lambda e: e.tensor_copy(hmrow[:], ps_h2[0:1, 0:8]),
             wait=[(hmm, histm_done)])
        hcopy = dv.n
        s.op("sync", lambda e: e.dma_start(histm_in.ap(), hmrow[:]),
             wait=[(dv, hcopy)], inc=ld, inck=16, dma=True)
        hst = ld.n
        s.op("gpsimd", lambda e: e.collective_compute(
            "AllGather", ALU.bypass, replica_groups=rg,
            ins=[histm_in.ap()], outs=[histm_all.ap()]),
            wait=[(ld, hst)], inc=cc, dma=True)
        s.op("gpsimd", lambda e: e.wait_ge(cc.sem, cc.n), dma=True)
        s.flush()

    with ExitStack() as c3:
        E3 = c3.enter_context
        histm = E3(nc.sbuf_tensor(f"{tag}_histm", [8, 8], F32))
        cmk = E3(nc.sbuf_tensor(f"{tag}_m3cmk", [8, 1], F32))
        offmb = E3(nc.sbuf_tensor(f"{tag}_offmb", [128, 8], F32))
        offmr = E3(nc.sbuf_tensor(f"{tag}_offmr", [1, 8], F32))
        ohm = E3(nc.sbuf_tensor(f"{tag}_m3ohm", [128, 8], F32))
        mcol = E3(nc.sbuf_tensor(f"{tag}_m3m", [128, 1], F32))
        rkm = E3(nc.sbuf_tensor(f"{tag}_m3rk", [128, 1], F32))
        offsel = E3(nc.sbuf_tensor(f"{tag}_m3of", [128, 1], F32))
        slot = E3(nc.sbuf_tensor(f"{tag}_m3sl", [128, 1], F32))
        ids_f = E3(nc.sbuf_tensor(f"{tag}_m3id", [128, 1], F32))
        slot_i = E3(nc.sbuf_tensor(f"{tag}_m3si", [128, 1], I32))
        bb = E3(nc.sbuf_tensor(f"{tag}_m3bb", [128, 1], F32))
        zero_big = E3(nc.sbuf_tensor(f"{tag}_m3z", [128, NE2 * PAD2 // 128], F32))
        ps_b = E3(nc.psum_tensor(f"{tag}_m3pb", [128, 512], F32))

        s = Sched(nc)
        s.bnd(NE2 * PAD2 - 1)
        dv, ac = s.v, s.a
        ld = Ctr(nc, "ld")
        mm = Ctr(nc, "mm")
        sc = Ctr(nc, "sc")
        cc = Ctr(nc, "cc")

        s.op("sync", lambda e: e.dma_start(histm[:], histm_all.ap()), inc=ld, inck=16, dma=True)
        s.op("sync", lambda e: e.dma_start(cmk[:], coremask.ap()), inc=ld, inck=16, dma=True)
        s.op("sync", lambda e: e.dma_start(base_sb[:], g.dr_base.ap()), inc=ld, inck=16, dma=True)
        s.op("vector", lambda e: e.memset(zero_big[:], 0.0))
        zb_done = dv.n
        s.op("sync", lambda e: e.dma_start(
            idt_loc.ap().rearrange("(p f) o -> p (f o)", p=128), zero_big[:]),
            wait=[(dv, zb_done)], inc=ld, inck=16, dma=True)
        zero_done = ld.n
        ld_done = ld.n
        s.op("tensor", lambda e: e.matmul(ps_b[0:1, 0:8], cmk[:], histm[:],
                                          start=True, stop=True),
             wait=[(ld, ld_done)], inc=mm)
        omm = mm.n
        s.op("vector", lambda e: e.tensor_copy(offmr[:], ps_b[0:1, 0:8]), wait=[(mm, omm)])
        od = dv.n
        s.op("tensor", lambda e: e.matmul(ps_b[:, 8:16], g.ones_row_f[:], offmr[:],
                                          start=True, stop=True), wait=[(dv, od)], inc=mm)
        s.op("tensor", lambda e: e.matmul(ps_b[:, 16:17], g.ones_row_f[:], base_sb[:],
                                          start=True, stop=True), inc=mm)
        bmm = mm.n
        s.op("vector", lambda e: e.tensor_copy(offmb[:], ps_b[:, 8:16]), wait=[(mm, bmm)])
        s.op("vector", lambda e: e.tensor_copy(bb[:], ps_b[:, 16:17]))
        prep_done = dv.n

        sc_prev = 0
        for tt in range(NTT):
            s.op("vector", lambda e, tt=tt: e.tensor_scalar(
                mcol[:], spc[:, tt:tt + 1], 1.0 / 65536.0, None, op0=ALU.mult),
                wait=[(dv, prep_done), (sc, sc_prev)])
            am_w = dv.n
            s.op("gpsimd", lambda e: e.tensor_copy(slot_i[:], mcol[:]), wait=[(dv, am_w)])
            s.op("vector", lambda e: e.tensor_copy(mcol[:], slot_i[:]), wait=[(s.g, s.g.n)])
            s.op("vector", lambda e, tt=tt: e.scalar_tensor_tensor(
                rkm[:], mcol[:], -65536.0, spc[:, tt:tt + 1], op0=ALU.mult, op1=ALU.add))
            s.op("vector", lambda e: e.tensor_scalar(ohm[:], g.iota16[:, 0:8], mcol[:, 0:1],
                                                     None, op0=ALU.is_equal))
            s.op("vector", lambda e: e.tensor_tensor(ohm[:], ohm[:], offmb[:], ALU.mult))
            s.op("vector", lambda e: e.reduce_sum(offsel[:], ohm[:], axis=AX.X))
            s.op("vector", lambda e: e.scalar_tensor_tensor(
                slot[:], mcol[:], float(PAD2), offsel[:], op0=ALU.mult, op1=ALU.add))
            s.op("vector", lambda e: e.tensor_tensor(slot[:], slot[:], rkm[:], ALU.add))
            s.op("vector", lambda e: e.tensor_scalar(slot[:], slot[:], -1.0, None, op0=ALU.add))
            s.op("vector", lambda e, tt=tt: e.tensor_scalar(
                offsel[:], selc[:, tt:tt + 1], -SENT, SENT, op0=ALU.mult, op1=ALU.add))
            s.op("vector", lambda e: e.tensor_tensor(slot[:], slot[:], offsel[:], ALU.add))
            s.op("vector", lambda e, tt=tt: e.tensor_copy(spc[:, tt:tt + 1], slot[:]))
            s.op("vector", lambda e, tt=tt: e.tensor_scalar(
                ids_f[:], g.iotacol[:], float(tt * 128 + 1), None, op0=ALU.add))
            s.op("vector", lambda e: e.tensor_tensor(ids_f[:], ids_f[:], bb[:], ALU.add))
            ids_done = dv.n
            s.op("gpsimd", lambda e: e.tensor_copy(slot_i[:], slot[:]), wait=[(dv, ids_done)])
            cast_done = s.g.n
            s.op("gpsimd", lambda e: e.indirect_dma_start(
                idt_loc.ap(), bass.IndirectOffsetOnAxis(slot_i[:], 0),
                ids_f[:], None, bounds_check=s._breg[s.bnd(NE2 * PAD2 - 1)],
                oob_is_err=False),
                wait=[(ld, zero_done), (s.g, cast_done)], inc=sc, inck=16, dma=True)
            sc_prev = sc.n
        s.op("gpsimd", lambda e: e.collective_compute(
            "AllReduce", ALU.add, replica_groups=rg,
            ins=[idt_loc.ap()], outs=[idt_all.ap()]),
            wait=[(sc, sc.n)], inc=cc, dma=True)
        s.op("gpsimd", lambda e: e.wait_ge(cc.sem, cc.n), dma=True)
        s.flush()

    with ExitStack() as c4:
        E4 = c4.enter_context
        w1_sb = E4(nc.sbuf_tensor(f"{tag}_w1", [128, NDC * HID2], BF16))
        w2_sb = E4(nc.sbuf_tensor(f"{tag}_w2", [128, NDC * HID2], BF16))
        w3_sb = E4(nc.sbuf_tensor(f"{tag}_w3", [128, (HID2 // 128) * D], BF16))
        H1 = E4(nc.sbuf_tensor(f"{tag}_H1", [128, (HID2 // 128) * 512], BF16))
        hT = E4(nc.sbuf_tensor(f"{tag}_hT", [128, (HID2 // 128) * 512], BF16))
        prodf2 = [E4(nc.sbuf_tensor(f"{tag}_prodf{i}", [128, 512], F32)) for i in range(2)]
        disp2 = [E4(nc.sbuf_tensor(f"{tag}_disp{i}", [128, 4 * D], BF16)) for i in range(2)]
        dispT = E4(nc.sbuf_tensor(f"{tag}_dispT", [128, NDC * 512], BF16))
        idcol_f = E4(nc.sbuf_tensor(f"{tag}_idcf", [128, 1], F32))
        idcol_i = E4(nc.sbuf_tensor(f"{tag}_idci", [128, 1], I32))
        negm = E4(nc.sbuf_tensor(f"{tag}_negm", [128, 1], F32))
        slotbase = E4(nc.sbuf_tensor(f"{tag}_slb", [1, 1], F32))
        slotbase_c = E4(nc.sbuf_tensor(f"{tag}_slbc", [128, 1], F32))
        ystage2 = [E4(nc.sbuf_tensor(f"{tag}_yst{i}", [128, D], BF16)) for i in range(2)]
        ps_f = [E4(nc.psum_tensor(f"{tag}_pf{i}", [128, 512], F32)) for i in range(2)]
        ps_h = [E4(nc.psum_tensor(f"{tag}_ph{i}", [128, 512], F32)) for i in range(2)]
        ps_y = [E4(nc.psum_tensor(f"{tag}_py{i}", [128, 512], F32)) for i in range(2)]
        ps_tr = [E4(nc.psum_tensor(f"{tag}_ptr{i}", [128, 512], BF16)) for i in range(2)]

        s = Sched(nc)
        s.bnd(NE2 * PAD2 - 1)
        s.bnd(NCORES * T - 1)
        dv, ac = s.v, s.a
        wld = Ctr(nc, "wld")
        mm = Ctr(nc, "mm")
        st2 = [Ctr(nc, "st0"), Ctr(nc, "st1")]
        ybc = Ctr(nc, "ybc")
        gg = Ctr(nc, "gg")
        yst_free = {0: 0, 1: 0}

        sbl = Ctr(nc, "sbl")
        s.op("sync", lambda e: e.dma_start(slotbase[:], g.dr_base.ap()), inc=sbl, inck=16,
             dma=True)
        s.op("vector", lambda e: e.tensor_scalar(slotbase[:], slotbase[:],
                                                 float(PAD2) / T, None, op0=ALU.mult),
             wait=[(sbl, sbl.n)])
        s.op("tensor", lambda e: e.matmul(ps_f[0][:, 0:1], g.ones_row_f[:], slotbase[:],
                                          start=True, stop=True), wait=[(dv, dv.n)], inc=mm)
        s.op("vector", lambda e: e.tensor_copy(slotbase_c[:], ps_f[0][:, 0:1]),
             wait=[(mm, mm.n)])
        sb_done = dv.n

        for i in range(NDC):
            s.op("gpsimd", lambda e, i=i: e.dma_start(
                w1_sb[:, i * HID2:(i + 1) * HID2], w1.ap()[0, i * 128:(i + 1) * 128, :]),
                inc=wld, inck=16, dma=True)
            s.op("gpsimd", lambda e, i=i: e.dma_start(
                w2_sb[:, i * HID2:(i + 1) * HID2], w2.ap()[0, i * 128:(i + 1) * 128, :]),
                inc=wld, inck=16, dma=True)
        for i in range(HID2 // 128):
            s.op("gpsimd", lambda e, i=i: e.dma_start(
                w3_sb[:, i * D:(i + 1) * D], w3.ap()[0, i * 128:(i + 1) * 128, :]),
                inc=wld, inck=16, dma=True)
        w_ready = wld.n

        gdone_prev = 0
        idc_free = 0
        NCH = PAD2 // 512
        g_done = {}
        trmm_done = {-1: 0, -2: 0}
        h2mul_last = {-1: 0}
        ymm_done = {-1: 0}
        gel = {}
        h1ev = {}
        h2mul = {}
        agws = []

        def emit_gathers(c):
            nonlocal gdone_prev, idc_free
            dp = disp2[c % 2]
            for tj in range(4):
                cslot = c * 512 + tj * 128
                s.op("gpsimd", lambda e, cslot=cslot: e.tensor_scalar(
                    idcol_f[:], g.iotacol[:], float(cslot), None, op0=ALU.add),
                    wait=[(dv, sb_done)])
                s.op("gpsimd", lambda e: e.tensor_tensor(idcol_f[:], idcol_f[:],
                                                         slotbase_c[:], ALU.add))
                s.op("gpsimd", lambda e: e.tensor_copy(idcol_i[:], idcol_f[:]),
                     wait=[(gg, gdone_prev)])
                cast1 = s.g.n
                s.op("gpsimd", lambda e: e.indirect_dma_start(
                    idcol_f[:], None,
                    idt_all.ap(), bass.IndirectOffsetOnAxis(idcol_i[:], 0),
                    bounds_check=s._breg[s.bnd(NE2 * PAD2 - 1)], oob_is_err=False),
                    wait=[(s.g, cast1)], inc=gg, inck=16, dma=True)
                gat1 = gg.n
                s.op("gpsimd", lambda e: e.tensor_scalar(idcol_f[:], idcol_f[:],
                                                         -1.0, None, op0=ALU.add),
                     wait=[(gg, gat1)])
                s.op("gpsimd", lambda e: e.tensor_scalar(negm[:], idcol_f[:], 0.0, None,
                                                         op0=ALU.is_lt))
                s.op("gpsimd", lambda e: e.tensor_scalar(negm[:], negm[:], SENT, None,
                                                         op0=ALU.mult))
                s.op("gpsimd", lambda e: e.tensor_tensor(
                    idcol_f[:], idcol_f[:], negm[:], ALU.add))
                s.op("gpsimd", lambda e: e.tensor_copy(idcol_i[:], idcol_f[:]))
                cast2 = s.g.n
                s.op("gpsimd", lambda e, tj=tj, dp=dp: e.memset(
                    dp[:, tj * D:(tj + 1) * D], 0.0),
                    wait=[(mm, trmm_done[c - 2])])
                s.op("gpsimd", lambda e, tj=tj, dp=dp: e.indirect_dma_start(
                    dp[:, tj * D:(tj + 1) * D], None,
                    xm_all.ap(), bass.IndirectOffsetOnAxis(idcol_i[:], 0),
                    bounds_check=s._breg[s.bnd(NCORES * T - 1)], oob_is_err=False),
                    wait=[(s.g, s.g.n)], inc=gg, inck=16, dma=True)
                g_done[(c, tj)] = gg.n
                gdone_prev = gg.n
                idc_free = s.g.n

        emit_gathers(0)
        for cchunk in range(NCH):
            dp = disp2[cchunk % 2]
            for tj in range(4):
                for i in range(NDC):
                    pslab = ps_tr[(tj * NDC + i) % 2]
                    s.op("tensor", lambda e, tj=tj, i=i, pslab=pslab, dp=dp: e.transpose(
                        pslab[:, 0:128], dp[:, tj * D + i * 128: tj * D + (i + 1) * 128],
                        g.idb[:]),
                        wait=[(gg, g_done[(cchunk, tj)]), (ac, max(0, s.a.n - 1)),
                              (dv, h2mul_last[cchunk - 1])], inc=mm)
                    mval = mm.n
                    s.op("scalar", lambda e, tj=tj, i=i, pslab=pslab: e.copy(
                        dispT[:, i * 512 + tj * 128: i * 512 + (tj + 1) * 128],
                        pslab[:, 0:128]), wait=[(mm, mval)])
            trmm_done[cchunk] = mm.n
            tr_done = s.a.n
            if cchunk + 1 < NCH:
                emit_gathers(cchunk + 1)
            for fc in range(HID2 // 128):
                pfa = ps_f[fc % 2]

                def h1grp(e, fc=fc, pfa=pfa):
                    for i in range(NDC):
                        inst = e.matmul(pfa[:],
                                        w1_sb[:, i * HID2 + fc * 128: i * HID2 + (fc + 1) * 128],
                                        dispT[:, i * 512:(i + 1) * 512],
                                        start=(i == 0), stop=(i == NDC - 1))
                    return inst
                s.op("tensor", h1grp, wait=[(wld, w_ready), (ac, tr_done),
                                            (ac, h1ev.get((cchunk, fc - 2), 0))], inc=mm)
                mval = mm.n
                s.op("scalar", lambda e, fc=fc, pfa=pfa: e.copy(
                    H1[:, fc * 512:(fc + 1) * 512], pfa[:]),
                    wait=[(mm, mval), (dv, h2mul_last[cchunk - 1])])
                h1ev[(cchunk, fc)] = s.a.n
                phb = ps_h[fc % 2]

                def h2grp(e, fc=fc, phb=phb):
                    for i in range(NDC):
                        inst = e.matmul(phb[:],
                                        w2_sb[:, i * HID2 + fc * 128: i * HID2 + (fc + 1) * 128],
                                        dispT[:, i * 512:(i + 1) * 512],
                                        start=(i == 0), stop=(i == NDC - 1))
                    return inst
                s.op("tensor", h2grp, wait=[(dv, h2mul.get((cchunk, fc - 2), 0))], inc=mm)
                mval2 = mm.n
                s.op("vector", lambda e, fc=fc, phb=phb: e.tensor_tensor(
                    prodf2[fc % 2][:], phb[:], H1[:, fc * 512:(fc + 1) * 512], ALU.mult),
                    wait=[(mm, mval2), (ac, h1ev[(cchunk, fc)]),
                          (ac, gel.get((cchunk, fc - 2), 0))])
                h2mul[(cchunk, fc)] = dv.n
                s.op("scalar", lambda e, fc=fc: e.activation(
                    hT[:, fc * 512:(fc + 1) * 512], prodf2[fc % 2][:], AF.Gelu),
                    wait=[(dv, h2mul[(cchunk, fc)]), (mm, ymm_done[cchunk - 1])])
                gel[(cchunk, fc)] = s.a.n
            h2mul_last[cchunk] = dv.n
            h_done = gel[(cchunk, HID2 // 128 - 1)]
            for tj in range(4):
                for dc2 in range(2):
                    pslab = ps_y[(tj * 2 + dc2) % 2]

                    def ygrp(e, tj=tj, dc2=dc2, pslab=pslab):
                        for fc in range(HID2 // 128):
                            inst = e.matmul(
                                pslab[:],
                                hT[:, fc * 512 + tj * 128: fc * 512 + (tj + 1) * 128],
                                w3_sb[:, fc * D + dc2 * 512: fc * D + (dc2 + 1) * 512],
                                start=(fc == 0), stop=(fc == HID2 // 128 - 1))
                        return inst
                    s.op("tensor", ygrp, wait=[(ac, h_done), (dv, max(0, s.v.n))], inc=mm)
                    mval = mm.n
                    s.op("vector", lambda e, dc2=dc2, pslab=pslab, tj=tj: e.tensor_copy(
                        ystage2[tj % 2][:, dc2 * 512:(dc2 + 1) * 512], pslab[:]),
                        wait=[(mm, mval), (st2[tj % 2], yst_free[tj % 2])])
                yst = dv.n
                s.op("sync", lambda e, cchunk=cchunk, tj=tj: e.dma_start(
                    yl.ap()[cchunk * 512 + tj * 128: cchunk * 512 + (tj + 1) * 128, :],
                    ystage2[tj % 2][:]),
                    wait=[(dv, yst)], inc=st2[tj % 2], inck=16, dma=True)
                yst_free[tj % 2] = st2[tj % 2].n
            ymm_done[cchunk] = mm.n
            if (cchunk + 1) % (NCH // 4) == 0:
                gi4 = (cchunk + 1) // (NCH // 4) - 1
                r0 = gi4 * (PAD2 // 4)
                r1 = (gi4 + 1) * (PAD2 // 4)
                ccx = Ctr(nc, f"ccy{gi4}")
                s.op("gpsimd", lambda e, r0=r0, r1=r1, gi4=gi4: e.collective_compute(
                    "AllGather", ALU.bypass, replica_groups=rg,
                    ins=[yl.ap()[r0:r1, :]], outs=[yq4[gi4].ap()]),
                    wait=[(st2[0], st2[0].n), (st2[1], st2[1].n)], inc=ccx, dma=True)
                yview = yall.ap().rearrange("(e p) d -> e p d", p=PAD2)[:, r0:r1, :]
                s.op("gpsimd", lambda e, gi4=gi4, yview=yview: e.dma_start(
                    yview, yq4[gi4].ap().rearrange("(e p) d -> e p d", p=PAD2 // 4)),
                    wait=[(ccx, ccx.n)], inc=ybc, inck=16, dma=True)
                agws.append(ccx)
        s.op("gpsimd", lambda e, a=ybc.n: e.wait_ge(ybc.sem, a), dma=True)
        s.flush()

    with ExitStack() as c5:
        E5 = c5.enter_context
        ytile = [E5(nc.sbuf_tensor(f"{tag}_ytl{i}", [128, D], BF16)) for i in range(2)]
        xmt = [E5(nc.sbuf_tensor(f"{tag}_xmt{i}", [128, D], F32)) for i in range(2)]
        x3t = [E5(nc.sbuf_tensor(f"{tag}_x3t{i}", [128, D], F32)) for i in range(2)]
        ht = [E5(nc.sbuf_tensor(f"{tag}_ht{i}", [128, D], F32)) for i in range(2)]
        fb = [E5(nc.sbuf_tensor(f"{tag}_fb{i}", [128, D], F32)) for i in range(2)]
        ob5 = [E5(nc.sbuf_tensor(f"{tag}_ob{i}", [128, D], F32)) for i in range(2)]
        wxcol = E5(nc.sbuf_tensor(f"{tag}_wx", [128, 1], F32))
        slot_i = E5(nc.sbuf_tensor(f"{tag}_m5si", [128, 1], I32))
        tmp = {"sqs": E5(nc.sbuf_tensor(f"{tag}_sqs5", [128, D], F32)),
               "s1": E5(nc.sbuf_tensor(f"{tag}_s15", [128, 1], F32)),
               "s2": E5(nc.sbuf_tensor(f"{tag}_s25", [128, 1], F32)),
               "mmean": E5(nc.sbuf_tensor(f"{tag}_mm5", [128, 1], F32)),
               "vvar": E5(nc.sbuf_tensor(f"{tag}_vv5", [128, 1], F32)),
               "rstd": E5(nc.sbuf_tensor(f"{tag}_rs5", [128, 1], F32))}
        epsb = E5(nc.sbuf_tensor(f"{tag}_eps5", [128, 1], F32))

        s = Sched(nc)
        s.bnd(NE2 * PAD2 - 1)
        dv, ac = s.v, s.a
        xld = Ctr(nc, "xld")
        gg = Ctr(nc, "gg")
        stp = [Ctr(nc, "sp0"), Ctr(nc, "sp1")]

        s.op("vector", lambda e: e.memset(epsb[:], 1e-5))
        h_user = {}
        y_prev = 0
        for tt in range(NTT):
            yb = ytile[tt % 2]
            mb = xmt[tt % 2]
            xb = x3t[tt % 2]
            hb = ht[tt % 2]
            f = fb[tt % 2]
            ob = ob5[tt % 2]
            s.op("sync", lambda e, tt=tt, mb=mb: e.dma_start(
                mb[:], xm_in.ap()[tt * 128:(tt + 1) * 128, :]),
                wait=[(dv, h_user.get(tt - 2, 0))], inc=xld, inck=16, dma=True)
            s.op("sync", lambda e, tt=tt, xb=xb: e.dma_start(
                xb[:], x3.ap()[tt * 128:(tt + 1) * 128, :]), inc=xld, inck=16, dma=True)
            s.op("sync", lambda e, tt=tt, hb=hb: e.dma_start(
                hb[:], h_in.ap()[tt * 128:(tt + 1) * 128, :]), inc=xld, inck=16, dma=True)
            xrd = xld.n
            s.op("gpsimd", lambda e, tt=tt: e.tensor_copy(slot_i[:], spc[:, tt:tt + 1]),
                 wait=[(dv, h_user.get(tt - 2, 0)), (gg, y_prev)])
            cast_done = s.g.n
            s.op("vector", lambda e, yb=yb: e.memset(yb[:], 0.0),
                 wait=[(dv, h_user.get(tt - 2, 0))])
            ym = dv.n
            s.op("gpsimd", lambda e, yb=yb: e.indirect_dma_start(
                yb[:], None, yall.ap(), bass.IndirectOffsetOnAxis(slot_i[:], 0),
                bounds_check=s._breg[s.bnd(NE2 * PAD2 - 1)], oob_is_err=False),
                wait=[(dv, ym), (s.g, cast_done)], inc=gg, inck=16, dma=True)
            y_done = gg.n
            y_prev = gg.n
            s.op("vector", lambda e, tt=tt: e.tensor_scalar(
                wxcol[:], wyc[:, tt:tt + 1], -1.0, 1.0, op0=ALU.mult, op1=ALU.add),
                wait=[(xld, xrd), (gg, y_done)])
            s.op("vector", lambda e, tt=tt, f=f, yb=yb: e.tensor_scalar(
                f[:], yb[:], wyc[:, tt:tt + 1], None, op0=ALU.mult))
            s.op("vector", lambda e, f=f, mb=mb: e.scalar_tensor_tensor(
                f[:], mb[:], wxcol[:, 0:1], f[:], op0=ALU.mult, op1=ALU.add))
            s.op("vector", lambda e, f=f, xb=xb: e.tensor_tensor(f[:], f[:], xb[:], ALU.add))
            s.op("vector", lambda e, f=f, hb=hb: e.tensor_tensor(f[:], f[:], hb[:], ALU.add))
            s.op("vector", lambda e: e.memset(tmp["s1"][0:1, 0:1], 0.0),
                 wait=[(stp[tt % 2], stp[tt % 2].n)])
            f_ready = dv.n
            out_done = emit_ln_tile(s, tmp, f, ob, epsb, f_ready)
            h_user[tt] = dv.n
            s.op("sync", lambda e, tt=tt, ob=ob: e.dma_start(
                out.ap()[tt * 128:(tt + 1) * 128, :], ob[:]),
                wait=[(dv, out_done)], inc=stp[tt % 2], inck=16, dma=True)
        s.op("sync", lambda e, a=stp[0].n: e.wait_ge(stp[0].sem, a))
        s.op("sync", lambda e, a=stp[1].n: e.wait_ge(stp[1].sem, a))
        s.flush()


def rotary_tables():
    t, dim = T, 64
    thetas = 10000.0 ** (-np.arange(0, dim, 2, dtype=np.float32) / dim)
    ticks = np.outer(np.arange(t, dtype=np.float32), thetas)
    ticks = np.tile(ticks, (1, 2))
    cos = np.cos(ticks).astype(np.float32)
    sin = np.sin(ticks).astype(np.float32)
    cosq = np.tile(cos, (1, 16)).copy()
    sinq = np.tile(sin, (1, 16)).copy()
    return cosq, sinq


def build_phase_d_nc(parts=("attn", "moe"), dbg_keys=()):
    dbg = None
    nc = bass.Bass("TRN2", target_bir_lowering=False)
    x3 = nc.dram_tensor("x3", [T, D], F32, kind="ExternalInput")
    wqd = nc.dram_tensor("lf_wq", [D, 1024], F32, kind="ExternalInput")
    wkd = nc.dram_tensor("lf_wk", [D, 256], F32, kind="ExternalInput")
    wvd = nc.dram_tensor("lf_wv", [D, 256], F32, kind="ExternalInput")
    wod = nc.dram_tensor("lf_wo", [1024, D], F32, kind="ExternalInput")
    g1bc = nc.dram_tensor("lf_g1bc", [128, D], F32, kind="ExternalInput")
    g2bc = nc.dram_tensor("lf_g2bc", [128, D], F32, kind="ExternalInput")
    cosq = nc.dram_tensor("c_cosq", [T, 1024], F32, kind="ExternalInput")
    sinq = nc.dram_tensor("c_sinq", [T, 1024], F32, kind="ExternalInput")
    rwT = nc.dram_tensor("lf_rw", [D, 8], F32, kind="ExternalInput")
    rbbc = nc.dram_tensor("lf_rbbc", [128, 8], F32, kind="ExternalInput")
    w1 = nc.dram_tensor("lf_w1", [1, D, HID2], F32, kind="ExternalInput")
    w2 = nc.dram_tensor("lf_w2", [1, D, HID2], F32, kind="ExternalInput")
    w3 = nc.dram_tensor("lf_w3", [1, HID2, D], F32, kind="ExternalInput")
    cmask = nc.dram_tensor("c_coremask", [8, 1], F32, kind="ExternalInput")
    if dbg_keys:
        shapes = {"qpark": [1024, T], "kpark": [256, T], "vpark": [T, 256],
                  "oT": [1024, T]}
        dbg = {k: nc.dram_tensor("dbg_" + k, shapes[k], BF16, kind="ExternalOutput")
               for k in dbg_keys}
    if "moe" in parts:
        out = nc.dram_tensor("out", [T, D], F32, kind="ExternalOutput")
    if "attn" in parts and "moe" in parts:
        h_out = nc.dram_tensor("h", [T, D], F32, kind="Internal")
        xm_out = nc.dram_tensor("xm", [T, D], F32, kind="Internal")
        xmbf = nc.dram_tensor("xmbf", [T, D], BF16, kind="Internal")
    elif "attn" in parts:
        h_out = nc.dram_tensor("h", [T, D], F32, kind="ExternalOutput")
        xm_out = nc.dram_tensor("xm", [T, D], F32, kind="ExternalOutput")
        xmbf = nc.dram_tensor("xmbf", [T, D], BF16, kind="Internal")
    else:
        h_out = nc.dram_tensor("h", [T, D], F32, kind="ExternalInput")
        xm_out = nc.dram_tensor("xm", [T, D], F32, kind="ExternalInput")
        xmbf_in = nc.dram_tensor("xmbf_ext", [T, D], F32, kind="ExternalInput")
        xmbf = nc.dram_tensor("xmbf", [T, D], BF16, kind="Internal")
    with ExitStack() as ctx:
        g = Common(nc, ctx)
        g.load(nc)
        if "attn" in parts:
            emit_phase_d_attn(nc, ctx, g, x3, wqd, wkd, wvd, wod, g1bc, g2bc, cosq, sinq,
                              h_out, xm_out, xmbf, tag="d", dbg=dbg)
        if "moe" in parts:
            if "attn" not in parts:
                s = Sched(nc)
                cst = Ctr(nc, "cst")
                s.op("gpsimd", lambda e: e.dma_start(xmbf.ap(), xmbf_in.ap()),
                     inc=cst, inck=16, dma=True)
                s.op("gpsimd", lambda e: e.wait_ge(cst.sem, cst.n), dma=True)
                s.flush()
            emit_phase_d_moe(nc, ctx, g, x3, h_out, xm_out, xmbf, rwT, rbbc, w1, w2, w3,
                             cmask, out, tag="e")
    return nc


def phase_d_inmaps(x3_full, z, parts=("attn", "moe"), extra=None):
    consts = host_consts()
    cosq, sinq = rotary_tables()
    in_maps = []
    for c in range(NCORES):
        m = dict(consts)
        m["x3"] = np.ascontiguousarray(x3_full[c])
        m["lf_wq"] = np.ascontiguousarray(z["L1_ffn_wq"])
        m["lf_wk"] = np.ascontiguousarray(z["L1_ffn_wk"])
        m["lf_wv"] = np.ascontiguousarray(z["L1_ffn_wv"])
        m["lf_wo"] = np.ascontiguousarray(z["L1_ffn_wo"])
        m["lf_g1bc"] = np.ascontiguousarray(
            np.broadcast_to(z["L1_ffn_g1"], (128, D)).astype(np.float32))
        m["lf_g2bc"] = np.ascontiguousarray(
            np.broadcast_to(z["L1_ffn_g2"], (128, D)).astype(np.float32))
        m["c_cosq"] = cosq
        m["c_sinq"] = sinq
        m["lf_rw"] = np.ascontiguousarray(z["L1_ffn_router_w"])
        m["lf_rbbc"] = np.ascontiguousarray(
            np.broadcast_to(z["L1_ffn_router_b"], (128, 8)).astype(np.float32))
        m["lf_w1"] = np.ascontiguousarray(z["L1_ffn_w1"][c:c + 1])
        m["lf_w2"] = np.ascontiguousarray(z["L1_ffn_w2"][c:c + 1])
        m["lf_w3"] = np.ascontiguousarray(z["L1_ffn_w3"][c:c + 1])
        m["c_coremask"] = (np.arange(8) < c).astype(np.float32).reshape(8, 1)
        m["c_base"] = np.full((1, 1), c * T, np.float32)
        if extra is not None:
            for k, v in extra.items():
                m[k] = np.ascontiguousarray(v[c])
        in_maps.append(m)
    return in_maps


def run_phase_d(x3_full, z, parts=("attn", "moe"), extra=None):
    nc = build_phase_d_nc(parts)
    in_maps = phase_d_inmaps(x3_full, z, parts, extra)
    res = run_bass_kernel_spmd(nc, in_maps, core_ids=list(range(NCORES)))
    outs = {}
    names = (["out"] if "moe" in parts else []) +             (["h", "xm"] if ("attn" in parts and "moe" not in parts) else [])
    for n in names:
        outs[n] = np.stack([res.results[c][n] for c in range(NCORES)])
    return outs




def build_phase_b_nc():
    nc = bass.Bass("TRN2", target_bir_lowering=False)
    x1 = nc.dram_tensor("x1", [T, D], F32, kind="ExternalInput")
    gwT = nc.dram_tensor("l0_gwT", [D, 16], F32, kind="ExternalInput")
    w1 = nc.dram_tensor("l0_w1", [EPC0, D, DFF0], F32, kind="ExternalInput")
    w2 = nc.dram_tensor("l0_w2", [EPC0, DFF0, D], F32, kind="ExternalInput")
    cmask = nc.dram_tensor("c_coremask", [8, 1], F32, kind="ExternalInput")
    x2 = nc.dram_tensor("x2", [T, D], F32, kind="ExternalOutput")
    with ExitStack() as ctx:
        g = Common(nc, ctx)
        g.load(nc)
        emit_phase_b(nc, ctx, g, x1, gwT, w1, w2, cmask, x2, tag="b")
    return nc


def phase_b_inmaps(x1_full, gate_w, w1_full, w2_full):
    consts = host_consts()
    in_maps = []
    for c in range(NCORES):
        m = dict(consts)
        m["x1"] = np.ascontiguousarray(x1_full[c])
        m["l0_gwT"] = np.ascontiguousarray(gate_w.T)
        m["l0_w1"] = np.ascontiguousarray(w1_full[EPC0 * c:EPC0 * (c + 1)])
        m["l0_w2"] = np.ascontiguousarray(w2_full[EPC0 * c:EPC0 * (c + 1)])
        m["c_coremask"] = (np.arange(8) < c).astype(np.float32).reshape(8, 1)
        m["c_base"] = np.full((1, 1), c * T, np.float32)
        in_maps.append(m)
    return in_maps


def run_phase_b(x1_full, gate_w, w1_full, w2_full):
    nc = build_phase_b_nc()
    in_maps = phase_b_inmaps(x1_full, gate_w, w1_full, w2_full)
    res = run_bass_kernel_spmd(nc, in_maps, core_ids=list(range(NCORES)))
    return np.stack([res.results[c]["x2"] for c in range(NCORES)])


def build_phase_c_nc():
    nc = bass.Bass("TRN2", target_bir_lowering=False)
    x2 = nc.dram_tensor("x2", [T, D], F32, kind="ExternalInput")
    wqT = nc.dram_tensor("l1_wqT", [D, E1], F32, kind="ExternalInput")
    wkT = nc.dram_tensor("l1_wkT", [D, E1], F32, kind="ExternalInput")
    wvT = nc.dram_tensor("l1_wvT", [D, E1], F32, kind="ExternalInput")
    woT = nc.dram_tensor("l1_woT", [E1, D], F32, kind="ExternalInput")
    lq1 = nc.dram_tensor("l1_lq1", [128, NH1], F32, kind="ExternalInput")
    lk1 = nc.dram_tensor("l1_lk1", [128, NH1], F32, kind="ExternalInput")
    lq2 = nc.dram_tensor("l1_lq2", [128, NH1], F32, kind="ExternalInput")
    lk2 = nc.dram_tensor("l1_lk2", [128, NH1], F32, kind="ExternalInput")
    rms_bc = nc.dram_tensor("l1_rmsbc", [128, 256], F32, kind="ExternalInput")
    x3 = nc.dram_tensor("x3", [T, D], F32, kind="ExternalOutput")
    with ExitStack() as ctx:
        g = Common(nc, ctx)
        g.load(nc)
        emit_phase_c(nc, ctx, g, x2, wqT, wkT, wvT, woT, lq1, lk1, lq2, lk2, rms_bc,
                     x3, tag="c")
    return nc


def phase_c_inmaps(x2_full, Wq, Wk, Wv, Wo, lq1, lk1, lq2, lk2, rms_scale):
    consts = host_consts()
    in_maps = []
    for c in range(NCORES):
        m = dict(consts)
        m["x2"] = np.ascontiguousarray(x2_full[c])
        m["l1_wqT"] = np.ascontiguousarray(Wq.T)
        m["l1_wkT"] = np.ascontiguousarray(Wk.T)
        m["l1_wvT"] = np.ascontiguousarray(Wv.T)
        m["l1_woT"] = np.ascontiguousarray(Wo.T)
        m["l1_lq1"] = np.ascontiguousarray(lq1.T)
        m["l1_lk1"] = np.ascontiguousarray(lk1.T)
        m["l1_lq2"] = np.ascontiguousarray(lq2.T)
        m["l1_lk2"] = np.ascontiguousarray(lk2.T)
        m["l1_rmsbc"] = np.ascontiguousarray(np.broadcast_to(rms_scale, (128, 256)).astype(np.float32))
        m["c_base"] = np.full((1, 1), c * T, np.float32)
        in_maps.append(m)
    return in_maps


def run_phase_c(x2_full, Wq, Wk, Wv, Wo, lq1, lk1, lq2, lk2, rms_scale):
    nc = build_phase_c_nc()
    in_maps = phase_c_inmaps(x2_full, Wq, Wk, Wv, Wo, lq1, lk1, lq2, lk2, rms_scale)
    res = run_bass_kernel_spmd(nc, in_maps, core_ids=list(range(NCORES)))
    return np.stack([res.results[c]["x3"] for c in range(NCORES)])


def build_phase_a_nc():
    nc = bass.Bass("TRN2", target_bir_lowering=False)
    x0 = nc.dram_tensor("x0", [T, D], F32, kind="ExternalInput")
    wqT = nc.dram_tensor("l0_wqT", [D, D], F32, kind="ExternalInput")
    wkT = nc.dram_tensor("l0_wkT", [D, D], F32, kind="ExternalInput")
    wvT = nc.dram_tensor("l0_wvT", [D, D], F32, kind="ExternalInput")
    woT = nc.dram_tensor("l0_woT", [D, D], F32, kind="ExternalInput")
    x1 = nc.dram_tensor("x1", [T, D], F32, kind="ExternalOutput")
    with ExitStack() as ctx:
        g = Common(nc, ctx)
        g.load(nc)
        emit_phase_a(nc, ctx, g, x0, wqT, wkT, wvT, woT, x1, tag="a")
    return nc


def run_phase_a(x_full, wq, wk, wv, wo):
    nc = build_phase_a_nc()
    consts = host_consts()
    in_maps = []
    for c in range(NCORES):
        m = dict(consts)
        m["x0"] = np.ascontiguousarray(x_full[c])
        m["l0_wqT"] = np.ascontiguousarray(wq.T)
        m["l0_wkT"] = np.ascontiguousarray(wk.T)
        m["l0_wvT"] = np.ascontiguousarray(wv.T)
        m["l0_woT"] = np.ascontiguousarray(wo.T)
        m["c_base"] = np.full((1, 1), c * T, np.float32)
        in_maps.append(m)
    res = run_bass_kernel_spmd(nc, in_maps, core_ids=list(range(NCORES)))
    return np.stack([res.results[c]["x1"] for c in range(NCORES)])


def build_fused_nc():
    nc = bass.Bass("TRN2", target_bir_lowering=False)
    x0 = nc.dram_tensor("x0", [T, D], F32, kind="ExternalInput")
    wqT = nc.dram_tensor("l0_wqT", [D, D], F32, kind="ExternalInput")
    wkT = nc.dram_tensor("l0_wkT", [D, D], F32, kind="ExternalInput")
    wvT = nc.dram_tensor("l0_wvT", [D, D], F32, kind="ExternalInput")
    woT = nc.dram_tensor("l0_woT", [D, D], F32, kind="ExternalInput")
    gwT = nc.dram_tensor("l0_gwT", [D, 16], F32, kind="ExternalInput")
    w1a = nc.dram_tensor("l0_w1", [EPC0, D, DFF0], F32, kind="ExternalInput")
    w2a = nc.dram_tensor("l0_w2", [EPC0, DFF0, D], F32, kind="ExternalInput")
    cwqT = nc.dram_tensor("l1_wqT", [D, E1], F32, kind="ExternalInput")
    cwkT = nc.dram_tensor("l1_wkT", [D, E1], F32, kind="ExternalInput")
    cwvT = nc.dram_tensor("l1_wvT", [D, E1], F32, kind="ExternalInput")
    cwoT = nc.dram_tensor("l1_woT", [E1, D], F32, kind="ExternalInput")
    lq1 = nc.dram_tensor("l1_lq1", [128, NH1], F32, kind="ExternalInput")
    lk1 = nc.dram_tensor("l1_lk1", [128, NH1], F32, kind="ExternalInput")
    lq2 = nc.dram_tensor("l1_lq2", [128, NH1], F32, kind="ExternalInput")
    lk2 = nc.dram_tensor("l1_lk2", [128, NH1], F32, kind="ExternalInput")
    rms_bc = nc.dram_tensor("l1_rmsbc", [128, 256], F32, kind="ExternalInput")
    wqd = nc.dram_tensor("lf_wq", [D, 1024], F32, kind="ExternalInput")
    wkd = nc.dram_tensor("lf_wk", [D, 256], F32, kind="ExternalInput")
    wvd = nc.dram_tensor("lf_wv", [D, 256], F32, kind="ExternalInput")
    wod = nc.dram_tensor("lf_wo", [1024, D], F32, kind="ExternalInput")
    g1bc = nc.dram_tensor("lf_g1bc", [128, D], F32, kind="ExternalInput")
    g2bc = nc.dram_tensor("lf_g2bc", [128, D], F32, kind="ExternalInput")
    cosq = nc.dram_tensor("c_cosq", [T, 1024], F32, kind="ExternalInput")
    sinq = nc.dram_tensor("c_sinq", [T, 1024], F32, kind="ExternalInput")
    rwT = nc.dram_tensor("lf_rw", [D, 8], F32, kind="ExternalInput")
    rbbc = nc.dram_tensor("lf_rbbc", [128, 8], F32, kind="ExternalInput")
    w1f = nc.dram_tensor("lf_w1", [1, D, HID2], F32, kind="ExternalInput")
    w2f = nc.dram_tensor("lf_w2", [1, D, HID2], F32, kind="ExternalInput")
    w3f = nc.dram_tensor("lf_w3", [1, HID2, D], F32, kind="ExternalInput")
    cmask = nc.dram_tensor("c_coremask", [8, 1], F32, kind="ExternalInput")
    x1 = nc.dram_tensor("x1", [T, D], F32, kind="Internal")
    x2 = nc.dram_tensor("x2", [T, D], F32, kind="Internal")
    x3 = nc.dram_tensor("x3", [T, D], F32, kind="Internal")
    h_out = nc.dram_tensor("h", [T, D], F32, kind="Internal")
    xm_out = nc.dram_tensor("xm", [T, D], F32, kind="Internal")
    xmbf = nc.dram_tensor("xmbf", [T, D], BF16, kind="Internal")
    out = nc.dram_tensor("out", [T, D], F32, kind="ExternalOutput")

    with ExitStack() as cg:
        g = Common(nc, cg)
        g.load(nc)
        mark = len(Ctr.created)
        with ExitStack() as c1:
            emit_phase_a(nc, c1, g, x0, wqT, wkT, wvT, woT, x1, tag="a")
        recycle_sems(nc, mark)
        mark = len(Ctr.created)
        with ExitStack() as c2:
            emit_phase_b(nc, c2, g, x1, gwT, w1a, w2a, cmask, x2, tag="b")
        recycle_sems(nc, mark)
        mark = len(Ctr.created)
        with ExitStack() as c3:
            emit_phase_c(nc, c3, g, x2, cwqT, cwkT, cwvT, cwoT, lq1, lk1, lq2, lk2,
                         rms_bc, x3, tag="c")
        recycle_sems(nc, mark)
        mark = len(Ctr.created)
        with ExitStack() as c4:
            emit_phase_d_attn(nc, c4, g, x3, wqd, wkd, wvd, wod, g1bc, g2bc, cosq, sinq,
                              h_out, xm_out, xmbf, tag="d")
        recycle_sems(nc, mark)
        mark = len(Ctr.created)
        with ExitStack() as c5:
            emit_phase_d_moe(nc, c5, g, x3, h_out, xm_out, xmbf, rwT, rbbc, w1f, w2f, w3f,
                             cmask, out, tag="e")
        recycle_sems(nc, mark)
    return nc


def fused_inmaps(inputs):
    consts = host_consts()
    cosq, sinq = rotary_tables()
    z = inputs
    in_maps = []
    for c in range(NCORES):
        m = dict(consts)
        m["x0"] = np.ascontiguousarray(np.asarray(z["L0_attn_in_features"][c], np.float32))
        m["l0_wqT"] = np.ascontiguousarray(np.asarray(z["L0_attn_w_q"]).T)
        m["l0_wkT"] = np.ascontiguousarray(np.asarray(z["L0_attn_w_k"]).T)
        m["l0_wvT"] = np.ascontiguousarray(np.asarray(z["L0_attn_w_v"]).T)
        m["l0_woT"] = np.ascontiguousarray(np.asarray(z["L0_attn_w_o"]).T)
        m["l0_gwT"] = np.ascontiguousarray(np.asarray(z["L0_ffn_gate_w"]).T)
        m["l0_w1"] = np.ascontiguousarray(z["L0_ffn_w1"][EPC0 * c:EPC0 * (c + 1)])
        m["l0_w2"] = np.ascontiguousarray(z["L0_ffn_w2"][EPC0 * c:EPC0 * (c + 1)])
        m["l1_wqT"] = np.ascontiguousarray(np.asarray(z["L1_attn_Wq"]).T)
        m["l1_wkT"] = np.ascontiguousarray(np.asarray(z["L1_attn_Wk"]).T)
        m["l1_wvT"] = np.ascontiguousarray(np.asarray(z["L1_attn_Wv"]).T)
        m["l1_woT"] = np.ascontiguousarray(np.asarray(z["L1_attn_Wo"]).T)
        m["l1_lq1"] = np.ascontiguousarray(np.asarray(z["L1_attn_lambda_q1"]).T)
        m["l1_lk1"] = np.ascontiguousarray(np.asarray(z["L1_attn_lambda_k1"]).T)
        m["l1_lq2"] = np.ascontiguousarray(np.asarray(z["L1_attn_lambda_q2"]).T)
        m["l1_lk2"] = np.ascontiguousarray(np.asarray(z["L1_attn_lambda_k2"]).T)
        m["l1_rmsbc"] = np.ascontiguousarray(
            np.broadcast_to(np.asarray(z["L1_attn_rms_scale"]), (128, 256)).astype(np.float32))
        m["lf_wq"] = np.ascontiguousarray(z["L1_ffn_wq"])
        m["lf_wk"] = np.ascontiguousarray(z["L1_ffn_wk"])
        m["lf_wv"] = np.ascontiguousarray(z["L1_ffn_wv"])
        m["lf_wo"] = np.ascontiguousarray(z["L1_ffn_wo"])
        m["lf_g1bc"] = np.ascontiguousarray(
            np.broadcast_to(np.asarray(z["L1_ffn_g1"]), (128, D)).astype(np.float32))
        m["lf_g2bc"] = np.ascontiguousarray(
            np.broadcast_to(np.asarray(z["L1_ffn_g2"]), (128, D)).astype(np.float32))
        m["c_cosq"] = cosq
        m["c_sinq"] = sinq
        m["lf_rw"] = np.ascontiguousarray(z["L1_ffn_router_w"])
        m["lf_rbbc"] = np.ascontiguousarray(
            np.broadcast_to(np.asarray(z["L1_ffn_router_b"]), (128, 8)).astype(np.float32))
        m["lf_w1"] = np.ascontiguousarray(z["L1_ffn_w1"][c:c + 1])
        m["lf_w2"] = np.ascontiguousarray(z["L1_ffn_w2"][c:c + 1])
        m["lf_w3"] = np.ascontiguousarray(z["L1_ffn_w3"][c:c + 1])
        m["c_coremask"] = (np.arange(8) < c).astype(np.float32).reshape(8, 1)
        m["c_base"] = np.full((1, 1), c * T, np.float32)
        in_maps.append(m)
    return in_maps


def kernel(**inputs):
    nc = build_fused_nc()
    in_maps = fused_inmaps(inputs)
    res = run_bass_kernel_spmd(nc, in_maps, core_ids=list(range(NCORES)))
    return np.ascontiguousarray(
        np.stack([res.results[c]["out"] for c in range(NCORES)]).astype(np.float32))


def kernel_unfused(**inputs):
    x = np.asarray(inputs["L0_attn_in_features"], dtype=np.float32)
    x1 = run_phase_a(x, inputs["L0_attn_w_q"], inputs["L0_attn_w_k"],
                     inputs["L0_attn_w_v"], inputs["L0_attn_w_o"])
    x2 = run_phase_b(x1, inputs["L0_ffn_gate_w"], inputs["L0_ffn_w1"],
                     inputs["L0_ffn_w2"])
    x3 = run_phase_c(x2, inputs["L1_attn_Wq"], inputs["L1_attn_Wk"],
                     inputs["L1_attn_Wv"], inputs["L1_attn_Wo"],
                     inputs["L1_attn_lambda_q1"], inputs["L1_attn_lambda_k1"],
                     inputs["L1_attn_lambda_q2"], inputs["L1_attn_lambda_k2"],
                     inputs["L1_attn_rms_scale"])
    outs = run_phase_d(x3, inputs, parts=("attn", "moe"))
    return np.ascontiguousarray(outs["out"].astype(np.float32))
